# version 03
# baseline (speedup 1.0000x reference)
import math
from contextlib import ExitStack

import numpy as np
import concourse.bass as bass
import concourse.mybir as mybir
from concourse.bass_utils import run_bass_kernel_spmd

F32 = mybir.dt.float32
BF16 = mybir.dt.bfloat16
AF = mybir.ActivationFunctionType
ALU = mybir.AluOpType
AX = mybir.AxisListType.X

NCORES = 8
S = 4096
D = 1024
NT = S // 128
MAGIC = 1.5 * 2 ** 23
TWO_PI = 2.0 * math.pi


class Prog:

    ENGINES = ("sync", "tensor", "vector", "scalar", "gpsimd")

    def __init__(self):
        self.nc = bass.Bass("TRN2", target_bir_lowering=False)
        self.es = ExitStack()
        self._cms = {}
        self._order = []
        self.steps = [[{e: [] for e in self.ENGINES}, self.ENGINES]]
        self._built = False
        self.ident = None
        self.ident_bf = None
        self.ones1 = None

    def sbuf(self, shape, dtype=F32):
        cm = self.nc.sbuf_tensor(shape, dtype)
        h = cm.__enter__()
        self._cms[id(h)] = cm
        self._order.append(id(h))
        return h

    def psum(self, shape, dtype=F32):
        cm = self.nc.psum_tensor(shape, dtype)
        h = cm.__enter__()
        self._cms[id(h)] = cm
        self._order.append(id(h))
        return h

    def free(self, h):
        cm = self._cms.pop(id(h))
        self._order.remove(id(h))
        cm.__exit__(None, None, None)

    def dram_in(self, name, shape, dtype=F32):
        return self.nc.dram_tensor(name, list(shape), dtype, kind="ExternalInput").ap()

    def dram_out(self, name, shape, dtype=F32):
        return self.nc.dram_tensor(name, list(shape), dtype, kind="ExternalOutput").ap()

    def dram_tmp(self, name, shape, dtype=F32):
        return self.nc.dram_tensor(name, list(shape), dtype, kind="Internal").ap()

    def step(self, engines=None):
        ops = self.steps[-1][0]
        used = {e for e in self.ENGINES if ops[e]}
        self.steps[-1][1] = tuple(e for e in self.ENGINES if e in used) or ("sync",)
        self.steps.append([{e: [] for e in self.ENGINES}, self.ENGINES])

    def on(self, engine, fn):
        self.steps[-1][0][engine].append(("op", fn))

    def dma(self, out, in_, engine="sync", **kw):
        self.steps[-1][0][engine].append(("dma", (out, in_, kw)))

    def psum_bf_pool(self, n=4):
        big = self.psum([128, 128 * n], BF16)
        return [big[:, j * 128:(j + 1) * 128] for j in range(n)]

    def constcol(self, value, dtype=F32):
        t = self.sbuf([128, 1], dtype)
        self.on("gpsimd", lambda g: g.memset(t[:], value))
        return t

    def init_consts(self):
        if self.ident is not None:
            return
        self.ident = make_identity(self, F32)
        self.ident_bf = make_identity(self, BF16)
        ones1 = self.sbuf([1, 1], F32)
        self.on("gpsimd", lambda g: g.memset(ones1[:], 1.0))
        self.ones1 = ones1
        self.stage = self.sbuf([128, 2048], F32)
        self.stage_bf = self.sbuf([128, 1024], BF16)
        self.step()

    def mark(self):
        return len(self._order)

    def free_to(self, m):
        for hid in list(reversed(self._order[m:])):
            cm = self._cms.pop(hid)
            self._order.remove(hid)
            cm.__exit__(None, None, None)

    def build(self):
        assert not self._built
        self._built = True
        ops = self.steps[-1][0]
        used = {e for e in self.ENGINES if ops[e]}
        self.steps[-1][1] = tuple(e for e in self.ENGINES if e in used) or ("sync",)
        nc = self.nc
        bar = self.es.enter_context(nc.semaphore("bar"))
        dsem = {e: self.es.enter_context(nc.semaphore("d_" + e)) for e in self.ENGINES}
        dtot = {e: 0 for e in self.ENGINES}
        block = self.es.enter_context(nc.Block())
        steps = self.steps
        nsteps = len(steps)

        ssem = {e: self.es.enter_context(nc.semaphore("s_" + e)) for e in self.ENGINES}
        _PASS = {"wait_ge", "sem_inc", "wait_op", "maybe_wait_ge", "nop", "drain"}

        class _Rec:

            def __init__(self, eng, sem, is_pe):
                self._eng = eng
                self._sem = sem
                self._pe = is_pe
                self.cnt = 0
                self.last = None

            def __getattr__(self, name):
                a = getattr(self._eng, name)
                if callable(a) and name not in _PASS:
                    def w(*args, **kw):
                        if not self._pe and self.cnt > 0:
                            self._eng.wait_ge(self._sem, self.cnt)
                        r = a(*args, **kw)
                        if isinstance(r, bass.BassInstruction):
                            if self._pe:
                                self.last = r
                            else:
                                r.then_inc(self._sem, 1)
                                self.cnt += 1
                        return r
                    return w
                return a

        cum = [0]
        for ops, engs in steps:
            cum.append(cum[-1] + len(engs))

        def mk(ename):
            is_pe = ename == "tensor"

            def body(eng):
                rec = _Rec(eng, ssem[ename], is_pe)
                for k in range(nsteps):
                    ops, engs = steps[k]
                    if ename not in engs:
                        continue
                    if k > 0:
                        eng.wait_ge(bar, cum[k])
                    n0 = rec.cnt
                    rec.last = None
                    had_dma = False
                    for kind, payload in ops[ename]:
                        if kind == "op":
                            payload(rec)
                        else:
                            out, in_, kw = payload
                            eng.dma_start(out=out, in_=in_, **kw).then_inc(dsem[ename], 16)
                            dtot[ename] += 16
                            had_dma = True
                    if is_pe and rec.last is not None:
                        rec.last.then_inc(ssem[ename], 1)
                        rec.cnt += 1
                    if had_dma:
                        eng.wait_ge(dsem[ename], dtot[ename])
                    if rec.cnt > n0:
                        eng.wait_ge(ssem[ename], rec.cnt)
                    eng.sem_inc(bar, 1)
            return body

        for ename in self.ENGINES:
            getattr(block, ename)(mk(ename))
        for hid in reversed(self._order):
            self._cms.pop(hid).__exit__(None, None, None)
        self._order.clear()
        self.es.close()
        return nc


def make_identity(pr, dtype=F32, n=128):
    t = pr.sbuf([n, n], dtype)
    pr.on("gpsimd", lambda g: g.memset(t[:], 0.0))
    pr.step()
    pr.on("gpsimd", lambda g: g.affine_select(out=t[:], in_=t[:], compare_op=ALU.not_equal,
                                              fill=1.0, base=0, pattern=[[-1, n]],
                                              channel_multiplier=1))
    pr.step()
    return t


def make_tri(pr, n=128, strict=True, lower=True, dtype=F32):
    t = pr.sbuf([n, n], dtype)
    pr.on("gpsimd", lambda g: g.memset(t[:], 1.0))
    pr.step()
    pr.on("gpsimd", lambda g: g.affine_select(out=t[:], in_=t[:], compare_op=ALU.is_gt,
                                              fill=0.0, base=0, pattern=[[-1, n]],
                                              channel_multiplier=1))
    pr.step()
    if strict and lower:
        return t
    if not strict and lower:
        pr.on("vector", lambda v: v.tensor_tensor(t[:], t[:], pr.ident[0:n, 0:n], ALU.add))
        pr.step()
        return t
    if not strict and not lower:
        pr.on("vector", lambda v: v.tensor_scalar(t[:], t[:], -1.0, 1.0, ALU.mult, ALU.add))
        pr.step()
        return t
    def f(v):
        v.tensor_scalar(t[:], t[:], -1.0, 1.0, ALU.mult, ALU.add)
        v.tensor_tensor(t[:], t[:], pr.ident[0:n, 0:n], ALU.subtract)
    pr.on("vector", f)
    pr.step()
    return t


class Loader:

    def __init__(self, pr):
        self.pr = pr
        self.stage = pr.stage

    def close(self):
        pass

    def load_bf(self, hbm, rows, cols, out=None, out_off=0, stride=None):
        pr = self.pr
        assert rows % 128 == 0
        if stride is None:
            stride = cols
        if out is None:
            out = pr.sbuf([128, out_off + (rows // 128) * stride], BF16)
        for i in range(rows // 128):
            for c0 in range(0, cols, 2048):
                cw = min(2048, cols - c0)
                pr.step()
                pr.dma(self.stage[:, 0:cw], hbm[i * 128:(i + 1) * 128, c0:c0 + cw])
                pr.step()
                pr.on("vector", lambda v, i=i, c0=c0, cw=cw, out=out:
                      v.tensor_copy(out[:, out_off + i * stride + c0:
                                        out_off + i * stride + c0 + cw],
                                    self.stage[:, 0:cw]))
        pr.step()
        return out

    def load_f32(self, hbm, rows, cols, out=None):
        pr = self.pr
        if out is None:
            out = pr.sbuf([max(rows, 1), cols], F32)
        pr.step()
        pr.dma(out[0:rows, 0:cols], hbm)
        pr.step()
        return out


def emit_ln_stats(pr, x_tile_fn, ntiles, d=D, eps=1e-5, bufs=None):
    if bufs is None:
        bufs = (pr.sbuf([128, ntiles], F32), pr.sbuf([128, ntiles], F32),
                pr.sbuf([128, ntiles], F32), pr.sbuf([128, ntiles], F32),
                pr.sbuf([128, d], F32))
    ssum, ssq, m, rstd, sq = bufs
    for i in range(ntiles):
        pr.on("vector", lambda v, i=i: v.reduce_sum(ssum[:, i:i + 1], x_tile_fn(i), axis=AX))
        pr.on("scalar", lambda sc, i=i: sc.activation(sq[:], x_tile_fn(i), AF.Square))
        pr.step()
        pr.on("vector", lambda v, i=i: v.reduce_sum(ssq[:, i:i + 1], sq[:], axis=AX))
        pr.step()

    def stats(v):
        inv = 1.0 / d
        v.tensor_scalar(m[:], ssum[:], inv, None, ALU.mult)
        v.tensor_scalar(ssq[:], ssq[:], inv, None, ALU.mult)
        v.tensor_tensor(ssum[:], m[:], m[:], ALU.mult)
        v.tensor_tensor(ssq[:], ssq[:], ssum[:], ALU.subtract)
        v.tensor_scalar(ssq[:], ssq[:], eps, None, ALU.add)
    pr.on("vector", stats)
    pr.step()
    pr.on("scalar", lambda sc: sc.activation(ssq[:], ssq[:], AF.Sqrt))
    pr.step()
    pr.on("vector", lambda v: v.reciprocal(rstd[:], ssq[:]))
    pr.step()
    return m, rstd, bufs


def build_l1f(pr=None, px='', x_h=None, y_h=None):
    standalone = pr is None
    if standalone:
        pr = Prog()
    if x_h is None:
        x_h = pr.dram_in(px + "x", [S, D])
    wr_h = pr.dram_in(px + "W_rT", [D, D])
    wk_h = pr.dram_in(px + "W_kT", [D, D])
    wv_h = pr.dram_in(px + "W_vT", [D, D])
    mur_h = pr.dram_in(px + "mu_r_cols", [128, 8])
    muk_h = pr.dram_in(px + "mu_k_cols", [128, 8])
    if y_h is None:
        y_h = pr.dram_out(px + "y", [S, D])
    pr.init_consts()

    ld = Loader(pr)
    wrT = ld.load_bf(wr_h, D, D)
    wkT = ld.load_bf(wk_h, D, D)
    wvT = ld.load_bf(wv_h, D, D)
    mur = ld.load_f32(mur_h, 128, 8)
    muk = ld.load_f32(muk_h, 128, 8)
    ld.close()
    pr.step()
    mur1 = pr.sbuf([128, 8], F32)
    muk1 = pr.sbuf([128, 8], F32)
    pr.on("vector", lambda v: v.tensor_scalar(mur1[:], mur[:], -1.0, 1.0, ALU.mult, ALU.add))
    pr.on("vector", lambda v: v.tensor_scalar(muk1[:], muk[:], -1.0, 1.0, ALU.mult, ALU.add))
    pr.step()

    TSL = 1024
    NSL = S // TSL
    NTT = TSL // 128
    TW = TSL + 1

    x_sl = pr.sbuf([128, NTT * D], F32)
    ln_tok = pr.sbuf([128, D], BF16)
    lnT = pr.sbuf([128, 8 * TW], BF16)
    halo = pr.sbuf([128, 8], BF16)
    xrT = pr.sbuf([128, 8 * TSL], BF16)
    xkT = pr.sbuf([128, 8 * TSL], BF16)
    r_T = pr.sbuf([128, 8 * TSL], BF16)
    kact_T = pr.sbuf([128, 8 * TSL], BF16)
    out_T = pr.sbuf([128, 8 * TSL], BF16)
    ps_bf = pr.psum_bf_pool(4)
    ps_a = pr.psum([128, 512], F32)
    ps_b = pr.psum([128, 512], F32)
    relu_tmp = pr.sbuf([128, 512], F32)
    ytile = pr.sbuf([128, D], F32)
    lnbufs = None
    pr.on("gpsimd", lambda g: g.memset(halo[:], 0.0))
    pr.step()

    for sl in range(NSL):
        for i in range(NTT):
            pr.dma(x_sl[:, i * D:(i + 1) * D],
                   x_h[(sl * NTT + i) * 128:(sl * NTT + i + 1) * 128, :])
        pr.step()
        m, rstd, lnbufs = emit_ln_stats(pr, lambda i: x_sl[:, i * D:(i + 1) * D], NTT,
                                        bufs=lnbufs)
        for i in range(NTT):
            pr.on("vector", lambda v, i=i: v.tensor_scalar(
                ln_tok[:], x_sl[:, i * D:(i + 1) * D], m[:, i:i + 1], rstd[:, i:i + 1],
                ALU.subtract, ALU.mult))
            pr.step()
            for j0 in range(0, 8, 4):
                def tr(te, j0=j0):
                    for j in range(4):
                        te.transpose(ps_bf[j][:], ln_tok[:, (j0 + j) * 128:(j0 + j + 1) * 128],
                                     pr.ident_bf[:])
                pr.on("tensor", tr)
                pr.step()

                def ev(v, i=i, j0=j0):
                    for j in range(4):
                        jj = j0 + j
                        v.tensor_copy(lnT[:, jj * TW + 1 + i * 128: jj * TW + 1 + (i + 1) * 128],
                                      ps_bf[j][:])
                pr.on("vector", ev)
                pr.step()
        def hal(v, sl=sl):
            for j in range(8):
                v.tensor_copy(lnT[:, j * TW:j * TW + 1], halo[:, j:j + 1])
        pr.on("vector", hal)
        pr.step()

        def sav(v):
            for j in range(8):
                v.tensor_copy(halo[:, j:j + 1], lnT[:, j * TW + TW - 1:j * TW + TW])
        pr.on("vector", sav)
        pr.step()

        def mix(v):
            for j in range(8):
                cur = lnT[:, j * TW + 1: j * TW + 1 + TSL]
                prv = lnT[:, j * TW: j * TW + TSL]
                sl_o = slice(j * TSL, (j + 1) * TSL)
                v.tensor_scalar(xrT[:, sl_o], prv, mur1[:, j:j + 1], None, ALU.mult)
                v.tensor_scalar(xkT[:, sl_o], prv, muk1[:, j:j + 1], None, ALU.mult)
                v.scalar_tensor_tensor(xrT[:, sl_o], cur, mur[:, j:j + 1], xrT[:, sl_o],
                                       ALU.mult, ALU.add)
                v.scalar_tensor_tensor(xkT[:, sl_o], cur, muk[:, j:j + 1], xkT[:, sl_o],
                                       ALU.mult, ALU.add)
        pr.on("vector", mix)
        pr.step()

        for dq in range(8):
            for ts in range(TSL // 512):
                def mm(te, dq=dq, ts=ts):
                    for c in range(8):
                        te.matmul(ps_a[:], wrT[:, c * D + dq * 128: c * D + dq * 128 + 128],
                                  xrT[:, c * TSL + ts * 512: c * TSL + ts * 512 + 512],
                                  start=(c == 0), stop=(c == 7))
                    for c in range(8):
                        te.matmul(ps_b[:], wkT[:, c * D + dq * 128: c * D + dq * 128 + 128],
                                  xkT[:, c * TSL + ts * 512: c * TSL + ts * 512 + 512],
                                  start=(c == 0), stop=(c == 7))
                pr.on("tensor", mm)
                pr.step()
                sl_o = slice(dq * TSL + ts * 512, dq * TSL + ts * 512 + 512)
                pr.on("scalar", lambda sc, sl_o=sl_o: sc.activation(r_T[:, sl_o], ps_a[:], AF.Sigmoid))
                pr.on("vector", lambda v: v.tensor_scalar(relu_tmp[:], ps_b[:], 0.0, None, ALU.max))
                pr.step()
                pr.on("scalar", lambda sc, sl_o=sl_o: sc.activation(kact_T[:, sl_o], relu_tmp[:], AF.Square))
                pr.step()

        for dq in range(8):
            for ts in range(TSL // 512):
                def mm(te, dq=dq, ts=ts):
                    for c in range(8):
                        te.matmul(ps_a[:], wvT[:, c * D + dq * 128: c * D + dq * 128 + 128],
                                  kact_T[:, c * TSL + ts * 512: c * TSL + ts * 512 + 512],
                                  start=(c == 0), stop=(c == 7))
                pr.on("tensor", mm)
                pr.step()
                sl_o = slice(dq * TSL + ts * 512, dq * TSL + ts * 512 + 512)
                pr.on("vector", lambda v, sl_o=sl_o: v.tensor_tensor(
                    out_T[:, sl_o], r_T[:, sl_o], ps_a[:], ALU.mult))
                pr.step()

        for i in range(NTT):
            for half in range(2):
                def trh(te, i=i, half=half):
                    for j in range(4):
                        dj = half * 4 + j
                        te.transpose(ps_bf[j][:],
                                     out_T[:, dj * TSL + i * 128: dj * TSL + i * 128 + 128],
                                     pr.ident_bf[:])
                pr.on("tensor", trh)
                pr.step()

                def evh(v, i=i, half=half):
                    for j in range(4):
                        dj = half * 4 + j
                        v.tensor_tensor(ytile[:, dj * 128:(dj + 1) * 128],
                                        x_sl[:, i * D + dj * 128: i * D + dj * 128 + 128],
                                        ps_bf[j][:], ALU.add)
                pr.on("vector", evh)
                pr.step()
            pr.dma(y_h[(sl * NTT + i) * 128:(sl * NTT + i + 1) * 128, :], ytile[:, :])
            pr.step()

    if standalone:
        return pr.build()
    return


def emit_ln_tile(pr, x_ap, out_ap, bufs=None, d=D, eps=1e-5):
    if bufs is None:
        bufs = (pr.sbuf([128, 1], F32), pr.sbuf([128, 1], F32), pr.sbuf([128, 1], F32),
                pr.sbuf([128, d], F32))
    s1, s2, m, sq = bufs

    def p1(v):
        v.reduce_sum(s1[:], x_ap, axis=AX)
    pr.on("vector", p1)
    pr.on("scalar", lambda sc: sc.activation(sq[:], x_ap, AF.Square))
    pr.step()

    def p2(v):
        inv = 1.0 / d
        v.reduce_sum(s2[:], sq[:], axis=AX)
        v.tensor_scalar(m[:], s1[:], inv, None, ALU.mult)
        v.tensor_scalar(s2[:], s2[:], inv, None, ALU.mult)
        v.tensor_tensor(s1[:], m[:], m[:], ALU.mult)
        v.tensor_tensor(s2[:], s2[:], s1[:], ALU.subtract)
        v.tensor_scalar(s2[:], s2[:], eps, None, ALU.add)
    pr.on("vector", p2)
    pr.step()
    pr.on("scalar", lambda sc: sc.activation(s2[:], s2[:], AF.Sqrt))
    pr.step()

    def p3(v):
        v.reciprocal(s1[:], s2[:])
        v.tensor_scalar(out_ap, x_ap, m[:], s1[:], ALU.subtract, ALU.mult)
    pr.on("vector", p3)
    pr.step()
    return bufs


def emit_ln_transpose_tile(pr, x_ap, ln_tok, lnT_off_fn, ps_bf, lnbufs, lnT):
    lnbufs = emit_ln_tile(pr, x_ap, ln_tok[:], bufs=lnbufs)
    for j0 in (0, 4):
        def tr(te, j0=j0):
            for j in range(4):
                te.transpose(ps_bf[j][:], ln_tok[:, (j0 + j) * 128:(j0 + j + 1) * 128],
                             pr.ident_bf[:])
        pr.on("tensor", tr)
        pr.step()

        def ev(v, j0=j0):
            for j in range(4):
                off = lnT_off_fn(j0 + j)
                v.tensor_copy(lnT[:, off:off + 128], ps_bf[j][:])
        pr.on("vector", ev)
        pr.step()
    return lnbufs


def build_l3f(pr=None, px='', x_h=None, y_h=None):
    standalone = pr is None
    if standalone:
        pr = Prog()
    if x_h is None:
        x_h = pr.dram_in(px + "x", [S, D])
    w1c_h = pr.dram_in(px + "w1c", [32, 128, 1024])
    w2c_h = pr.dram_in(px + "w2c", [32, 128, 1024])
    h1c_h = pr.dram_in(px + "h1c", [32, 128, 4096])
    h2c_h = pr.dram_in(px + "h2c", [32, 128, 4096])
    w3c_h = pr.dram_in(px + "w3c", [2, 32, 128, 512])
    s1c_h = pr.dram_in(px + "s1c", [128, 32])
    s2c_h = pr.dram_in(px + "s2c", [128, 32])
    h1sc_h = pr.dram_in(px + "h1sc", [128, 32])
    h2sc_h = pr.dram_in(px + "h2sc", [128, 32])
    s3r_h = pr.dram_in(px + "s3row", [1, 1024])
    if y_h is None:
        y_h = pr.dram_out(px + "y", [S, D])
    pr.init_consts()

    ld = Loader(pr)
    s1c = ld.load_f32(s1c_h, 128, 32)
    s2c = ld.load_f32(s2c_h, 128, 32)
    h1sc = ld.load_f32(h1sc_h, 128, 32)
    h2sc = ld.load_f32(h2sc_h, 128, 32)
    s3row = ld.load_f32(s3r_h, 1, 1024)
    s3rep = pr.sbuf([128, 1024], F32)
    ones_col = pr.sbuf([1, 128], F32)
    ps_rep = pr.psum([128, 512], F32)
    pr.on("gpsimd", lambda g: g.memset(ones_col[:], 1.0))
    pr.step()
    for c0 in (0, 512):
        pr.on("tensor", lambda te, c0=c0: te.matmul(ps_rep[:], ones_col[0:1, :],
                                                    s3row[0:1, c0:c0 + 512],
                                                    start=True, stop=True))
        pr.step()
        pr.on("vector", lambda v, c0=c0: v.tensor_copy(s3rep[:, c0:c0 + 512], ps_rep[:]))
        pr.step()
    pr.free(ps_rep)

    TSL = 1024
    NSL = S // TSL
    NTT = TSL // 128

    xt = pr.sbuf([128, D], F32)
    ln_tok = pr.sbuf([128, D], BF16)
    lnT = pr.sbuf([128, 8 * TSL], BF16)
    hA = pr.sbuf([128, 32 * TSL], BF16)
    hB = pr.sbuf([128, 32 * TSL], BF16)
    wbf = pr.sbuf([128, 4096], BF16)
    ps_bf = pr.psum_bf_pool(4)
    ps_a = pr.psum([128, 512], F32)
    ps_b = pr.psum([128, 512], F32)
    t_sil = pr.sbuf([128, 512], F32)
    t_gat = pr.sbuf([128, 512], F32)
    lnbufs = None

    for sl in range(NSL):
        for i in range(NTT):
            pr.dma(xt[:, :], x_h[(sl * NTT + i) * 128:(sl * NTT + i + 1) * 128, :])
            pr.step()
            lnbufs = emit_ln_transpose_tile(pr, xt[:, :], ln_tok,
                                            lambda j, i=i: j * TSL + i * 128,
                                            ps_bf, lnbufs, lnT)
        for o in range(32):
            pr.dma(pr.stage[:, 0:1024], w1c_h[o])
            pr.step()
            pr.on("vector", lambda v: v.tensor_copy(wbf[:, 0:1024], pr.stage[:, 0:1024]))
            pr.step()
            pr.dma(pr.stage[:, 0:1024], w2c_h[o])
            pr.step()
            pr.on("vector", lambda v: v.tensor_copy(wbf[:, 1024:2048], pr.stage[:, 0:1024]))
            pr.step()
            for tc in range(TSL // 512):
                def mm(te, o=o, tc=tc):
                    for c in range(8):
                        te.matmul(ps_a[:], wbf[:, c * 128:(c + 1) * 128],
                                  lnT[:, c * TSL + tc * 512: c * TSL + tc * 512 + 512],
                                  start=(c == 0), stop=(c == 7))
                    for c in range(8):
                        te.matmul(ps_b[:], wbf[:, 1024 + c * 128:1024 + (c + 1) * 128],
                                  lnT[:, c * TSL + tc * 512: c * TSL + tc * 512 + 512],
                                  start=(c == 0), stop=(c == 7))
                pr.on("tensor", mm)
                pr.step()
                pr.on("scalar", lambda sc, o=o: sc.activation(t_sil[:], ps_a[:], AF.Silu,
                                                              scale=s1c[:, o:o + 1]))
                pr.on("vector", lambda v, o=o: v.tensor_scalar(t_gat[:], ps_b[:],
                                                               s2c[:, o:o + 1], None, ALU.mult))
                pr.step()
                pr.on("vector", lambda v, o=o, tc=tc: v.tensor_tensor(
                    hA[:, o * TSL + tc * 512: o * TSL + tc * 512 + 512],
                    t_sil[:], t_gat[:], ALU.mult))
                pr.step()
        for src, dst, wc_h, sc in ((hA, hB, h1c_h, h1sc), (hB, hA, h2c_h, h2sc)):
            for o in range(32):
                for half in range(2):
                    pr.dma(pr.stage[:, :], wc_h[o, :, half * 2048:(half + 1) * 2048])
                    pr.step()
                    pr.on("vector", lambda v, half=half: v.tensor_copy(
                        wbf[:, half * 2048:(half + 1) * 2048], pr.stage[:, :]))
                    pr.step()
                for tc in range(TSL // 512):
                    def mm(te, o=o, tc=tc, src=src):
                        for c in range(32):
                            te.matmul(ps_a[:], wbf[:, c * 128:(c + 1) * 128],
                                      src[:, c * TSL + tc * 512: c * TSL + tc * 512 + 512],
                                      start=(c == 0), stop=(c == 31))
                    pr.on("tensor", mm)
                    pr.step()
                    pr.on("scalar", lambda sc2, o=o, tc=tc, dst=dst, sc=sc: sc2.activation(
                        dst[:, o * TSL + tc * 512: o * TSL + tc * 512 + 512],
                        ps_a[:], AF.Gelu, scale=sc[:, o:o + 1]))
                    pr.step()
        wr3 = hB
        for oh in range(2):
            for i in range(32):
                pr.dma(pr.stage[:, 0:512], w3c_h[oh, i])
                pr.step()
                pr.on("vector", lambda v, i=i: v.tensor_copy(
                    wr3[:, i * 512:(i + 1) * 512], pr.stage[:, 0:512]))
                pr.step()
            for tt in range(NTT):
                def mm(te, oh=oh, tt=tt):
                    for c in range(32):
                        te.matmul(ps_a[:], hA[:, c * TSL + tt * 128: c * TSL + tt * 128 + 128],
                                  wr3[:, c * 512:(c + 1) * 512],
                                  start=(c == 0), stop=(c == 31))
                pr.on("tensor", mm)
                pr.step()
                pr.dma(xt[:, 0:512],
                       x_h[(sl * NTT + tt) * 128:(sl * NTT + tt + 1) * 128,
                           oh * 512:oh * 512 + 512])
                pr.step()

                def ep(v, oh=oh, tt=tt):
                    v.tensor_tensor(t_sil[:], ps_a[:], s3rep[:, oh * 512:oh * 512 + 512],
                                    ALU.mult)
                    v.tensor_tensor(t_gat[:], t_sil[:], xt[:, 0:512], ALU.add)
                pr.on("vector", ep)
                pr.step()
                pr.dma(y_h[(sl * NTT + tt) * 128:(sl * NTT + tt + 1) * 128,
                           oh * 512:oh * 512 + 512], t_gat[:])
                pr.step()
    if standalone:
        return pr.build()
    return


def host_prep_l3f(inp):
    def chunk_lhsT(wT, co):
        I, O = wT.shape
        a = wT.reshape(I // 128, 128, O // co, co)
        return np.ascontiguousarray(a.transpose(2, 1, 0, 3).reshape(O // co, 128,
                                                                    (I // 128) * co))
    w1T = np.ascontiguousarray(inp["L3_ffn_w1_w"].T)
    w2T = np.ascontiguousarray(inp["L3_ffn_w2_w"].T)
    h1T = np.ascontiguousarray(inp["L3_ffn_h1_w"].T)
    h2T = np.ascontiguousarray(inp["L3_ffn_h2_w"].T)
    w3T = np.ascontiguousarray(inp["L3_ffn_w3_w"].T)
    w3c = np.stack([w3T[:, 0:512].reshape(32, 128, 512),
                    w3T[:, 512:1024].reshape(32, 128, 512)])
    cols = lambda s: np.ascontiguousarray(np.asarray(s).reshape(-1, 128).T)
    return {
        "w1c": chunk_lhsT(w1T, 128), "w2c": chunk_lhsT(w2T, 128),
        "h1c": chunk_lhsT(h1T, 128), "h2c": chunk_lhsT(h2T, 128),
        "w3c": np.ascontiguousarray(w3c),
        "s1c": cols(inp["L3_ffn_w1_s"]), "s2c": cols(inp["L3_ffn_w2_s"]),
        "h1sc": cols(inp["L3_ffn_h1_s"]), "h2sc": cols(inp["L3_ffn_h2_s"]),
        "s3row": np.ascontiguousarray(inp["L3_ffn_w3_s"].reshape(1, 1024)),
    }


def emit_top2(pr, logits_ap, w_out_ap, scl, E=8, scale=1.0):
    m1, m2, negm1, d21, eqm, l2, ge2, ex = scl

    def p1(v):
        v.reduce_max(m1[:], logits_ap, axis=AX)
    pr.on("vector", p1)
    pr.step()

    def p2(v):
        v.tensor_scalar(eqm[:], logits_ap, m1[:], None, ALU.is_equal)
        v.tensor_scalar(negm1[:], m1[:], -1.0, None, ALU.mult)
    pr.on("vector", p2)
    pr.step()

    def p3(v):
        v.scalar_tensor_tensor(l2[:], eqm[:], -1e30, logits_ap, ALU.mult, ALU.add)
    pr.on("vector", p3)
    pr.step()
    pr.on("vector", lambda v: v.reduce_max(m2[:], l2[:], axis=AX))
    pr.step()

    def p4(v):
        v.tensor_scalar(ge2[:], logits_ap, m2[:], None, ALU.is_ge)
        v.tensor_tensor(d21[:], m2[:], m1[:], ALU.subtract)
    pr.on("vector", p4)
    pr.step()
    pr.on("scalar", lambda sc: sc.activation(ex[:], logits_ap, AF.Exp, bias=negm1[:]))
    pr.step()
    pr.on("scalar", lambda sc: sc.activation(d21[:], d21[:], AF.Exp))
    pr.step()

    def p5(v):
        v.tensor_scalar(d21[:], d21[:], 1.0, None, ALU.add)
        v.tensor_tensor(ex[:], ex[:], ge2[:], ALU.mult)
    pr.on("vector", p5)
    pr.step()
    pr.on("vector", lambda v: v.reciprocal(m1[:], d21[:]))
    pr.step()

    def p6(v):
        v.tensor_scalar(ex[:], ex[:], m1[:], float(scale), ALU.mult, ALU.mult)
        v.tensor_copy(w_out_ap, ex[:])
    pr.on("vector", p6)
    pr.step()


def mk_top2_scratch(pr, E=8):
    return (pr.sbuf([128, 1], F32), pr.sbuf([128, 1], F32), pr.sbuf([128, 1], F32),
            pr.sbuf([128, 1], F32), pr.sbuf([128, E], F32), pr.sbuf([128, E], F32),
            pr.sbuf([128, E], F32), pr.sbuf([128, E], F32))


def build_l2f(pr=None, px='', x_h=None, y_h=None):
    standalone = pr is None
    if standalone:
        pr = Prog()
    if x_h is None:
        x_h = pr.dram_in(px + "x", [S, D])
    wgT_h = pr.dram_in(px + "wgT", [D, 8])
    bg_h = pr.dram_in(px + "bg_row", [1, 8])
    w1T_h = pr.dram_in(px + "w1T", [8, D, 64])
    w2T_h = pr.dram_in(px + "w2T", [8, 64, D])
    if y_h is None:
        y_h = pr.dram_out(px + "y", [S, D])
    pr.init_consts()

    ld = Loader(pr)
    wgT = ld.load_bf(wgT_h, D, 8)
    w1T = pr.sbuf([128, 8 * 8 * 64], BF16)
    for e in range(8):
        ld.load_bf(w1T_h[e], D, 64, out=w1T, out_off=e * 512, stride=64)
    w2T = pr.sbuf([64, 8 * D], BF16)
    for e in range(8):
        pr.step()
        pr.dma(pr.stage[:64, 0:D], w2T_h[e])
        pr.step()
        pr.on("vector", lambda v, e=e: v.tensor_copy(w2T[:, e * D:(e + 1) * D],
                                                     pr.stage[:64, 0:D]))
    bg_row = ld.load_f32(bg_h, 1, 8)
    bgrep = pr.sbuf([128, 8], F32)
    ones_col = pr.sbuf([1, 128], F32)
    ps_rep = pr.psum([128, 8], F32)
    pr.on("gpsimd", lambda g: g.memset(ones_col[:], 1.0))
    pr.step()
    pr.on("tensor", lambda te: te.matmul(ps_rep[:], ones_col[0:1, :], bg_row[0:1, :],
                                         start=True, stop=True))
    pr.step()
    pr.on("vector", lambda v: v.tensor_copy(bgrep[:], ps_rep[:]))
    pr.step()
    pr.free(ps_rep)

    TSL = 1024
    NSL = S // TSL
    NTT = TSL // 128

    xt = pr.sbuf([128, D], F32)
    ln_tok = pr.sbuf([128, D], BF16)
    lnT = pr.sbuf([128, 8 * TSL], BF16)
    hbuf = pr.sbuf([64, TSL], BF16)
    logit = pr.sbuf([128, NTT * 8], F32)
    wts = pr.sbuf([128, NTT * 8], F32)
    acc = pr.sbuf([128, NTT * D], F32)
    ps_bf = pr.psum_bf_pool(4)
    ps_a = pr.psum([128, 512], F32)
    ps_g = pr.psum([128, 8], F32)
    ps_h = pr.psum([64, 512], F32)
    lnbufs = None
    t2scl = mk_top2_scratch(pr)

    for sl in range(NSL):
        for i in range(NTT):
            pr.dma(xt[:, :], x_h[(sl * NTT + i) * 128:(sl * NTT + i + 1) * 128, :])
            pr.step()
            lnbufs = emit_ln_transpose_tile(pr, xt[:, :], ln_tok,
                                            lambda j, i=i: j * TSL + i * 128,
                                            ps_bf, lnbufs, lnT)
            def mm(te, i=i):
                for c in range(8):
                    te.matmul(ps_g[:], lnT[:, c * TSL + i * 128: c * TSL + i * 128 + 128],
                              wgT[:, c * 8:(c + 1) * 8], start=(c == 0), stop=(c == 7))
            pr.on("tensor", mm)
            pr.step()
            pr.on("vector", lambda v, i=i: v.tensor_tensor(
                logit[:, i * 8:(i + 1) * 8], ps_g[:], bgrep[:], ALU.add))
            pr.step()
            emit_top2(pr, logit[:, i * 8:(i + 1) * 8], wts[:, i * 8:(i + 1) * 8],
                      t2scl, scale=2.0)
        for e in range(8):
            for tc in range(TSL // 512):
                def mm(te, e=e, tc=tc):
                    for c in range(8):
                        te.matmul(ps_h[:, :], w1T[:, e * 512 + c * 64:(e * 512) + (c + 1) * 64],
                                  lnT[:, c * TSL + tc * 512: c * TSL + tc * 512 + 512],
                                  start=(c == 0), stop=(c == 7))
                pr.on("tensor", mm)
                pr.step()
                pr.on("scalar", lambda sc, tc=tc: sc.activation(
                    hbuf[:, tc * 512:(tc + 1) * 512], ps_h[:, :], AF.Gelu))
                pr.step()
            for tt in range(NTT):
                for ch in range(2):
                    def mm2(te, e=e, tt=tt, ch=ch):
                        te.matmul(ps_a[:], hbuf[:, tt * 128:(tt + 1) * 128],
                                  w2T[:, e * D + ch * 512: e * D + ch * 512 + 512],
                                  start=True, stop=True)
                    pr.on("tensor", mm2)
                    pr.step()

                    def ep(v, e=e, tt=tt, ch=ch):
                        dst = acc[:, tt * D + ch * 512: tt * D + ch * 512 + 512]
                        if e == 0:
                            v.tensor_scalar(dst, ps_a[:], wts[:, tt * 8 + e: tt * 8 + e + 1],
                                            None, ALU.mult)
                        else:
                            v.scalar_tensor_tensor(dst, ps_a[:],
                                                   wts[:, tt * 8 + e: tt * 8 + e + 1],
                                                   dst, ALU.mult, ALU.add)
                    pr.on("vector", ep)
                    pr.step()
        for tt in range(NTT):
            pr.dma(xt[:, :], x_h[(sl * NTT + tt) * 128:(sl * NTT + tt + 1) * 128, :])
            pr.step()
            pr.on("vector", lambda v, tt=tt: v.tensor_tensor(
                acc[:, tt * D:(tt + 1) * D], acc[:, tt * D:(tt + 1) * D], xt[:, :], ALU.add))
            pr.step()
            pr.dma(y_h[(sl * NTT + tt) * 128:(sl * NTT + tt + 1) * 128, :],
                   acc[:, tt * D:(tt + 1) * D])
            pr.step()
    if standalone:
        return pr.build()
    return


def host_prep_l2f(inp):
    return {
        "wgT": np.ascontiguousarray(inp["L2_ffn_Wg"].T),
        "bg_row": np.ascontiguousarray(inp["L2_ffn_bg"].reshape(1, 8)),
        "w1T": np.ascontiguousarray(inp["L2_ffn_W1"].transpose(0, 2, 1)),
        "w2T": np.ascontiguousarray(inp["L2_ffn_W2"].transpose(0, 2, 1)),
    }


def build_l2a(pr=None, px='', x_h=None, y_h=None, debug=False):
    standalone = pr is None
    if standalone:
        pr = Prog()
    RMS_EPS = 1.1920929e-07
    if x_h is None:
        x_h = pr.dram_in(px + "x", [S, D])
    wqkvT_h = pr.dram_in(px + "wqkvT", [D, 3 * D])
    bqkv_h = pr.dram_in(px + "bqkv_row", [1, 3 * D])
    gq_h = pr.dram_in(px + "gq_row", [1, D])
    gk_h = pr.dram_in(px + "gk_row", [1, D])
    woutT_h = pr.dram_in(px + "woutT", [D, D])
    bout_h = pr.dram_in(px + "bout_row", [1, D])
    gout_h = pr.dram_in(px + "gout_row", [1, D])
    if y_h is None:
        y_h = pr.dram_out(px + "y", [S, D])
    if debug:
        dbg_qkv = pr.dram_out(px + "dbg_qkv", [128, 3 * D])
        dbg_qf = pr.dram_out(px + "dbg_qf", [128, D], BF16)
        dbg_vk = pr.dram_out(px + "dbg_vk", [65, 1024])
        dbg_attn = pr.dram_out(px + "dbg_attn", [128, D], BF16)
    pr.init_consts()

    ld = Loader(pr)
    wqkvT = ld.load_bf(wqkvT_h, D, 3 * D)
    woutT = ld.load_bf(woutT_h, D, D)

    ones_col = pr.sbuf([1, 128], F32)
    pr.on("gpsimd", lambda g: g.memset(ones_col[:], 1.0))
    pr.step()
    ps_rep = pr.psum([128, 512], F32)

    def make_rep(row_src, n, is_handle=False):
        row = row_src if is_handle else ld.load_f32(row_src, 1, n)
        rep = pr.sbuf([128, n], F32)
        for c0 in range(0, n, 512):
            cw = min(512, n - c0)
            pr.on("tensor", lambda te, c0=c0, cw=cw, row=row: te.matmul(
                ps_rep[:, 0:cw], ones_col[0:1, :], row[0:1, c0:c0 + cw],
                start=True, stop=True))
            pr.step()
            pr.on("vector", lambda v, c0=c0, cw=cw, rep=rep: v.tensor_copy(
                rep[:, c0:c0 + cw], ps_rep[:, 0:cw]))
            pr.step()
        return rep

    bqkv_rep = make_rep(bqkv_h, 3 * D)
    gq_rep = make_rep(gq_h, D)
    gk_rep = make_rep(gk_h, D)
    bout_rep = make_rep(bout_h, D)
    gout_rep = make_rep(gout_h, D)

    jrow = pr.sbuf([1, 32], F32)
    pr.on("gpsimd", lambda g: g.iota(jrow[:], pattern=[[1, 32]], base=0,
                                     channel_multiplier=0,
                                     allow_small_or_imprecise_dtypes=True))
    pr.step()
    pr.on("vector", lambda v: v.tensor_scalar(jrow[:], jrow[:],
                                              -math.log(10000.0) / 32.0, None, ALU.mult))
    pr.step()
    pr.on("scalar", lambda sc: sc.activation(jrow[:], jrow[:], AF.Exp))
    pr.step()
    pr.on("vector", lambda v: v.tensor_scalar(jrow[:], jrow[:], 1.0 / TWO_PI, None, ALU.mult))
    pr.step()
    cj_rep = make_rep(jrow, 32, is_handle=True)
    pcol = pr.sbuf([128, 1], F32)
    pr.on("gpsimd", lambda g: g.iota(pcol[:], pattern=[[0, 1]], base=0,
                                     channel_multiplier=1,
                                     allow_small_or_imprecise_dtypes=True))
    pr.step()

    qfT_d = pr.dram_tmp(px + "qfT_d", [NT, 128, 8 * 128], BF16)
    qfT = pr.sbuf([128, 8 * 128], BF16)
    xt = pr.sbuf([128, D], F32)
    ln_tok = pr.sbuf([128, D], BF16)
    lnT = pr.sbuf([128, 8 * 128], BF16)
    qkv = pr.sbuf([128, 3 * D], F32)
    kf_bf = pr.sbuf([128, D], BF16)
    qf_bf = pr.sbuf([128, D], BF16)
    tcol = pr.sbuf([128, 1], F32)
    ang = pr.sbuf([128, 32], F32)
    sct = pr.sbuf([128, 64], F32)
    screp = pr.sbuf([128, 1024], F32)
    rt1 = pr.sbuf([128, 512], F32)
    rt2 = pr.sbuf([128, 512], F32)
    rt3 = pr.sbuf([128, 512], F32)
    stat = pr.sbuf([128, 4], F32)
    ps_bf = pr.psum_bf_pool(4)
    ps_a = pr.psum([128, 512], F32)
    ps_vk = pr.psum([65, 1024], F32)
    vk_acc = pr.sbuf([65, 1024], F32)
    ps_res = ps_rep[0:65, 0:128]
    ps_trf = ps_rep[:, 128:256]
    vk_sbT = pr.sbuf([128, 8 * 65], BF16)
    res_f = pr.sbuf([65, 128], F32)
    t_at = pr.sbuf([128, 65], F32)
    attn_tok = pr.sbuf([128, D], BF16)
    attnT = pr.sbuf([128, 8 * 128], BF16)
    vpad = pr.sbuf([128, 65], BF16)
    yt = pr.sbuf([128, D], F32)
    lnbufs = None

    pr.on("gpsimd", lambda g: g.memset(vpad[:, 64:65], 1.0))
    pr.step()

    for tt in range(NT):
        pr.dma(xt[:, :], x_h[tt * 128:(tt + 1) * 128, :])
        pr.step()
        lnbufs = emit_ln_transpose_tile(pr, xt[:, :], ln_tok,
                                        lambda j: j * 128, ps_bf, lnbufs, lnT)
        for ch in range(6):
            def mm(te, ch=ch):
                for c in range(8):
                    te.matmul(ps_a[:], lnT[:, c * 128:(c + 1) * 128],
                              wqkvT[:, c * 3072 + ch * 512: c * 3072 + ch * 512 + 512],
                              start=(c == 0), stop=(c == 7))
            pr.on("tensor", mm)
            pr.step()
            pr.on("vector", lambda v, ch=ch: v.tensor_tensor(
                qkv[:, ch * 512:(ch + 1) * 512], ps_a[:],
                bqkv_rep[:, ch * 512:(ch + 1) * 512], ALU.add))
            pr.step()
        for base, si in ((0, 0), (D, 1)):
            for half in range(2):
                pr.on("scalar", lambda sc, base=base, half=half: sc.activation(
                    rt1[:], qkv[:, base + half * 512: base + half * 512 + 512], AF.Square))
                pr.step()
                pr.on("vector", lambda v, half=half: v.reduce_sum(
                    stat[:, 2 + half:3 + half], rt1[:], axis=AX))
                pr.step()
            pr.on("vector", lambda v, si=si: v.tensor_scalar(
                stat[:, si:si + 1], stat[:, 2:3], stat[:, 3:4], None, ALU.add))
            pr.step()
            pr.on("vector", lambda v, si=si: v.tensor_scalar(
                stat[:, si:si + 1], stat[:, si:si + 1], 1.0 / D, RMS_EPS,
                ALU.mult, ALU.add))
            pr.step()
            pr.on("scalar", lambda sc, si=si: sc.activation(
                stat[:, si:si + 1], stat[:, si:si + 1], AF.Sqrt))
            pr.step()

        def recs(v):
            v.reciprocal(stat[:, 2:3], stat[:, 0:1])
            v.reciprocal(stat[:, 3:4], stat[:, 1:2])
        pr.on("vector", recs)
        pr.step()

        def nrm(v):
            v.tensor_scalar(qkv[:, 0:D], qkv[:, 0:D], stat[:, 2:3], None, ALU.mult)
            v.tensor_scalar(qkv[:, D:2 * D], qkv[:, D:2 * D], stat[:, 3:4], None, ALU.mult)
        pr.on("vector", nrm)
        pr.step()

        def gmul(v):
            v.tensor_tensor(qkv[:, 0:D], qkv[:, 0:D], gq_rep[:], ALU.mult)
            v.tensor_tensor(qkv[:, D:2 * D], qkv[:, D:2 * D], gk_rep[:], ALU.mult)
        pr.on("vector", gmul)
        pr.step()

        def angf(v, tt=tt):
            v.tensor_scalar(tcol[:], pcol[:], 1.0, float(128 * tt), ALU.mult, ALU.add)
            v.tensor_scalar(ang[:], cj_rep[:, 0:32], tcol[:, 0:1], None, ALU.mult)
        pr.on("vector", angf)
        pr.step()

        def fracs(v):
            v.tensor_scalar(sct[:, 0:32], ang[:], MAGIC, MAGIC, ALU.add, ALU.subtract)
            v.tensor_tensor(sct[:, 0:32], ang[:], sct[:, 0:32], ALU.subtract)
            v.tensor_scalar(ang[:], ang[:], 0.25, None, ALU.add)
            v.tensor_scalar(sct[:, 32:64], ang[:], MAGIC, MAGIC, ALU.add, ALU.subtract)
            v.tensor_tensor(sct[:, 32:64], ang[:], sct[:, 32:64], ALU.subtract)
        pr.on("vector", fracs)
        pr.step()
        pr.on("scalar", lambda sc: sc.activation(sct[:], sct[:], AF.Sin, scale=TWO_PI))
        pr.step()

        def screp_f(v):
            for h in range(16):
                v.tensor_copy(screp[:, h * 32:(h + 1) * 32], sct[:, 32:64])
                v.tensor_copy(screp[:, 512 + h * 32:512 + (h + 1) * 32], sct[:, 0:32])
        pr.on("vector", screp_f)
        pr.step()

        def d3(t):
            return t.rearrange("p (n o) -> p n o", o=1)
        for base, dst in ((0, qf_bf), (D, kf_bf)):
            ev = qkv[:, base:base + D].rearrange("p (n two) -> p n two", two=2)[:, :, 0:1]
            od = qkv[:, base:base + D].rearrange("p (n two) -> p n two", two=2)[:, :, 1:2]
            dev = dst[:, 0:D].rearrange("p (n two) -> p n two", two=2)[:, :, 0:1]
            dod = dst[:, 0:D].rearrange("p (n two) -> p n two", two=2)[:, :, 1:2]
            cos = screp[:, 0:512]
            sin = screp[:, 512:1024]

            def rope(v, ev=ev, od=od, dev=dev, dod=dod, cos=cos, sin=sin):
                v.tensor_tensor(d3(rt1[:]), ev, d3(cos), ALU.mult)
                v.tensor_tensor(d3(rt2[:]), od, d3(sin), ALU.mult)
                v.tensor_tensor(rt1[:], rt1[:], rt2[:], ALU.subtract)
                v.tensor_tensor(d3(rt2[:]), ev, d3(sin), ALU.mult)
                v.tensor_tensor(d3(rt3[:]), od, d3(cos), ALU.mult)
                v.tensor_tensor(rt2[:], rt2[:], rt3[:], ALU.add)
                v.tensor_scalar(dev, d3(rt1[:]), 0.0, None, ALU.max)
                v.tensor_scalar(dod, d3(rt2[:]), 0.0, None, ALU.max)
            pr.on("vector", rope)
            pr.step()

        if debug and tt == 0:
            pr.dma(dbg_qkv, qkv[:, :])
            pr.dma(dbg_qf, qf_bf[:, :])
            pr.step()
        if tt == 0:
            pr.on("gpsimd", lambda g: g.memset(vk_acc[:, :], 0.0))
            pr.step()
        for h in range(16):
            pr.on("vector", lambda v, h=h: v.tensor_copy(
                vpad[:, 0:64], qkv[:, 2 * D + h * 64: 2 * D + (h + 1) * 64]))
            pr.step()

            def mmvk(te, h=h):
                te.matmul(ps_vk[:, h * 64:(h + 1) * 64], vpad[:, :],
                          kf_bf[:, h * 64:(h + 1) * 64], start=True, stop=True)
            pr.on("tensor", mmvk)
            pr.step()
        pr.on("vector", lambda v: v.tensor_tensor(vk_acc[:, :], vk_acc[:, :],
                                                  ps_vk[:, :], ALU.add))
        pr.step()
        for hp in range(8):
            def trq(te, hp=hp):
                te.transpose(ps_bf[0][0:64, :], qf_bf[:, (2 * hp) * 64:(2 * hp + 1) * 64],
                             pr.ident_bf[:])
                te.transpose(ps_bf[0][64:128, :], qf_bf[:, (2 * hp + 1) * 64:(2 * hp + 2) * 64],
                             pr.ident_bf[:])
            pr.on("tensor", trq)
            pr.step()
            pr.on("vector", lambda v, hp=hp: v.tensor_copy(
                qfT[:, hp * 128:(hp + 1) * 128], ps_bf[0][:]))
            pr.step()
        pr.dma(qfT_d[tt], qfT[:, :])
        pr.step()

    if debug:
        pr.dma(dbg_vk, vk_acc[:, :])
        pr.step()
    vk_sb = pr.sbuf([65, 1024], BF16)
    pr.on("vector", lambda v: v.tensor_copy(vk_sb[:, :], vk_acc[:, :]))
    pr.step()
    for h in range(16):
        def trvk(te, h=h):
            te.transpose(ps_bf[0][(h % 2) * 64:(h % 2) * 64 + 64, 0:65],
                         vk_sb[:, h * 64:(h + 1) * 64], pr.ident_bf[0:65, 0:65])
        pr.on("tensor", trvk)
        pr.step()
        pr.on("vector", lambda v, h=h: v.tensor_copy(
            vk_sbT[(h % 2) * 64:(h % 2) * 64 + 64, (h // 2) * 65:(h // 2) * 65 + 65],
            ps_bf[0][(h % 2) * 64:(h % 2) * 64 + 64, 0:65]))
        pr.step()

    for tt in range(NT):
        pr.dma(qfT[:, :], qfT_d[tt])
        pr.step()
        for h in range(16):
            hp, ho = h // 2, (h % 2) * 64

            def mmres(te, hp=hp, ho=ho, tt=tt):
                te.matmul(ps_res[:, :], vk_sbT[ho:ho + 64, hp * 65:(hp + 1) * 65],
                          qfT[ho:ho + 64, hp * 128:(hp + 1) * 128],
                          start=True, stop=True)
            pr.on("tensor", mmres)
            pr.step()
            pr.on("vector", lambda v: v.tensor_copy(res_f[:, :], ps_res[:, :]))
            pr.step()

            def trres(te):
                te.transpose(ps_trf[:, 0:65], res_f[:, :], pr.ident[0:65, 0:65])
            pr.on("tensor", trres)
            pr.step()

            def nrm(v, h=h):
                v.tensor_copy(t_at[:, :], ps_trf[:, 0:65])
                v.tensor_scalar(t_at[:, 64:65], t_at[:, 64:65], 1e-6, None, ALU.add)
            pr.on("vector", nrm)
            pr.step()
            pr.on("vector", lambda v: v.reciprocal(t_at[:, 64:65], t_at[:, 64:65]))
            pr.step()
            pr.on("vector", lambda v, h=h: v.tensor_scalar(
                attn_tok[:, h * 64:(h + 1) * 64], t_at[:, 0:64], t_at[:, 64:65],
                None, ALU.mult))
            pr.step()
        if debug and tt == 0:
            pr.dma(dbg_attn, attn_tok[:, :])
            pr.step()
        for j0 in (0, 4):
            def tra(te, j0=j0):
                for j in range(4):
                    te.transpose(ps_bf[j][:], attn_tok[:, (j0 + j) * 128:(j0 + j + 1) * 128],
                                 pr.ident_bf[:])
            pr.on("tensor", tra)
            pr.step()

            def eva(v, j0=j0):
                for j in range(4):
                    v.tensor_copy(attnT[:, (j0 + j) * 128:(j0 + j + 1) * 128], ps_bf[j][:])
            pr.on("vector", eva)
            pr.step()
        for ch in range(2):
            def mmo(te, ch=ch, tt=tt):
                for c in range(8):
                    te.matmul(ps_a[:], attnT[:, c * 128:(c + 1) * 128],
                              woutT[:, c * D + ch * 512: c * D + ch * 512 + 512],
                              start=(c == 0), stop=(c == 7))
            pr.on("tensor", mmo)
            pr.step()
            pr.on("vector", lambda v, ch=ch: v.tensor_tensor(
                yt[:, ch * 512:(ch + 1) * 512], ps_a[:],
                bout_rep[:, ch * 512:(ch + 1) * 512], ALU.add))
            pr.step()
        for half in range(2):
            pr.on("scalar", lambda sc, half=half: sc.activation(
                rt1[:], yt[:, half * 512:(half + 1) * 512], AF.Square))
            pr.step()
            pr.on("vector", lambda v, half=half: v.reduce_sum(
                stat[:, 2 + half:3 + half], rt1[:], axis=AX))
            pr.step()

        def fin1(v):
            v.tensor_scalar(stat[:, 0:1], stat[:, 2:3], stat[:, 3:4], None, ALU.add)
        pr.on("vector", fin1)
        pr.step()
        pr.on("vector", lambda v: v.tensor_scalar(stat[:, 0:1], stat[:, 0:1],
                                                  1.0 / D, RMS_EPS, ALU.mult, ALU.add))
        pr.step()
        pr.on("scalar", lambda sc: sc.activation(stat[:, 0:1], stat[:, 0:1], AF.Sqrt))
        pr.step()
        pr.on("vector", lambda v: v.reciprocal(stat[:, 1:2], stat[:, 0:1]))
        pr.step()
        pr.dma(xt[:, :], x_h[tt * 128:(tt + 1) * 128, :])
        pr.step()

        def fin2(v):
            v.tensor_scalar(yt[:, :], yt[:, :], stat[:, 1:2], None, ALU.mult)
            v.tensor_tensor(yt[:, :], yt[:, :], gout_rep[:, :], ALU.mult)
            v.tensor_tensor(yt[:, :], yt[:, :], xt[:, :], ALU.add)
        pr.on("vector", fin2)
        pr.step()
        pr.dma(y_h[tt * 128:(tt + 1) * 128, :], yt[:, :])
        pr.step()
    if standalone:
        return pr.build()
    return


def host_prep_l2a(inp):
    return {
        "wqkvT": np.ascontiguousarray(inp["L2_attn_Wqkv"].T),
        "bqkv_row": np.ascontiguousarray(inp["L2_attn_bqkv"].reshape(1, -1)),
        "gq_row": np.ascontiguousarray(inp["L2_attn_gq"].reshape(1, -1)),
        "gk_row": np.ascontiguousarray(inp["L2_attn_gk"].reshape(1, -1)),
        "woutT": np.ascontiguousarray(inp["L2_attn_Wout"].T),
        "bout_row": np.ascontiguousarray(inp["L2_attn_bout"].reshape(1, -1)),
        "gout_row": np.ascontiguousarray(inp["L2_attn_gout"].reshape(1, -1)),
    }


def build_l0f(pr=None, px='', x_h=None, y_h=None):
    standalone = pr is None
    if standalone:
        pr = Prog()
    if x_h is None:
        x_h = pr.dram_in(px + "x", [S, D])
    gateT_h = pr.dram_in(px + "gateT", [D, 8])
    rmsw_h = pr.dram_in(px + "rmsw_row", [1, D])
    w1c_h = pr.dram_in(px + "w1c", [8, 22, 128, 1024])
    w2c_h = pr.dram_in(px + "w2c", [8, 22, 128, 1024])
    w3t_h = pr.dram_in(px + "w3t", [8, 22, 128, 1024])
    if y_h is None:
        y_h = pr.dram_out(px + "y", [S, D])
    pr.init_consts()

    ld = Loader(pr)
    gateT = ld.load_bf(gateT_h, D, 8)
    ones_col = pr.sbuf([1, 128], F32)
    pr.on("gpsimd", lambda g: g.memset(ones_col[:], 1.0))
    pr.step()
    ps_rep = pr.psum([128, 512], F32)
    rms_row = ld.load_f32(rmsw_h, 1, D)
    rms_rep = pr.sbuf([128, D], F32)
    for c0 in (0, 512):
        pr.on("tensor", lambda te, c0=c0: te.matmul(ps_rep[:], ones_col[0:1, :],
                                                    rms_row[0:1, c0:c0 + 512],
                                                    start=True, stop=True))
        pr.step()
        pr.on("vector", lambda v, c0=c0: v.tensor_copy(rms_rep[:, c0:c0 + 512], ps_rep[:]))
        pr.step()

    TSL = 1024
    NSL = S // TSL
    NTT = TSL // 128

    xt = pr.sbuf([128, D], F32)
    ln_f = pr.sbuf([128, D], F32)
    y_tok = pr.sbuf([128, D], BF16)
    yT = pr.sbuf([128, 8 * TSL], BF16)
    hbuf = pr.sbuf([128, 22 * TSL], BF16)
    w3sb = pr.sbuf([128, 22 * TSL], BF16)
    wbf = pr.sbuf([128, 2048], BF16)
    logit = pr.sbuf([128, NTT * 8], F32)
    wts = pr.sbuf([128, NTT * 8], F32)
    acc = pr.sbuf([128, NTT * D], F32)
    rt = pr.sbuf([128, 512], F32)
    stat2 = pr.sbuf([128, 4], F32)
    ps_bf = pr.psum_bf_pool(4)
    ps_a = pr.psum([128, 512], F32)
    ps_b = pr.psum([128, 512], F32)
    ps_g = pr.psum([128, 8], F32)
    lnbufs = None
    t2scl = mk_top2_scratch(pr)

    def ln_into(tile_idx, dst_f32):
        nonlocal lnbufs
        pr.dma(xt[:, :], x_h[tile_idx * 128:(tile_idx + 1) * 128, :])
        pr.step()
        lnbufs = emit_ln_tile(pr, xt[:, :], dst_f32, bufs=lnbufs)

    for sl in range(NSL):
        for i in range(NTT):
            ln_into(sl * NTT + i, ln_f[:, :])
            for half in range(2):
                pr.on("scalar", lambda sc, half=half: sc.activation(
                    rt[:], ln_f[:, half * 512:(half + 1) * 512], AF.Square))
                pr.step()
                pr.on("vector", lambda v, half=half: v.reduce_sum(
                    stat2[:, half:half + 1], rt[:], axis=AX))
                pr.step()

            def s1(v):
                v.tensor_scalar(stat2[:, 0:1], stat2[:, 0:1], stat2[:, 1:2], None, ALU.add)
            pr.on("vector", s1)
            pr.step()
            pr.on("vector", lambda v: v.tensor_scalar(stat2[:, 0:1], stat2[:, 0:1],
                                                      1.0 / D, 1e-6, ALU.mult, ALU.add))
            pr.step()
            pr.on("scalar", lambda sc: sc.activation(stat2[:, 0:1], stat2[:, 0:1], AF.Sqrt))
            pr.step()
            pr.on("vector", lambda v: v.reciprocal(stat2[:, 1:2], stat2[:, 0:1]))
            pr.step()

            def s2(v):
                v.tensor_scalar(y_tok[:, :], ln_f[:, :], stat2[:, 1:2], None, ALU.mult)
            pr.on("vector", s2)
            pr.step()
            pr.on("vector", lambda v: v.tensor_tensor(y_tok[:, :], y_tok[:, :],
                                                      rms_rep[:, :], ALU.mult))
            pr.step()
            for j0 in (0, 4):
                def tr(te, j0=j0, i=i):
                    for j in range(4):
                        te.transpose(ps_bf[j], y_tok[:, (j0 + j) * 128:(j0 + j + 1) * 128],
                                     pr.ident_bf[:])
                pr.on("tensor", tr)
                pr.step()

                def ev(v, j0=j0, i=i):
                    for j in range(4):
                        v.tensor_copy(yT[:, (j0 + j) * TSL + i * 128:(j0 + j) * TSL + (i + 1) * 128],
                                      ps_bf[j])
                pr.on("vector", ev)
                pr.step()
            def mmg(te, i=i):
                for c in range(8):
                    te.matmul(ps_g[:], yT[:, c * TSL + i * 128: c * TSL + i * 128 + 128],
                              gateT[:, c * 8:(c + 1) * 8], start=(c == 0), stop=(c == 7))
            pr.on("tensor", mmg)
            pr.step()
            pr.on("vector", lambda v, i=i: v.tensor_copy(logit[:, i * 8:(i + 1) * 8], ps_g[:]))
            pr.step()
            emit_top2(pr, logit[:, i * 8:(i + 1) * 8], wts[:, i * 8:(i + 1) * 8], t2scl)

        for e in range(8):
            for i in range(22):
                pr.dma(pr.stage[:, 0:1024], w3t_h[e, i])
                pr.step()
                pr.on("vector", lambda v, i=i: v.tensor_copy(
                    w3sb[:, i * TSL:i * TSL + 1024], pr.stage[:, 0:1024]))
                pr.step()
            for o in range(22):
                pr.dma(pr.stage[:, 0:1024], w1c_h[e, o])
                pr.step()
                pr.on("vector", lambda v: v.tensor_copy(wbf[:, 0:1024], pr.stage[:, 0:1024]))
                pr.step()
                pr.dma(pr.stage[:, 0:1024], w2c_h[e, o])
                pr.step()
                pr.on("vector", lambda v: v.tensor_copy(wbf[:, 1024:2048], pr.stage[:, 0:1024]))
                pr.step()
                for tc in range(TSL // 512):
                    def mm(te, o=o, tc=tc):
                        for c in range(8):
                            te.matmul(ps_a[:], wbf[:, c * 128:(c + 1) * 128],
                                      yT[:, c * TSL + tc * 512: c * TSL + tc * 512 + 512],
                                      start=(c == 0), stop=(c == 7))
                        for c in range(8):
                            te.matmul(ps_b[:], wbf[:, 1024 + c * 128:1024 + (c + 1) * 128],
                                      yT[:, c * TSL + tc * 512: c * TSL + tc * 512 + 512],
                                      start=(c == 0), stop=(c == 7))
                    pr.on("tensor", mm)
                    pr.step()
                    pr.on("scalar", lambda sc: sc.activation(rt[:], ps_a[:], AF.Silu))
                    pr.step()
                    pr.on("vector", lambda v, o=o, tc=tc: v.tensor_tensor(
                        hbuf[:, o * TSL + tc * 512: o * TSL + tc * 512 + 512],
                        rt[:], ps_b[:], ALU.mult))
                    pr.step()
            for tt in range(NTT):
                for ch in range(2):
                    def mm2(te, tt=tt, ch=ch):
                        for i in range(22):
                            te.matmul(ps_a[:], hbuf[:, i * TSL + tt * 128: i * TSL + tt * 128 + 128],
                                      w3sb[:, i * TSL + ch * 512: i * TSL + ch * 512 + 512],
                                      start=(i == 0), stop=(i == 21))
                    pr.on("tensor", mm2)
                    pr.step()

                    def ep(v, e=e, tt=tt, ch=ch):
                        dst = acc[:, tt * D + ch * 512: tt * D + ch * 512 + 512]
                        if e == 0:
                            v.tensor_scalar(dst, ps_a[:], wts[:, tt * 8 + e: tt * 8 + e + 1],
                                            None, ALU.mult)
                        else:
                            v.scalar_tensor_tensor(dst, ps_a[:],
                                                   wts[:, tt * 8 + e: tt * 8 + e + 1],
                                                   dst, ALU.mult, ALU.add)
                    pr.on("vector", ep)
                    pr.step()
        for tt in range(NTT):
            ln_into(sl * NTT + tt, ln_f[:, :])

            def fin(v, tt=tt):
                v.tensor_tensor(acc[:, tt * D:(tt + 1) * D], acc[:, tt * D:(tt + 1) * D],
                                ln_f[:, :], ALU.add)
                v.tensor_tensor(acc[:, tt * D:(tt + 1) * D], acc[:, tt * D:(tt + 1) * D],
                                xt[:, :], ALU.add)
            pr.on("vector", fin)
            pr.step()
            pr.dma(y_h[(sl * NTT + tt) * 128:(sl * NTT + tt + 1) * 128, :],
                   acc[:, tt * D:(tt + 1) * D])
            pr.step()
    if standalone:
        return pr.build()
    return


def host_prep_l0f(inp):
    def chunk_lhsT(wT, co):
        I, O = wT.shape
        a = wT.reshape(I // 128, 128, O // co, co)
        return np.ascontiguousarray(a.transpose(2, 1, 0, 3).reshape(O // co, 128,
                                                                    (I // 128) * co))
    W1, W2, W3 = inp["L0_ffn_W1"], inp["L0_ffn_W2"], inp["L0_ffn_W3"]
    w1c = np.stack([chunk_lhsT(np.ascontiguousarray(W1[e].T), 128) for e in range(8)])
    w2c = np.stack([chunk_lhsT(np.ascontiguousarray(W2[e].T), 128) for e in range(8)])
    w3t = np.stack([np.ascontiguousarray(W3[e].T).reshape(22, 128, 1024) for e in range(8)])
    return {
        "gateT": np.ascontiguousarray(inp["L0_ffn_gate_w"].T),
        "rmsw_row": np.ascontiguousarray(inp["L0_ffn_rms_w"].reshape(1, -1)),
        "w1c": w1c, "w2c": w2c, "w3t": w3t,
    }


def build_l3a(pr=None, px='', x_h=None, y_h=None):
    standalone = pr is None
    if standalone:
        pr = Prog()
    FREQS = [31.0, 37.0, 41.0, 43.0, 47.0]
    if x_h is None:
        x_h = pr.dram_in(px + "x", [S, D])
    wqT_h = pr.dram_in(px + "wqT", [D, D])
    wkT_h = pr.dram_in(px + "wkT", [D, D])
    wvT_h = pr.dram_in(px + "wvT", [D, D])
    woT_h = pr.dram_in(px + "woT", [D, D])
    bq_h = pr.dram_in(px + "bq_row", [1, D])
    bk_h = pr.dram_in(px + "bk_row", [1, D])
    bv_h = pr.dram_in(px + "bv_row", [1, D])
    bo_h = pr.dram_in(px + "bo_row", [1, D])
    alpha_h = pr.dram_in(px + "alpha_row", [1, 80])
    phi_h = pr.dram_in(px + "phi_row", [1, 80])
    temp_h = pr.dram_in(px + "temperature", [1, 1])
    gam_h = pr.dram_in(px + "gamma_row", [1, D])
    bet_h = pr.dram_in(px + "beta_row", [1, D])
    if y_h is None:
        y_h = pr.dram_out(px + "y", [S, D])

    if L3A_BONLY:
        qT_d = pr.dram_in(px + "qT_d", [16, 64, S], BF16)
        kT_d = pr.dram_in(px + "kT_d", [16, 64, S], BF16)
        v_d = pr.dram_in(px + "v_d", [S, D], BF16)
        at_d = pr.dram_out(px + "at_d", [S, D], BF16)
        dbg_sig_h = pr.dram_out(px + "dbg_sig", [128, 1024])
        dbg_acc_h = pr.dram_out(px + "dbg_acc", [128, 1024])
        dbg_kq_h = pr.dram_out(px + "dbg_kq", [64, 512], BF16)
        dbg_qq_h = pr.dram_out(px + "dbg_qq", [64, 512], BF16)
        dbg_ktd_h = pr.dram_out(px + "dbg_ktd", [64, 512], BF16)
    elif L3A_DEBUG:
        qT_d = pr.dram_out(px + "qT_d", [16, 64, S], BF16)
        kT_d = pr.dram_out(px + "kT_d", [16, 64, S], BF16)
        v_d = pr.dram_out(px + "v_d", [S, D], BF16)
        at_d = pr.dram_out(px + "at_d", [S, D], BF16)
        dbg_sig_h = pr.dram_out(px + "dbg_sig", [128, 1024])
        dbg_acc_h = pr.dram_out(px + "dbg_acc", [128, 1024])
        dbg_kq_h = pr.dram_out(px + "dbg_kq", [64, 512], BF16)
        dbg_qq_h = pr.dram_out(px + "dbg_qq", [64, 512], BF16)
        dbg_ktd_h = pr.dram_out(px + "dbg_ktd", [64, 512], BF16)
        dbg_lnT_h = pr.dram_out(px + "dbg_lnT", [128, 1024], BF16)
        dbg_prjq_h = pr.dram_out(px + "dbg_prjq", [128, 1024], BF16)
        dbg_prjk_h = pr.dram_out(px + "dbg_prjk", [128, 1024], BF16)
        dbg_ktd2_h = pr.dram_out(px + "dbg_ktd2", [64, 512], BF16)
        dbg_ktd3_h = pr.dram_out(px + "dbg_ktd3", [64, 512], BF16)
    else:
        qT_d = pr.dram_tmp(px + "qT_d", [16, 64, S], BF16)
        kT_d = pr.dram_tmp(px + "kT_d", [16, 64, S], BF16)
        v_d = pr.dram_tmp(px + "v_d", [S, D], BF16)
        at_d = pr.dram_tmp(px + "at_d", [S, D], BF16)
    pr.init_consts()

    ld = Loader(pr)
    ones_col = pr.sbuf([1, 128], F32)
    pr.on("gpsimd", lambda g: g.memset(ones_col[:], 1.0))
    pr.step()
    ps_rep = pr.psum([128, 512], F32)

    def make_rep(row_src, n, is_handle=False):
        row = row_src if is_handle else ld.load_f32(row_src, 1, n)
        rep = pr.sbuf([128, n], F32)
        for c0 in range(0, n, 512):
            cw = min(512, n - c0)
            pr.on("tensor", lambda te, c0=c0, cw=cw, row=row: te.matmul(
                ps_rep[:, 0:cw], ones_col[0:1, :], row[0:1, c0:c0 + cw],
                start=True, stop=True))
            pr.step()
            pr.on("vector", lambda v, c0=c0, cw=cw, rep=rep: v.tensor_copy(
                rep[:, c0:c0 + cw], ps_rep[:, 0:cw]))
            pr.step()
        return rep

    bo_rep = make_rep(bo_h, D)
    gam_rep = make_rep(gam_h, D)
    bet_rep = make_rep(bet_h, D)
    al_rep = make_rep(alpha_h, 80)
    ph_rep = make_rep(phi_h, 80)
    pr.on("vector", lambda v: v.tensor_scalar(ph_rep[:], ph_rep[:], 1.0 / TWO_PI,
                                              None, ALU.mult))
    pr.step()
    trow = ld.load_f32(temp_h, 1, 1)
    grow = pr.sbuf([1, 5], F32)
    pr.on("scalar", lambda sc: sc.activation(trow[:], trow[:], AF.Abs))
    pr.step()
    pr.on("vector", lambda v: v.tensor_scalar(trow[:], trow[:], 0.1, None, ALU.add))
    pr.step()
    for fi, f in enumerate(FREQS):
        pr.on("vector", lambda v, fi=fi, f=f: v.tensor_scalar(
            grow[:, fi:fi + 1], trow[:, 0:1], float(16.0 * math.pi * f), None, ALU.mult))
    pr.step()
    pr.on("vector", lambda v: v.reciprocal(grow[:], grow[:]))
    pr.step()
    g_rep = make_rep(grow, 5, is_handle=True)

    markA = pr.mark()
    if not L3A_BONLY:
        bq_rep = make_rep(bq_h, D)
        bk_rep = make_rep(bk_h, D)
        bv_rep = make_rep(bv_h, D)
        xt = pr.sbuf([128, D], F32)
        ln_tok = pr.sbuf([128, D], BF16)
        lnT = pr.sbuf([128, 8 * 128], BF16)
        ps_bf = pr.psum_bf_pool(4)
        ps_a = pr.psum([128, 512], F32)
        prj = pr.sbuf([128, D], BF16)
        lnbufs = (pr.sbuf([128, 1], F32), pr.sbuf([128, 1], F32), pr.sbuf([128, 1], F32),
                  pr.sbuf([128, D], F32))
        wqT = ld.load_bf(wqT_h, D, D)
        wkT = ld.load_bf(wkT_h, D, D)
        wvT = ld.load_bf(wvT_h, D, D)

        for tt in range(NT):
            pr.dma(xt[:, :], x_h[tt * 128:(tt + 1) * 128, :])
            pr.step()
            lnbufs = emit_ln_transpose_tile(pr, xt[:, :], ln_tok,
                                            lambda j: j * 128, ps_bf, lnbufs, lnT)
            if L3A_DEBUG and tt == 0:
                pr.dma(dbg_lnT_h, lnT[:, :])
                pr.step()
            for wT, brep, dst, trans in ((wqT, bq_rep, qT_d, True), (wkT, bk_rep, kT_d, True),
                                         (wvT, bv_rep, v_d, False)):
                for ch in range(2):
                    def mm(te, wT=wT, ch=ch):
                        for c in range(8):
                            te.matmul(ps_a[:], lnT[:, c * 128:(c + 1) * 128],
                                      wT[:, c * D + ch * 512: c * D + ch * 512 + 512],
                                      start=(c == 0), stop=(c == 7))
                    pr.on("tensor", mm)
                    pr.step()
                    pr.on("vector", lambda v, brep=brep, ch=ch: v.tensor_tensor(
                        prj[:, ch * 512:(ch + 1) * 512], ps_a[:],
                        brep[:, ch * 512:(ch + 1) * 512], ALU.add))
                    pr.step()
                if L3A_DEBUG and tt == 0 and dst is qT_d:
                    pr.dma(dbg_prjq_h, prj[:, :])
                    pr.step()
                if L3A_DEBUG and tt == 0 and dst is kT_d:
                    pr.dma(dbg_prjk_h, prj[:, :])
                    pr.step()
                if not trans:
                    pr.dma(v_d[tt * 128:(tt + 1) * 128, :], prj[:, :])
                    pr.step()
                else:
                    for hp in range(8):
                        def trq(te, hp=hp):
                            te.transpose(ps_bf[0][0:64, :], prj[:, (2 * hp) * 64:(2 * hp + 1) * 64],
                                         pr.ident_bf[:])
                            te.transpose(ps_bf[0][64:128, :],
                                         prj[:, (2 * hp + 1) * 64:(2 * hp + 2) * 64],
                                         pr.ident_bf[:])
                        pr.on("tensor", trq)
                        pr.step()
                        pr.on("vector", lambda v: v.tensor_copy(ln_tok[:, 0:128], ps_bf[0][:]))
                        pr.step()
                        pr.dma(dst[2 * hp, :, tt * 128:(tt + 1) * 128], ln_tok[0:64, 0:128])
                        pr.dma(dst[2 * hp + 1, :, tt * 128:(tt + 1) * 128], ln_tok[64:128, 0:128])
                        pr.step()

            if L3A_DEBUG and tt == 3:
                pr.dma(dbg_ktd2_h, kT_d[0][:, 0:512])
                pr.step()
            if L3A_DEBUG and tt == 31:
                pr.dma(dbg_ktd3_h, kT_d[0][:, 0:512])
                pr.step()

    pr.free_to(markA)
    if L3A_SKIPB:
        if standalone:
            return pr.build()
        return
    v_d_r = v_d.rearrange("(j p) d -> p j d", p=128)

    kq2 = [pr.sbuf([64, S], BF16) for _ in range(2)]
    qq2 = [pr.sbuf([64, S], BF16) for _ in range(2)]
    vph2 = [pr.sbuf([128, 32 * 65], BF16) for _ in range(2)]
    sgb2 = [pr.sbuf([128, 1024], F32) for _ in range(2)]
    RNG = [[pr.sbuf([128, 1024], F32) for _ in range(4)] for _ in range(5)]
    nsD = pr.sbuf([128, 1024], F32)
    nsP = pr.sbuf([128, 1024], F32)
    accb = [pr.sbuf([128, 1024], F32) for _ in range(2)]
    Twb = [pr.sbuf([128, 1024], BF16) for _ in range(2)]
    resn = pr.sbuf([65, 512], F32)
    t_at = pr.sbuf([128, 260], F32)
    at_tok = pr.sbuf([128, 256], BF16)
    ps_sig = pr.psum([128, 2048], F32)
    ps_av2 = pr.psum([65, 1024], F32)
    ps_tr = pr.psum([128, 512], F32)

    def ones_init(g):
        g.memset(vph2[0][:, :], 1.0)
        g.memset(vph2[1][:, :], 1.0)
    pr.on("gpsimd", ones_init)
    pr.step()
    for c in range(4):
        pr.dma(kq2[0][:, c * 1024:(c + 1) * 1024], kT_d[0][:, c * 1024:(c + 1) * 1024])
        pr.dma(qq2[0][:, c * 1024:(c + 1) * 1024], qT_d[0][:, c * 1024:(c + 1) * 1024])
    pr.step()
    for c in range(4):
        pr.dma(vph2[0].rearrange("p (j w) -> p j w", w=65)[:, c * 8:(c + 1) * 8, 0:64],
               v_d_r[:, c * 8:(c + 1) * 8, 0:64])
    pr.step()

    NU = 16 * 8 * 16
    MAGIC = float(1.5 * 2 ** 23)

    def unit(u):
        return u // 128, (u // 16) % 8, u % 16

    for v in range(NU + 12):
        def tops(te, v=v):
            if v >= 9 and (v - 9) < NU and (v - 9) % 16 == 15:
                for q4 in range(4):
                    te.transpose(ps_tr[:, q4 * 65:(q4 + 1) * 65],
                                 resn[0:65, q4 * 128:(q4 + 1) * 128],
                                 pr.ident[0:65, 0:65])
            if v < NU:
                h, qc, g = unit(v)
                for j in range(2):
                    jt = 2 * g + j
                    te.matmul(ps_sig[:, ((v % 2) * 2 + j) * 512:((v % 2) * 2 + j + 1) * 512],
                              kq2[h % 2][:, jt * 128:(jt + 1) * 128],
                              qq2[h % 2][:, qc * 512:(qc + 1) * 512],
                              start=True, stop=True)
            if v >= 7 and (v - 7) < NU:
                u = v - 7
                h, qc, g = unit(u)
                row = u // 16
                for j in range(2):
                    jt = 2 * g + j
                    te.matmul(ps_av2[:, (row % 2) * 512:(row % 2 + 1) * 512],
                              vph2[h % 2][:, jt * 65:(jt + 1) * 65],
                              Twb[u % 2][:, j * 512:(j + 1) * 512],
                              start=(g == 0 and j == 0), stop=(g == 15 and j == 1))
        if v < NU + 8 or (9 <= v < NU + 9 and (v - 9) % 16 == 15):
            pr.on("tensor", tops)

        def aops(sc, v=v):
            if v >= 1 and (v - 1) < NU:
                u = v - 1
                for j in range(2):
                    sc.activation(sgb2[u % 2][:, j * 512:(j + 1) * 512],
                                  ps_sig[:, ((u % 2) * 2 + j) * 512:((u % 2) * 2 + j + 1) * 512],
                                  AF.Copy)
            if v >= 4 and (v - 4) < NU:
                u = v - 4
                for f in range(5):
                    sc.activation(RNG[f][u % 4][:], RNG[f][u % 4][:], AF.Sin, scale=TWO_PI)
            if v >= 6 and (v - 6) < NU:
                u = v - 6
                sc.activation(Twb[u % 2][:], accb[u % 2][:], AF.Sigmoid)
        if 1 <= v < NU + 6:
            pr.on("scalar", aops)

        def vops(ve, v=v):
            if v >= 2 and (v - 2) < NU:
                u = v - 2
                h = u // 128
                for f in range(5):
                    ve.tensor_scalar(RNG[f][u % 4][:], sgb2[u % 2][:],
                                     g_rep[:, f:f + 1], ph_rep[:, h * 5 + f:h * 5 + f + 1],
                                     ALU.mult, ALU.add)
                for f in (0, 1):
                    ve.tensor_scalar(nsD[:], RNG[f][u % 4][:], MAGIC, MAGIC,
                                     ALU.add, ALU.subtract)
                    ve.tensor_tensor(RNG[f][u % 4][:], RNG[f][u % 4][:], nsD[:],
                                     ALU.subtract)
            if v >= 5 and (v - 5) < NU:
                u = v - 5
                h = u // 128
                ve.tensor_scalar(accb[u % 2][:], RNG[0][u % 4][:],
                                 al_rep[:, h * 5:h * 5 + 1], None, ALU.mult)
                for f in range(1, 5):
                    ve.scalar_tensor_tensor(accb[u % 2][:], RNG[f][u % 4][:],
                                            al_rep[:, h * 5 + f:h * 5 + f + 1],
                                            accb[u % 2][:], ALU.mult, ALU.add)
            if v >= 8 and (v - 8) < NU and (v - 8) % 16 == 15:
                row = (v - 8) // 16
                ve.tensor_copy(resn[:, :], ps_av2[:, (row % 2) * 512:(row % 2 + 1) * 512])
            if v >= 10 and (v - 10) < NU and (v - 10) % 16 == 15:
                ve.tensor_copy(t_at[:, 0:260], ps_tr[:, 0:260])
                for q4 in range(4):
                    c = q4 * 65 + 64
                    ve.tensor_scalar(t_at[:, c:c + 1], t_at[:, c:c + 1], 1e-10,
                                     None, ALU.add)
                    ve.reciprocal(t_at[:, c:c + 1], t_at[:, c:c + 1])
                for q4 in range(4):
                    ve.tensor_scalar(at_tok[:, q4 * 64:(q4 + 1) * 64],
                                     t_at[:, q4 * 65:q4 * 65 + 64],
                                     t_at[:, q4 * 65 + 64:q4 * 65 + 65], None, ALU.mult)
        if 2 <= v < NU + 11:
            pr.on("vector", vops)

        def pops(g_, v=v):
            if v >= 3 and (v - 3) < NU:
                u = v - 3
                for f in (2, 3, 4):
                    g_.tensor_scalar(nsP[:], RNG[f][u % 4][:], MAGIC, MAGIC,
                                     ALU.add, ALU.subtract)
                    g_.tensor_tensor(RNG[f][u % 4][:], RNG[f][u % 4][:], nsP[:],
                                     ALU.subtract)
        if 3 <= v < NU + 3:
            pr.on("gpsimd", pops)

        if v < NU:
            h, qc, g = unit(v)
            loc = v % 128
            hn = h + 1
            if hn < 16:
                if 100 <= loc < 104 and not L3A_NO_VPF:
                    c = loc - 100
                    pr.dma(vph2[hn % 2].rearrange("p (j w) -> p j w", w=65)
                           [:, c * 8:(c + 1) * 8, 0:64],
                           v_d_r[:, c * 8:(c + 1) * 8, hn * 64:hn * 64 + 64])
                if 116 <= loc < 120 and not L3A_NO_KQPF:
                    c = loc - 116
                    pr.dma(kq2[hn % 2][:, c * 1024:(c + 1) * 1024],
                           kT_d[hn][:, c * 1024:(c + 1) * 1024])
                if 120 <= loc < 124 and not L3A_NO_KQPF:
                    c = loc - 120
                    pr.dma(qq2[hn % 2][:, c * 1024:(c + 1) * 1024],
                           qT_d[hn][:, c * 1024:(c + 1) * 1024])
        if v >= 11 and (v - 11) < NU and (v - 11) % 16 == 15 and not L3A_NO_ATD:
            row = (v - 11) // 16
            h, qc = row // 8, row % 8
            for q4 in range(4):
                pr.dma(at_d[(qc * 4 + q4) * 128:(qc * 4 + q4 + 1) * 128,
                            h * 64:(h + 1) * 64], at_tok[:, q4 * 64:(q4 + 1) * 64])
        if (L3A_DEBUG or L3A_BONLY) and v == 2:
            pr.dma(dbg_sig_h, sgb2[0][:, :])
            pr.dma(dbg_kq_h, kq2[0][:, 0:512])
            pr.dma(dbg_qq_h, qq2[0][:, 0:512])
            if L3A_DEBUG:
                pr.dma(dbg_ktd_h, kT_d[0][:, 0:512])
        if (L3A_DEBUG or L3A_BONLY) and v == 6:
            pr.dma(dbg_acc_h, accb[0][:, :])
        pr.step()

    pr.free_to(markA)
    if L3A_BONLY:
        if standalone:
            return pr.build()
        return
    ps_bfC = pr.psum_bf_pool(4)
    ps_aC = pr.psum([128, 512], F32)
    woT = ld.load_bf(woT_h, D, D)
    xtC = pr.sbuf([128, D], F32)
    atokC = pr.sbuf([128, D], BF16)
    atT = pr.sbuf([128, 8 * 128], BF16)
    yt = pr.sbuf([128, D], F32)
    stat2 = pr.sbuf([128, 4], F32)
    sqb = pr.sbuf([128, D], F32)
    for tt in range(NT):
        pr.dma(atokC[:, :], at_d[tt * 128:(tt + 1) * 128, :])
        pr.step()
        for j0 in (0, 4):
            def tra(te, j0=j0):
                for j in range(4):
                    te.transpose(ps_bfC[j], atokC[:, (j0 + j) * 128:(j0 + j + 1) * 128],
                                 pr.ident_bf[:])
            pr.on("tensor", tra)
            pr.step()

            def eva(v, j0=j0):
                for j in range(4):
                    v.tensor_copy(atT[:, (j0 + j) * 128:(j0 + j + 1) * 128], ps_bfC[j])
            pr.on("vector", eva)
            pr.step()
        for ch in range(2):
            def mmo(te, ch=ch):
                for c in range(8):
                    te.matmul(ps_aC[:], atT[:, c * 128:(c + 1) * 128],
                              woT[:, c * D + ch * 512: c * D + ch * 512 + 512],
                              start=(c == 0), stop=(c == 7))
            pr.on("tensor", mmo)
            pr.step()
            pr.on("vector", lambda v, ch=ch: v.tensor_tensor(
                yt[:, ch * 512:(ch + 1) * 512], ps_aC[:],
                bo_rep[:, ch * 512:(ch + 1) * 512], ALU.add))
            pr.step()
        def lna(v):
            v.reduce_sum(stat2[:, 0:1], yt[:, :], axis=AX)
        pr.on("vector", lna)
        pr.on("scalar", lambda sc: sc.activation(sqb[:], yt[:, :], AF.Square))
        pr.step()

        def lnb(v):
            v.reduce_sum(stat2[:, 1:2], sqb[:], axis=AX)
            v.tensor_scalar(stat2[:, 0:1], stat2[:, 0:1], 1.0 / D, None, ALU.mult)
            v.tensor_scalar(stat2[:, 1:2], stat2[:, 1:2], 1.0 / D, None, ALU.mult)
            v.tensor_tensor(stat2[:, 2:3], stat2[:, 0:1], stat2[:, 0:1], ALU.mult)
            v.tensor_tensor(stat2[:, 1:2], stat2[:, 1:2], stat2[:, 2:3], ALU.subtract)
            v.tensor_scalar(stat2[:, 1:2], stat2[:, 1:2], 1e-5, None, ALU.add)
        pr.on("vector", lnb)
        pr.step()
        pr.on("scalar", lambda sc: sc.activation(stat2[:, 1:2], stat2[:, 1:2], AF.Sqrt))
        pr.step()
        pr.on("vector", lambda v: v.reciprocal(stat2[:, 2:3], stat2[:, 1:2]))
        pr.step()
        pr.dma(xtC[:, :], x_h[tt * 128:(tt + 1) * 128, :])
        pr.step()

        def fin(v):
            v.tensor_scalar(yt[:, :], yt[:, :], stat2[:, 0:1], stat2[:, 2:3],
                            ALU.subtract, ALU.mult)
            v.tensor_tensor(yt[:, :], yt[:, :], gam_rep[:, :], ALU.mult)
            v.tensor_tensor(yt[:, :], yt[:, :], bet_rep[:, :], ALU.add)
            v.tensor_tensor(yt[:, :], yt[:, :], xtC[:, :], ALU.add)
        pr.on("vector", fin)
        pr.step()
        pr.dma(y_h[tt * 128:(tt + 1) * 128, :], yt[:, :])
        pr.step()
    if standalone:
        return pr.build()
    return




def host_prep_l3a(inp):
    return {
        "wqT": np.ascontiguousarray(inp["L3_attn_Wq"].T),
        "wkT": np.ascontiguousarray(inp["L3_attn_Wk"].T),
        "wvT": np.ascontiguousarray(inp["L3_attn_Wv"].T),
        "woT": np.ascontiguousarray(inp["L3_attn_Wo"].T),
        "bq_row": np.ascontiguousarray(inp["L3_attn_bq"].reshape(1, -1)),
        "bk_row": np.ascontiguousarray(inp["L3_attn_bk"].reshape(1, -1)),
        "bv_row": np.ascontiguousarray(inp["L3_attn_bv"].reshape(1, -1)),
        "bo_row": np.ascontiguousarray(inp["L3_attn_bo"].reshape(1, -1)),
        "alpha_row": np.ascontiguousarray(inp["L3_attn_alpha"].reshape(1, 80)),
        "phi_row": np.ascontiguousarray(inp["L3_attn_phi"].reshape(1, 80)),
        "temperature": np.ascontiguousarray(inp["L3_attn_temperature"].reshape(1, 1)),
        "gamma_row": np.ascontiguousarray(inp["L3_attn_gamma"].reshape(1, -1)),
        "beta_row": np.ascontiguousarray(inp["L3_attn_beta"].reshape(1, -1)),
    }


def _emit_tchain(pr, M_bf, MT_bf, Tt_bf, Ibf, ps_x, ps_y, B, BT, X, sign):
    Xn = Tt_bf
    op0 = ALU.subtract if sign > 0 else ALU.add

    def init(v):
        v.tensor_tensor(Xn[:], Ibf[:], M_bf[:], op0)
        v.tensor_tensor(X[:], Ibf[:], MT_bf[:], op0)
        v.tensor_copy(B[:], M_bf[:])
        v.tensor_copy(BT[:], MT_bf[:])
    pr.on("vector", init)
    pr.step()
    for lvl in range(6):
        def mmsq(te):
            te.matmul(ps_x[:, 0:128], BT[:], B[:], start=True, stop=True)
            te.matmul(ps_x[:, 128:256], B[:], BT[:], start=True, stop=True)
        pr.on("tensor", mmsq)
        pr.step()

        def evsq(v):
            v.tensor_copy(B[:], ps_x[:, 0:128])
            v.tensor_copy(BT[:], ps_x[:, 128:256])
            v.tensor_tensor(B[:], B[:], Ibf[:], ALU.add)
            v.tensor_tensor(BT[:], BT[:], Ibf[:], ALU.add)
        pr.on("vector", evsq)
        pr.step()

        def mmpr(te):
            te.matmul(ps_y[:, 0:128], X[:], B[:], start=True, stop=True)
            te.matmul(ps_y[:, 128:256], B[:], X[:], start=True, stop=True)
        pr.on("tensor", mmpr)
        pr.step()

        def evpr(v, last=(lvl == 5)):
            v.tensor_copy(Xn[:], ps_y[:, 0:128])
            v.tensor_copy(X[:], ps_y[:, 128:256])
            if not last:
                v.tensor_tensor(B[:], B[:], Ibf[:], ALU.subtract)
                v.tensor_tensor(BT[:], BT[:], Ibf[:], ALU.subtract)
        pr.on("vector", evpr)
        pr.step()

    pr.on("vector", lambda v: v.tensor_copy(Tt_bf[:], X[:]))
    pr.step()


L0A_NT = NT
L0A_SKIP = set()
L3A_DEBUG = False
L3A_SKIPB = False
L3A_BONLY = False
L3A_NO_ATD = False
L3A_NO_KQPF = False
L3A_NO_VPF = False


def build_l0a(pr=None, px='', x_h=None, y_h=None):
    standalone = pr is None
    if standalone:
        pr = Prog()
    if x_h is None:
        x_h = pr.dram_in(px + "x", [S, D])
    wqkvT_h = pr.dram_in(px + "wqkvT", [D, 2048])
    convc_h = pr.dram_in(px + "convcols", [128, 64])
    wzT_h = pr.dram_in(px + "wzT", [D, D])
    wbT_h = pr.dram_in(px + "wbT", [D, 16])
    waT_h = pr.dram_in(px + "waT", [D, 16])
    dtb_h = pr.dram_in(px + "dtb_row", [1, 16])
    alog_h = pr.dram_in(px + "alog_row", [1, 16])
    nwt_h = pr.dram_in(px + "nw_tiled_row", [1, D])
    woutT_h = pr.dram_in(px + "woutT", [D, D])
    if y_h is None:
        y_h = pr.dram_out(px + "y", [S, D])
    pr.init_consts()
    onecol = pr.constcol(1.0)
    pr.step()

    ld = Loader(pr)
    ones_col = pr.sbuf([1, 128], F32)
    pr.on("gpsimd", lambda g: g.memset(ones_col[:], 1.0))
    pr.step()
    ps_rep = pr.psum([128, 512], F32)

    def make_rep(row_src, n, is_handle=False, nrep=128):
        row = row_src if is_handle else ld.load_f32(row_src, 1, n)
        rep = pr.sbuf([nrep, n], F32)
        for c0 in range(0, n, 512):
            cw = min(512, n - c0)
            pr.on("tensor", lambda te, c0=c0, cw=cw, row=row: te.matmul(
                ps_rep[0:nrep, 0:cw], ones_col[0:1, 0:nrep], row[0:1, c0:c0 + cw],
                start=True, stop=True))
            pr.step()
            pr.on("vector", lambda v, c0=c0, cw=cw, rep=rep: v.tensor_copy(
                rep[:, c0:c0 + cw], ps_rep[0:nrep, 0:cw]))
            pr.step()
        return rep

    wqkvT = ld.load_bf(wqkvT_h, D, 2048)
    wzT = ld.load_bf(wzT_h, D, D)
    wbaT = ld.load_bf(wbT_h, D, 16)
    waT = ld.load_bf(waT_h, D, 16)
    woutT = ld.load_bf(woutT_h, D, D)
    convc = ld.load_f32(convc_h, 128, 64)
    dtb_rep = make_rep(dtb_h, 16)
    alog_row = ld.load_f32(alog_h, 1, 16)
    pr.on("scalar", lambda sc: sc.activation(alog_row[:], alog_row[:], AF.Exp))
    pr.step()
    pr.on("vector", lambda v: v.tensor_scalar(alog_row[:], alog_row[:], -1.0, None, ALU.mult))
    pr.step()
    nega_rep = make_rep(alog_row, 16, is_handle=True)
    nw_rep = make_rep(nwt_h, D)

    onesmat = pr.sbuf([128, 128], F32)
    pr.on("gpsimd", lambda g: g.memset(onesmat[:], 1.0))
    pr.step()
    trilS = make_tri(pr, 128, strict=True, lower=True)
    triuI8 = make_tri(pr, 128, strict=False, lower=False)
    pr.on("vector", lambda v: v.tensor_scalar(triuI8[:], triuI8[:], 0.125, None, ALU.mult))
    pr.step()

    MIXW = 132
    mixraw = pr.sbuf([128, 16 * MIXW], F32)
    convout = pr.sbuf([128, 16 * 128], BF16)
    xt = pr.sbuf([128, D], F32)
    ln_tok = pr.sbuf([128, D], BF16)
    lnT = pr.sbuf([128, 8 * 128], BF16)
    z_tok = pr.sbuf([128, D], F32)
    beta_tok = pr.sbuf([128, 16], F32)
    g_tok = pr.sbuf([128, 16], F32)
    G_T = pr.sbuf([16, 128], F32)
    G_tok = pr.sbuf([128, 16], F32)
    kts_T = pr.sbuf([16, 128], F32)
    kts_tok = pr.sbuf([128, 16], F32)
    gam_tok = pr.sbuf([128, 16], F32)
    gcrep = pr.sbuf([64, 16], F32)
    rowt = pr.sbuf([1, 16], F32)

    GW = 8
    MW = GW * 128
    KW = GW * 64
    Gcm = pr.sbuf([128, MW], F32)
    Dm_ = pr.sbuf([128, MW], F32)
    Em = pr.sbuf([128, MW], F32)
    E2m = pr.sbuf([128, MW], F32)
    Mbf = pr.sbuf([128, MW], BF16)
    MTbf = pr.sbuf([128, MW], BF16)
    Bs = pr.sbuf([128, MW], BF16)
    BTs = pr.sbuf([128, MW], BF16)
    Xs = pr.sbuf([128, MW], BF16)
    Ttbf = pr.sbuf([128, MW], BF16)
    AqT = pr.sbuf([128, MW], BF16)
    Ibf8 = pr.sbuf([128, MW], BF16)
    trilS8 = pr.sbuf([128, MW], F32)
    triuI8w = pr.sbuf([128, MW], F32)
    Qal = pr.sbuf([64, MW], BF16)
    Kal = pr.sbuf([64, MW], BF16)
    Val = pr.sbuf([64, MW], BF16)
    Ksd = pr.sbuf([128, KW], BF16)
    Vsd = pr.sbuf([128, KW], BF16)
    BV = pr.sbuf([128, KW], BF16)
    BGK = pr.sbuf([128, KW], BF16)
    Ktil = pr.sbuf([128, KW], BF16)
    Pmat = pr.sbuf([128, KW], BF16)
    WT = pr.sbuf([64, MW], BF16)
    Ubf = pr.sbuf([128, KW], BF16)
    Osb = pr.sbuf([128, D], F32)
    Ssb = [pr.sbuf([64, 16 * 64], F32) for _ in range(2)]
    Sbf = pr.sbuf([64, KW], BF16)
    hsc = pr.sbuf([128, D], F32)
    rstd16 = pr.sbuf([128, 16], F32)
    hnorm = pr.sbuf([128, D], BF16)
    hT = pr.sbuf([128, 8 * 128], BF16)
    yt = pr.sbuf([128, D], F32)
    ps_bf = pr.psum_bf_pool(4)
    ps_a = pr.psum([128, 512], F32)
    ps_x = pr.psum([128, 1024], F32)
    ps_y = pr.psum([128, 1024], F32)
    ps_k = pr.psum([128, 128], F32)
    lnbufs = None

    def tile8(dst, src_ap):
        def f(v, dst=dst, src_ap=src_ap):
            for i in range(GW):
                v.tensor_copy(dst[:, i * 128:(i + 1) * 128], src_ap)
        pr.on("vector", f)
        pr.step()
    tile8(Ibf8, pr.ident_bf[:, :])
    tile8(trilS8, trilS[:, :])
    tile8(triuI8w, triuI8[:, :])

    pr.on("gpsimd", lambda g: g.memset(mixraw[:], 0.0))
    pr.step()
    pr.on("gpsimd", lambda g: g.memset(Ssb[0][:], 0.0))
    pr.step()
    if L0A_SKIP:
        for b in (Ksd, Vsd, Qal, Em, E2m, Mbf, MTbf, AqT, Ttbf, Pmat, WT, Ktil, BV, BGK,
                  Bs, BTs, Xs, Gcm, Dm_):
            pr.on("gpsimd", lambda g, b=b: g.memset(b[:], 0.0))
            pr.step()

    for c in range(L0A_NT):
        cur, nxt = Ssb[c % 2], Ssb[(c + 1) % 2]
        pr.dma(xt[:, :], x_h[c * 128:(c + 1) * 128, :])
        pr.step()
        lnbufs = emit_ln_transpose_tile(pr, xt[:, :], ln_tok,
                                        lambda j: j * 128, ps_bf, lnbufs, lnT)
        for fh in range(2):
            def mm(te, fh=fh):
                for ft in range(fh * 8, fh * 8 + 8):
                    for cd in range(8):
                        te.matmul(ps_x[:, (ft % 8) * 128:(ft % 8 + 1) * 128],
                                  wqkvT[:, cd * 2048 + ft * 128: cd * 2048 + (ft + 1) * 128],
                                  lnT[:, cd * 128:(cd + 1) * 128],
                                  start=(cd == 0), stop=(cd == 7))
            pr.on("tensor", mm)
            pr.step()

            def ev(v, fh=fh):
                for ft in range(fh * 8, fh * 8 + 8):
                    v.tensor_copy(mixraw[:, ft * MIXW + 4: ft * MIXW + 132],
                                  ps_x[:, (ft % 8) * 128:(ft % 8 + 1) * 128])
            pr.on("vector", ev)
            pr.step()
        def cv(v):
            for ft in range(16):
                base = ft * MIXW
                v.tensor_scalar(hsc[:, ft * 64:ft * 64 + 128] if False else hsc[:, 0:128],
                                mixraw[:, base + 1: base + 129],
                                convc[:, ft * 4:ft * 4 + 1], None, ALU.mult)
                for j in (1, 2, 3):
                    v.scalar_tensor_tensor(hsc[:, 0:128],
                                           mixraw[:, base + 1 + j: base + 129 + j],
                                           convc[:, ft * 4 + j:ft * 4 + j + 1],
                                           hsc[:, 0:128], ALU.mult, ALU.add)
                v.tensor_copy(yt[:, ft * 64:ft * 64 + 64] if False else yt[:, 0:128], hsc[:, 0:128])
        for ft in range(16):
            def cv1(v, ft=ft):
                base = ft * MIXW
                v.tensor_scalar(hsc[:, 0:128], mixraw[:, base + 1: base + 129],
                                convc[:, ft * 4:ft * 4 + 1], None, ALU.mult)
                for j in (1, 2, 3):
                    v.scalar_tensor_tensor(hsc[:, 0:128],
                                           mixraw[:, base + 1 + j: base + 129 + j],
                                           convc[:, ft * 4 + j:ft * 4 + j + 1],
                                           hsc[:, 0:128], ALU.mult, ALU.add)
            pr.on("vector", cv1)
            pr.step()
            pr.on("scalar", lambda sc, ft=ft: sc.activation(
                convout[:, ft * 128:(ft + 1) * 128], hsc[:, 0:128], AF.Silu))
            pr.step()

        def halo(v):
            for ft in range(16):
                base = ft * MIXW
                v.tensor_copy(mixraw[:, base:base + 4], mixraw[:, base + 128:base + 132])
        pr.on("vector", halo)
        pr.step()
        for ch in range(2):
            def mmz(te, ch=ch):
                for cd in range(8):
                    te.matmul(ps_a[:], lnT[:, cd * 128:(cd + 1) * 128],
                              wzT[:, cd * D + ch * 512: cd * D + ch * 512 + 512],
                              start=(cd == 0), stop=(cd == 7))
            pr.on("tensor", mmz)
            pr.step()
            pr.on("vector", lambda v, ch=ch: v.tensor_copy(
                z_tok[:, ch * 512:(ch + 1) * 512], ps_a[:]))
            pr.step()

        def mmba(te):
            for cd in range(8):
                te.matmul(ps_x[:, 0:16], lnT[:, cd * 128:(cd + 1) * 128],
                          wbaT[:, cd * 16:(cd + 1) * 16], start=(cd == 0), stop=(cd == 7))
            for cd in range(8):
                te.matmul(ps_x[:, 16:32], lnT[:, cd * 128:(cd + 1) * 128],
                          waT[:, cd * 16:(cd + 1) * 16], start=(cd == 0), stop=(cd == 7))
        pr.on("tensor", mmba)
        pr.step()
        pr.on("scalar", lambda sc: sc.activation(beta_tok[:], ps_x[:, 0:16], AF.Sigmoid))
        pr.step()
        pr.on("vector", lambda v: v.tensor_tensor(g_tok[:], ps_x[:, 16:32], dtb_rep[:, 0:16],
                                                  ALU.add))
        pr.step()
        pr.on("scalar", lambda sc: sc.activation(hsc[:, 0:16], g_tok[:], AF.Abs))
        pr.step()
        pr.on("scalar", lambda sc: sc.activation(hsc[:, 0:16], hsc[:, 0:16], AF.Exp,
                                                 scale=-1.0))
        pr.step()
        pr.on("scalar", lambda sc: sc.activation(hsc[:, 0:16], hsc[:, 0:16], AF.Ln,
                                                 bias=onecol[:, 0:1]))
        pr.step()

        def spl(v):
            v.tensor_scalar(g_tok[:], g_tok[:], 0.0, None, ALU.max)
            v.tensor_tensor(g_tok[:], g_tok[:], hsc[:, 0:16], ALU.add)
            v.tensor_tensor(g_tok[:], g_tok[:], nega_rep[:, 0:16], ALU.mult)
        pr.on("vector", spl)
        pr.step()
        if "gpre" in L0A_SKIP:
            continue
        pr.on("tensor", lambda te: te.transpose(ps_k[0:16, :], g_tok[:, :], pr.ident[:]))
        pr.step()
        pr.on("vector", lambda v: v.tensor_copy(G_T[:, :], ps_k[0:16, :]))
        pr.step()
        pr.on("vector", lambda v: v.tensor_tensor_scan(G_T[:, :], G_T[:, :], G_T[:, :],
                                                       0.0, ALU.add, ALU.bypass))
        pr.step()
        pr.on("vector", lambda v: v.tensor_scalar(kts_T[:, :], G_T[:, :],
                                                  G_T[:, 127:128], None, ALU.subtract))
        pr.step()
        pr.on("scalar", lambda sc: sc.activation(kts_T[:, :], kts_T[:, :], AF.Exp,
                                                 scale=-1.0))
        pr.step()

        def trg2(te):
            te.transpose(ps_x[:, 0:16], G_T[:, :], pr.ident[0:16, 0:16])
            te.transpose(ps_x[:, 16:32], kts_T[:, :], pr.ident[0:16, 0:16])
            te.transpose(ps_x[0:1, 32:48], G_T[:, 127:128], pr.ident[0:16, 0:16])
        pr.on("tensor", trg2)
        pr.step()

        def evg(v):
            v.tensor_copy(G_tok[:, :], ps_x[:, 0:16])
            v.tensor_copy(kts_tok[:, :], ps_x[:, 16:32])
            v.tensor_copy(rowt[:, :], ps_x[0:1, 32:48])
        pr.on("vector", evg)
        pr.step()
        pr.on("scalar", lambda sc: sc.activation(gam_tok[:, :], G_tok[:, :], AF.Exp))
        pr.step()
        pr.on("scalar", lambda sc: sc.activation(rowt[:, :], rowt[:, :], AF.Exp))
        pr.step()
        pr.on("tensor", lambda te: te.matmul(ps_x[0:64, 0:16], ones_col[0:1, 0:64],
                                             rowt[0:1, :], start=True, stop=True))
        pr.step()
        pr.on("vector", lambda v: v.tensor_copy(gcrep[:, :], ps_x[0:64, 0:16]))
        pr.step()

        for g8 in (() if "heads" in L0A_SKIP else range(2)):
            H0 = g8 * 8

            if "trkv" not in L0A_SKIP:
                def qcp(v, H0=H0):
                    for i in range(GW):
                        hh = H0 + i
                        kh = hh // 2
                        qft, qrow = kh // 2, (kh % 2) * 64
                        kft, krow = 4 + kh // 2, (kh % 2) * 64
                        vft, vrow = 8 + hh // 2, (hh % 2) * 64
                        v.tensor_copy(Qal[:, i * 128:(i + 1) * 128],
                                      convout[qrow:qrow + 64, qft * 128:(qft + 1) * 128])
                        v.tensor_copy(Kal[:, i * 128:(i + 1) * 128],
                                      convout[krow:krow + 64, kft * 128:(kft + 1) * 128])
                        v.tensor_copy(Val[:, i * 128:(i + 1) * 128],
                                      convout[vrow:vrow + 64, vft * 128:(vft + 1) * 128])
                pr.on("vector", qcp)
                pr.step()
            for rnd in (() if "trkv" in L0A_SKIP else range(4)):
                def trk(te, rnd=rnd):
                    src_b = Kal if rnd < 2 else Val
                    for i in range(4):
                        hi = (rnd % 2) * 4 + i
                        te.transpose(ps_bf[i][:, 0:64],
                                     src_b[:, hi * 128:(hi + 1) * 128],
                                     pr.ident_bf[0:64, 0:64])
                pr.on("tensor", trk)
                pr.step()

                def evk(v, rnd=rnd):
                    dst = Ksd if rnd < 2 else Vsd
                    for i in range(4):
                        hi = (rnd % 2) * 4 + i
                        v.tensor_copy(dst[:, hi * 64:(hi + 1) * 64], ps_bf[i][:, 0:64])
                pr.on("vector", evk)
                pr.step()

            if "gcmd" not in L0A_SKIP:
                def gcmf(v, H0=H0):
                    for i in range(GW):
                        v.tensor_scalar(Gcm[:, i * 128:(i + 1) * 128], onesmat[:, :],
                                        G_tok[:, H0 + i:H0 + i + 1], None, ALU.mult)
                pr.on("vector", gcmf)
                pr.step()

                def gtr(te):
                    for i in range(GW):
                        te.transpose(ps_x[:, i * 128:(i + 1) * 128],
                                     Gcm[:, i * 128:(i + 1) * 128], pr.ident[:])
                pr.on("tensor", gtr)
                pr.step()
                pr.on("vector", lambda v: v.tensor_tensor(Dm_[:, :], Gcm[:, :], ps_x[:, :],
                                                          ALU.subtract))
                pr.step()

                def emin(v):
                    v.tensor_scalar(Em[:, :], Dm_[:, :], 0.0, None, ALU.min)
                    v.tensor_scalar(E2m[:, :], Dm_[:, :], -1.0, 0.0, ALU.mult, ALU.min)
                pr.on("vector", emin)
                pr.step()
                pr.on("scalar", lambda sc: sc.activation(Em[:, :], Em[:, :], AF.Exp))
                pr.step()
                pr.on("scalar", lambda sc: sc.activation(E2m[:, :], E2m[:, :], AF.Exp))
                pr.step()

            if "mkk" not in L0A_SKIP:
                def mkk(te, H0=H0):
                    for i in range(GW):
                        Kfm = Kal[:, i * 128:(i + 1) * 128]
                        Qfm = Qal[:, i * 128:(i + 1) * 128]
                        te.matmul(ps_x[:, i * 128:(i + 1) * 128], Kfm, Kfm,
                                  start=True, stop=True)
                        te.matmul(ps_y[:, i * 128:(i + 1) * 128], Kfm, Qfm,
                                  start=True, stop=True)
                pr.on("tensor", mkk)
                pr.step()

                def mkm(v, H0=H0):
                    v.tensor_tensor(Dm_[:, :], ps_x[:, :], Em[:, :], ALU.mult)
                    for i in range(GW):
                        v.tensor_scalar(Dm_[:, i * 128:(i + 1) * 128],
                                        Dm_[:, i * 128:(i + 1) * 128],
                                        beta_tok[:, H0 + i:H0 + i + 1], None, ALU.mult)
                    v.tensor_tensor(Mbf[:, :], Dm_[:, :], trilS8[:, :], ALU.mult)
                    v.tensor_tensor(Gcm[:, :], ps_y[:, :], E2m[:, :], ALU.mult)
                    v.tensor_tensor(AqT[:, :], Gcm[:, :], triuI8w[:, :], ALU.mult)
                pr.on("vector", mkm)
                pr.step()

            for rnd in (() if "mt" in L0A_SKIP else range(2)):
                def mtr(te, rnd=rnd):
                    for i in range(4):
                        te.transpose(ps_bf[i], Mbf[:, (rnd * 4 + i) * 128:(rnd * 4 + i + 1) * 128],
                                     pr.ident_bf[:])
                pr.on("tensor", mtr)
                pr.step()

                def mte(v, rnd=rnd):
                    for i in range(4):
                        v.tensor_copy(MTbf[:, (rnd * 4 + i) * 128:(rnd * 4 + i + 1) * 128],
                                      ps_bf[i])
                pr.on("vector", mte)
                pr.step()

            def tini(v):
                v.tensor_tensor(Ttbf[:, :], Ibf8[:, :], Mbf[:, :], ALU.subtract)
                v.tensor_tensor(Xs[:, :], Ibf8[:, :], MTbf[:, :], ALU.subtract)
                v.tensor_copy(Bs[:, :], Mbf[:, :])
                v.tensor_copy(BTs[:, :], MTbf[:, :])
            pr.on("vector", tini)
            pr.step()
            for lvl in (() if "tchain" in L0A_SKIP else range(6)):
                def msq(te):
                    for i in range(GW):
                        sl = slice(i * 128, (i + 1) * 128)
                        te.matmul(ps_x[:, sl], BTs[:, sl], Bs[:, sl], start=True, stop=True)
                        te.matmul(ps_y[:, sl], Bs[:, sl], BTs[:, sl], start=True, stop=True)
                pr.on("tensor", msq)
                pr.step()

                def esq(v):
                    v.tensor_copy(Bs[:, :], ps_x[:, :])
                    v.tensor_copy(BTs[:, :], ps_y[:, :])
                    v.tensor_tensor(Bs[:, :], Bs[:, :], Ibf8[:, :], ALU.add)
                    v.tensor_tensor(BTs[:, :], BTs[:, :], Ibf8[:, :], ALU.add)
                pr.on("vector", esq)
                pr.step()

                def mpr(te):
                    for i in range(GW):
                        sl = slice(i * 128, (i + 1) * 128)
                        te.matmul(ps_x[:, sl], Xs[:, sl], Bs[:, sl], start=True, stop=True)
                        te.matmul(ps_y[:, sl], Bs[:, sl], Xs[:, sl], start=True, stop=True)
                pr.on("tensor", mpr)
                pr.step()

                def epr(v, last=(lvl == 5)):
                    v.tensor_copy(Ttbf[:, :], ps_x[:, :])
                    v.tensor_copy(Xs[:, :], ps_y[:, :])
                    if not last:
                        v.tensor_tensor(Bs[:, :], Bs[:, :], Ibf8[:, :], ALU.subtract)
                        v.tensor_tensor(BTs[:, :], BTs[:, :], Ibf8[:, :], ALU.subtract)
                pr.on("vector", epr)
                pr.step()
            pr.on("vector", lambda v: v.tensor_copy(Ttbf[:, :], Xs[:, :]))
            pr.step()

            def bvf(v, H0=H0):
                for i in range(GW):
                    h = H0 + i
                    v.tensor_scalar(BV[:, i * 64:(i + 1) * 64], Vsd[:, i * 64:(i + 1) * 64],
                                    beta_tok[:, h:h + 1], None, ALU.mult)
                    v.tensor_scalar(Gcm[:, 0:1], beta_tok[:, h:h + 1],
                                    gam_tok[:, h:h + 1], None, ALU.mult)
                    v.tensor_scalar(BGK[:, i * 64:(i + 1) * 64], Ksd[:, i * 64:(i + 1) * 64],
                                    Gcm[:, 0:1], None, ALU.mult)
                    v.tensor_scalar(Ktil[:, i * 64:(i + 1) * 64], Ksd[:, i * 64:(i + 1) * 64],
                                    kts_tok[:, h:h + 1], None, ALU.mult)
            pr.on("vector", bvf)
            pr.step()

            def mpw(te):
                for i in range(GW):
                    te.matmul(ps_x[:, i * 64:(i + 1) * 64],
                              Ttbf[:, i * 128:(i + 1) * 128], BV[:, i * 64:(i + 1) * 64],
                              start=True, stop=True)
                    te.matmul(ps_y[0:64, i * 128:(i + 1) * 128],
                              BGK[:, i * 64:(i + 1) * 64], Ttbf[:, i * 128:(i + 1) * 128],
                              start=True, stop=True)
            pr.on("tensor", mpw)
            pr.step()

            def evpw(v):
                v.tensor_copy(Pmat[:, 0:KW], ps_x[:, 0:KW])
                v.tensor_copy(WT[:, :], ps_y[0:64, :])
            pr.on("vector", evpw)
            pr.step()

            if "seq" in L0A_SKIP:
                pr.on("gpsimd", lambda g: g.memset(Osb[:, :], 0.25))
                pr.step()
                continue
            pr.on("vector", lambda v, H0=H0, cur=cur: v.tensor_copy(
                Sbf[:, :], cur[:, H0 * 64:(H0 + GW) * 64]))
            pr.step()

            def mseq(te):
                for i in range(GW):
                    te.matmul(ps_x[:, i * 64:(i + 1) * 64], WT[:, i * 128:(i + 1) * 128],
                              Sbf[:, i * 64:(i + 1) * 64], start=True, stop=True)
                    te.matmul(ps_y[:, i * 64:(i + 1) * 64], Qal[:, i * 128:(i + 1) * 128],
                              Sbf[:, i * 64:(i + 1) * 64], start=True, stop=True)
            pr.on("tensor", mseq)
            pr.step()
            pr.on("vector", lambda v: v.tensor_tensor(Ubf[:, :], Pmat[:, :], ps_x[:, 0:KW],
                                                      ALU.subtract))
            pr.step()

            def mou(te):
                for i in range(GW):
                    te.matmul(ps_x[:, i * 64:(i + 1) * 64], AqT[:, i * 128:(i + 1) * 128],
                              Ubf[:, i * 64:(i + 1) * 64], start=True, stop=True)
                    te.matmul(ps_y[0:64, 512 + i * 64:512 + (i + 1) * 64],
                              Ktil[:, i * 64:(i + 1) * 64], Ubf[:, i * 64:(i + 1) * 64],
                              start=True, stop=True)
            pr.on("tensor", mou)
            pr.step()

            def evo(v, H0=H0, cur=cur, nxt=nxt):
                for i in range(GW):
                    h = H0 + i
                    v.tensor_scalar(Osb[:, h * 64:(h + 1) * 64], ps_y[:, i * 64:(i + 1) * 64],
                                    gam_tok[:, h:h + 1], 0.125, ALU.mult, ALU.mult)
                v.tensor_tensor(Osb[:, H0 * 64:(H0 + GW) * 64],
                                Osb[:, H0 * 64:(H0 + GW) * 64], ps_x[:, 0:KW], ALU.add)
                for i in range(GW):
                    h = H0 + i
                    v.scalar_tensor_tensor(nxt[:, h * 64:(h + 1) * 64],
                                           cur[:, h * 64:(h + 1) * 64],
                                           gcrep[:, h:h + 1],
                                           ps_y[0:64, 512 + i * 64:512 + (i + 1) * 64],
                                           ALU.mult, ALU.add)
            pr.on("vector", evo)
            pr.step()
        if "heads" in L0A_SKIP:
            pr.on("gpsimd", lambda g: g.memset(Osb[:, :], 0.5))
            pr.step()
        pr.on("scalar", lambda sc: sc.activation(hsc[:, :], z_tok[:, :], AF.Silu))
        pr.step()
        pr.on("vector", lambda v: v.tensor_tensor(hsc[:, :], hsc[:, :], Osb[:, :], ALU.mult))
        pr.step()
        pr.on("scalar", lambda sc: sc.activation(yt[:, :], hsc[:, :], AF.Square))
        pr.step()
        pr.on("vector", lambda v: v.reduce_sum(
            rstd16[:, :].rearrange("p (a o) -> p a o", o=1),
            yt[:, :].rearrange("p (a n) -> p a n", a=16), axis=AX))
        pr.step()
        pr.on("vector", lambda v: v.tensor_scalar(rstd16[:, :], rstd16[:, :], 1.0 / 64,
                                                  1e-6, ALU.mult, ALU.add))
        pr.step()
        pr.on("scalar", lambda sc: sc.activation(rstd16[:, :], rstd16[:, :], AF.Sqrt))
        pr.step()
        pr.on("vector", lambda v: v.reciprocal(rstd16[:, :], rstd16[:, :]))
        pr.step()

        def hnf(v):
            for h in range(16):
                v.tensor_scalar(hnorm[:, h * 64:(h + 1) * 64], hsc[:, h * 64:(h + 1) * 64],
                                rstd16[:, h:h + 1], None, ALU.mult)
        pr.on("vector", hnf)
        pr.step()
        pr.on("vector", lambda v: v.tensor_tensor(hnorm[:, :], hnorm[:, :], nw_rep[:, :],
                                                  ALU.mult))
        pr.step()
        for j0 in (0, 4):
            def trh(te, j0=j0):
                for j in range(4):
                    te.transpose(ps_bf[j], hnorm[:, (j0 + j) * 128:(j0 + j + 1) * 128],
                                 pr.ident_bf[:])
            pr.on("tensor", trh)
            pr.step()

            def evh(v, j0=j0):
                for j in range(4):
                    v.tensor_copy(hT[:, (j0 + j) * 128:(j0 + j + 1) * 128], ps_bf[j])
            pr.on("vector", evh)
            pr.step()
        for ch in range(2):
            def mmo(te, ch=ch):
                for cd in range(8):
                    te.matmul(ps_a[:], hT[:, cd * 128:(cd + 1) * 128],
                              woutT[:, cd * D + ch * 512: cd * D + ch * 512 + 512],
                              start=(cd == 0), stop=(cd == 7))
            pr.on("tensor", mmo)
            pr.step()
            pr.on("vector", lambda v, ch=ch: v.tensor_tensor(
                yt[:, ch * 512:(ch + 1) * 512], ps_a[:], xt[:, ch * 512:(ch + 1) * 512],
                ALU.add))
            pr.step()
        pr.dma(y_h[c * 128:(c + 1) * 128, :], yt[:, :])
        pr.step()
    if standalone:
        return pr.build()
    return


def host_prep_l0a(inp):
    cw = inp["L0_attn_conv_w"][:, 0, :]
    convcols = np.ascontiguousarray(cw.reshape(16, 128, 4).transpose(1, 0, 2).reshape(128, 64))
    return {
        "wqkvT": np.ascontiguousarray(inp["L0_attn_W_qkv"].T),
        "convcols": convcols,
        "wzT": np.ascontiguousarray(inp["L0_attn_W_z"].T),
        "wbT": np.ascontiguousarray(inp["L0_attn_W_b"].T),
        "waT": np.ascontiguousarray(inp["L0_attn_W_a"].T),
        "dtb_row": np.ascontiguousarray(inp["L0_attn_dt_bias"].reshape(1, 16)),
        "alog_row": np.ascontiguousarray(inp["L0_attn_A_log"].reshape(1, 16)),
        "nw_tiled_row": np.ascontiguousarray(np.tile(inp["L0_attn_norm_weight"], 16).reshape(1, D)),
        "woutT": np.ascontiguousarray(inp["L0_attn_W_out"].T),
    }


def build_l1a(pr=None, px='', x_h=None, y_h=None):
    standalone = pr is None
    if standalone:
        pr = Prog()
    if x_h is None:
        x_h = pr.dram_in(px + "x", [S, D])
    wq_h = pr.dram_in(px + "wq", [D, D])
    wk_h = pr.dram_in(px + "wk", [D, D])
    wv_h = pr.dram_in(px + "wv", [D, D])
    wgam_h = pr.dram_in(px + "wgam", [D, 16])
    wf1_h = pr.dram_in(px + "wf1", [D, 64])
    wf2_h = pr.dram_in(px + "wf2", [64, D])
    wog1_h = pr.dram_in(px + "wog1", [D, 64])
    wog2_h = pr.dram_in(px + "wog2", [64, D])
    nw_h = pr.dram_in(px + "nw_row", [1, D])
    wo_h = pr.dram_in(px + "wo", [D, D])
    if y_h is None:
        y_h = pr.dram_out(px + "y", [S, D])
    pr.init_consts()
    onecol = pr.constcol(1.0)
    pr.step()

    ld = Loader(pr)
    ones_col = pr.sbuf([1, 128], F32)
    pr.on("gpsimd", lambda g: g.memset(ones_col[:], 1.0))
    pr.step()
    ps_rep = pr.psum([128, 512], F32)

    def make_rep(row_src, n):
        row = ld.load_f32(row_src, 1, n)
        rep = pr.sbuf([128, n], F32)
        for c0 in range(0, n, 512):
            cw = min(512, n - c0)
            pr.on("tensor", lambda te, c0=c0, cw=cw, row=row: te.matmul(
                ps_rep[:, 0:cw], ones_col[0:1, :], row[0:1, c0:c0 + cw],
                start=True, stop=True))
            pr.step()
            pr.on("vector", lambda v, c0=c0, cw=cw, rep=rep: v.tensor_copy(
                rep[:, c0:c0 + cw], ps_rep[:, 0:cw]))
            pr.step()
        return rep

    wq = ld.load_bf(wq_h, D, D)
    wk = ld.load_bf(wk_h, D, D)
    wv = ld.load_bf(wv_h, D, D)
    wgam = ld.load_bf(wgam_h, D, 16)
    wf1 = ld.load_bf(wf1_h, D, 64)
    wog1 = ld.load_bf(wog1_h, D, 64)
    wo = ld.load_bf(wo_h, D, D)
    wf2 = pr.sbuf([64, D], BF16)
    wog2 = pr.sbuf([64, D], BF16)
    for dst, hh in ((wf2, wf2_h), (wog2, wog2_h)):
        pr.step()
        pr.dma(pr.stage[0:64, 0:D], hh)
        pr.step()
        pr.on("vector", lambda v, dst=dst: v.tensor_copy(dst[:, :], pr.stage[0:64, 0:D]))
        pr.step()
    nw_rep = make_rep(nw_h, D)

    trilS64 = make_tri(pr, 64, strict=True, lower=True)
    triuI64 = make_tri(pr, 64, strict=False, lower=False)
    triuS64 = make_tri(pr, 64, strict=True, lower=False)

    xt = pr.sbuf([128, D], F32)
    ln_tok = pr.sbuf([128, D], BF16)
    lnT = pr.sbuf([128, 8 * 128], BF16)
    q_tok = pr.sbuf([128, D], F32)
    kn_tok = pr.sbuf([128, D], F32)
    v_tok = pr.sbuf([128, D], BF16)
    a_tok = pr.sbuf([128, D], F32)
    gam16 = pr.sbuf([128, 16], F32)
    f1fm = pr.sbuf([64, 128], BF16)
    Gfm = pr.sbuf([128, 8 * 128], F32)
    gkfm = pr.sbuf([128, 8 * 128], F32)
    eGtot = pr.sbuf([128, 8], F32)
    eGcols = pr.sbuf([64, 16], F32)
    Fq = pr.sbuf([128, D], F32)
    Fk = pr.sbuf([128, D], F32)
    Fc = pr.sbuf([128, D], F32)
    Fs = pr.sbuf([128, D], F32)
    FA = pr.sbuf([128, D], F32)
    qt_p = pr.sbuf([128, 512], BF16)
    kt_p = pr.sbuf([128, 512], BF16)
    kc_p = pr.sbuf([128, 512], BF16)
    Qt_p = pr.sbuf([128, 512], BF16)
    At_p = pr.sbuf([128, 512], BF16)
    at_p = pr.sbuf([128, 512], BF16)
    Kb_p = pr.sbuf([128, 512], BF16)
    GW = 8
    MW = GW * 128
    KW = GW * 64
    qtF = pr.sbuf([64, MW], BF16)
    ktF = pr.sbuf([64, MW], BF16)
    kcF = pr.sbuf([64, MW], BF16)
    QtF = pr.sbuf([64, MW], BF16)
    AtF = pr.sbuf([64, MW], BF16)
    atF = pr.sbuf([64, MW], BF16)
    Nbf = pr.sbuf([128, MW], BF16)
    NTbf = pr.sbuf([128, MW], BF16)
    AqT = pr.sbuf([128, MW], BF16)
    Bs = pr.sbuf([128, MW], BF16)
    BTs = pr.sbuf([128, MW], BF16)
    Xs = pr.sbuf([128, MW], BF16)
    Ttbf = pr.sbuf([128, MW], BF16)
    Ibf8 = pr.sbuf([128, MW], BF16)
    rhsb = pr.sbuf([128, KW], BF16)
    Ubf = pr.sbuf([128, KW], BF16)
    Osb = pr.sbuf([128, D], F32)
    Ssb = [pr.sbuf([64, 16 * 64], F32) for _ in range(2)]
    Sbf = pr.sbuf([64, KW], BF16)
    hsc = pr.sbuf([128, D], F32)
    hnorm = pr.sbuf([128, D], BF16)
    hT = pr.sbuf([128, 8 * 128], BF16)
    yt = pr.sbuf([128, D], F32)
    st4 = pr.sbuf([128, 16], F32)
    ps_bf = pr.psum_bf_pool(4)
    ps_a = pr.psum([128, 512], F32)
    ps_x = pr.psum([128, 1024], F32)
    ps_y = pr.psum([128, 1024], F32)
    lnbufs = None

    def tile8(dst, src_ap):
        def f(v, dst=dst, src_ap=src_ap):
            for i in range(GW):
                v.tensor_copy(dst[:, i * 128:(i + 1) * 128], src_ap)
        pr.on("vector", f)
        pr.step()
    tile8(Ibf8, pr.ident_bf[:, :])

    pr.on("gpsimd", lambda g: g.memset(Ssb[0][:], 0.0))
    pr.step()
    pr.on("gpsimd", lambda g: g.memset(Nbf[:], 0.0))
    pr.step()
    pr.on("gpsimd", lambda g: g.memset(NTbf[:], 0.0))
    pr.step()
    pr.on("gpsimd", lambda g: g.memset(AqT[:], 0.0))
    pr.step()

    for c in range(NT):
        cur, nxt = Ssb[c % 2], Ssb[(c + 1) % 2]
        pr.dma(xt[:, :], x_h[c * 128:(c + 1) * 128, :])
        pr.step()
        lnbufs = emit_ln_transpose_tile(pr, xt[:, :], ln_tok,
                                        lambda j: j * 128, ps_bf, lnbufs, lnT)
        for (wmat, n_out, tag) in ((wq, D, "q"), (wk, D, "k"), (wv, D, "v")):
            for ch in range(n_out // 512):
                def mmp(te, wmat=wmat, ch=ch):
                    for cd in range(8):
                        te.matmul(ps_a[:], lnT[:, cd * 128:(cd + 1) * 128],
                                  wmat[:, cd * D + ch * 512: cd * D + ch * 512 + 512],
                                  start=(cd == 0), stop=(cd == 7))
                pr.on("tensor", mmp)
                pr.step()
                if tag == "q":
                    pr.on("scalar", lambda sc, ch=ch: sc.activation(
                        q_tok[:, ch * 512:(ch + 1) * 512], ps_a[:], AF.Silu))
                elif tag == "k":
                    pr.on("scalar", lambda sc, ch=ch: sc.activation(
                        kn_tok[:, ch * 512:(ch + 1) * 512], ps_a[:], AF.Silu))
                else:
                    pr.on("vector", lambda v, ch=ch: v.tensor_copy(
                        v_tok[:, ch * 512:(ch + 1) * 512], ps_a[:]))
                pr.step()
        pr.on("scalar", lambda sc: sc.activation(hsc[:, :], kn_tok[:, :], AF.Square))
        pr.step()
        pr.on("vector", lambda v: v.reduce_sum(
            st4[:, :].rearrange("p (a o) -> p a o", o=1),
            hsc[:, :].rearrange("p (a n) -> p a n", a=16), axis=AX))
        pr.step()
        pr.on("scalar", lambda sc: sc.activation(st4[:, :], st4[:, :], AF.Sqrt))
        pr.step()
        pr.on("vector", lambda v: v.tensor_scalar(st4[:, :], st4[:, :], 1e-12, None,
                                                  ALU.max))
        pr.step()
        pr.on("vector", lambda v: v.reciprocal(st4[:, :], st4[:, :]))
        pr.step()

        def knn(v):
            for h in range(16):
                v.tensor_scalar(kn_tok[:, h * 64:(h + 1) * 64], kn_tok[:, h * 64:(h + 1) * 64],
                                st4[:, h:h + 1], None, ALU.mult)
        pr.on("vector", knn)
        pr.step()
        def mmg(te):
            for cd in range(8):
                te.matmul(ps_x[:, 0:16], lnT[:, cd * 128:(cd + 1) * 128],
                          wgam[:, cd * 16:(cd + 1) * 16], start=(cd == 0), stop=(cd == 7))
            for cd in range(8):
                te.matmul(ps_x[:, 16:80], lnT[:, cd * 128:(cd + 1) * 128],
                          wf1[:, cd * 64:(cd + 1) * 64], start=(cd == 0), stop=(cd == 7))
            for cd in range(8):
                te.matmul(ps_x[:, 80:144], lnT[:, cd * 128:(cd + 1) * 128],
                          wog1[:, cd * 64:(cd + 1) * 64], start=(cd == 0), stop=(cd == 7))
        pr.on("tensor", mmg)
        pr.step()
        pr.on("scalar", lambda sc: sc.activation(gam16[:, :], ps_x[:, 0:16], AF.Sigmoid))
        pr.step()
        pr.on("vector", lambda v: v.tensor_scalar(gam16[:, :], gam16[:, :], -1.0, None,
                                                  ALU.mult))
        pr.step()
        def evf(v):
            v.tensor_copy(hsc[:, 0:64], ps_x[:, 16:80])
            v.tensor_copy(hsc[:, 64:128], ps_x[:, 80:144])
        pr.on("vector", evf)
        pr.step()
        pr.on("vector", lambda v: v.tensor_copy(ln_tok[:, 0:128], hsc[:, 0:128]))
        pr.step()

        def trf(te):
            te.transpose(ps_bf[0], ln_tok[:, 0:128], pr.ident_bf[:])
        pr.on("tensor", trf)
        pr.step()
        pr.on("vector", lambda v: v.tensor_copy(f1fm[:, :], ps_bf[0][0:64, :]))
        pr.step()
        ogfm = ps_bf[0][64:128, :]
        og1fm = pr.stage_bf

        def cpog(v):
            v.tensor_copy(og1fm[0:64, 0:128], ps_bf[0][64:128, :])
        pr.on("vector", cpog)
        pr.step()
        for i in range(8):
            def mmgk(te, i=i):
                te.matmul(ps_x[:, i * 128:(i + 1) * 128], wf2[:, i * 128:(i + 1) * 128],
                          f1fm[:, :], start=True, stop=True)
            pr.on("tensor", mmgk)
            pr.step()
        def lsig1(v):
            v.tensor_copy(gkfm[:, :], ps_x[:, :])
        pr.on("vector", lsig1)
        pr.step()
        pr.on("scalar", lambda sc: sc.activation(Gfm[:, :], gkfm[:, :], AF.Abs))
        pr.step()
        pr.on("scalar", lambda sc: sc.activation(Gfm[:, :], Gfm[:, :], AF.Exp, scale=-1.0))
        pr.step()
        pr.on("scalar", lambda sc: sc.activation(Gfm[:, :], Gfm[:, :], AF.Ln,
                                                 bias=onecol[:, 0:1]))
        pr.step()

        def lsig2(v):
            v.tensor_scalar(gkfm[:, :], gkfm[:, :], 0.0, None, ALU.min)
            v.tensor_tensor(gkfm[:, :], gkfm[:, :], Gfm[:, :], ALU.subtract)
        pr.on("vector", lsig2)
        pr.step()
        def trfield(dstT, negate):
            pr.on("scalar", lambda sc: sc.activation(hsc[:, :], gkfm[:, :], AF.Exp,
                                                     scale=(-1.0 if negate else 1.0)))
            pr.step()

            def trx(te):
                for j in range(8):
                    te.transpose(ps_x[:, j * 128:(j + 1) * 128],
                                 hsc[:, j * 128:(j + 1) * 128], pr.ident[:])
            pr.on("tensor", trx)
            pr.step()
            pr.on("vector", lambda v, dstT=dstT: v.tensor_copy(dstT[:, :], ps_x[:, :]))
            pr.step()
        trfield(FA, False)

        def mka(v):
            for h in range(16):
                v.tensor_scalar(a_tok[:, h * 64:(h + 1) * 64],
                                kn_tok[:, h * 64:(h + 1) * 64],
                                gam16[:, h:h + 1], None, ALU.mult)
            v.tensor_tensor(a_tok[:, :], a_tok[:, :], FA[:, :], ALU.mult)
        pr.on("vector", mka)
        pr.step()
        def scn(v):
            for i in range(8):
                v.tensor_tensor_scan(Gfm[:, i * 128:(i + 1) * 128],
                                     gkfm[:, i * 128:(i + 1) * 128],
                                     gkfm[:, i * 128:(i + 1) * 128], 0.0,
                                     ALU.add, ALU.bypass)
        pr.on("vector", scn)
        pr.step()
        def ffq(v):
            for i in range(8):
                sl = slice(i * 128, i * 128 + 64)
                sr = slice(i * 128 + 64, (i + 1) * 128)
                v.tensor_copy(gkfm[:, sl], Gfm[:, sl])
                v.tensor_scalar(gkfm[:, sr], Gfm[:, sr], Gfm[:, i * 128 + 63:i * 128 + 64],
                                None, ALU.subtract)
        pr.on("vector", ffq)
        pr.step()

        trfield(Fq, False)
        trfield(Fk, True)
        def ffa(v):
            v.tensor_copy(gkfm[:, :], Gfm[:, :])
        pr.on("vector", ffa)
        pr.step()
        trfield(FA, False)
        def ffc(v):
            for i in range(8):
                v.tensor_scalar(gkfm[:, i * 128:(i + 1) * 128],
                                Gfm[:, i * 128:(i + 1) * 128],
                                Gfm[:, i * 128 + 63:i * 128 + 64], None, ALU.subtract)
        pr.on("vector", ffc)
        pr.step()
        trfield(Fc, True)
        def ffs(v):
            for i in range(8):
                v.tensor_scalar(gkfm[:, i * 128:(i + 1) * 128],
                                Gfm[:, i * 128:(i + 1) * 128],
                                Gfm[:, i * 128 + 127:i * 128 + 128], None, ALU.subtract)
        pr.on("vector", ffs)
        pr.step()
        trfield(Fs, True)
        def egt(v):
            for i in range(8):
                v.tensor_copy(eGtot[:, i:i + 1], Gfm[:, i * 128 + 127:i * 128 + 128])
        pr.on("vector", egt)
        pr.step()
        pr.on("scalar", lambda sc: sc.activation(eGtot[:, :], eGtot[:, :], AF.Exp))
        pr.step()

        def egc(v):
            for h in range(16):
                ti, trow = h // 2, (h % 2) * 64
                v.tensor_copy(eGcols[:, h:h + 1], eGtot[trow:trow + 64, ti:ti + 1])
        pr.on("vector", egc)
        pr.step()
        for g8 in range(2):
            H0 = g8 * 8
            gsl = slice(H0 * 64, (H0 + 8) * 64)

            def prod(v, gsl=gsl):
                v.tensor_tensor(qt_p[:, :], q_tok[:, gsl], Fq[:, gsl], ALU.mult)
                v.tensor_tensor(kt_p[:, :], kn_tok[:, gsl], Fk[:, gsl], ALU.mult)
                v.tensor_tensor(kc_p[:, :], kn_tok[:, gsl], Fc[:, gsl], ALU.mult)
                v.tensor_tensor(Qt_p[:, :], q_tok[:, gsl], FA[:, gsl], ALU.mult)
                v.tensor_tensor(At_p[:, :], a_tok[:, gsl], FA[:, gsl], ALU.mult)
                v.tensor_tensor(at_p[:, :], a_tok[:, gsl], Fq[:, gsl], ALU.mult)
                v.tensor_tensor(Kb_p[:, :], kn_tok[:, gsl], Fs[:, gsl], ALU.mult)
            pr.on("vector", prod)
            pr.step()
            for fld_src, fld_dst in ((qt_p, qtF), (kt_p, ktF), (kc_p, kcF),
                                     (Qt_p, QtF), (At_p, AtF), (at_p, atF)):
                for rnd in range(2):
                    def trp(te, fld_src=fld_src, rnd=rnd):
                        for j in range(4):
                            hi = rnd * 4 + j
                            te.transpose(ps_bf[j][0:64, 0:128],
                                         fld_src[:, hi * 64:(hi + 1) * 64],
                                         pr.ident_bf[:])
                    pr.on("tensor", trp)
                    pr.step()

                    def evp(v, fld_dst=fld_dst, rnd=rnd):
                        for j in range(4):
                            i = rnd * 4 + j
                            v.tensor_copy(fld_dst[:, i * 128:(i + 1) * 128],
                                          ps_bf[j][0:64, :])
                    pr.on("vector", evp)
                    pr.step()
            def mmn(te):
                for i in range(GW):
                    b = i * 128
                    a1 = atF[:, b:b + 64]
                    a2 = atF[:, b + 64:b + 128]
                    k1 = ktF[:, b:b + 64]
                    k2 = ktF[:, b + 64:b + 128]
                    kc1 = kcF[:, b:b + 64]
                    q1 = qtF[:, b:b + 64]
                    q2 = qtF[:, b + 64:b + 128]
                    te.matmul(ps_x[0:64, b:b + 64], a1, k1, start=True, stop=True)
                    te.matmul(ps_x[64:128, b + 64:b + 128], a2, k2, start=True, stop=True)
                    te.matmul(ps_x[64:128, b:b + 64], a2, kc1, start=True, stop=True)
                    te.matmul(ps_y[0:64, b:b + 64], k1, q1, start=True, stop=True)
                    te.matmul(ps_y[64:128, b + 64:b + 128], k2, q2, start=True, stop=True)
                    te.matmul(ps_y[0:64, b + 64:b + 128], kc1, q2, start=True, stop=True)
            pr.on("tensor", mmn)
            pr.step()

            def evn(v):
                for i in range(GW):
                    b = i * 128
                    v.tensor_tensor(Nbf[0:64, b:b + 64], ps_x[0:64, b:b + 64],
                                    trilS64[:, :], ALU.mult)
                    v.tensor_tensor(Nbf[64:128, b + 64:b + 128],
                                    ps_x[64:128, b + 64:b + 128], trilS64[:, :], ALU.mult)
                    v.tensor_copy(Nbf[64:128, b:b + 64], ps_x[64:128, b:b + 64])
                    v.tensor_tensor(AqT[0:64, b:b + 64], ps_y[0:64, b:b + 64],
                                    triuI64[:, :], ALU.mult)
                    v.tensor_tensor(AqT[64:128, b + 64:b + 128],
                                    ps_y[64:128, b + 64:b + 128], triuI64[:, :], ALU.mult)
                    v.tensor_copy(AqT[0:64, b + 64:b + 128], ps_y[0:64, b + 64:b + 128])
            pr.on("vector", evn)
            pr.step()
            def mmnt(te):
                for i in range(GW):
                    b = i * 128
                    a1 = atF[:, b:b + 64]
                    a2 = atF[:, b + 64:b + 128]
                    k1 = ktF[:, b:b + 64]
                    k2 = ktF[:, b + 64:b + 128]
                    kc1 = kcF[:, b:b + 64]
                    te.matmul(ps_x[0:64, b:b + 64], k1, a1, start=True, stop=True)
                    te.matmul(ps_x[64:128, b + 64:b + 128], k2, a2, start=True, stop=True)
                    te.matmul(ps_x[0:64, b + 64:b + 128], kc1, a2, start=True, stop=True)
            pr.on("tensor", mmnt)
            pr.step()

            def evnt(v):
                for i in range(GW):
                    b = i * 128
                    v.tensor_tensor(NTbf[0:64, b:b + 64], ps_x[0:64, b:b + 64],
                                    triuS64[:, :], ALU.mult)
                    v.tensor_tensor(NTbf[64:128, b + 64:b + 128],
                                    ps_x[64:128, b + 64:b + 128], triuS64[:, :], ALU.mult)
                    v.tensor_copy(NTbf[0:64, b + 64:b + 128], ps_x[0:64, b + 64:b + 128])
            pr.on("vector", evnt)
            pr.step()
            def tini(v):
                v.tensor_tensor(Ttbf[:, :], Ibf8[:, :], Nbf[:, :], ALU.add)
                v.tensor_tensor(Xs[:, :], Ibf8[:, :], NTbf[:, :], ALU.add)
                v.tensor_copy(Bs[:, :], Nbf[:, :])
                v.tensor_copy(BTs[:, :], NTbf[:, :])
            pr.on("vector", tini)
            pr.step()
            for lvl in range(6):
                def msq(te):
                    for i in range(GW):
                        sl = slice(i * 128, (i + 1) * 128)
                        te.matmul(ps_x[:, sl], BTs[:, sl], Bs[:, sl], start=True, stop=True)
                        te.matmul(ps_y[:, sl], Bs[:, sl], BTs[:, sl], start=True, stop=True)
                pr.on("tensor", msq)
                pr.step()

                def esq(v):
                    v.tensor_copy(Bs[:, :], ps_x[:, :])
                    v.tensor_copy(BTs[:, :], ps_y[:, :])
                    v.tensor_tensor(Bs[:, :], Bs[:, :], Ibf8[:, :], ALU.add)
                    v.tensor_tensor(BTs[:, :], BTs[:, :], Ibf8[:, :], ALU.add)
                pr.on("vector", esq)
                pr.step()

                def mpr(te):
                    for i in range(GW):
                        sl = slice(i * 128, (i + 1) * 128)
                        te.matmul(ps_x[:, sl], Xs[:, sl], Bs[:, sl], start=True, stop=True)
                        te.matmul(ps_y[:, sl], Bs[:, sl], Xs[:, sl], start=True, stop=True)
                pr.on("tensor", mpr)
                pr.step()

                def epr(v, last=(lvl == 5)):
                    v.tensor_copy(Ttbf[:, :], ps_x[:, :])
                    v.tensor_copy(Xs[:, :], ps_y[:, :])
                    if not last:
                        v.tensor_tensor(Bs[:, :], Bs[:, :], Ibf8[:, :], ALU.subtract)
                        v.tensor_tensor(BTs[:, :], BTs[:, :], Ibf8[:, :], ALU.subtract)
                pr.on("vector", epr)
                pr.step()
            pr.on("vector", lambda v: v.tensor_copy(Ttbf[:, :], Xs[:, :]))
            pr.step()

            pr.on("vector", lambda v, H0=H0, cur=cur: v.tensor_copy(
                Sbf[:, :], cur[:, H0 * 64:(H0 + GW) * 64]))
            pr.step()

            def mmr(te):
                for i in range(GW):
                    te.matmul(ps_x[:, i * 64:(i + 1) * 64], AtF[:, i * 128:(i + 1) * 128],
                              Sbf[:, i * 64:(i + 1) * 64], start=True, stop=True)
            pr.on("tensor", mmr)
            pr.step()

            def evr(v, H0=H0):
                v.tensor_tensor(rhsb[:, :], v_tok[:, H0 * 64:(H0 + GW) * 64],
                                ps_x[:, 0:KW], ALU.add)
            pr.on("vector", evr)
            pr.step()

            def mmu(te):
                for i in range(GW):
                    te.matmul(ps_x[:, i * 64:(i + 1) * 64], Ttbf[:, i * 128:(i + 1) * 128],
                              rhsb[:, i * 64:(i + 1) * 64], start=True, stop=True)
            pr.on("tensor", mmu)
            pr.step()
            pr.on("vector", lambda v: v.tensor_copy(Ubf[:, :], ps_x[:, 0:KW]))
            pr.step()
            def mmo(te):
                for i in range(GW):
                    te.matmul(ps_y[:, i * 64:(i + 1) * 64], QtF[:, i * 128:(i + 1) * 128],
                              Sbf[:, i * 64:(i + 1) * 64], start=True, stop=False)
                    te.matmul(ps_y[:, i * 64:(i + 1) * 64], AqT[:, i * 128:(i + 1) * 128],
                              Ubf[:, i * 64:(i + 1) * 64], start=False, stop=True)
            pr.on("tensor", mmo)
            pr.step()

            def mmd(te):
                for i in range(GW):
                    te.matmul(ps_x[0:64, 512 + i * 64:512 + (i + 1) * 64],
                              Kb_p[:, i * 64:(i + 1) * 64],
                              Ubf[:, i * 64:(i + 1) * 64], start=True, stop=True)
            pr.on("tensor", mmd)
            pr.step()

            def evo(v, H0=H0, cur=cur, nxt=nxt):
                v.tensor_copy(Osb[:, H0 * 64:(H0 + GW) * 64], ps_y[:, 0:KW])
                for i in range(GW):
                    h = H0 + i
                    v.scalar_tensor_tensor(nxt[:, h * 64:(h + 1) * 64],
                                           cur[:, h * 64:(h + 1) * 64],
                                           eGcols[:, h:h + 1],
                                           ps_x[0:64, 512 + i * 64:512 + (i + 1) * 64],
                                           ALU.mult, ALU.add)
            pr.on("vector", evo)
            pr.step()

        for ch in range(2):
            def mmog(te, ch=ch):
                te.matmul(ps_a[:], og1fm[0:64, 0:128],
                          wog2[:, ch * 512:(ch + 1) * 512], start=True, stop=True)
            pr.on("tensor", mmog)
            pr.step()
            pr.on("scalar", lambda sc, ch=ch: sc.activation(
                hsc[:, ch * 512:(ch + 1) * 512], ps_a[:], AF.Sigmoid))
            pr.step()
        pr.on("vector", lambda v: v.tensor_tensor(hsc[:, :], Osb[:, :], hsc[:, :],
                                                  ALU.mult))
        pr.step()
        def lna(v):
            v.reduce_sum(st4[:, 0:1], hsc[:, :], axis=AX)
        pr.on("vector", lna)
        pr.on("scalar", lambda sc: sc.activation(yt[:, :], hsc[:, :], AF.Square))
        pr.step()

        def lnb(v):
            v.reduce_sum(st4[:, 1:2], yt[:, :], axis=AX)
            v.tensor_scalar(st4[:, 0:1], st4[:, 0:1], 1.0 / D, None, ALU.mult)
            v.tensor_scalar(st4[:, 1:2], st4[:, 1:2], 1.0 / D, None, ALU.mult)
            v.tensor_tensor(st4[:, 2:3], st4[:, 0:1], st4[:, 0:1], ALU.mult)
            v.tensor_tensor(st4[:, 1:2], st4[:, 1:2], st4[:, 2:3], ALU.subtract)
            v.tensor_scalar(st4[:, 1:2], st4[:, 1:2], 1e-5, None, ALU.add)
        pr.on("vector", lnb)
        pr.step()
        pr.on("scalar", lambda sc: sc.activation(st4[:, 1:2], st4[:, 1:2], AF.Sqrt))
        pr.step()
        pr.on("vector", lambda v: v.reciprocal(st4[:, 2:3], st4[:, 1:2]))
        pr.step()

        def nrm2(v):
            v.tensor_scalar(hsc[:, :], hsc[:, :], st4[:, 0:1], st4[:, 2:3],
                            ALU.subtract, ALU.mult)
            v.tensor_tensor(hnorm[:, :], hsc[:, :], nw_rep[:, :], ALU.mult)
        pr.on("vector", nrm2)
        pr.step()
        for j0 in (0, 4):
            def trh(te, j0=j0):
                for j in range(4):
                    te.transpose(ps_bf[j], hnorm[:, (j0 + j) * 128:(j0 + j + 1) * 128],
                                 pr.ident_bf[:])
            pr.on("tensor", trh)
            pr.step()

            def evh(v, j0=j0):
                for j in range(4):
                    v.tensor_copy(hT[:, (j0 + j) * 128:(j0 + j + 1) * 128], ps_bf[j])
            pr.on("vector", evh)
            pr.step()
        for ch in range(2):
            def mmw(te, ch=ch):
                for cd in range(8):
                    te.matmul(ps_a[:], hT[:, cd * 128:(cd + 1) * 128],
                              wo[:, cd * D + ch * 512: cd * D + ch * 512 + 512],
                              start=(cd == 0), stop=(cd == 7))
            pr.on("tensor", mmw)
            pr.step()
            pr.on("vector", lambda v, ch=ch: v.tensor_tensor(
                yt[:, ch * 512:(ch + 1) * 512], ps_a[:], xt[:, ch * 512:(ch + 1) * 512],
                ALU.add))
            pr.step()
        pr.dma(y_h[c * 128:(c + 1) * 128, :], yt[:, :])
        pr.step()
    if standalone:
        return pr.build()
    return


def host_prep_l1a(inp):
    return {
        "wq": inp["L1_attn_Wq"], "wk": inp["L1_attn_Wk"], "wv": inp["L1_attn_Wv"],
        "wgam": inp["L1_attn_Wgamma"], "wf1": inp["L1_attn_Wf1"], "wf2": inp["L1_attn_Wf2"],
        "wog1": inp["L1_attn_Wog1"], "wog2": inp["L1_attn_Wog2"],
        "nw_row": np.ascontiguousarray(inp["L1_attn_norm_w"].reshape(1, -1)),
        "wo": inp["L1_attn_Wo"],
    }


_PROGS = {}

_BLOCKS = [
    ("l0a", build_l0a, host_prep_l0a),
    ("l0f", build_l0f, host_prep_l0f),
    ("l1a", build_l1a, host_prep_l1a),
    ("l1f", build_l1f, lambda inp: {
        "W_rT": np.ascontiguousarray(inp["L1_ffn_W_r"].T),
        "W_kT": np.ascontiguousarray(inp["L1_ffn_W_k"].T),
        "W_vT": np.ascontiguousarray(inp["L1_ffn_W_v"].T),
        "mu_r_cols": np.ascontiguousarray(inp["L1_ffn_mu_r"].reshape(8, 128).T),
        "mu_k_cols": np.ascontiguousarray(inp["L1_ffn_mu_k"].reshape(8, 128).T),
    }),
    ("l2a", build_l2a, host_prep_l2a),
    ("l2f", build_l2f, host_prep_l2f),
    ("l3a", build_l3a, host_prep_l3a),
    ("l3f", build_l3f, host_prep_l3f),
]


def build_fused():
    pr = Prog()
    pr.init_consts()
    x0 = pr.dram_in("x", [S, D])
    xs = [x0] + [pr.dram_tmp("xt%d" % i, [S, D]) for i in range(len(_BLOCKS) - 1)]
    y = pr.dram_out("y", [S, D])
    xs.append(y)
    for bi, (name, build_fn, _) in enumerate(_BLOCKS):
        m = pr.mark()
        build_fn(pr=pr, px=name + "_", x_h=xs[bi], y_h=xs[bi + 1])
        pr.free_to(m)
    return pr.build()


def _weight_map(inputs):
    wmap = {}
    for name, _, prep_fn in _BLOCKS:
        for k, v in prep_fn(inputs).items():
            wmap[name + "_" + k] = v
    return wmap


def kernel(**inputs):
    x = np.ascontiguousarray(np.asarray(inputs["L0_attn_hidden_states"], np.float32))
    B = x.shape[0]
    assert B == NCORES
    if "fused" not in _PROGS:
        _PROGS["fused"] = build_fused()
    nc = _PROGS["fused"]
    wmap = _weight_map(inputs)
    in_maps = [{"x": np.ascontiguousarray(x[b]), **wmap} for b in range(B)]
    res = run_bass_kernel_spmd(nc, in_maps, core_ids=list(range(B)))
    return np.stack([np.asarray(res.results[b]["y"], np.float32) for b in range(B)])


# version 05
# speedup vs baseline: 1.4567x; 1.4567x over previous
import math
from contextlib import ExitStack

import numpy as np
import concourse.bass as bass
import concourse.mybir as mybir
from concourse.bass_utils import run_bass_kernel_spmd

F32 = mybir.dt.float32
BF16 = mybir.dt.bfloat16
AF = mybir.ActivationFunctionType
ALU = mybir.AluOpType
AX = mybir.AxisListType.X

NCORES = 8
S = 4096
D = 1024
NT = S // 128
MAGIC = 1.5 * 2 ** 23
TWO_PI = 2.0 * math.pi


class Prog:

    ENGINES = ("sync", "tensor", "vector", "scalar", "gpsimd")

    def __init__(self):
        self.nc = bass.Bass("TRN2", target_bir_lowering=False)
        self.es = ExitStack()
        self._cms = {}
        self._order = []
        self.steps = [[{e: [] for e in self.ENGINES}, self.ENGINES]]
        self._built = False
        self.ident = None
        self.ident_bf = None
        self.ones1 = None

    def sbuf(self, shape, dtype=F32):
        cm = self.nc.sbuf_tensor(shape, dtype)
        h = cm.__enter__()
        self._cms[id(h)] = cm
        self._order.append(id(h))
        return h

    def psum(self, shape, dtype=F32):
        cm = self.nc.psum_tensor(shape, dtype)
        h = cm.__enter__()
        self._cms[id(h)] = cm
        self._order.append(id(h))
        return h

    def free(self, h):
        cm = self._cms.pop(id(h))
        self._order.remove(id(h))
        cm.__exit__(None, None, None)

    def dram_in(self, name, shape, dtype=F32):
        return self.nc.dram_tensor(name, list(shape), dtype, kind="ExternalInput").ap()

    def dram_out(self, name, shape, dtype=F32):
        return self.nc.dram_tensor(name, list(shape), dtype, kind="ExternalOutput").ap()

    def dram_tmp(self, name, shape, dtype=F32):
        return self.nc.dram_tensor(name, list(shape), dtype, kind="Internal").ap()

    def step(self, engines=None):
        ops = self.steps[-1][0]
        used = {e for e in self.ENGINES if ops[e]}
        self.steps[-1][1] = tuple(e for e in self.ENGINES if e in used) or ("sync",)
        self.steps.append([{e: [] for e in self.ENGINES}, self.ENGINES])

    def on(self, engine, fn):
        self.steps[-1][0][engine].append(("op", fn))

    def dma(self, out, in_, engine="sync", **kw):
        self.steps[-1][0][engine].append(("dma", (out, in_, kw)))

    def psum_bf_pool(self, n=4):
        big = self.psum([128, 128 * n], BF16)
        return [big[:, j * 128:(j + 1) * 128] for j in range(n)]

    def constcol(self, value, dtype=F32):
        t = self.sbuf([128, 1], dtype)
        self.on("gpsimd", lambda g: g.memset(t[:], value))
        return t

    def init_consts(self):
        if self.ident is not None:
            return
        self.ident = make_identity(self, F32)
        self.ident_bf = make_identity(self, BF16)
        ones1 = self.sbuf([1, 1], F32)
        self.on("gpsimd", lambda g: g.memset(ones1[:], 1.0))
        self.ones1 = ones1
        self.stage = self.sbuf([128, 2048], F32)
        self.stage_bf = self.sbuf([128, 1024], BF16)
        self.step()

    def mark(self):
        return len(self._order)

    def free_to(self, m):
        for hid in list(reversed(self._order[m:])):
            cm = self._cms.pop(hid)
            self._order.remove(hid)
            cm.__exit__(None, None, None)

    def build(self):
        assert not self._built
        self._built = True
        ops = self.steps[-1][0]
        used = {e for e in self.ENGINES if ops[e]}
        self.steps[-1][1] = tuple(e for e in self.ENGINES if e in used) or ("sync",)
        nc = self.nc
        bar = self.es.enter_context(nc.semaphore("bar"))
        dsem = {e: self.es.enter_context(nc.semaphore("d_" + e)) for e in self.ENGINES}
        dtot = {e: 0 for e in self.ENGINES}
        block = self.es.enter_context(nc.Block())
        steps = self.steps
        nsteps = len(steps)

        ssem = {e: self.es.enter_context(nc.semaphore("s_" + e)) for e in self.ENGINES}
        _PASS = {"wait_ge", "sem_inc", "wait_op", "maybe_wait_ge", "nop", "drain"}

        class _Rec:

            def __init__(self, eng, sem, is_pe):
                self._eng = eng
                self._sem = sem
                self._pe = is_pe
                self.cnt = 0
                self.last = None

            def __getattr__(self, name):
                a = getattr(self._eng, name)
                if callable(a) and name not in _PASS:
                    def w(*args, **kw):
                        if not self._pe and self.cnt > 0:
                            self._eng.wait_ge(self._sem, self.cnt)
                        r = a(*args, **kw)
                        if isinstance(r, bass.BassInstruction):
                            if self._pe:
                                self.last = r
                            else:
                                r.then_inc(self._sem, 1)
                                self.cnt += 1
                        return r
                    return w
                return a

        cum = [0]
        for ops, engs in steps:
            cum.append(cum[-1] + len(engs))

        def mk(ename):
            is_pe = ename == "tensor"

            def body(eng):
                rec = _Rec(eng, ssem[ename], is_pe)
                for k in range(nsteps):
                    ops, engs = steps[k]
                    if ename not in engs:
                        continue
                    if k > 0:
                        eng.wait_ge(bar, cum[k])
                    n0 = rec.cnt
                    rec.last = None
                    had_dma = False
                    for kind, payload in ops[ename]:
                        if kind == "op":
                            payload(rec)
                        else:
                            out, in_, kw = payload
                            eng.dma_start(out=out, in_=in_, **kw).then_inc(dsem[ename], 16)
                            dtot[ename] += 16
                            had_dma = True
                    if is_pe and rec.last is not None:
                        rec.last.then_inc(ssem[ename], 1)
                        rec.cnt += 1
                    if had_dma:
                        eng.wait_ge(dsem[ename], dtot[ename])
                    if rec.cnt > n0:
                        eng.wait_ge(ssem[ename], rec.cnt)
                    eng.sem_inc(bar, 1)
            return body

        for ename in self.ENGINES:
            getattr(block, ename)(mk(ename))
        for hid in reversed(self._order):
            self._cms.pop(hid).__exit__(None, None, None)
        self._order.clear()
        self.es.close()
        return nc


def make_identity(pr, dtype=F32, n=128):
    t = pr.sbuf([n, n], dtype)
    pr.on("gpsimd", lambda g: g.memset(t[:], 0.0))
    pr.step()
    pr.on("gpsimd", lambda g: g.affine_select(out=t[:], in_=t[:], compare_op=ALU.not_equal,
                                              fill=1.0, base=0, pattern=[[-1, n]],
                                              channel_multiplier=1))
    pr.step()
    return t


def make_tri(pr, n=128, strict=True, lower=True, dtype=F32):
    t = pr.sbuf([n, n], dtype)
    pr.on("gpsimd", lambda g: g.memset(t[:], 1.0))
    pr.step()
    pr.on("gpsimd", lambda g: g.affine_select(out=t[:], in_=t[:], compare_op=ALU.is_gt,
                                              fill=0.0, base=0, pattern=[[-1, n]],
                                              channel_multiplier=1))
    pr.step()
    if strict and lower:
        return t
    if not strict and lower:
        pr.on("vector", lambda v: v.tensor_tensor(t[:], t[:], pr.ident[0:n, 0:n], ALU.add))
        pr.step()
        return t
    if not strict and not lower:
        pr.on("vector", lambda v: v.tensor_scalar(t[:], t[:], -1.0, 1.0, ALU.mult, ALU.add))
        pr.step()
        return t
    def f(v):
        v.tensor_scalar(t[:], t[:], -1.0, 1.0, ALU.mult, ALU.add)
        v.tensor_tensor(t[:], t[:], pr.ident[0:n, 0:n], ALU.subtract)
    pr.on("vector", f)
    pr.step()
    return t


class Loader:

    def __init__(self, pr):
        self.pr = pr
        self.stage = pr.stage

    def close(self):
        pass

    def load_bf(self, hbm, rows, cols, out=None, out_off=0, stride=None):
        pr = self.pr
        assert rows % 128 == 0
        if stride is None:
            stride = cols
        if out is None:
            out = pr.sbuf([128, out_off + (rows // 128) * stride], BF16)
        for i in range(rows // 128):
            for c0 in range(0, cols, 2048):
                cw = min(2048, cols - c0)
                pr.step()
                pr.dma(self.stage[:, 0:cw], hbm[i * 128:(i + 1) * 128, c0:c0 + cw])
                pr.step()
                pr.on("vector", lambda v, i=i, c0=c0, cw=cw, out=out:
                      v.tensor_copy(out[:, out_off + i * stride + c0:
                                        out_off + i * stride + c0 + cw],
                                    self.stage[:, 0:cw]))
        pr.step()
        return out

    def load_f32(self, hbm, rows, cols, out=None):
        pr = self.pr
        if out is None:
            out = pr.sbuf([max(rows, 1), cols], F32)
        pr.step()
        pr.dma(out[0:rows, 0:cols], hbm)
        pr.step()
        return out


def emit_ln_stats(pr, x_tile_fn, ntiles, d=D, eps=1e-5, bufs=None):
    if bufs is None:
        bufs = (pr.sbuf([128, ntiles], F32), pr.sbuf([128, ntiles], F32),
                pr.sbuf([128, ntiles], F32), pr.sbuf([128, ntiles], F32),
                pr.sbuf([128, d], F32))
    ssum, ssq, m, rstd, sq = bufs
    for i in range(ntiles):
        pr.on("vector", lambda v, i=i: v.reduce_sum(ssum[:, i:i + 1], x_tile_fn(i), axis=AX))
        pr.on("scalar", lambda sc, i=i: sc.activation(sq[:], x_tile_fn(i), AF.Square))
        pr.step()
        pr.on("vector", lambda v, i=i: v.reduce_sum(ssq[:, i:i + 1], sq[:], axis=AX))
        pr.step()

    def stats(v):
        inv = 1.0 / d
        v.tensor_scalar(m[:], ssum[:], inv, None, ALU.mult)
        v.tensor_scalar(ssq[:], ssq[:], inv, None, ALU.mult)
        v.tensor_tensor(ssum[:], m[:], m[:], ALU.mult)
        v.tensor_tensor(ssq[:], ssq[:], ssum[:], ALU.subtract)
        v.tensor_scalar(ssq[:], ssq[:], eps, None, ALU.add)
    pr.on("vector", stats)
    pr.step()
    pr.on("scalar", lambda sc: sc.activation(ssq[:], ssq[:], AF.Sqrt))
    pr.step()
    pr.on("vector", lambda v: v.reciprocal(rstd[:], ssq[:]))
    pr.step()
    return m, rstd, bufs


def build_l1f(pr=None, px='', x_h=None, y_h=None):
    standalone = pr is None
    if standalone:
        pr = Prog()
    if x_h is None:
        x_h = pr.dram_in(px + "x", [S, D])
    wr_h = pr.dram_in(px + "W_rT", [D, D])
    wk_h = pr.dram_in(px + "W_kT", [D, D])
    wv_h = pr.dram_in(px + "W_vT", [D, D])
    mur_h = pr.dram_in(px + "mu_r_cols", [128, 8])
    muk_h = pr.dram_in(px + "mu_k_cols", [128, 8])
    if y_h is None:
        y_h = pr.dram_out(px + "y", [S, D])
    pr.init_consts()

    ld = Loader(pr)
    wrT = ld.load_bf(wr_h, D, D)
    wkT = ld.load_bf(wk_h, D, D)
    wvT = ld.load_bf(wv_h, D, D)
    mur = ld.load_f32(mur_h, 128, 8)
    muk = ld.load_f32(muk_h, 128, 8)
    ld.close()
    pr.step()
    mur1 = pr.sbuf([128, 8], F32)
    muk1 = pr.sbuf([128, 8], F32)
    pr.on("vector", lambda v: v.tensor_scalar(mur1[:], mur[:], -1.0, 1.0, ALU.mult, ALU.add))
    pr.on("vector", lambda v: v.tensor_scalar(muk1[:], muk[:], -1.0, 1.0, ALU.mult, ALU.add))
    pr.step()

    TSL = 1024
    NSL = S // TSL
    NTT = TSL // 128
    TW = TSL + 1

    x_sl = pr.sbuf([128, NTT * D], F32)
    ln_tok = pr.sbuf([128, D], BF16)
    lnT = pr.sbuf([128, 8 * TW], BF16)
    halo = pr.sbuf([128, 8], BF16)
    xrT = pr.sbuf([128, 8 * TSL], BF16)
    xkT = pr.sbuf([128, 8 * TSL], BF16)
    r_T = pr.sbuf([128, 8 * TSL], BF16)
    kact_T = pr.sbuf([128, 8 * TSL], BF16)
    out_T = pr.sbuf([128, 8 * TSL], BF16)
    ps_bf = pr.psum_bf_pool(4)
    ps_a = pr.psum([128, 512], F32)
    ps_b = pr.psum([128, 512], F32)
    relu_tmp = pr.sbuf([128, 512], F32)
    ytile = pr.sbuf([128, D], F32)
    lnbufs = None
    pr.on("gpsimd", lambda g: g.memset(halo[:], 0.0))
    pr.step()

    for sl in range(NSL):
        for i in range(NTT):
            pr.dma(x_sl[:, i * D:(i + 1) * D],
                   x_h[(sl * NTT + i) * 128:(sl * NTT + i + 1) * 128, :])
        pr.step()
        m, rstd, lnbufs = emit_ln_stats(pr, lambda i: x_sl[:, i * D:(i + 1) * D], NTT,
                                        bufs=lnbufs)
        for i in range(NTT):
            pr.on("vector", lambda v, i=i: v.tensor_scalar(
                ln_tok[:], x_sl[:, i * D:(i + 1) * D], m[:, i:i + 1], rstd[:, i:i + 1],
                ALU.subtract, ALU.mult))
            pr.step()
            for j0 in range(0, 8, 4):
                def tr(te, j0=j0):
                    for j in range(4):
                        te.transpose(ps_bf[j][:], ln_tok[:, (j0 + j) * 128:(j0 + j + 1) * 128],
                                     pr.ident_bf[:])
                pr.on("tensor", tr)
                pr.step()

                def ev(v, i=i, j0=j0):
                    for j in range(4):
                        jj = j0 + j
                        v.tensor_copy(lnT[:, jj * TW + 1 + i * 128: jj * TW + 1 + (i + 1) * 128],
                                      ps_bf[j][:])
                pr.on("vector", ev)
                pr.step()
        def hal(v, sl=sl):
            for j in range(8):
                v.tensor_copy(lnT[:, j * TW:j * TW + 1], halo[:, j:j + 1])
        pr.on("vector", hal)
        pr.step()

        def sav(v):
            for j in range(8):
                v.tensor_copy(halo[:, j:j + 1], lnT[:, j * TW + TW - 1:j * TW + TW])
        pr.on("vector", sav)
        pr.step()

        def mix(v):
            for j in range(8):
                cur = lnT[:, j * TW + 1: j * TW + 1 + TSL]
                prv = lnT[:, j * TW: j * TW + TSL]
                sl_o = slice(j * TSL, (j + 1) * TSL)
                v.tensor_scalar(xrT[:, sl_o], prv, mur1[:, j:j + 1], None, ALU.mult)
                v.tensor_scalar(xkT[:, sl_o], prv, muk1[:, j:j + 1], None, ALU.mult)
                v.scalar_tensor_tensor(xrT[:, sl_o], cur, mur[:, j:j + 1], xrT[:, sl_o],
                                       ALU.mult, ALU.add)
                v.scalar_tensor_tensor(xkT[:, sl_o], cur, muk[:, j:j + 1], xkT[:, sl_o],
                                       ALU.mult, ALU.add)
        pr.on("vector", mix)
        pr.step()

        for dq in range(8):
            for ts in range(TSL // 512):
                def mm(te, dq=dq, ts=ts):
                    for c in range(8):
                        te.matmul(ps_a[:], wrT[:, c * D + dq * 128: c * D + dq * 128 + 128],
                                  xrT[:, c * TSL + ts * 512: c * TSL + ts * 512 + 512],
                                  start=(c == 0), stop=(c == 7))
                    for c in range(8):
                        te.matmul(ps_b[:], wkT[:, c * D + dq * 128: c * D + dq * 128 + 128],
                                  xkT[:, c * TSL + ts * 512: c * TSL + ts * 512 + 512],
                                  start=(c == 0), stop=(c == 7))
                pr.on("tensor", mm)
                pr.step()
                sl_o = slice(dq * TSL + ts * 512, dq * TSL + ts * 512 + 512)
                pr.on("scalar", lambda sc, sl_o=sl_o: sc.activation(r_T[:, sl_o], ps_a[:], AF.Sigmoid))
                pr.on("vector", lambda v: v.tensor_scalar(relu_tmp[:], ps_b[:], 0.0, None, ALU.max))
                pr.step()
                pr.on("scalar", lambda sc, sl_o=sl_o: sc.activation(kact_T[:, sl_o], relu_tmp[:], AF.Square))
                pr.step()

        for dq in range(8):
            for ts in range(TSL // 512):
                def mm(te, dq=dq, ts=ts):
                    for c in range(8):
                        te.matmul(ps_a[:], wvT[:, c * D + dq * 128: c * D + dq * 128 + 128],
                                  kact_T[:, c * TSL + ts * 512: c * TSL + ts * 512 + 512],
                                  start=(c == 0), stop=(c == 7))
                pr.on("tensor", mm)
                pr.step()
                sl_o = slice(dq * TSL + ts * 512, dq * TSL + ts * 512 + 512)
                pr.on("vector", lambda v, sl_o=sl_o: v.tensor_tensor(
                    out_T[:, sl_o], r_T[:, sl_o], ps_a[:], ALU.mult))
                pr.step()

        for i in range(NTT):
            for half in range(2):
                def trh(te, i=i, half=half):
                    for j in range(4):
                        dj = half * 4 + j
                        te.transpose(ps_bf[j][:],
                                     out_T[:, dj * TSL + i * 128: dj * TSL + i * 128 + 128],
                                     pr.ident_bf[:])
                pr.on("tensor", trh)
                pr.step()

                def evh(v, i=i, half=half):
                    for j in range(4):
                        dj = half * 4 + j
                        v.tensor_tensor(ytile[:, dj * 128:(dj + 1) * 128],
                                        x_sl[:, i * D + dj * 128: i * D + dj * 128 + 128],
                                        ps_bf[j][:], ALU.add)
                pr.on("vector", evh)
                pr.step()
            pr.dma(y_h[(sl * NTT + i) * 128:(sl * NTT + i + 1) * 128, :], ytile[:, :])
            pr.step()

    if standalone:
        return pr.build()
    return


def emit_ln_tile(pr, x_ap, out_ap, bufs=None, d=D, eps=1e-5):
    if bufs is None:
        bufs = (pr.sbuf([128, 1], F32), pr.sbuf([128, 1], F32), pr.sbuf([128, 1], F32),
                pr.sbuf([128, d], F32))
    s1, s2, m, sq = bufs

    def p1(v):
        v.reduce_sum(s1[:], x_ap, axis=AX)
    pr.on("vector", p1)
    pr.on("scalar", lambda sc: sc.activation(sq[:], x_ap, AF.Square))
    pr.step()

    def p2(v):
        inv = 1.0 / d
        v.reduce_sum(s2[:], sq[:], axis=AX)
        v.tensor_scalar(m[:], s1[:], inv, None, ALU.mult)
        v.tensor_scalar(s2[:], s2[:], inv, None, ALU.mult)
        v.tensor_tensor(s1[:], m[:], m[:], ALU.mult)
        v.tensor_tensor(s2[:], s2[:], s1[:], ALU.subtract)
        v.tensor_scalar(s2[:], s2[:], eps, None, ALU.add)
    pr.on("vector", p2)
    pr.step()
    pr.on("scalar", lambda sc: sc.activation(s2[:], s2[:], AF.Sqrt))
    pr.step()

    def p3(v):
        v.reciprocal(s1[:], s2[:])
        v.tensor_scalar(out_ap, x_ap, m[:], s1[:], ALU.subtract, ALU.mult)
    pr.on("vector", p3)
    pr.step()
    return bufs


def emit_ln_transpose_tile(pr, x_ap, ln_tok, lnT_off_fn, ps_bf, lnbufs, lnT):
    lnbufs = emit_ln_tile(pr, x_ap, ln_tok[:], bufs=lnbufs)
    for j0 in (0, 4):
        def tr(te, j0=j0):
            for j in range(4):
                te.transpose(ps_bf[j][:], ln_tok[:, (j0 + j) * 128:(j0 + j + 1) * 128],
                             pr.ident_bf[:])
        pr.on("tensor", tr)
        pr.step()

        def ev(v, j0=j0):
            for j in range(4):
                off = lnT_off_fn(j0 + j)
                v.tensor_copy(lnT[:, off:off + 128], ps_bf[j][:])
        pr.on("vector", ev)
        pr.step()
    return lnbufs


def build_l3f(pr=None, px='', x_h=None, y_h=None):
    standalone = pr is None
    if standalone:
        pr = Prog()
    if x_h is None:
        x_h = pr.dram_in(px + "x", [S, D])
    w1c_h = pr.dram_in(px + "w1c", [32, 128, 1024])
    w2c_h = pr.dram_in(px + "w2c", [32, 128, 1024])
    h1c_h = pr.dram_in(px + "h1c", [32, 128, 4096])
    h2c_h = pr.dram_in(px + "h2c", [32, 128, 4096])
    w3c_h = pr.dram_in(px + "w3c", [2, 32, 128, 512])
    s1c_h = pr.dram_in(px + "s1c", [128, 32])
    s2c_h = pr.dram_in(px + "s2c", [128, 32])
    h1sc_h = pr.dram_in(px + "h1sc", [128, 32])
    h2sc_h = pr.dram_in(px + "h2sc", [128, 32])
    s3r_h = pr.dram_in(px + "s3row", [1, 1024])
    if y_h is None:
        y_h = pr.dram_out(px + "y", [S, D])
    pr.init_consts()

    ld = Loader(pr)
    s1c = ld.load_f32(s1c_h, 128, 32)
    s2c = ld.load_f32(s2c_h, 128, 32)
    h1sc = ld.load_f32(h1sc_h, 128, 32)
    h2sc = ld.load_f32(h2sc_h, 128, 32)
    s3row = ld.load_f32(s3r_h, 1, 1024)
    s3rep = pr.sbuf([128, 1024], F32)
    ones_col = pr.sbuf([1, 128], F32)
    ps_rep = pr.psum([128, 512], F32)
    pr.on("gpsimd", lambda g: g.memset(ones_col[:], 1.0))
    pr.step()
    for c0 in (0, 512):
        pr.on("tensor", lambda te, c0=c0: te.matmul(ps_rep[:], ones_col[0:1, :],
                                                    s3row[0:1, c0:c0 + 512],
                                                    start=True, stop=True))
        pr.step()
        pr.on("vector", lambda v, c0=c0: v.tensor_copy(s3rep[:, c0:c0 + 512], ps_rep[:]))
        pr.step()
    pr.free(ps_rep)

    TSL = 1024
    NSL = S // TSL
    NTT = TSL // 128

    xt = pr.sbuf([128, D], F32)
    ln_tok = pr.sbuf([128, D], BF16)
    lnT = pr.sbuf([128, 8 * TSL], BF16)
    hA = pr.sbuf([128, 32 * TSL], BF16)
    hB = pr.sbuf([128, 32 * TSL], BF16)
    wbf = pr.sbuf([128, 4096], BF16)
    ps_bf = pr.psum_bf_pool(4)
    ps_a = pr.psum([128, 512], F32)
    ps_b = pr.psum([128, 512], F32)
    t_sil = pr.sbuf([128, 512], F32)
    t_gat = pr.sbuf([128, 512], F32)
    lnbufs = None

    for sl in range(NSL):
        for i in range(NTT):
            pr.dma(xt[:, :], x_h[(sl * NTT + i) * 128:(sl * NTT + i + 1) * 128, :])
            pr.step()
            lnbufs = emit_ln_transpose_tile(pr, xt[:, :], ln_tok,
                                            lambda j, i=i: j * TSL + i * 128,
                                            ps_bf, lnbufs, lnT)
        for o in range(32):
            pr.dma(pr.stage[:, 0:1024], w1c_h[o])
            pr.step()
            pr.on("vector", lambda v: v.tensor_copy(wbf[:, 0:1024], pr.stage[:, 0:1024]))
            pr.step()
            pr.dma(pr.stage[:, 0:1024], w2c_h[o])
            pr.step()
            pr.on("vector", lambda v: v.tensor_copy(wbf[:, 1024:2048], pr.stage[:, 0:1024]))
            pr.step()
            for tc in range(TSL // 512):
                def mm(te, o=o, tc=tc):
                    for c in range(8):
                        te.matmul(ps_a[:], wbf[:, c * 128:(c + 1) * 128],
                                  lnT[:, c * TSL + tc * 512: c * TSL + tc * 512 + 512],
                                  start=(c == 0), stop=(c == 7))
                    for c in range(8):
                        te.matmul(ps_b[:], wbf[:, 1024 + c * 128:1024 + (c + 1) * 128],
                                  lnT[:, c * TSL + tc * 512: c * TSL + tc * 512 + 512],
                                  start=(c == 0), stop=(c == 7))
                pr.on("tensor", mm)
                pr.step()
                pr.on("scalar", lambda sc, o=o: sc.activation(t_sil[:], ps_a[:], AF.Silu,
                                                              scale=s1c[:, o:o + 1]))
                pr.on("vector", lambda v, o=o: v.tensor_scalar(t_gat[:], ps_b[:],
                                                               s2c[:, o:o + 1], None, ALU.mult))
                pr.step()
                pr.on("vector", lambda v, o=o, tc=tc: v.tensor_tensor(
                    hA[:, o * TSL + tc * 512: o * TSL + tc * 512 + 512],
                    t_sil[:], t_gat[:], ALU.mult))
                pr.step()
        for src, dst, wc_h, sc in ((hA, hB, h1c_h, h1sc), (hB, hA, h2c_h, h2sc)):
            for o in range(32):
                for half in range(2):
                    pr.dma(pr.stage[:, :], wc_h[o, :, half * 2048:(half + 1) * 2048])
                    pr.step()
                    pr.on("vector", lambda v, half=half: v.tensor_copy(
                        wbf[:, half * 2048:(half + 1) * 2048], pr.stage[:, :]))
                    pr.step()
                for tc in range(TSL // 512):
                    def mm(te, o=o, tc=tc, src=src):
                        for c in range(32):
                            te.matmul(ps_a[:], wbf[:, c * 128:(c + 1) * 128],
                                      src[:, c * TSL + tc * 512: c * TSL + tc * 512 + 512],
                                      start=(c == 0), stop=(c == 31))
                    pr.on("tensor", mm)
                    pr.step()
                    pr.on("scalar", lambda sc2, o=o, tc=tc, dst=dst, sc=sc: sc2.activation(
                        dst[:, o * TSL + tc * 512: o * TSL + tc * 512 + 512],
                        ps_a[:], AF.Gelu, scale=sc[:, o:o + 1]))
                    pr.step()
        wr3 = hB
        for oh in range(2):
            for i in range(32):
                pr.dma(pr.stage[:, 0:512], w3c_h[oh, i])
                pr.step()
                pr.on("vector", lambda v, i=i: v.tensor_copy(
                    wr3[:, i * 512:(i + 1) * 512], pr.stage[:, 0:512]))
                pr.step()
            for tt in range(NTT):
                def mm(te, oh=oh, tt=tt):
                    for c in range(32):
                        te.matmul(ps_a[:], hA[:, c * TSL + tt * 128: c * TSL + tt * 128 + 128],
                                  wr3[:, c * 512:(c + 1) * 512],
                                  start=(c == 0), stop=(c == 31))
                pr.on("tensor", mm)
                pr.step()
                pr.dma(xt[:, 0:512],
                       x_h[(sl * NTT + tt) * 128:(sl * NTT + tt + 1) * 128,
                           oh * 512:oh * 512 + 512])
                pr.step()

                def ep(v, oh=oh, tt=tt):
                    v.tensor_tensor(t_sil[:], ps_a[:], s3rep[:, oh * 512:oh * 512 + 512],
                                    ALU.mult)
                    v.tensor_tensor(t_gat[:], t_sil[:], xt[:, 0:512], ALU.add)
                pr.on("vector", ep)
                pr.step()
                pr.dma(y_h[(sl * NTT + tt) * 128:(sl * NTT + tt + 1) * 128,
                           oh * 512:oh * 512 + 512], t_gat[:])
                pr.step()
    if standalone:
        return pr.build()
    return


def host_prep_l3f(inp):
    def chunk_lhsT(wT, co):
        I, O = wT.shape
        a = wT.reshape(I // 128, 128, O // co, co)
        return np.ascontiguousarray(a.transpose(2, 1, 0, 3).reshape(O // co, 128,
                                                                    (I // 128) * co))
    w1T = np.ascontiguousarray(inp["L3_ffn_w1_w"].T)
    w2T = np.ascontiguousarray(inp["L3_ffn_w2_w"].T)
    h1T = np.ascontiguousarray(inp["L3_ffn_h1_w"].T)
    h2T = np.ascontiguousarray(inp["L3_ffn_h2_w"].T)
    w3T = np.ascontiguousarray(inp["L3_ffn_w3_w"].T)
    w3c = np.stack([w3T[:, 0:512].reshape(32, 128, 512),
                    w3T[:, 512:1024].reshape(32, 128, 512)])
    cols = lambda s: np.ascontiguousarray(np.asarray(s).reshape(-1, 128).T)
    return {
        "w1c": chunk_lhsT(w1T, 128), "w2c": chunk_lhsT(w2T, 128),
        "h1c": chunk_lhsT(h1T, 128), "h2c": chunk_lhsT(h2T, 128),
        "w3c": np.ascontiguousarray(w3c),
        "s1c": cols(inp["L3_ffn_w1_s"]), "s2c": cols(inp["L3_ffn_w2_s"]),
        "h1sc": cols(inp["L3_ffn_h1_s"]), "h2sc": cols(inp["L3_ffn_h2_s"]),
        "s3row": np.ascontiguousarray(inp["L3_ffn_w3_s"].reshape(1, 1024)),
    }


def emit_top2(pr, logits_ap, w_out_ap, scl, E=8, scale=1.0):
    m1, m2, negm1, d21, eqm, l2, ge2, ex = scl

    def p1(v):
        v.reduce_max(m1[:], logits_ap, axis=AX)
    pr.on("vector", p1)
    pr.step()

    def p2(v):
        v.tensor_scalar(eqm[:], logits_ap, m1[:], None, ALU.is_equal)
        v.tensor_scalar(negm1[:], m1[:], -1.0, None, ALU.mult)
    pr.on("vector", p2)
    pr.step()

    def p3(v):
        v.scalar_tensor_tensor(l2[:], eqm[:], -1e30, logits_ap, ALU.mult, ALU.add)
    pr.on("vector", p3)
    pr.step()
    pr.on("vector", lambda v: v.reduce_max(m2[:], l2[:], axis=AX))
    pr.step()

    def p4(v):
        v.tensor_scalar(ge2[:], logits_ap, m2[:], None, ALU.is_ge)
        v.tensor_tensor(d21[:], m2[:], m1[:], ALU.subtract)
    pr.on("vector", p4)
    pr.step()
    pr.on("scalar", lambda sc: sc.activation(ex[:], logits_ap, AF.Exp, bias=negm1[:]))
    pr.step()
    pr.on("scalar", lambda sc: sc.activation(d21[:], d21[:], AF.Exp))
    pr.step()

    def p5(v):
        v.tensor_scalar(d21[:], d21[:], 1.0, None, ALU.add)
        v.tensor_tensor(ex[:], ex[:], ge2[:], ALU.mult)
    pr.on("vector", p5)
    pr.step()
    pr.on("vector", lambda v: v.reciprocal(m1[:], d21[:]))
    pr.step()

    def p6(v):
        v.tensor_scalar(ex[:], ex[:], m1[:], float(scale), ALU.mult, ALU.mult)
        v.tensor_copy(w_out_ap, ex[:])
    pr.on("vector", p6)
    pr.step()


def mk_top2_scratch(pr, E=8):
    return (pr.sbuf([128, 1], F32), pr.sbuf([128, 1], F32), pr.sbuf([128, 1], F32),
            pr.sbuf([128, 1], F32), pr.sbuf([128, E], F32), pr.sbuf([128, E], F32),
            pr.sbuf([128, E], F32), pr.sbuf([128, E], F32))


def build_l2f(pr=None, px='', x_h=None, y_h=None):
    standalone = pr is None
    if standalone:
        pr = Prog()
    if x_h is None:
        x_h = pr.dram_in(px + "x", [S, D])
    wgT_h = pr.dram_in(px + "wgT", [D, 8])
    bg_h = pr.dram_in(px + "bg_row", [1, 8])
    w1T_h = pr.dram_in(px + "w1T", [8, D, 64])
    w2T_h = pr.dram_in(px + "w2T", [8, 64, D])
    if y_h is None:
        y_h = pr.dram_out(px + "y", [S, D])
    pr.init_consts()

    ld = Loader(pr)
    wgT = ld.load_bf(wgT_h, D, 8)
    w1T = pr.sbuf([128, 8 * 8 * 64], BF16)
    for e in range(8):
        ld.load_bf(w1T_h[e], D, 64, out=w1T, out_off=e * 512, stride=64)
    w2T = pr.sbuf([64, 8 * D], BF16)
    for e in range(8):
        pr.step()
        pr.dma(pr.stage[:64, 0:D], w2T_h[e])
        pr.step()
        pr.on("vector", lambda v, e=e: v.tensor_copy(w2T[:, e * D:(e + 1) * D],
                                                     pr.stage[:64, 0:D]))
    bg_row = ld.load_f32(bg_h, 1, 8)
    bgrep = pr.sbuf([128, 8], F32)
    ones_col = pr.sbuf([1, 128], F32)
    ps_rep = pr.psum([128, 8], F32)
    pr.on("gpsimd", lambda g: g.memset(ones_col[:], 1.0))
    pr.step()
    pr.on("tensor", lambda te: te.matmul(ps_rep[:], ones_col[0:1, :], bg_row[0:1, :],
                                         start=True, stop=True))
    pr.step()
    pr.on("vector", lambda v: v.tensor_copy(bgrep[:], ps_rep[:]))
    pr.step()
    pr.free(ps_rep)

    TSL = 1024
    NSL = S // TSL
    NTT = TSL // 128

    xt = pr.sbuf([128, D], F32)
    ln_tok = pr.sbuf([128, D], BF16)
    lnT = pr.sbuf([128, 8 * TSL], BF16)
    hbuf = pr.sbuf([64, TSL], BF16)
    logit = pr.sbuf([128, NTT * 8], F32)
    wts = pr.sbuf([128, NTT * 8], F32)
    acc = pr.sbuf([128, NTT * D], F32)
    ps_bf = pr.psum_bf_pool(4)
    ps_a = pr.psum([128, 512], F32)
    ps_g = pr.psum([128, 8], F32)
    ps_h = pr.psum([64, 512], F32)
    lnbufs = None
    t2scl = mk_top2_scratch(pr)

    for sl in range(NSL):
        for i in range(NTT):
            pr.dma(xt[:, :], x_h[(sl * NTT + i) * 128:(sl * NTT + i + 1) * 128, :])
            pr.step()
            lnbufs = emit_ln_transpose_tile(pr, xt[:, :], ln_tok,
                                            lambda j, i=i: j * TSL + i * 128,
                                            ps_bf, lnbufs, lnT)
            def mm(te, i=i):
                for c in range(8):
                    te.matmul(ps_g[:], lnT[:, c * TSL + i * 128: c * TSL + i * 128 + 128],
                              wgT[:, c * 8:(c + 1) * 8], start=(c == 0), stop=(c == 7))
            pr.on("tensor", mm)
            pr.step()
            pr.on("vector", lambda v, i=i: v.tensor_tensor(
                logit[:, i * 8:(i + 1) * 8], ps_g[:], bgrep[:], ALU.add))
            pr.step()
            emit_top2(pr, logit[:, i * 8:(i + 1) * 8], wts[:, i * 8:(i + 1) * 8],
                      t2scl, scale=2.0)
        for e in range(8):
            for tc in range(TSL // 512):
                def mm(te, e=e, tc=tc):
                    for c in range(8):
                        te.matmul(ps_h[:, :], w1T[:, e * 512 + c * 64:(e * 512) + (c + 1) * 64],
                                  lnT[:, c * TSL + tc * 512: c * TSL + tc * 512 + 512],
                                  start=(c == 0), stop=(c == 7))
                pr.on("tensor", mm)
                pr.step()
                pr.on("scalar", lambda sc, tc=tc: sc.activation(
                    hbuf[:, tc * 512:(tc + 1) * 512], ps_h[:, :], AF.Gelu))
                pr.step()
            for tt in range(NTT):
                for ch in range(2):
                    def mm2(te, e=e, tt=tt, ch=ch):
                        te.matmul(ps_a[:], hbuf[:, tt * 128:(tt + 1) * 128],
                                  w2T[:, e * D + ch * 512: e * D + ch * 512 + 512],
                                  start=True, stop=True)
                    pr.on("tensor", mm2)
                    pr.step()

                    def ep(v, e=e, tt=tt, ch=ch):
                        dst = acc[:, tt * D + ch * 512: tt * D + ch * 512 + 512]
                        if e == 0:
                            v.tensor_scalar(dst, ps_a[:], wts[:, tt * 8 + e: tt * 8 + e + 1],
                                            None, ALU.mult)
                        else:
                            v.scalar_tensor_tensor(dst, ps_a[:],
                                                   wts[:, tt * 8 + e: tt * 8 + e + 1],
                                                   dst, ALU.mult, ALU.add)
                    pr.on("vector", ep)
                    pr.step()
        for tt in range(NTT):
            pr.dma(xt[:, :], x_h[(sl * NTT + tt) * 128:(sl * NTT + tt + 1) * 128, :])
            pr.step()
            pr.on("vector", lambda v, tt=tt: v.tensor_tensor(
                acc[:, tt * D:(tt + 1) * D], acc[:, tt * D:(tt + 1) * D], xt[:, :], ALU.add))
            pr.step()
            pr.dma(y_h[(sl * NTT + tt) * 128:(sl * NTT + tt + 1) * 128, :],
                   acc[:, tt * D:(tt + 1) * D])
            pr.step()
    if standalone:
        return pr.build()
    return


def host_prep_l2f(inp):
    return {
        "wgT": np.ascontiguousarray(inp["L2_ffn_Wg"].T),
        "bg_row": np.ascontiguousarray(inp["L2_ffn_bg"].reshape(1, 8)),
        "w1T": np.ascontiguousarray(inp["L2_ffn_W1"].transpose(0, 2, 1)),
        "w2T": np.ascontiguousarray(inp["L2_ffn_W2"].transpose(0, 2, 1)),
    }


def build_l2a(pr=None, px='', x_h=None, y_h=None, debug=False):
    standalone = pr is None
    if standalone:
        pr = Prog()
    RMS_EPS = 1.1920929e-07
    if x_h is None:
        x_h = pr.dram_in(px + "x", [S, D])
    wqkvT_h = pr.dram_in(px + "wqkvT", [D, 3 * D])
    bqkv_h = pr.dram_in(px + "bqkv_row", [1, 3 * D])
    gq_h = pr.dram_in(px + "gq_row", [1, D])
    gk_h = pr.dram_in(px + "gk_row", [1, D])
    woutT_h = pr.dram_in(px + "woutT", [D, D])
    bout_h = pr.dram_in(px + "bout_row", [1, D])
    gout_h = pr.dram_in(px + "gout_row", [1, D])
    if y_h is None:
        y_h = pr.dram_out(px + "y", [S, D])
    if debug:
        dbg_qkv = pr.dram_out(px + "dbg_qkv", [128, 3 * D])
        dbg_qf = pr.dram_out(px + "dbg_qf", [128, D], BF16)
        dbg_vk = pr.dram_out(px + "dbg_vk", [65, 1024])
        dbg_attn = pr.dram_out(px + "dbg_attn", [128, D], BF16)
    pr.init_consts()

    ld = Loader(pr)
    wqkvT = ld.load_bf(wqkvT_h, D, 3 * D)
    woutT = ld.load_bf(woutT_h, D, D)

    ones_col = pr.sbuf([1, 128], F32)
    pr.on("gpsimd", lambda g: g.memset(ones_col[:], 1.0))
    pr.step()
    ps_rep = pr.psum([128, 512], F32)

    def make_rep(row_src, n, is_handle=False):
        row = row_src if is_handle else ld.load_f32(row_src, 1, n)
        rep = pr.sbuf([128, n], F32)
        for c0 in range(0, n, 512):
            cw = min(512, n - c0)
            pr.on("tensor", lambda te, c0=c0, cw=cw, row=row: te.matmul(
                ps_rep[:, 0:cw], ones_col[0:1, :], row[0:1, c0:c0 + cw],
                start=True, stop=True))
            pr.step()
            pr.on("vector", lambda v, c0=c0, cw=cw, rep=rep: v.tensor_copy(
                rep[:, c0:c0 + cw], ps_rep[:, 0:cw]))
            pr.step()
        return rep

    bqkv_rep = make_rep(bqkv_h, 3 * D)
    gq_rep = make_rep(gq_h, D)
    gk_rep = make_rep(gk_h, D)
    bout_rep = make_rep(bout_h, D)
    gout_rep = make_rep(gout_h, D)

    jrow = pr.sbuf([1, 32], F32)
    pr.on("gpsimd", lambda g: g.iota(jrow[:], pattern=[[1, 32]], base=0,
                                     channel_multiplier=0,
                                     allow_small_or_imprecise_dtypes=True))
    pr.step()
    pr.on("vector", lambda v: v.tensor_scalar(jrow[:], jrow[:],
                                              -math.log(10000.0) / 32.0, None, ALU.mult))
    pr.step()
    pr.on("scalar", lambda sc: sc.activation(jrow[:], jrow[:], AF.Exp))
    pr.step()
    pr.on("vector", lambda v: v.tensor_scalar(jrow[:], jrow[:], 1.0 / TWO_PI, None, ALU.mult))
    pr.step()
    cj_rep = make_rep(jrow, 32, is_handle=True)
    pcol = pr.sbuf([128, 1], F32)
    pr.on("gpsimd", lambda g: g.iota(pcol[:], pattern=[[0, 1]], base=0,
                                     channel_multiplier=1,
                                     allow_small_or_imprecise_dtypes=True))
    pr.step()

    qfT_d = pr.dram_tmp(px + "qfT_d", [NT, 128, 8 * 128], BF16)
    qfT = pr.sbuf([128, 8 * 128], BF16)
    xt = pr.sbuf([128, D], F32)
    ln_tok = pr.sbuf([128, D], BF16)
    lnT = pr.sbuf([128, 8 * 128], BF16)
    qkv = pr.sbuf([128, 3 * D], F32)
    kf_bf = pr.sbuf([128, D], BF16)
    qf_bf = pr.sbuf([128, D], BF16)
    tcol = pr.sbuf([128, 1], F32)
    ang = pr.sbuf([128, 32], F32)
    sct = pr.sbuf([128, 64], F32)
    screp = pr.sbuf([128, 1024], F32)
    rt1 = pr.sbuf([128, 512], F32)
    rt2 = pr.sbuf([128, 512], F32)
    rt3 = pr.sbuf([128, 512], F32)
    stat = pr.sbuf([128, 4], F32)
    ps_bf = pr.psum_bf_pool(4)
    ps_a = pr.psum([128, 512], F32)
    ps_vk = pr.psum([65, 1024], F32)
    vk_acc = pr.sbuf([65, 1024], F32)
    ps_res = ps_rep[0:65, 0:128]
    ps_trf = ps_rep[:, 128:256]
    vk_sbT = pr.sbuf([128, 8 * 65], BF16)
    res_f = pr.sbuf([65, 128], F32)
    t_at = pr.sbuf([128, 65], F32)
    attn_tok = pr.sbuf([128, D], BF16)
    attnT = pr.sbuf([128, 8 * 128], BF16)
    vpad = pr.sbuf([128, 65], BF16)
    yt = pr.sbuf([128, D], F32)
    lnbufs = None

    pr.on("gpsimd", lambda g: g.memset(vpad[:, 64:65], 1.0))
    pr.step()

    for tt in range(NT):
        pr.dma(xt[:, :], x_h[tt * 128:(tt + 1) * 128, :])
        pr.step()
        lnbufs = emit_ln_transpose_tile(pr, xt[:, :], ln_tok,
                                        lambda j: j * 128, ps_bf, lnbufs, lnT)
        for ch in range(6):
            def mm(te, ch=ch):
                for c in range(8):
                    te.matmul(ps_a[:], lnT[:, c * 128:(c + 1) * 128],
                              wqkvT[:, c * 3072 + ch * 512: c * 3072 + ch * 512 + 512],
                              start=(c == 0), stop=(c == 7))
            pr.on("tensor", mm)
            pr.step()
            pr.on("vector", lambda v, ch=ch: v.tensor_tensor(
                qkv[:, ch * 512:(ch + 1) * 512], ps_a[:],
                bqkv_rep[:, ch * 512:(ch + 1) * 512], ALU.add))
            pr.step()
        for base, si in ((0, 0), (D, 1)):
            for half in range(2):
                pr.on("scalar", lambda sc, base=base, half=half: sc.activation(
                    rt1[:], qkv[:, base + half * 512: base + half * 512 + 512], AF.Square))
                pr.step()
                pr.on("vector", lambda v, half=half: v.reduce_sum(
                    stat[:, 2 + half:3 + half], rt1[:], axis=AX))
                pr.step()
            pr.on("vector", lambda v, si=si: v.tensor_scalar(
                stat[:, si:si + 1], stat[:, 2:3], stat[:, 3:4], None, ALU.add))
            pr.step()
            pr.on("vector", lambda v, si=si: v.tensor_scalar(
                stat[:, si:si + 1], stat[:, si:si + 1], 1.0 / D, RMS_EPS,
                ALU.mult, ALU.add))
            pr.step()
            pr.on("scalar", lambda sc, si=si: sc.activation(
                stat[:, si:si + 1], stat[:, si:si + 1], AF.Sqrt))
            pr.step()

        def recs(v):
            v.reciprocal(stat[:, 2:3], stat[:, 0:1])
            v.reciprocal(stat[:, 3:4], stat[:, 1:2])
        pr.on("vector", recs)
        pr.step()

        def nrm(v):
            v.tensor_scalar(qkv[:, 0:D], qkv[:, 0:D], stat[:, 2:3], None, ALU.mult)
            v.tensor_scalar(qkv[:, D:2 * D], qkv[:, D:2 * D], stat[:, 3:4], None, ALU.mult)
        pr.on("vector", nrm)
        pr.step()

        def gmul(v):
            v.tensor_tensor(qkv[:, 0:D], qkv[:, 0:D], gq_rep[:], ALU.mult)
            v.tensor_tensor(qkv[:, D:2 * D], qkv[:, D:2 * D], gk_rep[:], ALU.mult)
        pr.on("vector", gmul)
        pr.step()

        def angf(v, tt=tt):
            v.tensor_scalar(tcol[:], pcol[:], 1.0, float(128 * tt), ALU.mult, ALU.add)
            v.tensor_scalar(ang[:], cj_rep[:, 0:32], tcol[:, 0:1], None, ALU.mult)
        pr.on("vector", angf)
        pr.step()

        def fracs(v):
            v.tensor_scalar(sct[:, 0:32], ang[:], MAGIC, MAGIC, ALU.add, ALU.subtract)
            v.tensor_tensor(sct[:, 0:32], ang[:], sct[:, 0:32], ALU.subtract)
            v.tensor_scalar(ang[:], ang[:], 0.25, None, ALU.add)
            v.tensor_scalar(sct[:, 32:64], ang[:], MAGIC, MAGIC, ALU.add, ALU.subtract)
            v.tensor_tensor(sct[:, 32:64], ang[:], sct[:, 32:64], ALU.subtract)
        pr.on("vector", fracs)
        pr.step()
        pr.on("scalar", lambda sc: sc.activation(sct[:], sct[:], AF.Sin, scale=TWO_PI))
        pr.step()

        def screp_f(v):
            for h in range(16):
                v.tensor_copy(screp[:, h * 32:(h + 1) * 32], sct[:, 32:64])
                v.tensor_copy(screp[:, 512 + h * 32:512 + (h + 1) * 32], sct[:, 0:32])
        pr.on("vector", screp_f)
        pr.step()

        def d3(t):
            return t.rearrange("p (n o) -> p n o", o=1)
        for base, dst in ((0, qf_bf), (D, kf_bf)):
            ev = qkv[:, base:base + D].rearrange("p (n two) -> p n two", two=2)[:, :, 0:1]
            od = qkv[:, base:base + D].rearrange("p (n two) -> p n two", two=2)[:, :, 1:2]
            dev = dst[:, 0:D].rearrange("p (n two) -> p n two", two=2)[:, :, 0:1]
            dod = dst[:, 0:D].rearrange("p (n two) -> p n two", two=2)[:, :, 1:2]
            cos = screp[:, 0:512]
            sin = screp[:, 512:1024]

            def rope(v, ev=ev, od=od, dev=dev, dod=dod, cos=cos, sin=sin):
                v.tensor_tensor(d3(rt1[:]), ev, d3(cos), ALU.mult)
                v.tensor_tensor(d3(rt2[:]), od, d3(sin), ALU.mult)
                v.tensor_tensor(rt1[:], rt1[:], rt2[:], ALU.subtract)
                v.tensor_tensor(d3(rt2[:]), ev, d3(sin), ALU.mult)
                v.tensor_tensor(d3(rt3[:]), od, d3(cos), ALU.mult)
                v.tensor_tensor(rt2[:], rt2[:], rt3[:], ALU.add)
                v.tensor_scalar(dev, d3(rt1[:]), 0.0, None, ALU.max)
                v.tensor_scalar(dod, d3(rt2[:]), 0.0, None, ALU.max)
            pr.on("vector", rope)
            pr.step()

        if debug and tt == 0:
            pr.dma(dbg_qkv, qkv[:, :])
            pr.dma(dbg_qf, qf_bf[:, :])
            pr.step()
        if tt == 0:
            pr.on("gpsimd", lambda g: g.memset(vk_acc[:, :], 0.0))
            pr.step()
        for h in range(16):
            pr.on("vector", lambda v, h=h: v.tensor_copy(
                vpad[:, 0:64], qkv[:, 2 * D + h * 64: 2 * D + (h + 1) * 64]))
            pr.step()

            def mmvk(te, h=h):
                te.matmul(ps_vk[:, h * 64:(h + 1) * 64], vpad[:, :],
                          kf_bf[:, h * 64:(h + 1) * 64], start=True, stop=True)
            pr.on("tensor", mmvk)
            pr.step()
        pr.on("vector", lambda v: v.tensor_tensor(vk_acc[:, :], vk_acc[:, :],
                                                  ps_vk[:, :], ALU.add))
        pr.step()
        for hp in range(8):
            def trq(te, hp=hp):
                te.transpose(ps_bf[0][0:64, :], qf_bf[:, (2 * hp) * 64:(2 * hp + 1) * 64],
                             pr.ident_bf[:])
                te.transpose(ps_bf[0][64:128, :], qf_bf[:, (2 * hp + 1) * 64:(2 * hp + 2) * 64],
                             pr.ident_bf[:])
            pr.on("tensor", trq)
            pr.step()
            pr.on("vector", lambda v, hp=hp: v.tensor_copy(
                qfT[:, hp * 128:(hp + 1) * 128], ps_bf[0][:]))
            pr.step()
        pr.dma(qfT_d[tt], qfT[:, :])
        pr.step()

    if debug:
        pr.dma(dbg_vk, vk_acc[:, :])
        pr.step()
    vk_sb = pr.sbuf([65, 1024], BF16)
    pr.on("vector", lambda v: v.tensor_copy(vk_sb[:, :], vk_acc[:, :]))
    pr.step()
    for h in range(16):
        def trvk(te, h=h):
            te.transpose(ps_bf[0][(h % 2) * 64:(h % 2) * 64 + 64, 0:65],
                         vk_sb[:, h * 64:(h + 1) * 64], pr.ident_bf[0:65, 0:65])
        pr.on("tensor", trvk)
        pr.step()
        pr.on("vector", lambda v, h=h: v.tensor_copy(
            vk_sbT[(h % 2) * 64:(h % 2) * 64 + 64, (h // 2) * 65:(h // 2) * 65 + 65],
            ps_bf[0][(h % 2) * 64:(h % 2) * 64 + 64, 0:65]))
        pr.step()

    for tt in range(NT):
        pr.dma(qfT[:, :], qfT_d[tt])
        pr.step()
        for h in range(16):
            hp, ho = h // 2, (h % 2) * 64

            def mmres(te, hp=hp, ho=ho, tt=tt):
                te.matmul(ps_res[:, :], vk_sbT[ho:ho + 64, hp * 65:(hp + 1) * 65],
                          qfT[ho:ho + 64, hp * 128:(hp + 1) * 128],
                          start=True, stop=True)
            pr.on("tensor", mmres)
            pr.step()
            pr.on("vector", lambda v: v.tensor_copy(res_f[:, :], ps_res[:, :]))
            pr.step()

            def trres(te):
                te.transpose(ps_trf[:, 0:65], res_f[:, :], pr.ident[0:65, 0:65])
            pr.on("tensor", trres)
            pr.step()

            def nrm(v, h=h):
                v.tensor_copy(t_at[:, :], ps_trf[:, 0:65])
                v.tensor_scalar(t_at[:, 64:65], t_at[:, 64:65], 1e-6, None, ALU.add)
            pr.on("vector", nrm)
            pr.step()
            pr.on("vector", lambda v: v.reciprocal(t_at[:, 64:65], t_at[:, 64:65]))
            pr.step()
            pr.on("vector", lambda v, h=h: v.tensor_scalar(
                attn_tok[:, h * 64:(h + 1) * 64], t_at[:, 0:64], t_at[:, 64:65],
                None, ALU.mult))
            pr.step()
        if debug and tt == 0:
            pr.dma(dbg_attn, attn_tok[:, :])
            pr.step()
        for j0 in (0, 4):
            def tra(te, j0=j0):
                for j in range(4):
                    te.transpose(ps_bf[j][:], attn_tok[:, (j0 + j) * 128:(j0 + j + 1) * 128],
                                 pr.ident_bf[:])
            pr.on("tensor", tra)
            pr.step()

            def eva(v, j0=j0):
                for j in range(4):
                    v.tensor_copy(attnT[:, (j0 + j) * 128:(j0 + j + 1) * 128], ps_bf[j][:])
            pr.on("vector", eva)
            pr.step()
        for ch in range(2):
            def mmo(te, ch=ch, tt=tt):
                for c in range(8):
                    te.matmul(ps_a[:], attnT[:, c * 128:(c + 1) * 128],
                              woutT[:, c * D + ch * 512: c * D + ch * 512 + 512],
                              start=(c == 0), stop=(c == 7))
            pr.on("tensor", mmo)
            pr.step()
            pr.on("vector", lambda v, ch=ch: v.tensor_tensor(
                yt[:, ch * 512:(ch + 1) * 512], ps_a[:],
                bout_rep[:, ch * 512:(ch + 1) * 512], ALU.add))
            pr.step()
        for half in range(2):
            pr.on("scalar", lambda sc, half=half: sc.activation(
                rt1[:], yt[:, half * 512:(half + 1) * 512], AF.Square))
            pr.step()
            pr.on("vector", lambda v, half=half: v.reduce_sum(
                stat[:, 2 + half:3 + half], rt1[:], axis=AX))
            pr.step()

        def fin1(v):
            v.tensor_scalar(stat[:, 0:1], stat[:, 2:3], stat[:, 3:4], None, ALU.add)
        pr.on("vector", fin1)
        pr.step()
        pr.on("vector", lambda v: v.tensor_scalar(stat[:, 0:1], stat[:, 0:1],
                                                  1.0 / D, RMS_EPS, ALU.mult, ALU.add))
        pr.step()
        pr.on("scalar", lambda sc: sc.activation(stat[:, 0:1], stat[:, 0:1], AF.Sqrt))
        pr.step()
        pr.on("vector", lambda v: v.reciprocal(stat[:, 1:2], stat[:, 0:1]))
        pr.step()
        pr.dma(xt[:, :], x_h[tt * 128:(tt + 1) * 128, :])
        pr.step()

        def fin2(v):
            v.tensor_scalar(yt[:, :], yt[:, :], stat[:, 1:2], None, ALU.mult)
            v.tensor_tensor(yt[:, :], yt[:, :], gout_rep[:, :], ALU.mult)
            v.tensor_tensor(yt[:, :], yt[:, :], xt[:, :], ALU.add)
        pr.on("vector", fin2)
        pr.step()
        pr.dma(y_h[tt * 128:(tt + 1) * 128, :], yt[:, :])
        pr.step()
    if standalone:
        return pr.build()
    return


def host_prep_l2a(inp):
    return {
        "wqkvT": np.ascontiguousarray(inp["L2_attn_Wqkv"].T),
        "bqkv_row": np.ascontiguousarray(inp["L2_attn_bqkv"].reshape(1, -1)),
        "gq_row": np.ascontiguousarray(inp["L2_attn_gq"].reshape(1, -1)),
        "gk_row": np.ascontiguousarray(inp["L2_attn_gk"].reshape(1, -1)),
        "woutT": np.ascontiguousarray(inp["L2_attn_Wout"].T),
        "bout_row": np.ascontiguousarray(inp["L2_attn_bout"].reshape(1, -1)),
        "gout_row": np.ascontiguousarray(inp["L2_attn_gout"].reshape(1, -1)),
    }


def build_l0f(pr=None, px='', x_h=None, y_h=None):
    standalone = pr is None
    if standalone:
        pr = Prog()
    if x_h is None:
        x_h = pr.dram_in(px + "x", [S, D])
    gateT_h = pr.dram_in(px + "gateT", [D, 8])
    rmsw_h = pr.dram_in(px + "rmsw_row", [1, D])
    w1c_h = pr.dram_in(px + "w1c", [8, 22, 128, 1024])
    w2c_h = pr.dram_in(px + "w2c", [8, 22, 128, 1024])
    w3t_h = pr.dram_in(px + "w3t", [8, 22, 128, 1024])
    if y_h is None:
        y_h = pr.dram_out(px + "y", [S, D])
    pr.init_consts()

    ld = Loader(pr)
    gateT = ld.load_bf(gateT_h, D, 8)
    ones_col = pr.sbuf([1, 128], F32)
    pr.on("gpsimd", lambda g: g.memset(ones_col[:], 1.0))
    pr.step()
    ps_rep = pr.psum([128, 512], F32)
    rms_row = ld.load_f32(rmsw_h, 1, D)
    rms_rep = pr.sbuf([128, D], F32)
    for c0 in (0, 512):
        pr.on("tensor", lambda te, c0=c0: te.matmul(ps_rep[:], ones_col[0:1, :],
                                                    rms_row[0:1, c0:c0 + 512],
                                                    start=True, stop=True))
        pr.step()
        pr.on("vector", lambda v, c0=c0: v.tensor_copy(rms_rep[:, c0:c0 + 512], ps_rep[:]))
        pr.step()

    TSL = 1024
    NSL = S // TSL
    NTT = TSL // 128

    xt = pr.sbuf([128, D], F32)
    ln_f = pr.sbuf([128, D], F32)
    y_tok = pr.sbuf([128, D], BF16)
    yT = pr.sbuf([128, 8 * TSL], BF16)
    hbuf = pr.sbuf([128, 22 * TSL], BF16)
    w3sb = pr.sbuf([128, 22 * TSL], BF16)
    wbf = pr.sbuf([128, 2048], BF16)
    logit = pr.sbuf([128, NTT * 8], F32)
    wts = pr.sbuf([128, NTT * 8], F32)
    acc = pr.sbuf([128, NTT * D], F32)
    rt = pr.sbuf([128, 512], F32)
    stat2 = pr.sbuf([128, 4], F32)
    ps_bf = pr.psum_bf_pool(4)
    ps_a = pr.psum([128, 512], F32)
    ps_b = pr.psum([128, 512], F32)
    ps_g = pr.psum([128, 8], F32)
    lnbufs = None
    t2scl = mk_top2_scratch(pr)

    def ln_into(tile_idx, dst_f32):
        nonlocal lnbufs
        pr.dma(xt[:, :], x_h[tile_idx * 128:(tile_idx + 1) * 128, :])
        pr.step()
        lnbufs = emit_ln_tile(pr, xt[:, :], dst_f32, bufs=lnbufs)

    for sl in range(NSL):
        for i in range(NTT):
            ln_into(sl * NTT + i, ln_f[:, :])
            for half in range(2):
                pr.on("scalar", lambda sc, half=half: sc.activation(
                    rt[:], ln_f[:, half * 512:(half + 1) * 512], AF.Square))
                pr.step()
                pr.on("vector", lambda v, half=half: v.reduce_sum(
                    stat2[:, half:half + 1], rt[:], axis=AX))
                pr.step()

            def s1(v):
                v.tensor_scalar(stat2[:, 0:1], stat2[:, 0:1], stat2[:, 1:2], None, ALU.add)
            pr.on("vector", s1)
            pr.step()
            pr.on("vector", lambda v: v.tensor_scalar(stat2[:, 0:1], stat2[:, 0:1],
                                                      1.0 / D, 1e-6, ALU.mult, ALU.add))
            pr.step()
            pr.on("scalar", lambda sc: sc.activation(stat2[:, 0:1], stat2[:, 0:1], AF.Sqrt))
            pr.step()
            pr.on("vector", lambda v: v.reciprocal(stat2[:, 1:2], stat2[:, 0:1]))
            pr.step()

            def s2(v):
                v.tensor_scalar(y_tok[:, :], ln_f[:, :], stat2[:, 1:2], None, ALU.mult)
            pr.on("vector", s2)
            pr.step()
            pr.on("vector", lambda v: v.tensor_tensor(y_tok[:, :], y_tok[:, :],
                                                      rms_rep[:, :], ALU.mult))
            pr.step()
            for j0 in (0, 4):
                def tr(te, j0=j0, i=i):
                    for j in range(4):
                        te.transpose(ps_bf[j], y_tok[:, (j0 + j) * 128:(j0 + j + 1) * 128],
                                     pr.ident_bf[:])
                pr.on("tensor", tr)
                pr.step()

                def ev(v, j0=j0, i=i):
                    for j in range(4):
                        v.tensor_copy(yT[:, (j0 + j) * TSL + i * 128:(j0 + j) * TSL + (i + 1) * 128],
                                      ps_bf[j])
                pr.on("vector", ev)
                pr.step()
            def mmg(te, i=i):
                for c in range(8):
                    te.matmul(ps_g[:], yT[:, c * TSL + i * 128: c * TSL + i * 128 + 128],
                              gateT[:, c * 8:(c + 1) * 8], start=(c == 0), stop=(c == 7))
            pr.on("tensor", mmg)
            pr.step()
            pr.on("vector", lambda v, i=i: v.tensor_copy(logit[:, i * 8:(i + 1) * 8], ps_g[:]))
            pr.step()
            emit_top2(pr, logit[:, i * 8:(i + 1) * 8], wts[:, i * 8:(i + 1) * 8], t2scl)

        for e in range(8):
            for i in range(22):
                pr.dma(pr.stage[:, 0:1024], w3t_h[e, i])
                pr.step()
                pr.on("vector", lambda v, i=i: v.tensor_copy(
                    w3sb[:, i * TSL:i * TSL + 1024], pr.stage[:, 0:1024]))
                pr.step()
            for o in range(22):
                pr.dma(pr.stage[:, 0:1024], w1c_h[e, o])
                pr.step()
                pr.on("vector", lambda v: v.tensor_copy(wbf[:, 0:1024], pr.stage[:, 0:1024]))
                pr.step()
                pr.dma(pr.stage[:, 0:1024], w2c_h[e, o])
                pr.step()
                pr.on("vector", lambda v: v.tensor_copy(wbf[:, 1024:2048], pr.stage[:, 0:1024]))
                pr.step()
                for tc in range(TSL // 512):
                    def mm(te, o=o, tc=tc):
                        for c in range(8):
                            te.matmul(ps_a[:], wbf[:, c * 128:(c + 1) * 128],
                                      yT[:, c * TSL + tc * 512: c * TSL + tc * 512 + 512],
                                      start=(c == 0), stop=(c == 7))
                        for c in range(8):
                            te.matmul(ps_b[:], wbf[:, 1024 + c * 128:1024 + (c + 1) * 128],
                                      yT[:, c * TSL + tc * 512: c * TSL + tc * 512 + 512],
                                      start=(c == 0), stop=(c == 7))
                    pr.on("tensor", mm)
                    pr.step()
                    pr.on("scalar", lambda sc: sc.activation(rt[:], ps_a[:], AF.Silu))
                    pr.step()
                    pr.on("vector", lambda v, o=o, tc=tc: v.tensor_tensor(
                        hbuf[:, o * TSL + tc * 512: o * TSL + tc * 512 + 512],
                        rt[:], ps_b[:], ALU.mult))
                    pr.step()
            for tt in range(NTT):
                for ch in range(2):
                    def mm2(te, tt=tt, ch=ch):
                        for i in range(22):
                            te.matmul(ps_a[:], hbuf[:, i * TSL + tt * 128: i * TSL + tt * 128 + 128],
                                      w3sb[:, i * TSL + ch * 512: i * TSL + ch * 512 + 512],
                                      start=(i == 0), stop=(i == 21))
                    pr.on("tensor", mm2)
                    pr.step()

                    def ep(v, e=e, tt=tt, ch=ch):
                        dst = acc[:, tt * D + ch * 512: tt * D + ch * 512 + 512]
                        if e == 0:
                            v.tensor_scalar(dst, ps_a[:], wts[:, tt * 8 + e: tt * 8 + e + 1],
                                            None, ALU.mult)
                        else:
                            v.scalar_tensor_tensor(dst, ps_a[:],
                                                   wts[:, tt * 8 + e: tt * 8 + e + 1],
                                                   dst, ALU.mult, ALU.add)
                    pr.on("vector", ep)
                    pr.step()
        for tt in range(NTT):
            ln_into(sl * NTT + tt, ln_f[:, :])

            def fin(v, tt=tt):
                v.tensor_tensor(acc[:, tt * D:(tt + 1) * D], acc[:, tt * D:(tt + 1) * D],
                                ln_f[:, :], ALU.add)
                v.tensor_tensor(acc[:, tt * D:(tt + 1) * D], acc[:, tt * D:(tt + 1) * D],
                                xt[:, :], ALU.add)
            pr.on("vector", fin)
            pr.step()
            pr.dma(y_h[(sl * NTT + tt) * 128:(sl * NTT + tt + 1) * 128, :],
                   acc[:, tt * D:(tt + 1) * D])
            pr.step()
    if standalone:
        return pr.build()
    return


def host_prep_l0f(inp):
    def chunk_lhsT(wT, co):
        I, O = wT.shape
        a = wT.reshape(I // 128, 128, O // co, co)
        return np.ascontiguousarray(a.transpose(2, 1, 0, 3).reshape(O // co, 128,
                                                                    (I // 128) * co))
    W1, W2, W3 = inp["L0_ffn_W1"], inp["L0_ffn_W2"], inp["L0_ffn_W3"]
    w1c = np.stack([chunk_lhsT(np.ascontiguousarray(W1[e].T), 128) for e in range(8)])
    w2c = np.stack([chunk_lhsT(np.ascontiguousarray(W2[e].T), 128) for e in range(8)])
    w3t = np.stack([np.ascontiguousarray(W3[e].T).reshape(22, 128, 1024) for e in range(8)])
    return {
        "gateT": np.ascontiguousarray(inp["L0_ffn_gate_w"].T),
        "rmsw_row": np.ascontiguousarray(inp["L0_ffn_rms_w"].reshape(1, -1)),
        "w1c": w1c, "w2c": w2c, "w3t": w3t,
    }


def build_l3a(pr=None, px='', x_h=None, y_h=None):
    standalone = pr is None
    if standalone:
        pr = Prog()
    FREQS = [31.0, 37.0, 41.0, 43.0, 47.0]
    if x_h is None:
        x_h = pr.dram_in(px + "x", [S, D])
    wqT_h = pr.dram_in(px + "wqT", [D, D])
    wkT_h = pr.dram_in(px + "wkT", [D, D])
    wvT_h = pr.dram_in(px + "wvT", [D, D])
    woT_h = pr.dram_in(px + "woT", [D, D])
    bq_h = pr.dram_in(px + "bq_row", [1, D])
    bk_h = pr.dram_in(px + "bk_row", [1, D])
    bv_h = pr.dram_in(px + "bv_row", [1, D])
    bo_h = pr.dram_in(px + "bo_row", [1, D])
    alpha_h = pr.dram_in(px + "alpha_row", [1, 80])
    phi_h = pr.dram_in(px + "phi_row", [1, 80])
    temp_h = pr.dram_in(px + "temperature", [1, 1])
    gam_h = pr.dram_in(px + "gamma_row", [1, D])
    bet_h = pr.dram_in(px + "beta_row", [1, D])
    if y_h is None:
        y_h = pr.dram_out(px + "y", [S, D])

    qT_d = pr.dram_tmp(px + "qT_d", [16, 64, S], BF16)
    kT_d = pr.dram_tmp(px + "kT_d", [16, 64, S], BF16)
    v_d = pr.dram_tmp(px + "v_d", [S, D], BF16)
    at_d = pr.dram_tmp(px + "at_d", [S, D], BF16)
    pr.init_consts()

    ld = Loader(pr)
    ones_col = pr.sbuf([1, 128], F32)
    pr.on("gpsimd", lambda g: g.memset(ones_col[:], 1.0))
    pr.step()
    ps_rep = pr.psum([128, 512], F32)

    def make_rep(row_src, n, is_handle=False):
        row = row_src if is_handle else ld.load_f32(row_src, 1, n)
        rep = pr.sbuf([128, n], F32)
        for c0 in range(0, n, 512):
            cw = min(512, n - c0)
            pr.on("tensor", lambda te, c0=c0, cw=cw, row=row: te.matmul(
                ps_rep[:, 0:cw], ones_col[0:1, :], row[0:1, c0:c0 + cw],
                start=True, stop=True))
            pr.step()
            pr.on("vector", lambda v, c0=c0, cw=cw, rep=rep: v.tensor_copy(
                rep[:, c0:c0 + cw], ps_rep[:, 0:cw]))
            pr.step()
        return rep

    bq_rep = make_rep(bq_h, D)
    bk_rep = make_rep(bk_h, D)
    bv_rep = make_rep(bv_h, D)
    bo_rep = make_rep(bo_h, D)
    gam_rep = make_rep(gam_h, D)
    bet_rep = make_rep(bet_h, D)
    al_rep = make_rep(alpha_h, 80)
    ph_rep = make_rep(phi_h, 80)
    pr.on("vector", lambda v: v.tensor_scalar(ph_rep[:], ph_rep[:], 1.0 / TWO_PI,
                                              None, ALU.mult))
    pr.step()
    trow = ld.load_f32(temp_h, 1, 1)
    grow = pr.sbuf([1, 5], F32)
    pr.on("scalar", lambda sc: sc.activation(trow[:], trow[:], AF.Abs))
    pr.step()
    pr.on("vector", lambda v: v.tensor_scalar(trow[:], trow[:], 0.1, None, ALU.add))
    pr.step()
    for fi, f in enumerate(FREQS):
        pr.on("vector", lambda v, fi=fi, f=f: v.tensor_scalar(
            grow[:, fi:fi + 1], trow[:, 0:1], float(16.0 * math.pi * f), None, ALU.mult))
    pr.step()
    pr.on("vector", lambda v: v.reciprocal(grow[:], grow[:]))
    pr.step()
    g_rep = make_rep(grow, 5, is_handle=True)

    xt = pr.sbuf([128, D], F32)
    ln_tok = pr.sbuf([128, D], BF16)
    lnT = pr.sbuf([128, 8 * 128], BF16)
    ps_bf = pr.psum_bf_pool(4)
    ps_a = pr.psum([128, 512], F32)
    prj = pr.sbuf([128, D], BF16)
    lnbufs = None
    if lnbufs is None:
        lnbufs = (pr.sbuf([128, 1], F32), pr.sbuf([128, 1], F32), pr.sbuf([128, 1], F32),
                  pr.sbuf([128, D], F32))
    wqT = ld.load_bf(wqT_h, D, D)
    wkT = ld.load_bf(wkT_h, D, D)
    wvT = ld.load_bf(wvT_h, D, D)

    for tt in range(NT):
        pr.dma(xt[:, :], x_h[tt * 128:(tt + 1) * 128, :])
        pr.step()
        lnbufs = emit_ln_transpose_tile(pr, xt[:, :], ln_tok,
                                        lambda j: j * 128, ps_bf, lnbufs, lnT)
        for wT, brep, dst, trans in ((wqT, bq_rep, qT_d, True), (wkT, bk_rep, kT_d, True),
                                     (wvT, bv_rep, v_d, False)):
            for ch in range(2):
                def mm(te, wT=wT, ch=ch):
                    for c in range(8):
                        te.matmul(ps_a[:], lnT[:, c * 128:(c + 1) * 128],
                                  wT[:, c * D + ch * 512: c * D + ch * 512 + 512],
                                  start=(c == 0), stop=(c == 7))
                pr.on("tensor", mm)
                pr.step()
                pr.on("vector", lambda v, brep=brep, ch=ch: v.tensor_tensor(
                    prj[:, ch * 512:(ch + 1) * 512], ps_a[:],
                    brep[:, ch * 512:(ch + 1) * 512], ALU.add))
                pr.step()
            if not trans:
                pr.dma(v_d[tt * 128:(tt + 1) * 128, :], prj[:, :])
                pr.step()
            else:
                for hp in range(8):
                    def trq(te, hp=hp):
                        te.transpose(ps_bf[0][0:64, :], prj[:, (2 * hp) * 64:(2 * hp + 1) * 64],
                                     pr.ident_bf[:])
                        te.transpose(ps_bf[0][64:128, :],
                                     prj[:, (2 * hp + 1) * 64:(2 * hp + 2) * 64],
                                     pr.ident_bf[:])
                    pr.on("tensor", trq)
                    pr.step()
                    pr.on("vector", lambda v: v.tensor_copy(ln_tok[:, 0:128], ps_bf[0][:]))
                    pr.step()
                    pr.dma(dst[2 * hp, :, tt * 128:(tt + 1) * 128], ln_tok[0:64, 0:128])
                    pr.dma(dst[2 * hp + 1, :, tt * 128:(tt + 1) * 128], ln_tok[64:128, 0:128])
                    pr.step()

    pr.free(wvT)
    pr.free(wkT)
    pr.free(wqT)
    kTh = pr.sbuf([64, S], BF16)
    qTh = pr.sbuf([64, S], BF16)
    vph = pr.sbuf([128, 32 * 65], BF16)
    sig = pr.sbuf([128, 4096], F32)
    rb1 = pr.sbuf([128, 4096], F32)
    rb2 = pr.sbuf([128, 4096], F32)
    accc = pr.sbuf([128, 4096], F32)
    Tw = pr.sbuf([128, 4096], BF16)
    ps_av = pr.psum([65, 512], F32)
    t_at = pr.sbuf([128, 65], F32)
    at_tok = pr.sbuf([128, 64], BF16)
    resn = pr.sbuf([65, 512], F32)
    ps_a4 = pr.psum([128, 2048], F32)

    for h in range(16):
        pr.dma(kTh[:, :], kT_d[h])
        pr.dma(qTh[:, :], qT_d[h])
        pr.step()
        pr.on("gpsimd", lambda g: g.memset(vph[:, :], 0.0))
        pr.step()
        for jt in range(32):
            pr.dma(pr.stage_bf[:, :], v_d[jt * 128:(jt + 1) * 128, :])
            pr.step()

            def cpv(v, jt=jt, h=h):
                v.tensor_copy(vph[:, jt * 65:jt * 65 + 64],
                              pr.stage_bf[:, h * 64:(h + 1) * 64])
                v.memset(vph[:, jt * 65 + 64:jt * 65 + 65], 1.0)
            pr.on("vector", cpv)
            pr.step()
        for qc in range(8):
            for grp in range(4):
                for q4b in range(2):
                    def mm4(te, q4b=q4b, grp=grp, qc=qc):
                        for jj in range(4):
                            jt = grp * 8 + q4b * 4 + jj
                            te.matmul(ps_a4[:, jj * 512:(jj + 1) * 512],
                                      kTh[:, jt * 128:(jt + 1) * 128],
                                      qTh[:, qc * 512:(qc + 1) * 512],
                                      start=True, stop=True)
                    pr.on("tensor", mm4)
                    pr.step()

                    def ev4(v, q4b=q4b):
                        for jj in range(4):
                            j8 = q4b * 4 + jj
                            v.tensor_copy(sig[:, j8 * 512:(j8 + 1) * 512],
                                          ps_a4[:, jj * 512:(jj + 1) * 512])
                    pr.on("vector", ev4)
                    pr.step()
                for fi in range(5):
                    def ew1(v, fi=fi, h=h):
                        v.tensor_scalar(rb1[:], sig[:], g_rep[:, fi:fi + 1],
                                        ph_rep[:, h * 5 + fi:h * 5 + fi + 1],
                                        ALU.mult, ALU.add)
                        v.tensor_scalar(rb2[:], rb1[:], MAGIC, MAGIC, ALU.add, ALU.subtract)
                        v.tensor_tensor(rb1[:], rb1[:], rb2[:], ALU.subtract)
                    pr.on("vector", ew1)
                    pr.step()
                    pr.on("scalar", lambda sc: sc.activation(rb1[:], rb1[:], AF.Sin,
                                                             scale=TWO_PI))
                    pr.step()

                    def ew2(v, fi=fi, h=h):
                        if fi == 0:
                            v.tensor_scalar(accc[:], rb1[:],
                                            al_rep[:, h * 5:h * 5 + 1], None, ALU.mult)
                        else:
                            v.scalar_tensor_tensor(accc[:], rb1[:],
                                                   al_rep[:, h * 5 + fi:h * 5 + fi + 1],
                                                   accc[:], ALU.mult, ALU.add)
                    pr.on("vector", ew2)
                    pr.step()
                pr.on("scalar", lambda sc: sc.activation(Tw[:], accc[:], AF.Sigmoid))
                pr.step()

                def mmav(te, grp=grp):
                    for j8 in range(8):
                        jt = grp * 8 + j8
                        te.matmul(ps_av[:, :], vph[:, jt * 65:(jt + 1) * 65],
                                  Tw[:, j8 * 512:(j8 + 1) * 512],
                                  start=(grp == 0 and j8 == 0),
                                  stop=(grp == 3 and j8 == 7))
                pr.on("tensor", mmav)
                pr.step()
            pr.on("vector", lambda v: v.tensor_copy(resn[:, :], ps_av[:, :]))
            pr.step()
            for q4 in range(4):
                def trat(te, q4=q4):
                    te.transpose(ps_a[:, 0:65], resn[0:65, q4 * 128:(q4 + 1) * 128],
                                 pr.ident[0:65, 0:65])
                pr.on("tensor", trat)
                pr.step()

                def nrm(v):
                    v.tensor_copy(t_at[:, :], ps_a[:, 0:65])
                    v.tensor_scalar(t_at[:, 64:65], t_at[:, 64:65], 1e-10, None, ALU.add)
                pr.on("vector", nrm)
                pr.step()
                pr.on("vector", lambda v: v.reciprocal(t_at[:, 64:65], t_at[:, 64:65]))
                pr.step()
                pr.on("vector", lambda v: v.tensor_scalar(at_tok[:, :], t_at[:, 0:64],
                                                          t_at[:, 64:65], None, ALU.mult))
                pr.step()
                pr.dma(at_d[(qc * 4 + q4) * 128:(qc * 4 + q4 + 1) * 128,
                            h * 64:(h + 1) * 64], at_tok[:, :])
                pr.step()

    woT = ld.load_bf(woT_h, D, D)
    atT = pr.sbuf([128, 8 * 128], BF16)
    yt = pr.sbuf([128, D], F32)
    stat2 = pr.sbuf([128, 4], F32)
    sqb = pr.sbuf([128, D], F32)
    for tt in range(NT):
        pr.dma(ln_tok[:, :], at_d[tt * 128:(tt + 1) * 128, :])
        pr.step()
        for j0 in (0, 4):
            def tra(te, j0=j0):
                for j in range(4):
                    te.transpose(ps_bf[j], ln_tok[:, (j0 + j) * 128:(j0 + j + 1) * 128],
                                 pr.ident_bf[:])
            pr.on("tensor", tra)
            pr.step()

            def eva(v, j0=j0):
                for j in range(4):
                    v.tensor_copy(atT[:, (j0 + j) * 128:(j0 + j + 1) * 128], ps_bf[j])
            pr.on("vector", eva)
            pr.step()
        for ch in range(2):
            def mmo(te, ch=ch):
                for c in range(8):
                    te.matmul(ps_a[:], atT[:, c * 128:(c + 1) * 128],
                              woT[:, c * D + ch * 512: c * D + ch * 512 + 512],
                              start=(c == 0), stop=(c == 7))
            pr.on("tensor", mmo)
            pr.step()
            pr.on("vector", lambda v, ch=ch: v.tensor_tensor(
                yt[:, ch * 512:(ch + 1) * 512], ps_a[:],
                bo_rep[:, ch * 512:(ch + 1) * 512], ALU.add))
            pr.step()
        def lna(v):
            v.reduce_sum(stat2[:, 0:1], yt[:, :], axis=AX)
        pr.on("vector", lna)
        pr.on("scalar", lambda sc: sc.activation(sqb[:], yt[:, :], AF.Square))
        pr.step()

        def lnb(v):
            v.reduce_sum(stat2[:, 1:2], sqb[:], axis=AX)
            v.tensor_scalar(stat2[:, 0:1], stat2[:, 0:1], 1.0 / D, None, ALU.mult)
            v.tensor_scalar(stat2[:, 1:2], stat2[:, 1:2], 1.0 / D, None, ALU.mult)
            v.tensor_tensor(stat2[:, 2:3], stat2[:, 0:1], stat2[:, 0:1], ALU.mult)
            v.tensor_tensor(stat2[:, 1:2], stat2[:, 1:2], stat2[:, 2:3], ALU.subtract)
            v.tensor_scalar(stat2[:, 1:2], stat2[:, 1:2], 1e-5, None, ALU.add)
        pr.on("vector", lnb)
        pr.step()
        pr.on("scalar", lambda sc: sc.activation(stat2[:, 1:2], stat2[:, 1:2], AF.Sqrt))
        pr.step()
        pr.on("vector", lambda v: v.reciprocal(stat2[:, 2:3], stat2[:, 1:2]))
        pr.step()
        pr.dma(xt[:, :], x_h[tt * 128:(tt + 1) * 128, :])
        pr.step()

        def fin(v):
            v.tensor_scalar(yt[:, :], yt[:, :], stat2[:, 0:1], stat2[:, 2:3],
                            ALU.subtract, ALU.mult)
            v.tensor_tensor(yt[:, :], yt[:, :], gam_rep[:, :], ALU.mult)
            v.tensor_tensor(yt[:, :], yt[:, :], bet_rep[:, :], ALU.add)
            v.tensor_tensor(yt[:, :], yt[:, :], xt[:, :], ALU.add)
        pr.on("vector", fin)
        pr.step()
        pr.dma(y_h[tt * 128:(tt + 1) * 128, :], yt[:, :])
        pr.step()
    if standalone:
        return pr.build()
    return


def host_prep_l3a(inp):
    return {
        "wqT": np.ascontiguousarray(inp["L3_attn_Wq"].T),
        "wkT": np.ascontiguousarray(inp["L3_attn_Wk"].T),
        "wvT": np.ascontiguousarray(inp["L3_attn_Wv"].T),
        "woT": np.ascontiguousarray(inp["L3_attn_Wo"].T),
        "bq_row": np.ascontiguousarray(inp["L3_attn_bq"].reshape(1, -1)),
        "bk_row": np.ascontiguousarray(inp["L3_attn_bk"].reshape(1, -1)),
        "bv_row": np.ascontiguousarray(inp["L3_attn_bv"].reshape(1, -1)),
        "bo_row": np.ascontiguousarray(inp["L3_attn_bo"].reshape(1, -1)),
        "alpha_row": np.ascontiguousarray(inp["L3_attn_alpha"].reshape(1, 80)),
        "phi_row": np.ascontiguousarray(inp["L3_attn_phi"].reshape(1, 80)),
        "temperature": np.ascontiguousarray(inp["L3_attn_temperature"].reshape(1, 1)),
        "gamma_row": np.ascontiguousarray(inp["L3_attn_gamma"].reshape(1, -1)),
        "beta_row": np.ascontiguousarray(inp["L3_attn_beta"].reshape(1, -1)),
    }


def _emit_tchain(pr, M_bf, MT_bf, Tt_bf, Ibf, ps_x, ps_y, B, BT, X, sign):
    Xn = Tt_bf
    op0 = ALU.subtract if sign > 0 else ALU.add

    def init(v):
        v.tensor_tensor(Xn[:], Ibf[:], M_bf[:], op0)
        v.tensor_tensor(X[:], Ibf[:], MT_bf[:], op0)
        v.tensor_copy(B[:], M_bf[:])
        v.tensor_copy(BT[:], MT_bf[:])
    pr.on("vector", init)
    pr.step()
    for lvl in range(6):
        def mmsq(te):
            te.matmul(ps_x[:, 0:128], BT[:], B[:], start=True, stop=True)
            te.matmul(ps_x[:, 128:256], B[:], BT[:], start=True, stop=True)
        pr.on("tensor", mmsq)
        pr.step()

        def evsq(v):
            v.tensor_copy(B[:], ps_x[:, 0:128])
            v.tensor_copy(BT[:], ps_x[:, 128:256])
            v.tensor_tensor(B[:], B[:], Ibf[:], ALU.add)
            v.tensor_tensor(BT[:], BT[:], Ibf[:], ALU.add)
        pr.on("vector", evsq)
        pr.step()

        def mmpr(te):
            te.matmul(ps_y[:, 0:128], X[:], B[:], start=True, stop=True)
            te.matmul(ps_y[:, 128:256], B[:], X[:], start=True, stop=True)
        pr.on("tensor", mmpr)
        pr.step()

        def evpr(v, last=(lvl == 5)):
            v.tensor_copy(Xn[:], ps_y[:, 0:128])
            v.tensor_copy(X[:], ps_y[:, 128:256])
            if not last:
                v.tensor_tensor(B[:], B[:], Ibf[:], ALU.subtract)
                v.tensor_tensor(BT[:], BT[:], Ibf[:], ALU.subtract)
        pr.on("vector", evpr)
        pr.step()

    pr.on("vector", lambda v: v.tensor_copy(Tt_bf[:], X[:]))
    pr.step()


L0A_NT = NT
L0A_SKIP = set()


def build_l0a(pr=None, px='', x_h=None, y_h=None):
    standalone = pr is None
    if standalone:
        pr = Prog()
    if x_h is None:
        x_h = pr.dram_in(px + "x", [S, D])
    wqkvT_h = pr.dram_in(px + "wqkvT", [D, 2048])
    convc_h = pr.dram_in(px + "convcols", [128, 64])
    wzT_h = pr.dram_in(px + "wzT", [D, D])
    wbT_h = pr.dram_in(px + "wbT", [D, 16])
    waT_h = pr.dram_in(px + "waT", [D, 16])
    dtb_h = pr.dram_in(px + "dtb_row", [1, 16])
    alog_h = pr.dram_in(px + "alog_row", [1, 16])
    nwt_h = pr.dram_in(px + "nw_tiled_row", [1, D])
    woutT_h = pr.dram_in(px + "woutT", [D, D])
    if y_h is None:
        y_h = pr.dram_out(px + "y", [S, D])
    pr.init_consts()
    onecol = pr.constcol(1.0)
    pr.step()

    ld = Loader(pr)
    ones_col = pr.sbuf([1, 128], F32)
    pr.on("gpsimd", lambda g: g.memset(ones_col[:], 1.0))
    pr.step()
    ps_rep = pr.psum([128, 512], F32)

    def make_rep(row_src, n, is_handle=False, nrep=128):
        row = row_src if is_handle else ld.load_f32(row_src, 1, n)
        rep = pr.sbuf([nrep, n], F32)
        for c0 in range(0, n, 512):
            cw = min(512, n - c0)
            pr.on("tensor", lambda te, c0=c0, cw=cw, row=row: te.matmul(
                ps_rep[0:nrep, 0:cw], ones_col[0:1, 0:nrep], row[0:1, c0:c0 + cw],
                start=True, stop=True))
            pr.step()
            pr.on("vector", lambda v, c0=c0, cw=cw, rep=rep: v.tensor_copy(
                rep[:, c0:c0 + cw], ps_rep[0:nrep, 0:cw]))
            pr.step()
        return rep

    wqkvT = ld.load_bf(wqkvT_h, D, 2048)
    wzT = ld.load_bf(wzT_h, D, D)
    wbaT = ld.load_bf(wbT_h, D, 16)
    waT = ld.load_bf(waT_h, D, 16)
    woutT = ld.load_bf(woutT_h, D, D)
    convc = ld.load_f32(convc_h, 128, 64)
    dtb_rep = make_rep(dtb_h, 16)
    alog_row = ld.load_f32(alog_h, 1, 16)
    pr.on("scalar", lambda sc: sc.activation(alog_row[:], alog_row[:], AF.Exp))
    pr.step()
    pr.on("vector", lambda v: v.tensor_scalar(alog_row[:], alog_row[:], -1.0, None, ALU.mult))
    pr.step()
    nega_rep = make_rep(alog_row, 16, is_handle=True)
    nw_rep = make_rep(nwt_h, D)

    onesmat = pr.sbuf([128, 128], F32)
    pr.on("gpsimd", lambda g: g.memset(onesmat[:], 1.0))
    pr.step()
    trilS = make_tri(pr, 128, strict=True, lower=True)
    triuI8 = make_tri(pr, 128, strict=False, lower=False)
    pr.on("vector", lambda v: v.tensor_scalar(triuI8[:], triuI8[:], 0.125, None, ALU.mult))
    pr.step()

    MIXW = 132
    mixraw = pr.sbuf([128, 16 * MIXW], F32)
    convout = pr.sbuf([128, 16 * 128], BF16)
    xt = pr.sbuf([128, D], F32)
    ln_tok = pr.sbuf([128, D], BF16)
    lnT = pr.sbuf([128, 8 * 128], BF16)
    z_tok = pr.sbuf([128, D], F32)
    beta_tok = pr.sbuf([128, 16], F32)
    g_tok = pr.sbuf([128, 16], F32)
    G_T = pr.sbuf([16, 128], F32)
    G_tok = pr.sbuf([128, 16], F32)
    kts_T = pr.sbuf([16, 128], F32)
    kts_tok = pr.sbuf([128, 16], F32)
    gam_tok = pr.sbuf([128, 16], F32)
    gcrep = pr.sbuf([64, 16], F32)
    rowt = pr.sbuf([1, 16], F32)

    GW = 8
    MW = GW * 128
    KW = GW * 64
    Gcm = pr.sbuf([128, MW], F32)
    Dm_ = pr.sbuf([128, MW], F32)
    Em = pr.sbuf([128, MW], F32)
    E2m = pr.sbuf([128, MW], F32)
    Mbf = pr.sbuf([128, MW], BF16)
    MTbf = pr.sbuf([128, MW], BF16)
    Bs = pr.sbuf([128, MW], BF16)
    BTs = pr.sbuf([128, MW], BF16)
    Xs = pr.sbuf([128, MW], BF16)
    Ttbf = pr.sbuf([128, MW], BF16)
    AqT = pr.sbuf([128, MW], BF16)
    Ibf8 = pr.sbuf([128, MW], BF16)
    trilS8 = pr.sbuf([128, MW], F32)
    triuI8w = pr.sbuf([128, MW], F32)
    Qal = pr.sbuf([64, MW], BF16)
    Kal = pr.sbuf([64, MW], BF16)
    Val = pr.sbuf([64, MW], BF16)
    Ksd = pr.sbuf([128, KW], BF16)
    Vsd = pr.sbuf([128, KW], BF16)
    BV = pr.sbuf([128, KW], BF16)
    BGK = pr.sbuf([128, KW], BF16)
    Ktil = pr.sbuf([128, KW], BF16)
    Pmat = pr.sbuf([128, KW], BF16)
    WT = pr.sbuf([64, MW], BF16)
    Ubf = pr.sbuf([128, KW], BF16)
    Osb = pr.sbuf([128, D], F32)
    Ssb = [pr.sbuf([64, 16 * 64], F32) for _ in range(2)]
    Sbf = pr.sbuf([64, KW], BF16)
    hsc = pr.sbuf([128, D], F32)
    rstd16 = pr.sbuf([128, 16], F32)
    hnorm = pr.sbuf([128, D], BF16)
    hT = pr.sbuf([128, 8 * 128], BF16)
    yt = pr.sbuf([128, D], F32)
    ps_bf = pr.psum_bf_pool(4)
    ps_a = pr.psum([128, 512], F32)
    ps_x = pr.psum([128, 1024], F32)
    ps_y = pr.psum([128, 1024], F32)
    ps_k = pr.psum([128, 128], F32)
    lnbufs = None

    def tile8(dst, src_ap):
        def f(v, dst=dst, src_ap=src_ap):
            for i in range(GW):
                v.tensor_copy(dst[:, i * 128:(i + 1) * 128], src_ap)
        pr.on("vector", f)
        pr.step()
    tile8(Ibf8, pr.ident_bf[:, :])
    tile8(trilS8, trilS[:, :])
    tile8(triuI8w, triuI8[:, :])

    pr.on("gpsimd", lambda g: g.memset(mixraw[:], 0.0))
    pr.step()
    pr.on("gpsimd", lambda g: g.memset(Ssb[0][:], 0.0))
    pr.step()
    if L0A_SKIP:
        for b in (Ksd, Vsd, Qal, Em, E2m, Mbf, MTbf, AqT, Ttbf, Pmat, WT, Ktil, BV, BGK,
                  Bs, BTs, Xs, Gcm, Dm_):
            pr.on("gpsimd", lambda g, b=b: g.memset(b[:], 0.0))
            pr.step()

    for c in range(L0A_NT):
        cur, nxt = Ssb[c % 2], Ssb[(c + 1) % 2]
        pr.dma(xt[:, :], x_h[c * 128:(c + 1) * 128, :])
        pr.step()
        lnbufs = emit_ln_transpose_tile(pr, xt[:, :], ln_tok,
                                        lambda j: j * 128, ps_bf, lnbufs, lnT)
        for fh in range(2):
            def mm(te, fh=fh):
                for ft in range(fh * 8, fh * 8 + 8):
                    for cd in range(8):
                        te.matmul(ps_x[:, (ft % 8) * 128:(ft % 8 + 1) * 128],
                                  wqkvT[:, cd * 2048 + ft * 128: cd * 2048 + (ft + 1) * 128],
                                  lnT[:, cd * 128:(cd + 1) * 128],
                                  start=(cd == 0), stop=(cd == 7))
            pr.on("tensor", mm)
            pr.step()

            def ev(v, fh=fh):
                for ft in range(fh * 8, fh * 8 + 8):
                    v.tensor_copy(mixraw[:, ft * MIXW + 4: ft * MIXW + 132],
                                  ps_x[:, (ft % 8) * 128:(ft % 8 + 1) * 128])
            pr.on("vector", ev)
            pr.step()
        def cv(v):
            for ft in range(16):
                base = ft * MIXW
                v.tensor_scalar(hsc[:, ft * 64:ft * 64 + 128] if False else hsc[:, 0:128],
                                mixraw[:, base + 1: base + 129],
                                convc[:, ft * 4:ft * 4 + 1], None, ALU.mult)
                for j in (1, 2, 3):
                    v.scalar_tensor_tensor(hsc[:, 0:128],
                                           mixraw[:, base + 1 + j: base + 129 + j],
                                           convc[:, ft * 4 + j:ft * 4 + j + 1],
                                           hsc[:, 0:128], ALU.mult, ALU.add)
                v.tensor_copy(yt[:, ft * 64:ft * 64 + 64] if False else yt[:, 0:128], hsc[:, 0:128])
        for ft in range(16):
            def cv1(v, ft=ft):
                base = ft * MIXW
                v.tensor_scalar(hsc[:, 0:128], mixraw[:, base + 1: base + 129],
                                convc[:, ft * 4:ft * 4 + 1], None, ALU.mult)
                for j in (1, 2, 3):
                    v.scalar_tensor_tensor(hsc[:, 0:128],
                                           mixraw[:, base + 1 + j: base + 129 + j],
                                           convc[:, ft * 4 + j:ft * 4 + j + 1],
                                           hsc[:, 0:128], ALU.mult, ALU.add)
            pr.on("vector", cv1)
            pr.step()
            pr.on("scalar", lambda sc, ft=ft: sc.activation(
                convout[:, ft * 128:(ft + 1) * 128], hsc[:, 0:128], AF.Silu))
            pr.step()

        def halo(v):
            for ft in range(16):
                base = ft * MIXW
                v.tensor_copy(mixraw[:, base:base + 4], mixraw[:, base + 128:base + 132])
        pr.on("vector", halo)
        pr.step()
        for ch in range(2):
            def mmz(te, ch=ch):
                for cd in range(8):
                    te.matmul(ps_a[:], lnT[:, cd * 128:(cd + 1) * 128],
                              wzT[:, cd * D + ch * 512: cd * D + ch * 512 + 512],
                              start=(cd == 0), stop=(cd == 7))
            pr.on("tensor", mmz)
            pr.step()
            pr.on("vector", lambda v, ch=ch: v.tensor_copy(
                z_tok[:, ch * 512:(ch + 1) * 512], ps_a[:]))
            pr.step()

        def mmba(te):
            for cd in range(8):
                te.matmul(ps_x[:, 0:16], lnT[:, cd * 128:(cd + 1) * 128],
                          wbaT[:, cd * 16:(cd + 1) * 16], start=(cd == 0), stop=(cd == 7))
            for cd in range(8):
                te.matmul(ps_x[:, 16:32], lnT[:, cd * 128:(cd + 1) * 128],
                          waT[:, cd * 16:(cd + 1) * 16], start=(cd == 0), stop=(cd == 7))
        pr.on("tensor", mmba)
        pr.step()
        pr.on("scalar", lambda sc: sc.activation(beta_tok[:], ps_x[:, 0:16], AF.Sigmoid))
        pr.step()
        pr.on("vector", lambda v: v.tensor_tensor(g_tok[:], ps_x[:, 16:32], dtb_rep[:, 0:16],
                                                  ALU.add))
        pr.step()
        pr.on("scalar", lambda sc: sc.activation(hsc[:, 0:16], g_tok[:], AF.Abs))
        pr.step()
        pr.on("scalar", lambda sc: sc.activation(hsc[:, 0:16], hsc[:, 0:16], AF.Exp,
                                                 scale=-1.0))
        pr.step()
        pr.on("scalar", lambda sc: sc.activation(hsc[:, 0:16], hsc[:, 0:16], AF.Ln,
                                                 bias=onecol[:, 0:1]))
        pr.step()

        def spl(v):
            v.tensor_scalar(g_tok[:], g_tok[:], 0.0, None, ALU.max)
            v.tensor_tensor(g_tok[:], g_tok[:], hsc[:, 0:16], ALU.add)
            v.tensor_tensor(g_tok[:], g_tok[:], nega_rep[:, 0:16], ALU.mult)
        pr.on("vector", spl)
        pr.step()
        if "gpre" in L0A_SKIP:
            continue
        pr.on("tensor", lambda te: te.transpose(ps_k[0:16, :], g_tok[:, :], pr.ident[:]))
        pr.step()
        pr.on("vector", lambda v: v.tensor_copy(G_T[:, :], ps_k[0:16, :]))
        pr.step()
        pr.on("vector", lambda v: v.tensor_tensor_scan(G_T[:, :], G_T[:, :], G_T[:, :],
                                                       0.0, ALU.add, ALU.bypass))
        pr.step()
        pr.on("vector", lambda v: v.tensor_scalar(kts_T[:, :], G_T[:, :],
                                                  G_T[:, 127:128], None, ALU.subtract))
        pr.step()
        pr.on("scalar", lambda sc: sc.activation(kts_T[:, :], kts_T[:, :], AF.Exp,
                                                 scale=-1.0))
        pr.step()

        def trg2(te):
            te.transpose(ps_x[:, 0:16], G_T[:, :], pr.ident[0:16, 0:16])
            te.transpose(ps_x[:, 16:32], kts_T[:, :], pr.ident[0:16, 0:16])
            te.transpose(ps_x[0:1, 32:48], G_T[:, 127:128], pr.ident[0:16, 0:16])
        pr.on("tensor", trg2)
        pr.step()

        def evg(v):
            v.tensor_copy(G_tok[:, :], ps_x[:, 0:16])
            v.tensor_copy(kts_tok[:, :], ps_x[:, 16:32])
            v.tensor_copy(rowt[:, :], ps_x[0:1, 32:48])
        pr.on("vector", evg)
        pr.step()
        pr.on("scalar", lambda sc: sc.activation(gam_tok[:, :], G_tok[:, :], AF.Exp))
        pr.step()
        pr.on("scalar", lambda sc: sc.activation(rowt[:, :], rowt[:, :], AF.Exp))
        pr.step()
        pr.on("tensor", lambda te: te.matmul(ps_x[0:64, 0:16], ones_col[0:1, 0:64],
                                             rowt[0:1, :], start=True, stop=True))
        pr.step()
        pr.on("vector", lambda v: v.tensor_copy(gcrep[:, :], ps_x[0:64, 0:16]))
        pr.step()

        for g8 in (() if "heads" in L0A_SKIP else range(2)):
            H0 = g8 * 8

            if "trkv" not in L0A_SKIP:
                def qcp(v, H0=H0):
                    for i in range(GW):
                        hh = H0 + i
                        kh = hh // 2
                        qft, qrow = kh // 2, (kh % 2) * 64
                        kft, krow = 4 + kh // 2, (kh % 2) * 64
                        vft, vrow = 8 + hh // 2, (hh % 2) * 64
                        v.tensor_copy(Qal[:, i * 128:(i + 1) * 128],
                                      convout[qrow:qrow + 64, qft * 128:(qft + 1) * 128])
                        v.tensor_copy(Kal[:, i * 128:(i + 1) * 128],
                                      convout[krow:krow + 64, kft * 128:(kft + 1) * 128])
                        v.tensor_copy(Val[:, i * 128:(i + 1) * 128],
                                      convout[vrow:vrow + 64, vft * 128:(vft + 1) * 128])
                pr.on("vector", qcp)
                pr.step()
            for rnd in (() if "trkv" in L0A_SKIP else range(4)):
                def trk(te, rnd=rnd):
                    src_b = Kal if rnd < 2 else Val
                    for i in range(4):
                        hi = (rnd % 2) * 4 + i
                        te.transpose(ps_bf[i][:, 0:64],
                                     src_b[:, hi * 128:(hi + 1) * 128],
                                     pr.ident_bf[0:64, 0:64])
                pr.on("tensor", trk)
                pr.step()

                def evk(v, rnd=rnd):
                    dst = Ksd if rnd < 2 else Vsd
                    for i in range(4):
                        hi = (rnd % 2) * 4 + i
                        v.tensor_copy(dst[:, hi * 64:(hi + 1) * 64], ps_bf[i][:, 0:64])
                pr.on("vector", evk)
                pr.step()

            if "gcmd" not in L0A_SKIP:
                def gcmf(v, H0=H0):
                    for i in range(GW):
                        v.tensor_scalar(Gcm[:, i * 128:(i + 1) * 128], onesmat[:, :],
                                        G_tok[:, H0 + i:H0 + i + 1], None, ALU.mult)
                pr.on("vector", gcmf)
                pr.step()

                def gtr(te):
                    for i in range(GW):
                        te.transpose(ps_x[:, i * 128:(i + 1) * 128],
                                     Gcm[:, i * 128:(i + 1) * 128], pr.ident[:])
                pr.on("tensor", gtr)
                pr.step()
                pr.on("vector", lambda v: v.tensor_tensor(Dm_[:, :], Gcm[:, :], ps_x[:, :],
                                                          ALU.subtract))
                pr.step()

                def emin(v):
                    v.tensor_scalar(Em[:, :], Dm_[:, :], 0.0, None, ALU.min)
                    v.tensor_scalar(E2m[:, :], Dm_[:, :], -1.0, 0.0, ALU.mult, ALU.min)
                pr.on("vector", emin)
                pr.step()
                pr.on("scalar", lambda sc: sc.activation(Em[:, :], Em[:, :], AF.Exp))
                pr.step()
                pr.on("scalar", lambda sc: sc.activation(E2m[:, :], E2m[:, :], AF.Exp))
                pr.step()

            if "mkk" not in L0A_SKIP:
                def mkk(te, H0=H0):
                    for i in range(GW):
                        Kfm = Kal[:, i * 128:(i + 1) * 128]
                        Qfm = Qal[:, i * 128:(i + 1) * 128]
                        te.matmul(ps_x[:, i * 128:(i + 1) * 128], Kfm, Kfm,
                                  start=True, stop=True)
                        te.matmul(ps_y[:, i * 128:(i + 1) * 128], Kfm, Qfm,
                                  start=True, stop=True)
                pr.on("tensor", mkk)
                pr.step()

                def mkm(v, H0=H0):
                    v.tensor_tensor(Dm_[:, :], ps_x[:, :], Em[:, :], ALU.mult)
                    for i in range(GW):
                        v.tensor_scalar(Dm_[:, i * 128:(i + 1) * 128],
                                        Dm_[:, i * 128:(i + 1) * 128],
                                        beta_tok[:, H0 + i:H0 + i + 1], None, ALU.mult)
                    v.tensor_tensor(Mbf[:, :], Dm_[:, :], trilS8[:, :], ALU.mult)
                    v.tensor_tensor(Gcm[:, :], ps_y[:, :], E2m[:, :], ALU.mult)
                    v.tensor_tensor(AqT[:, :], Gcm[:, :], triuI8w[:, :], ALU.mult)
                pr.on("vector", mkm)
                pr.step()

            for rnd in (() if "mt" in L0A_SKIP else range(2)):
                def mtr(te, rnd=rnd):
                    for i in range(4):
                        te.transpose(ps_bf[i], Mbf[:, (rnd * 4 + i) * 128:(rnd * 4 + i + 1) * 128],
                                     pr.ident_bf[:])
                pr.on("tensor", mtr)
                pr.step()

                def mte(v, rnd=rnd):
                    for i in range(4):
                        v.tensor_copy(MTbf[:, (rnd * 4 + i) * 128:(rnd * 4 + i + 1) * 128],
                                      ps_bf[i])
                pr.on("vector", mte)
                pr.step()

            def tini(v):
                v.tensor_tensor(Ttbf[:, :], Ibf8[:, :], Mbf[:, :], ALU.subtract)
                v.tensor_tensor(Xs[:, :], Ibf8[:, :], MTbf[:, :], ALU.subtract)
                v.tensor_copy(Bs[:, :], Mbf[:, :])
                v.tensor_copy(BTs[:, :], MTbf[:, :])
            pr.on("vector", tini)
            pr.step()
            for lvl in (() if "tchain" in L0A_SKIP else range(6)):
                def msq(te):
                    for i in range(GW):
                        sl = slice(i * 128, (i + 1) * 128)
                        te.matmul(ps_x[:, sl], BTs[:, sl], Bs[:, sl], start=True, stop=True)
                        te.matmul(ps_y[:, sl], Bs[:, sl], BTs[:, sl], start=True, stop=True)
                pr.on("tensor", msq)
                pr.step()

                def esq(v):
                    v.tensor_copy(Bs[:, :], ps_x[:, :])
                    v.tensor_copy(BTs[:, :], ps_y[:, :])
                    v.tensor_tensor(Bs[:, :], Bs[:, :], Ibf8[:, :], ALU.add)
                    v.tensor_tensor(BTs[:, :], BTs[:, :], Ibf8[:, :], ALU.add)
                pr.on("vector", esq)
                pr.step()

                def mpr(te):
                    for i in range(GW):
                        sl = slice(i * 128, (i + 1) * 128)
                        te.matmul(ps_x[:, sl], Xs[:, sl], Bs[:, sl], start=True, stop=True)
                        te.matmul(ps_y[:, sl], Bs[:, sl], Xs[:, sl], start=True, stop=True)
                pr.on("tensor", mpr)
                pr.step()

                def epr(v, last=(lvl == 5)):
                    v.tensor_copy(Ttbf[:, :], ps_x[:, :])
                    v.tensor_copy(Xs[:, :], ps_y[:, :])
                    if not last:
                        v.tensor_tensor(Bs[:, :], Bs[:, :], Ibf8[:, :], ALU.subtract)
                        v.tensor_tensor(BTs[:, :], BTs[:, :], Ibf8[:, :], ALU.subtract)
                pr.on("vector", epr)
                pr.step()
            pr.on("vector", lambda v: v.tensor_copy(Ttbf[:, :], Xs[:, :]))
            pr.step()

            def bvf(v, H0=H0):
                for i in range(GW):
                    h = H0 + i
                    v.tensor_scalar(BV[:, i * 64:(i + 1) * 64], Vsd[:, i * 64:(i + 1) * 64],
                                    beta_tok[:, h:h + 1], None, ALU.mult)
                    v.tensor_scalar(Gcm[:, 0:1], beta_tok[:, h:h + 1],
                                    gam_tok[:, h:h + 1], None, ALU.mult)
                    v.tensor_scalar(BGK[:, i * 64:(i + 1) * 64], Ksd[:, i * 64:(i + 1) * 64],
                                    Gcm[:, 0:1], None, ALU.mult)
                    v.tensor_scalar(Ktil[:, i * 64:(i + 1) * 64], Ksd[:, i * 64:(i + 1) * 64],
                                    kts_tok[:, h:h + 1], None, ALU.mult)
            pr.on("vector", bvf)
            pr.step()

            def mpw(te):
                for i in range(GW):
                    te.matmul(ps_x[:, i * 64:(i + 1) * 64],
                              Ttbf[:, i * 128:(i + 1) * 128], BV[:, i * 64:(i + 1) * 64],
                              start=True, stop=True)
                    te.matmul(ps_y[0:64, i * 128:(i + 1) * 128],
                              BGK[:, i * 64:(i + 1) * 64], Ttbf[:, i * 128:(i + 1) * 128],
                              start=True, stop=True)
            pr.on("tensor", mpw)
            pr.step()

            def evpw(v):
                v.tensor_copy(Pmat[:, 0:KW], ps_x[:, 0:KW])
                v.tensor_copy(WT[:, :], ps_y[0:64, :])
            pr.on("vector", evpw)
            pr.step()

            if "seq" in L0A_SKIP:
                pr.on("gpsimd", lambda g: g.memset(Osb[:, :], 0.25))
                pr.step()
                continue
            pr.on("vector", lambda v, H0=H0, cur=cur: v.tensor_copy(
                Sbf[:, :], cur[:, H0 * 64:(H0 + GW) * 64]))
            pr.step()

            def mseq(te):
                for i in range(GW):
                    te.matmul(ps_x[:, i * 64:(i + 1) * 64], WT[:, i * 128:(i + 1) * 128],
                              Sbf[:, i * 64:(i + 1) * 64], start=True, stop=True)
                    te.matmul(ps_y[:, i * 64:(i + 1) * 64], Qal[:, i * 128:(i + 1) * 128],
                              Sbf[:, i * 64:(i + 1) * 64], start=True, stop=True)
            pr.on("tensor", mseq)
            pr.step()
            pr.on("vector", lambda v: v.tensor_tensor(Ubf[:, :], Pmat[:, :], ps_x[:, 0:KW],
                                                      ALU.subtract))
            pr.step()

            def mou(te):
                for i in range(GW):
                    te.matmul(ps_x[:, i * 64:(i + 1) * 64], AqT[:, i * 128:(i + 1) * 128],
                              Ubf[:, i * 64:(i + 1) * 64], start=True, stop=True)
                    te.matmul(ps_y[0:64, 512 + i * 64:512 + (i + 1) * 64],
                              Ktil[:, i * 64:(i + 1) * 64], Ubf[:, i * 64:(i + 1) * 64],
                              start=True, stop=True)
            pr.on("tensor", mou)
            pr.step()

            def evo(v, H0=H0, cur=cur, nxt=nxt):
                for i in range(GW):
                    h = H0 + i
                    v.tensor_scalar(Osb[:, h * 64:(h + 1) * 64], ps_y[:, i * 64:(i + 1) * 64],
                                    gam_tok[:, h:h + 1], 0.125, ALU.mult, ALU.mult)
                v.tensor_tensor(Osb[:, H0 * 64:(H0 + GW) * 64],
                                Osb[:, H0 * 64:(H0 + GW) * 64], ps_x[:, 0:KW], ALU.add)
                for i in range(GW):
                    h = H0 + i
                    v.scalar_tensor_tensor(nxt[:, h * 64:(h + 1) * 64],
                                           cur[:, h * 64:(h + 1) * 64],
                                           gcrep[:, h:h + 1],
                                           ps_y[0:64, 512 + i * 64:512 + (i + 1) * 64],
                                           ALU.mult, ALU.add)
            pr.on("vector", evo)
            pr.step()
        if "heads" in L0A_SKIP:
            pr.on("gpsimd", lambda g: g.memset(Osb[:, :], 0.5))
            pr.step()
        pr.on("scalar", lambda sc: sc.activation(hsc[:, :], z_tok[:, :], AF.Silu))
        pr.step()
        pr.on("vector", lambda v: v.tensor_tensor(hsc[:, :], hsc[:, :], Osb[:, :], ALU.mult))
        pr.step()
        pr.on("scalar", lambda sc: sc.activation(yt[:, :], hsc[:, :], AF.Square))
        pr.step()
        pr.on("vector", lambda v: v.reduce_sum(
            rstd16[:, :].rearrange("p (a o) -> p a o", o=1),
            yt[:, :].rearrange("p (a n) -> p a n", a=16), axis=AX))
        pr.step()
        pr.on("vector", lambda v: v.tensor_scalar(rstd16[:, :], rstd16[:, :], 1.0 / 64,
                                                  1e-6, ALU.mult, ALU.add))
        pr.step()
        pr.on("scalar", lambda sc: sc.activation(rstd16[:, :], rstd16[:, :], AF.Sqrt))
        pr.step()
        pr.on("vector", lambda v: v.reciprocal(rstd16[:, :], rstd16[:, :]))
        pr.step()

        def hnf(v):
            for h in range(16):
                v.tensor_scalar(hnorm[:, h * 64:(h + 1) * 64], hsc[:, h * 64:(h + 1) * 64],
                                rstd16[:, h:h + 1], None, ALU.mult)
        pr.on("vector", hnf)
        pr.step()
        pr.on("vector", lambda v: v.tensor_tensor(hnorm[:, :], hnorm[:, :], nw_rep[:, :],
                                                  ALU.mult))
        pr.step()
        for j0 in (0, 4):
            def trh(te, j0=j0):
                for j in range(4):
                    te.transpose(ps_bf[j], hnorm[:, (j0 + j) * 128:(j0 + j + 1) * 128],
                                 pr.ident_bf[:])
            pr.on("tensor", trh)
            pr.step()

            def evh(v, j0=j0):
                for j in range(4):
                    v.tensor_copy(hT[:, (j0 + j) * 128:(j0 + j + 1) * 128], ps_bf[j])
            pr.on("vector", evh)
            pr.step()
        for ch in range(2):
            def mmo(te, ch=ch):
                for cd in range(8):
                    te.matmul(ps_a[:], hT[:, cd * 128:(cd + 1) * 128],
                              woutT[:, cd * D + ch * 512: cd * D + ch * 512 + 512],
                              start=(cd == 0), stop=(cd == 7))
            pr.on("tensor", mmo)
            pr.step()
            pr.on("vector", lambda v, ch=ch: v.tensor_tensor(
                yt[:, ch * 512:(ch + 1) * 512], ps_a[:], xt[:, ch * 512:(ch + 1) * 512],
                ALU.add))
            pr.step()
        pr.dma(y_h[c * 128:(c + 1) * 128, :], yt[:, :])
        pr.step()
    if standalone:
        return pr.build()
    return


def host_prep_l0a(inp):
    cw = inp["L0_attn_conv_w"][:, 0, :]
    convcols = np.ascontiguousarray(cw.reshape(16, 128, 4).transpose(1, 0, 2).reshape(128, 64))
    return {
        "wqkvT": np.ascontiguousarray(inp["L0_attn_W_qkv"].T),
        "convcols": convcols,
        "wzT": np.ascontiguousarray(inp["L0_attn_W_z"].T),
        "wbT": np.ascontiguousarray(inp["L0_attn_W_b"].T),
        "waT": np.ascontiguousarray(inp["L0_attn_W_a"].T),
        "dtb_row": np.ascontiguousarray(inp["L0_attn_dt_bias"].reshape(1, 16)),
        "alog_row": np.ascontiguousarray(inp["L0_attn_A_log"].reshape(1, 16)),
        "nw_tiled_row": np.ascontiguousarray(np.tile(inp["L0_attn_norm_weight"], 16).reshape(1, D)),
        "woutT": np.ascontiguousarray(inp["L0_attn_W_out"].T),
    }


def build_l1a(pr=None, px='', x_h=None, y_h=None):
    standalone = pr is None
    if standalone:
        pr = Prog()
    if x_h is None:
        x_h = pr.dram_in(px + "x", [S, D])
    wq_h = pr.dram_in(px + "wq", [D, D])
    wk_h = pr.dram_in(px + "wk", [D, D])
    wv_h = pr.dram_in(px + "wv", [D, D])
    wgam_h = pr.dram_in(px + "wgam", [D, 16])
    wf1_h = pr.dram_in(px + "wf1", [D, 64])
    wf2_h = pr.dram_in(px + "wf2", [64, D])
    wog1_h = pr.dram_in(px + "wog1", [D, 64])
    wog2_h = pr.dram_in(px + "wog2", [64, D])
    nw_h = pr.dram_in(px + "nw_row", [1, D])
    wo_h = pr.dram_in(px + "wo", [D, D])
    if y_h is None:
        y_h = pr.dram_out(px + "y", [S, D])
    pr.init_consts()
    onecol = pr.constcol(1.0)
    pr.step()

    ld = Loader(pr)
    ones_col = pr.sbuf([1, 128], F32)
    pr.on("gpsimd", lambda g: g.memset(ones_col[:], 1.0))
    pr.step()
    ps_rep = pr.psum([128, 512], F32)

    def make_rep(row_src, n):
        row = ld.load_f32(row_src, 1, n)
        rep = pr.sbuf([128, n], F32)
        for c0 in range(0, n, 512):
            cw = min(512, n - c0)
            pr.on("tensor", lambda te, c0=c0, cw=cw, row=row: te.matmul(
                ps_rep[:, 0:cw], ones_col[0:1, :], row[0:1, c0:c0 + cw],
                start=True, stop=True))
            pr.step()
            pr.on("vector", lambda v, c0=c0, cw=cw, rep=rep: v.tensor_copy(
                rep[:, c0:c0 + cw], ps_rep[:, 0:cw]))
            pr.step()
        return rep

    wq = ld.load_bf(wq_h, D, D)
    wk = ld.load_bf(wk_h, D, D)
    wv = ld.load_bf(wv_h, D, D)
    wgam = ld.load_bf(wgam_h, D, 16)
    wf1 = ld.load_bf(wf1_h, D, 64)
    wog1 = ld.load_bf(wog1_h, D, 64)
    wo = ld.load_bf(wo_h, D, D)
    wf2 = pr.sbuf([64, D], BF16)
    wog2 = pr.sbuf([64, D], BF16)
    for dst, hh in ((wf2, wf2_h), (wog2, wog2_h)):
        pr.step()
        pr.dma(pr.stage[0:64, 0:D], hh)
        pr.step()
        pr.on("vector", lambda v, dst=dst: v.tensor_copy(dst[:, :], pr.stage[0:64, 0:D]))
        pr.step()
    nw_rep = make_rep(nw_h, D)

    trilS64 = make_tri(pr, 64, strict=True, lower=True)
    triuI64 = make_tri(pr, 64, strict=False, lower=False)
    triuS64 = make_tri(pr, 64, strict=True, lower=False)

    xt = pr.sbuf([128, D], F32)
    ln_tok = pr.sbuf([128, D], BF16)
    lnT = pr.sbuf([128, 8 * 128], BF16)
    q_tok = pr.sbuf([128, D], F32)
    kn_tok = pr.sbuf([128, D], F32)
    v_tok = pr.sbuf([128, D], BF16)
    a_tok = pr.sbuf([128, D], F32)
    gam16 = pr.sbuf([128, 16], F32)
    f1fm = pr.sbuf([64, 128], BF16)
    Gfm = pr.sbuf([128, 8 * 128], F32)
    gkfm = pr.sbuf([128, 8 * 128], F32)
    eGtot = pr.sbuf([128, 8], F32)
    eGcols = pr.sbuf([64, 16], F32)
    Fq = pr.sbuf([128, D], F32)
    Fk = pr.sbuf([128, D], F32)
    Fc = pr.sbuf([128, D], F32)
    Fs = pr.sbuf([128, D], F32)
    FA = pr.sbuf([128, D], F32)
    qt_p = pr.sbuf([128, 512], BF16)
    kt_p = pr.sbuf([128, 512], BF16)
    kc_p = pr.sbuf([128, 512], BF16)
    Qt_p = pr.sbuf([128, 512], BF16)
    At_p = pr.sbuf([128, 512], BF16)
    at_p = pr.sbuf([128, 512], BF16)
    Kb_p = pr.sbuf([128, 512], BF16)
    GW = 8
    MW = GW * 128
    KW = GW * 64
    qtF = pr.sbuf([64, MW], BF16)
    ktF = pr.sbuf([64, MW], BF16)
    kcF = pr.sbuf([64, MW], BF16)
    QtF = pr.sbuf([64, MW], BF16)
    AtF = pr.sbuf([64, MW], BF16)
    atF = pr.sbuf([64, MW], BF16)
    Nbf = pr.sbuf([128, MW], BF16)
    NTbf = pr.sbuf([128, MW], BF16)
    AqT = pr.sbuf([128, MW], BF16)
    Bs = pr.sbuf([128, MW], BF16)
    BTs = pr.sbuf([128, MW], BF16)
    Xs = pr.sbuf([128, MW], BF16)
    Ttbf = pr.sbuf([128, MW], BF16)
    Ibf8 = pr.sbuf([128, MW], BF16)
    rhsb = pr.sbuf([128, KW], BF16)
    Ubf = pr.sbuf([128, KW], BF16)
    Osb = pr.sbuf([128, D], F32)
    Ssb = [pr.sbuf([64, 16 * 64], F32) for _ in range(2)]
    Sbf = pr.sbuf([64, KW], BF16)
    hsc = pr.sbuf([128, D], F32)
    hnorm = pr.sbuf([128, D], BF16)
    hT = pr.sbuf([128, 8 * 128], BF16)
    yt = pr.sbuf([128, D], F32)
    st4 = pr.sbuf([128, 16], F32)
    ps_bf = pr.psum_bf_pool(4)
    ps_a = pr.psum([128, 512], F32)
    ps_x = pr.psum([128, 1024], F32)
    ps_y = pr.psum([128, 1024], F32)
    lnbufs = None

    def tile8(dst, src_ap):
        def f(v, dst=dst, src_ap=src_ap):
            for i in range(GW):
                v.tensor_copy(dst[:, i * 128:(i + 1) * 128], src_ap)
        pr.on("vector", f)
        pr.step()
    tile8(Ibf8, pr.ident_bf[:, :])

    pr.on("gpsimd", lambda g: g.memset(Ssb[0][:], 0.0))
    pr.step()
    pr.on("gpsimd", lambda g: g.memset(Nbf[:], 0.0))
    pr.step()
    pr.on("gpsimd", lambda g: g.memset(NTbf[:], 0.0))
    pr.step()
    pr.on("gpsimd", lambda g: g.memset(AqT[:], 0.0))
    pr.step()

    for c in range(NT):
        cur, nxt = Ssb[c % 2], Ssb[(c + 1) % 2]
        pr.dma(xt[:, :], x_h[c * 128:(c + 1) * 128, :])
        pr.step()
        lnbufs = emit_ln_transpose_tile(pr, xt[:, :], ln_tok,
                                        lambda j: j * 128, ps_bf, lnbufs, lnT)
        for (wmat, n_out, tag) in ((wq, D, "q"), (wk, D, "k"), (wv, D, "v")):
            for ch in range(n_out // 512):
                def mmp(te, wmat=wmat, ch=ch):
                    for cd in range(8):
                        te.matmul(ps_a[:], lnT[:, cd * 128:(cd + 1) * 128],
                                  wmat[:, cd * D + ch * 512: cd * D + ch * 512 + 512],
                                  start=(cd == 0), stop=(cd == 7))
                pr.on("tensor", mmp)
                pr.step()
                if tag == "q":
                    pr.on("scalar", lambda sc, ch=ch: sc.activation(
                        q_tok[:, ch * 512:(ch + 1) * 512], ps_a[:], AF.Silu))
                elif tag == "k":
                    pr.on("scalar", lambda sc, ch=ch: sc.activation(
                        kn_tok[:, ch * 512:(ch + 1) * 512], ps_a[:], AF.Silu))
                else:
                    pr.on("vector", lambda v, ch=ch: v.tensor_copy(
                        v_tok[:, ch * 512:(ch + 1) * 512], ps_a[:]))
                pr.step()
        pr.on("scalar", lambda sc: sc.activation(hsc[:, :], kn_tok[:, :], AF.Square))
        pr.step()
        pr.on("vector", lambda v: v.reduce_sum(
            st4[:, :].rearrange("p (a o) -> p a o", o=1),
            hsc[:, :].rearrange("p (a n) -> p a n", a=16), axis=AX))
        pr.step()
        pr.on("scalar", lambda sc: sc.activation(st4[:, :], st4[:, :], AF.Sqrt))
        pr.step()
        pr.on("vector", lambda v: v.tensor_scalar(st4[:, :], st4[:, :], 1e-12, None,
                                                  ALU.max))
        pr.step()
        pr.on("vector", lambda v: v.reciprocal(st4[:, :], st4[:, :]))
        pr.step()

        def knn(v):
            for h in range(16):
                v.tensor_scalar(kn_tok[:, h * 64:(h + 1) * 64], kn_tok[:, h * 64:(h + 1) * 64],
                                st4[:, h:h + 1], None, ALU.mult)
        pr.on("vector", knn)
        pr.step()
        def mmg(te):
            for cd in range(8):
                te.matmul(ps_x[:, 0:16], lnT[:, cd * 128:(cd + 1) * 128],
                          wgam[:, cd * 16:(cd + 1) * 16], start=(cd == 0), stop=(cd == 7))
            for cd in range(8):
                te.matmul(ps_x[:, 16:80], lnT[:, cd * 128:(cd + 1) * 128],
                          wf1[:, cd * 64:(cd + 1) * 64], start=(cd == 0), stop=(cd == 7))
            for cd in range(8):
                te.matmul(ps_x[:, 80:144], lnT[:, cd * 128:(cd + 1) * 128],
                          wog1[:, cd * 64:(cd + 1) * 64], start=(cd == 0), stop=(cd == 7))
        pr.on("tensor", mmg)
        pr.step()
        pr.on("scalar", lambda sc: sc.activation(gam16[:, :], ps_x[:, 0:16], AF.Sigmoid))
        pr.step()
        pr.on("vector", lambda v: v.tensor_scalar(gam16[:, :], gam16[:, :], -1.0, None,
                                                  ALU.mult))
        pr.step()
        def evf(v):
            v.tensor_copy(hsc[:, 0:64], ps_x[:, 16:80])
            v.tensor_copy(hsc[:, 64:128], ps_x[:, 80:144])
        pr.on("vector", evf)
        pr.step()
        pr.on("vector", lambda v: v.tensor_copy(ln_tok[:, 0:128], hsc[:, 0:128]))
        pr.step()

        def trf(te):
            te.transpose(ps_bf[0], ln_tok[:, 0:128], pr.ident_bf[:])
        pr.on("tensor", trf)
        pr.step()
        pr.on("vector", lambda v: v.tensor_copy(f1fm[:, :], ps_bf[0][0:64, :]))
        pr.step()
        ogfm = ps_bf[0][64:128, :]
        og1fm = pr.stage_bf

        def cpog(v):
            v.tensor_copy(og1fm[0:64, 0:128], ps_bf[0][64:128, :])
        pr.on("vector", cpog)
        pr.step()
        for i in range(8):
            def mmgk(te, i=i):
                te.matmul(ps_x[:, i * 128:(i + 1) * 128], wf2[:, i * 128:(i + 1) * 128],
                          f1fm[:, :], start=True, stop=True)
            pr.on("tensor", mmgk)
            pr.step()
        def lsig1(v):
            v.tensor_copy(gkfm[:, :], ps_x[:, :])
        pr.on("vector", lsig1)
        pr.step()
        pr.on("scalar", lambda sc: sc.activation(Gfm[:, :], gkfm[:, :], AF.Abs))
        pr.step()
        pr.on("scalar", lambda sc: sc.activation(Gfm[:, :], Gfm[:, :], AF.Exp, scale=-1.0))
        pr.step()
        pr.on("scalar", lambda sc: sc.activation(Gfm[:, :], Gfm[:, :], AF.Ln,
                                                 bias=onecol[:, 0:1]))
        pr.step()

        def lsig2(v):
            v.tensor_scalar(gkfm[:, :], gkfm[:, :], 0.0, None, ALU.min)
            v.tensor_tensor(gkfm[:, :], gkfm[:, :], Gfm[:, :], ALU.subtract)
        pr.on("vector", lsig2)
        pr.step()
        def trfield(dstT, negate):
            pr.on("scalar", lambda sc: sc.activation(hsc[:, :], gkfm[:, :], AF.Exp,
                                                     scale=(-1.0 if negate else 1.0)))
            pr.step()

            def trx(te):
                for j in range(8):
                    te.transpose(ps_x[:, j * 128:(j + 1) * 128],
                                 hsc[:, j * 128:(j + 1) * 128], pr.ident[:])
            pr.on("tensor", trx)
            pr.step()
            pr.on("vector", lambda v, dstT=dstT: v.tensor_copy(dstT[:, :], ps_x[:, :]))
            pr.step()
        trfield(FA, False)

        def mka(v):
            for h in range(16):
                v.tensor_scalar(a_tok[:, h * 64:(h + 1) * 64],
                                kn_tok[:, h * 64:(h + 1) * 64],
                                gam16[:, h:h + 1], None, ALU.mult)
            v.tensor_tensor(a_tok[:, :], a_tok[:, :], FA[:, :], ALU.mult)
        pr.on("vector", mka)
        pr.step()
        def scn(v):
            for i in range(8):
                v.tensor_tensor_scan(Gfm[:, i * 128:(i + 1) * 128],
                                     gkfm[:, i * 128:(i + 1) * 128],
                                     gkfm[:, i * 128:(i + 1) * 128], 0.0,
                                     ALU.add, ALU.bypass)
        pr.on("vector", scn)
        pr.step()
        def ffq(v):
            for i in range(8):
                sl = slice(i * 128, i * 128 + 64)
                sr = slice(i * 128 + 64, (i + 1) * 128)
                v.tensor_copy(gkfm[:, sl], Gfm[:, sl])
                v.tensor_scalar(gkfm[:, sr], Gfm[:, sr], Gfm[:, i * 128 + 63:i * 128 + 64],
                                None, ALU.subtract)
        pr.on("vector", ffq)
        pr.step()

        trfield(Fq, False)
        trfield(Fk, True)
        def ffa(v):
            v.tensor_copy(gkfm[:, :], Gfm[:, :])
        pr.on("vector", ffa)
        pr.step()
        trfield(FA, False)
        def ffc(v):
            for i in range(8):
                v.tensor_scalar(gkfm[:, i * 128:(i + 1) * 128],
                                Gfm[:, i * 128:(i + 1) * 128],
                                Gfm[:, i * 128 + 63:i * 128 + 64], None, ALU.subtract)
        pr.on("vector", ffc)
        pr.step()
        trfield(Fc, True)
        def ffs(v):
            for i in range(8):
                v.tensor_scalar(gkfm[:, i * 128:(i + 1) * 128],
                                Gfm[:, i * 128:(i + 1) * 128],
                                Gfm[:, i * 128 + 127:i * 128 + 128], None, ALU.subtract)
        pr.on("vector", ffs)
        pr.step()
        trfield(Fs, True)
        def egt(v):
            for i in range(8):
                v.tensor_copy(eGtot[:, i:i + 1], Gfm[:, i * 128 + 127:i * 128 + 128])
        pr.on("vector", egt)
        pr.step()
        pr.on("scalar", lambda sc: sc.activation(eGtot[:, :], eGtot[:, :], AF.Exp))
        pr.step()

        def egc(v):
            for h in range(16):
                ti, trow = h // 2, (h % 2) * 64
                v.tensor_copy(eGcols[:, h:h + 1], eGtot[trow:trow + 64, ti:ti + 1])
        pr.on("vector", egc)
        pr.step()
        for g8 in range(2):
            H0 = g8 * 8
            gsl = slice(H0 * 64, (H0 + 8) * 64)

            def prod(v, gsl=gsl):
                v.tensor_tensor(qt_p[:, :], q_tok[:, gsl], Fq[:, gsl], ALU.mult)
                v.tensor_tensor(kt_p[:, :], kn_tok[:, gsl], Fk[:, gsl], ALU.mult)
                v.tensor_tensor(kc_p[:, :], kn_tok[:, gsl], Fc[:, gsl], ALU.mult)
                v.tensor_tensor(Qt_p[:, :], q_tok[:, gsl], FA[:, gsl], ALU.mult)
                v.tensor_tensor(At_p[:, :], a_tok[:, gsl], FA[:, gsl], ALU.mult)
                v.tensor_tensor(at_p[:, :], a_tok[:, gsl], Fq[:, gsl], ALU.mult)
                v.tensor_tensor(Kb_p[:, :], kn_tok[:, gsl], Fs[:, gsl], ALU.mult)
            pr.on("vector", prod)
            pr.step()
            for fld_src, fld_dst in ((qt_p, qtF), (kt_p, ktF), (kc_p, kcF),
                                     (Qt_p, QtF), (At_p, AtF), (at_p, atF)):
                for rnd in range(2):
                    def trp(te, fld_src=fld_src, rnd=rnd):
                        for j in range(4):
                            hi = rnd * 4 + j
                            te.transpose(ps_bf[j][0:64, 0:128],
                                         fld_src[:, hi * 64:(hi + 1) * 64],
                                         pr.ident_bf[:])
                    pr.on("tensor", trp)
                    pr.step()

                    def evp(v, fld_dst=fld_dst, rnd=rnd):
                        for j in range(4):
                            i = rnd * 4 + j
                            v.tensor_copy(fld_dst[:, i * 128:(i + 1) * 128],
                                          ps_bf[j][0:64, :])
                    pr.on("vector", evp)
                    pr.step()
            def mmn(te):
                for i in range(GW):
                    b = i * 128
                    a1 = atF[:, b:b + 64]
                    a2 = atF[:, b + 64:b + 128]
                    k1 = ktF[:, b:b + 64]
                    k2 = ktF[:, b + 64:b + 128]
                    kc1 = kcF[:, b:b + 64]
                    q1 = qtF[:, b:b + 64]
                    q2 = qtF[:, b + 64:b + 128]
                    te.matmul(ps_x[0:64, b:b + 64], a1, k1, start=True, stop=True)
                    te.matmul(ps_x[64:128, b + 64:b + 128], a2, k2, start=True, stop=True)
                    te.matmul(ps_x[64:128, b:b + 64], a2, kc1, start=True, stop=True)
                    te.matmul(ps_y[0:64, b:b + 64], k1, q1, start=True, stop=True)
                    te.matmul(ps_y[64:128, b + 64:b + 128], k2, q2, start=True, stop=True)
                    te.matmul(ps_y[0:64, b + 64:b + 128], kc1, q2, start=True, stop=True)
            pr.on("tensor", mmn)
            pr.step()

            def evn(v):
                for i in range(GW):
                    b = i * 128
                    v.tensor_tensor(Nbf[0:64, b:b + 64], ps_x[0:64, b:b + 64],
                                    trilS64[:, :], ALU.mult)
                    v.tensor_tensor(Nbf[64:128, b + 64:b + 128],
                                    ps_x[64:128, b + 64:b + 128], trilS64[:, :], ALU.mult)
                    v.tensor_copy(Nbf[64:128, b:b + 64], ps_x[64:128, b:b + 64])
                    v.tensor_tensor(AqT[0:64, b:b + 64], ps_y[0:64, b:b + 64],
                                    triuI64[:, :], ALU.mult)
                    v.tensor_tensor(AqT[64:128, b + 64:b + 128],
                                    ps_y[64:128, b + 64:b + 128], triuI64[:, :], ALU.mult)
                    v.tensor_copy(AqT[0:64, b + 64:b + 128], ps_y[0:64, b + 64:b + 128])
            pr.on("vector", evn)
            pr.step()
            def mmnt(te):
                for i in range(GW):
                    b = i * 128
                    a1 = atF[:, b:b + 64]
                    a2 = atF[:, b + 64:b + 128]
                    k1 = ktF[:, b:b + 64]
                    k2 = ktF[:, b + 64:b + 128]
                    kc1 = kcF[:, b:b + 64]
                    te.matmul(ps_x[0:64, b:b + 64], k1, a1, start=True, stop=True)
                    te.matmul(ps_x[64:128, b + 64:b + 128], k2, a2, start=True, stop=True)
                    te.matmul(ps_x[0:64, b + 64:b + 128], kc1, a2, start=True, stop=True)
            pr.on("tensor", mmnt)
            pr.step()

            def evnt(v):
                for i in range(GW):
                    b = i * 128
                    v.tensor_tensor(NTbf[0:64, b:b + 64], ps_x[0:64, b:b + 64],
                                    triuS64[:, :], ALU.mult)
                    v.tensor_tensor(NTbf[64:128, b + 64:b + 128],
                                    ps_x[64:128, b + 64:b + 128], triuS64[:, :], ALU.mult)
                    v.tensor_copy(NTbf[0:64, b + 64:b + 128], ps_x[0:64, b + 64:b + 128])
            pr.on("vector", evnt)
            pr.step()
            def tini(v):
                v.tensor_tensor(Ttbf[:, :], Ibf8[:, :], Nbf[:, :], ALU.add)
                v.tensor_tensor(Xs[:, :], Ibf8[:, :], NTbf[:, :], ALU.add)
                v.tensor_copy(Bs[:, :], Nbf[:, :])
                v.tensor_copy(BTs[:, :], NTbf[:, :])
            pr.on("vector", tini)
            pr.step()
            for lvl in range(6):
                def msq(te):
                    for i in range(GW):
                        sl = slice(i * 128, (i + 1) * 128)
                        te.matmul(ps_x[:, sl], BTs[:, sl], Bs[:, sl], start=True, stop=True)
                        te.matmul(ps_y[:, sl], Bs[:, sl], BTs[:, sl], start=True, stop=True)
                pr.on("tensor", msq)
                pr.step()

                def esq(v):
                    v.tensor_copy(Bs[:, :], ps_x[:, :])
                    v.tensor_copy(BTs[:, :], ps_y[:, :])
                    v.tensor_tensor(Bs[:, :], Bs[:, :], Ibf8[:, :], ALU.add)
                    v.tensor_tensor(BTs[:, :], BTs[:, :], Ibf8[:, :], ALU.add)
                pr.on("vector", esq)
                pr.step()

                def mpr(te):
                    for i in range(GW):
                        sl = slice(i * 128, (i + 1) * 128)
                        te.matmul(ps_x[:, sl], Xs[:, sl], Bs[:, sl], start=True, stop=True)
                        te.matmul(ps_y[:, sl], Bs[:, sl], Xs[:, sl], start=True, stop=True)
                pr.on("tensor", mpr)
                pr.step()

                def epr(v, last=(lvl == 5)):
                    v.tensor_copy(Ttbf[:, :], ps_x[:, :])
                    v.tensor_copy(Xs[:, :], ps_y[:, :])
                    if not last:
                        v.tensor_tensor(Bs[:, :], Bs[:, :], Ibf8[:, :], ALU.subtract)
                        v.tensor_tensor(BTs[:, :], BTs[:, :], Ibf8[:, :], ALU.subtract)
                pr.on("vector", epr)
                pr.step()
            pr.on("vector", lambda v: v.tensor_copy(Ttbf[:, :], Xs[:, :]))
            pr.step()

            pr.on("vector", lambda v, H0=H0, cur=cur: v.tensor_copy(
                Sbf[:, :], cur[:, H0 * 64:(H0 + GW) * 64]))
            pr.step()

            def mmr(te):
                for i in range(GW):
                    te.matmul(ps_x[:, i * 64:(i + 1) * 64], AtF[:, i * 128:(i + 1) * 128],
                              Sbf[:, i * 64:(i + 1) * 64], start=True, stop=True)
            pr.on("tensor", mmr)
            pr.step()

            def evr(v, H0=H0):
                v.tensor_tensor(rhsb[:, :], v_tok[:, H0 * 64:(H0 + GW) * 64],
                                ps_x[:, 0:KW], ALU.add)
            pr.on("vector", evr)
            pr.step()

            def mmu(te):
                for i in range(GW):
                    te.matmul(ps_x[:, i * 64:(i + 1) * 64], Ttbf[:, i * 128:(i + 1) * 128],
                              rhsb[:, i * 64:(i + 1) * 64], start=True, stop=True)
            pr.on("tensor", mmu)
            pr.step()
            pr.on("vector", lambda v: v.tensor_copy(Ubf[:, :], ps_x[:, 0:KW]))
            pr.step()
            def mmo(te):
                for i in range(GW):
                    te.matmul(ps_y[:, i * 64:(i + 1) * 64], QtF[:, i * 128:(i + 1) * 128],
                              Sbf[:, i * 64:(i + 1) * 64], start=True, stop=False)
                    te.matmul(ps_y[:, i * 64:(i + 1) * 64], AqT[:, i * 128:(i + 1) * 128],
                              Ubf[:, i * 64:(i + 1) * 64], start=False, stop=True)
            pr.on("tensor", mmo)
            pr.step()

            def mmd(te):
                for i in range(GW):
                    te.matmul(ps_x[0:64, 512 + i * 64:512 + (i + 1) * 64],
                              Kb_p[:, i * 64:(i + 1) * 64],
                              Ubf[:, i * 64:(i + 1) * 64], start=True, stop=True)
            pr.on("tensor", mmd)
            pr.step()

            def evo(v, H0=H0, cur=cur, nxt=nxt):
                v.tensor_copy(Osb[:, H0 * 64:(H0 + GW) * 64], ps_y[:, 0:KW])
                for i in range(GW):
                    h = H0 + i
                    v.scalar_tensor_tensor(nxt[:, h * 64:(h + 1) * 64],
                                           cur[:, h * 64:(h + 1) * 64],
                                           eGcols[:, h:h + 1],
                                           ps_x[0:64, 512 + i * 64:512 + (i + 1) * 64],
                                           ALU.mult, ALU.add)
            pr.on("vector", evo)
            pr.step()

        for ch in range(2):
            def mmog(te, ch=ch):
                te.matmul(ps_a[:], og1fm[0:64, 0:128],
                          wog2[:, ch * 512:(ch + 1) * 512], start=True, stop=True)
            pr.on("tensor", mmog)
            pr.step()
            pr.on("scalar", lambda sc, ch=ch: sc.activation(
                hsc[:, ch * 512:(ch + 1) * 512], ps_a[:], AF.Sigmoid))
            pr.step()
        pr.on("vector", lambda v: v.tensor_tensor(hsc[:, :], Osb[:, :], hsc[:, :],
                                                  ALU.mult))
        pr.step()
        def lna(v):
            v.reduce_sum(st4[:, 0:1], hsc[:, :], axis=AX)
        pr.on("vector", lna)
        pr.on("scalar", lambda sc: sc.activation(yt[:, :], hsc[:, :], AF.Square))
        pr.step()

        def lnb(v):
            v.reduce_sum(st4[:, 1:2], yt[:, :], axis=AX)
            v.tensor_scalar(st4[:, 0:1], st4[:, 0:1], 1.0 / D, None, ALU.mult)
            v.tensor_scalar(st4[:, 1:2], st4[:, 1:2], 1.0 / D, None, ALU.mult)
            v.tensor_tensor(st4[:, 2:3], st4[:, 0:1], st4[:, 0:1], ALU.mult)
            v.tensor_tensor(st4[:, 1:2], st4[:, 1:2], st4[:, 2:3], ALU.subtract)
            v.tensor_scalar(st4[:, 1:2], st4[:, 1:2], 1e-5, None, ALU.add)
        pr.on("vector", lnb)
        pr.step()
        pr.on("scalar", lambda sc: sc.activation(st4[:, 1:2], st4[:, 1:2], AF.Sqrt))
        pr.step()
        pr.on("vector", lambda v: v.reciprocal(st4[:, 2:3], st4[:, 1:2]))
        pr.step()

        def nrm2(v):
            v.tensor_scalar(hsc[:, :], hsc[:, :], st4[:, 0:1], st4[:, 2:3],
                            ALU.subtract, ALU.mult)
            v.tensor_tensor(hnorm[:, :], hsc[:, :], nw_rep[:, :], ALU.mult)
        pr.on("vector", nrm2)
        pr.step()
        for j0 in (0, 4):
            def trh(te, j0=j0):
                for j in range(4):
                    te.transpose(ps_bf[j], hnorm[:, (j0 + j) * 128:(j0 + j + 1) * 128],
                                 pr.ident_bf[:])
            pr.on("tensor", trh)
            pr.step()

            def evh(v, j0=j0):
                for j in range(4):
                    v.tensor_copy(hT[:, (j0 + j) * 128:(j0 + j + 1) * 128], ps_bf[j])
            pr.on("vector", evh)
            pr.step()
        for ch in range(2):
            def mmw(te, ch=ch):
                for cd in range(8):
                    te.matmul(ps_a[:], hT[:, cd * 128:(cd + 1) * 128],
                              wo[:, cd * D + ch * 512: cd * D + ch * 512 + 512],
                              start=(cd == 0), stop=(cd == 7))
            pr.on("tensor", mmw)
            pr.step()
            pr.on("vector", lambda v, ch=ch: v.tensor_tensor(
                yt[:, ch * 512:(ch + 1) * 512], ps_a[:], xt[:, ch * 512:(ch + 1) * 512],
                ALU.add))
            pr.step()
        pr.dma(y_h[c * 128:(c + 1) * 128, :], yt[:, :])
        pr.step()
    if standalone:
        return pr.build()
    return


def host_prep_l1a(inp):
    return {
        "wq": inp["L1_attn_Wq"], "wk": inp["L1_attn_Wk"], "wv": inp["L1_attn_Wv"],
        "wgam": inp["L1_attn_Wgamma"], "wf1": inp["L1_attn_Wf1"], "wf2": inp["L1_attn_Wf2"],
        "wog1": inp["L1_attn_Wog1"], "wog2": inp["L1_attn_Wog2"],
        "nw_row": np.ascontiguousarray(inp["L1_attn_norm_w"].reshape(1, -1)),
        "wo": inp["L1_attn_Wo"],
    }


_PROGS = {}

_BLOCKS = [
    ("l0a", build_l0a, host_prep_l0a),
    ("l0f", build_l0f, host_prep_l0f),
    ("l1a", build_l1a, host_prep_l1a),
    ("l1f", build_l1f, lambda inp: {
        "W_rT": np.ascontiguousarray(inp["L1_ffn_W_r"].T),
        "W_kT": np.ascontiguousarray(inp["L1_ffn_W_k"].T),
        "W_vT": np.ascontiguousarray(inp["L1_ffn_W_v"].T),
        "mu_r_cols": np.ascontiguousarray(inp["L1_ffn_mu_r"].reshape(8, 128).T),
        "mu_k_cols": np.ascontiguousarray(inp["L1_ffn_mu_k"].reshape(8, 128).T),
    }),
    ("l2a", build_l2a, host_prep_l2a),
    ("l2f", build_l2f, host_prep_l2f),
    ("l3a", build_l3a, host_prep_l3a),
    ("l3f", build_l3f, host_prep_l3f),
]


def build_fused():
    pr = Prog()
    pr.init_consts()
    x0 = pr.dram_in("x", [S, D])
    xs = [x0] + [pr.dram_tmp("xt%d" % i, [S, D]) for i in range(len(_BLOCKS) - 1)]
    y = pr.dram_out("y", [S, D])
    xs.append(y)
    for bi, (name, build_fn, _) in enumerate(_BLOCKS):
        m = pr.mark()
        build_fn(pr=pr, px=name + "_", x_h=xs[bi], y_h=xs[bi + 1])
        pr.free_to(m)
    return pr.build()


def _weight_map(inputs):
    wmap = {}
    for name, _, prep_fn in _BLOCKS:
        for k, v in prep_fn(inputs).items():
            wmap[name + "_" + k] = v
    return wmap


def kernel(**inputs):
    x = np.ascontiguousarray(np.asarray(inputs["L0_attn_hidden_states"], np.float32))
    B = x.shape[0]
    assert B == NCORES
    if "fused" not in _PROGS:
        _PROGS["fused"] = build_fused()
    nc = _PROGS["fused"]
    wmap = _weight_map(inputs)
    in_maps = [{"x": np.ascontiguousarray(x[b]), **wmap} for b in range(B)]
    res = run_bass_kernel_spmd(nc, in_maps, core_ids=list(range(B)))
    return np.stack([np.asarray(res.results[b]["y"], np.float32) for b in range(B)])


# version 06 (final)
# speedup vs baseline: 1.4680x; 1.0078x over previous
import math
from contextlib import ExitStack

import numpy as np
import concourse.bass as bass
import concourse.mybir as mybir
from concourse.bass_utils import run_bass_kernel_spmd

F32 = mybir.dt.float32
BF16 = mybir.dt.bfloat16
AF = mybir.ActivationFunctionType
ALU = mybir.AluOpType
AX = mybir.AxisListType.X

NCORES = 8
S = 4096
D = 1024
NT = S // 128
MAGIC = 1.5 * 2 ** 23
TWO_PI = 2.0 * math.pi


class Prog:

    ENGINES = ("sync", "tensor", "vector", "scalar", "gpsimd")

    def __init__(self):
        self.nc = bass.Bass("TRN2", target_bir_lowering=False)
        self.es = ExitStack()
        self._cms = {}
        self._order = []
        self.steps = [[{e: [] for e in self.ENGINES}, self.ENGINES]]
        self._built = False
        self.ident = None
        self.ident_bf = None
        self.ones1 = None

    def sbuf(self, shape, dtype=F32):
        cm = self.nc.sbuf_tensor(shape, dtype)
        h = cm.__enter__()
        self._cms[id(h)] = cm
        self._order.append(id(h))
        return h

    def psum(self, shape, dtype=F32):
        cm = self.nc.psum_tensor(shape, dtype)
        h = cm.__enter__()
        self._cms[id(h)] = cm
        self._order.append(id(h))
        return h

    def free(self, h):
        cm = self._cms.pop(id(h))
        self._order.remove(id(h))
        cm.__exit__(None, None, None)

    def dram_in(self, name, shape, dtype=F32):
        return self.nc.dram_tensor(name, list(shape), dtype, kind="ExternalInput").ap()

    def dram_out(self, name, shape, dtype=F32):
        return self.nc.dram_tensor(name, list(shape), dtype, kind="ExternalOutput").ap()

    def dram_tmp(self, name, shape, dtype=F32):
        return self.nc.dram_tensor(name, list(shape), dtype, kind="Internal").ap()

    def step(self, engines=None):
        ops = self.steps[-1][0]
        used = {e for e in self.ENGINES if ops[e]}
        self.steps[-1][1] = tuple(e for e in self.ENGINES if e in used) or ("sync",)
        self.steps.append([{e: [] for e in self.ENGINES}, self.ENGINES])

    def on(self, engine, fn):
        self.steps[-1][0][engine].append(("op", fn))

    def dma(self, out, in_, engine="sync", **kw):
        self.steps[-1][0][engine].append(("dma", (out, in_, kw)))

    def psum_bf_pool(self, n=4):
        big = self.psum([128, 128 * n], BF16)
        return [big[:, j * 128:(j + 1) * 128] for j in range(n)]

    def constcol(self, value, dtype=F32):
        t = self.sbuf([128, 1], dtype)
        self.on("gpsimd", lambda g: g.memset(t[:], value))
        return t

    def init_consts(self):
        if self.ident is not None:
            return
        self.ident = make_identity(self, F32)
        self.ident_bf = make_identity(self, BF16)
        ones1 = self.sbuf([1, 1], F32)
        self.on("gpsimd", lambda g: g.memset(ones1[:], 1.0))
        self.ones1 = ones1
        self.stage = self.sbuf([128, 2048], F32)
        self.stage_bf = self.sbuf([128, 1024], BF16)
        self.step()

    def mark(self):
        return len(self._order)

    def free_to(self, m):
        for hid in list(reversed(self._order[m:])):
            cm = self._cms.pop(hid)
            self._order.remove(hid)
            cm.__exit__(None, None, None)

    def build(self):
        assert not self._built
        self._built = True
        ops = self.steps[-1][0]
        used = {e for e in self.ENGINES if ops[e]}
        self.steps[-1][1] = tuple(e for e in self.ENGINES if e in used) or ("sync",)
        nc = self.nc
        bar = self.es.enter_context(nc.semaphore("bar"))
        dsem = {e: self.es.enter_context(nc.semaphore("d_" + e)) for e in self.ENGINES}
        dtot = {e: 0 for e in self.ENGINES}
        block = self.es.enter_context(nc.Block())
        steps = self.steps
        nsteps = len(steps)

        ssem = {e: self.es.enter_context(nc.semaphore("s_" + e)) for e in self.ENGINES}
        _PASS = {"wait_ge", "sem_inc", "wait_op", "maybe_wait_ge", "nop", "drain"}

        class _Rec:

            def __init__(self, eng, sem, is_pe):
                self._eng = eng
                self._sem = sem
                self._pe = is_pe
                self.cnt = 0
                self.last = None

            def __getattr__(self, name):
                a = getattr(self._eng, name)
                if callable(a) and name not in _PASS:
                    def w(*args, **kw):
                        if not self._pe and self.cnt > 0:
                            self._eng.wait_ge(self._sem, self.cnt)
                        r = a(*args, **kw)
                        if isinstance(r, bass.BassInstruction):
                            if self._pe:
                                self.last = r
                            else:
                                r.then_inc(self._sem, 1)
                                self.cnt += 1
                        return r
                    return w
                return a

        cum = [0]
        for ops, engs in steps:
            cum.append(cum[-1] + len(engs))

        def mk(ename):
            is_pe = ename == "tensor"

            def body(eng):
                rec = _Rec(eng, ssem[ename], is_pe)
                for k in range(nsteps):
                    ops, engs = steps[k]
                    if ename not in engs:
                        continue
                    if k > 0:
                        eng.wait_ge(bar, cum[k])
                    n0 = rec.cnt
                    rec.last = None
                    had_dma = False
                    for kind, payload in ops[ename]:
                        if kind == "op":
                            payload(rec)
                        else:
                            out, in_, kw = payload
                            eng.dma_start(out=out, in_=in_, **kw).then_inc(dsem[ename], 16)
                            dtot[ename] += 16
                            had_dma = True
                    if is_pe and rec.last is not None:
                        rec.last.then_inc(ssem[ename], 1)
                        rec.cnt += 1
                    if had_dma:
                        eng.wait_ge(dsem[ename], dtot[ename])
                    if rec.cnt > n0:
                        eng.wait_ge(ssem[ename], rec.cnt)
                    eng.sem_inc(bar, 1)
            return body

        for ename in self.ENGINES:
            getattr(block, ename)(mk(ename))
        for hid in reversed(self._order):
            self._cms.pop(hid).__exit__(None, None, None)
        self._order.clear()
        self.es.close()
        return nc


def make_identity(pr, dtype=F32, n=128):
    t = pr.sbuf([n, n], dtype)
    pr.on("gpsimd", lambda g: g.memset(t[:], 0.0))
    pr.step()
    pr.on("gpsimd", lambda g: g.affine_select(out=t[:], in_=t[:], compare_op=ALU.not_equal,
                                              fill=1.0, base=0, pattern=[[-1, n]],
                                              channel_multiplier=1))
    pr.step()
    return t


def make_tri(pr, n=128, strict=True, lower=True, dtype=F32):
    t = pr.sbuf([n, n], dtype)
    pr.on("gpsimd", lambda g: g.memset(t[:], 1.0))
    pr.step()
    pr.on("gpsimd", lambda g: g.affine_select(out=t[:], in_=t[:], compare_op=ALU.is_gt,
                                              fill=0.0, base=0, pattern=[[-1, n]],
                                              channel_multiplier=1))
    pr.step()
    if strict and lower:
        return t
    if not strict and lower:
        pr.on("vector", lambda v: v.tensor_tensor(t[:], t[:], pr.ident[0:n, 0:n], ALU.add))
        pr.step()
        return t
    if not strict and not lower:
        pr.on("vector", lambda v: v.tensor_scalar(t[:], t[:], -1.0, 1.0, ALU.mult, ALU.add))
        pr.step()
        return t
    def f(v):
        v.tensor_scalar(t[:], t[:], -1.0, 1.0, ALU.mult, ALU.add)
        v.tensor_tensor(t[:], t[:], pr.ident[0:n, 0:n], ALU.subtract)
    pr.on("vector", f)
    pr.step()
    return t


class Loader:

    def __init__(self, pr):
        self.pr = pr
        self.stage = pr.stage

    def close(self):
        pass

    def load_bf(self, hbm, rows, cols, out=None, out_off=0, stride=None):
        pr = self.pr
        assert rows % 128 == 0
        if stride is None:
            stride = cols
        if out is None:
            out = pr.sbuf([128, out_off + (rows // 128) * stride], BF16)
        for i in range(rows // 128):
            for c0 in range(0, cols, 2048):
                cw = min(2048, cols - c0)
                pr.step()
                pr.dma(self.stage[:, 0:cw], hbm[i * 128:(i + 1) * 128, c0:c0 + cw])
                pr.step()
                pr.on("vector", lambda v, i=i, c0=c0, cw=cw, out=out:
                      v.tensor_copy(out[:, out_off + i * stride + c0:
                                        out_off + i * stride + c0 + cw],
                                    self.stage[:, 0:cw]))
        pr.step()
        return out

    def load_f32(self, hbm, rows, cols, out=None):
        pr = self.pr
        if out is None:
            out = pr.sbuf([max(rows, 1), cols], F32)
        pr.step()
        pr.dma(out[0:rows, 0:cols], hbm)
        pr.step()
        return out


def emit_ln_stats(pr, x_tile_fn, ntiles, d=D, eps=1e-5, bufs=None):
    if bufs is None:
        bufs = (pr.sbuf([128, ntiles], F32), pr.sbuf([128, ntiles], F32),
                pr.sbuf([128, ntiles], F32), pr.sbuf([128, ntiles], F32),
                pr.sbuf([128, d], F32))
    ssum, ssq, m, rstd, sq = bufs
    for i in range(ntiles):
        pr.on("vector", lambda v, i=i: v.reduce_sum(ssum[:, i:i + 1], x_tile_fn(i), axis=AX))
        pr.on("scalar", lambda sc, i=i: sc.activation(sq[:], x_tile_fn(i), AF.Square))
        pr.step()
        pr.on("vector", lambda v, i=i: v.reduce_sum(ssq[:, i:i + 1], sq[:], axis=AX))
        pr.step()

    def stats(v):
        inv = 1.0 / d
        v.tensor_scalar(m[:], ssum[:], inv, None, ALU.mult)
        v.tensor_scalar(ssq[:], ssq[:], inv, None, ALU.mult)
        v.tensor_tensor(ssum[:], m[:], m[:], ALU.mult)
        v.tensor_tensor(ssq[:], ssq[:], ssum[:], ALU.subtract)
        v.tensor_scalar(ssq[:], ssq[:], eps, None, ALU.add)
    pr.on("vector", stats)
    pr.step()
    pr.on("scalar", lambda sc: sc.activation(ssq[:], ssq[:], AF.Sqrt))
    pr.step()
    pr.on("vector", lambda v: v.reciprocal(rstd[:], ssq[:]))
    pr.step()
    return m, rstd, bufs


def build_l1f(pr=None, px='', x_h=None, y_h=None):
    standalone = pr is None
    if standalone:
        pr = Prog()
    if x_h is None:
        x_h = pr.dram_in(px + "x", [S, D])
    wr_h = pr.dram_in(px + "W_rT", [D, D])
    wk_h = pr.dram_in(px + "W_kT", [D, D])
    wv_h = pr.dram_in(px + "W_vT", [D, D])
    mur_h = pr.dram_in(px + "mu_r_cols", [128, 8])
    muk_h = pr.dram_in(px + "mu_k_cols", [128, 8])
    if y_h is None:
        y_h = pr.dram_out(px + "y", [S, D])
    pr.init_consts()

    ld = Loader(pr)
    wrT = ld.load_bf(wr_h, D, D)
    wkT = ld.load_bf(wk_h, D, D)
    wvT = ld.load_bf(wv_h, D, D)
    mur = ld.load_f32(mur_h, 128, 8)
    muk = ld.load_f32(muk_h, 128, 8)
    ld.close()
    pr.step()
    mur1 = pr.sbuf([128, 8], F32)
    muk1 = pr.sbuf([128, 8], F32)
    pr.on("vector", lambda v: v.tensor_scalar(mur1[:], mur[:], -1.0, 1.0, ALU.mult, ALU.add))
    pr.on("vector", lambda v: v.tensor_scalar(muk1[:], muk[:], -1.0, 1.0, ALU.mult, ALU.add))
    pr.step()

    TSL = 1024
    NSL = S // TSL
    NTT = TSL // 128
    TW = TSL + 1

    x_sl = pr.sbuf([128, NTT * D], F32)
    ln_tok = pr.sbuf([128, D], BF16)
    lnT = pr.sbuf([128, 8 * TW], BF16)
    halo = pr.sbuf([128, 8], BF16)
    xrT = pr.sbuf([128, 8 * TSL], BF16)
    xkT = pr.sbuf([128, 8 * TSL], BF16)
    r_T = pr.sbuf([128, 8 * TSL], BF16)
    kact_T = pr.sbuf([128, 8 * TSL], BF16)
    out_T = pr.sbuf([128, 8 * TSL], BF16)
    ps_bf = pr.psum_bf_pool(4)
    ps_a = pr.psum([128, 512], F32)
    ps_b = pr.psum([128, 512], F32)
    relu_tmp = pr.sbuf([128, 512], F32)
    ytile = pr.sbuf([128, D], F32)
    lnbufs = None
    pr.on("gpsimd", lambda g: g.memset(halo[:], 0.0))
    pr.step()

    for sl in range(NSL):
        for i in range(NTT):
            pr.dma(x_sl[:, i * D:(i + 1) * D],
                   x_h[(sl * NTT + i) * 128:(sl * NTT + i + 1) * 128, :])
        pr.step()
        m, rstd, lnbufs = emit_ln_stats(pr, lambda i: x_sl[:, i * D:(i + 1) * D], NTT,
                                        bufs=lnbufs)
        for i in range(NTT):
            pr.on("vector", lambda v, i=i: v.tensor_scalar(
                ln_tok[:], x_sl[:, i * D:(i + 1) * D], m[:, i:i + 1], rstd[:, i:i + 1],
                ALU.subtract, ALU.mult))
            pr.step()
            for j0 in range(0, 8, 4):
                def tr(te, j0=j0):
                    for j in range(4):
                        te.transpose(ps_bf[j][:], ln_tok[:, (j0 + j) * 128:(j0 + j + 1) * 128],
                                     pr.ident_bf[:])
                pr.on("tensor", tr)
                pr.step()

                def ev(v, i=i, j0=j0):
                    for j in range(4):
                        jj = j0 + j
                        v.tensor_copy(lnT[:, jj * TW + 1 + i * 128: jj * TW + 1 + (i + 1) * 128],
                                      ps_bf[j][:])
                pr.on("vector", ev)
                pr.step()
        def hal(v, sl=sl):
            for j in range(8):
                v.tensor_copy(lnT[:, j * TW:j * TW + 1], halo[:, j:j + 1])
        pr.on("vector", hal)
        pr.step()

        def sav(v):
            for j in range(8):
                v.tensor_copy(halo[:, j:j + 1], lnT[:, j * TW + TW - 1:j * TW + TW])
        pr.on("vector", sav)
        pr.step()

        def mix(v):
            for j in range(8):
                cur = lnT[:, j * TW + 1: j * TW + 1 + TSL]
                prv = lnT[:, j * TW: j * TW + TSL]
                sl_o = slice(j * TSL, (j + 1) * TSL)
                v.tensor_scalar(xrT[:, sl_o], prv, mur1[:, j:j + 1], None, ALU.mult)
                v.tensor_scalar(xkT[:, sl_o], prv, muk1[:, j:j + 1], None, ALU.mult)
                v.scalar_tensor_tensor(xrT[:, sl_o], cur, mur[:, j:j + 1], xrT[:, sl_o],
                                       ALU.mult, ALU.add)
                v.scalar_tensor_tensor(xkT[:, sl_o], cur, muk[:, j:j + 1], xkT[:, sl_o],
                                       ALU.mult, ALU.add)
        pr.on("vector", mix)
        pr.step()

        for dq in range(8):
            for ts in range(TSL // 512):
                def mm(te, dq=dq, ts=ts):
                    for c in range(8):
                        te.matmul(ps_a[:], wrT[:, c * D + dq * 128: c * D + dq * 128 + 128],
                                  xrT[:, c * TSL + ts * 512: c * TSL + ts * 512 + 512],
                                  start=(c == 0), stop=(c == 7))
                    for c in range(8):
                        te.matmul(ps_b[:], wkT[:, c * D + dq * 128: c * D + dq * 128 + 128],
                                  xkT[:, c * TSL + ts * 512: c * TSL + ts * 512 + 512],
                                  start=(c == 0), stop=(c == 7))
                pr.on("tensor", mm)
                pr.step()
                sl_o = slice(dq * TSL + ts * 512, dq * TSL + ts * 512 + 512)
                pr.on("scalar", lambda sc, sl_o=sl_o: sc.activation(r_T[:, sl_o], ps_a[:], AF.Sigmoid))
                pr.on("vector", lambda v: v.tensor_scalar(relu_tmp[:], ps_b[:], 0.0, None, ALU.max))
                pr.step()
                pr.on("scalar", lambda sc, sl_o=sl_o: sc.activation(kact_T[:, sl_o], relu_tmp[:], AF.Square))
                pr.step()

        for dq in range(8):
            for ts in range(TSL // 512):
                def mm(te, dq=dq, ts=ts):
                    for c in range(8):
                        te.matmul(ps_a[:], wvT[:, c * D + dq * 128: c * D + dq * 128 + 128],
                                  kact_T[:, c * TSL + ts * 512: c * TSL + ts * 512 + 512],
                                  start=(c == 0), stop=(c == 7))
                pr.on("tensor", mm)
                pr.step()
                sl_o = slice(dq * TSL + ts * 512, dq * TSL + ts * 512 + 512)
                pr.on("vector", lambda v, sl_o=sl_o: v.tensor_tensor(
                    out_T[:, sl_o], r_T[:, sl_o], ps_a[:], ALU.mult))
                pr.step()

        for i in range(NTT):
            for half in range(2):
                def trh(te, i=i, half=half):
                    for j in range(4):
                        dj = half * 4 + j
                        te.transpose(ps_bf[j][:],
                                     out_T[:, dj * TSL + i * 128: dj * TSL + i * 128 + 128],
                                     pr.ident_bf[:])
                pr.on("tensor", trh)
                pr.step()

                def evh(v, i=i, half=half):
                    for j in range(4):
                        dj = half * 4 + j
                        v.tensor_tensor(ytile[:, dj * 128:(dj + 1) * 128],
                                        x_sl[:, i * D + dj * 128: i * D + dj * 128 + 128],
                                        ps_bf[j][:], ALU.add)
                pr.on("vector", evh)
                pr.step()
            pr.dma(y_h[(sl * NTT + i) * 128:(sl * NTT + i + 1) * 128, :], ytile[:, :])
            pr.step()

    if standalone:
        return pr.build()
    return


def emit_ln_tile(pr, x_ap, out_ap, bufs=None, d=D, eps=1e-5):
    if bufs is None:
        bufs = (pr.sbuf([128, 1], F32), pr.sbuf([128, 1], F32), pr.sbuf([128, 1], F32),
                pr.sbuf([128, d], F32))
    s1, s2, m, sq = bufs

    def p1(v):
        v.reduce_sum(s1[:], x_ap, axis=AX)
    pr.on("vector", p1)
    pr.on("scalar", lambda sc: sc.activation(sq[:], x_ap, AF.Square))
    pr.step()

    def p2(v):
        inv = 1.0 / d
        v.reduce_sum(s2[:], sq[:], axis=AX)
        v.tensor_scalar(m[:], s1[:], inv, None, ALU.mult)
        v.tensor_scalar(s2[:], s2[:], inv, None, ALU.mult)
        v.tensor_tensor(s1[:], m[:], m[:], ALU.mult)
        v.tensor_tensor(s2[:], s2[:], s1[:], ALU.subtract)
        v.tensor_scalar(s2[:], s2[:], eps, None, ALU.add)
    pr.on("vector", p2)
    pr.step()
    pr.on("scalar", lambda sc: sc.activation(s2[:], s2[:], AF.Sqrt))
    pr.step()

    def p3(v):
        v.reciprocal(s1[:], s2[:])
        v.tensor_scalar(out_ap, x_ap, m[:], s1[:], ALU.subtract, ALU.mult)
    pr.on("vector", p3)
    pr.step()
    return bufs


def emit_ln_transpose_tile(pr, x_ap, ln_tok, lnT_off_fn, ps_bf, lnbufs, lnT):
    lnbufs = emit_ln_tile(pr, x_ap, ln_tok[:], bufs=lnbufs)
    for j0 in (0, 4):
        def tr(te, j0=j0):
            for j in range(4):
                te.transpose(ps_bf[j][:], ln_tok[:, (j0 + j) * 128:(j0 + j + 1) * 128],
                             pr.ident_bf[:])
        pr.on("tensor", tr)
        pr.step()

        def ev(v, j0=j0):
            for j in range(4):
                off = lnT_off_fn(j0 + j)
                v.tensor_copy(lnT[:, off:off + 128], ps_bf[j][:])
        pr.on("vector", ev)
        pr.step()
    return lnbufs


def build_l3f(pr=None, px='', x_h=None, y_h=None):
    standalone = pr is None
    if standalone:
        pr = Prog()
    if x_h is None:
        x_h = pr.dram_in(px + "x", [S, D])
    w1c_h = pr.dram_in(px + "w1c", [32, 128, 1024])
    w2c_h = pr.dram_in(px + "w2c", [32, 128, 1024])
    h1c_h = pr.dram_in(px + "h1c", [32, 128, 4096])
    h2c_h = pr.dram_in(px + "h2c", [32, 128, 4096])
    w3c_h = pr.dram_in(px + "w3c", [2, 32, 128, 512])
    s1c_h = pr.dram_in(px + "s1c", [128, 32])
    s2c_h = pr.dram_in(px + "s2c", [128, 32])
    h1sc_h = pr.dram_in(px + "h1sc", [128, 32])
    h2sc_h = pr.dram_in(px + "h2sc", [128, 32])
    s3r_h = pr.dram_in(px + "s3row", [1, 1024])
    if y_h is None:
        y_h = pr.dram_out(px + "y", [S, D])
    pr.init_consts()

    ld = Loader(pr)
    s1c = ld.load_f32(s1c_h, 128, 32)
    s2c = ld.load_f32(s2c_h, 128, 32)
    h1sc = ld.load_f32(h1sc_h, 128, 32)
    h2sc = ld.load_f32(h2sc_h, 128, 32)
    s3row = ld.load_f32(s3r_h, 1, 1024)
    s3rep = pr.sbuf([128, 1024], F32)
    ones_col = pr.sbuf([1, 128], F32)
    ps_rep = pr.psum([128, 512], F32)
    pr.on("gpsimd", lambda g: g.memset(ones_col[:], 1.0))
    pr.step()
    for c0 in (0, 512):
        pr.on("tensor", lambda te, c0=c0: te.matmul(ps_rep[:], ones_col[0:1, :],
                                                    s3row[0:1, c0:c0 + 512],
                                                    start=True, stop=True))
        pr.step()
        pr.on("vector", lambda v, c0=c0: v.tensor_copy(s3rep[:, c0:c0 + 512], ps_rep[:]))
        pr.step()
    pr.free(ps_rep)

    TSL = 1024
    NSL = S // TSL
    NTT = TSL // 128

    xt = pr.sbuf([128, D], F32)
    ln_tok = pr.sbuf([128, D], BF16)
    lnT = pr.sbuf([128, 8 * TSL], BF16)
    hA = pr.sbuf([128, 32 * TSL], BF16)
    hB = pr.sbuf([128, 32 * TSL], BF16)
    wbf = pr.sbuf([128, 4096], BF16)
    ps_bf = pr.psum_bf_pool(4)
    ps_a = pr.psum([128, 512], F32)
    ps_b = pr.psum([128, 512], F32)
    t_sil = pr.sbuf([128, 512], F32)
    t_gat = pr.sbuf([128, 512], F32)
    lnbufs = None

    for sl in range(NSL):
        for i in range(NTT):
            pr.dma(xt[:, :], x_h[(sl * NTT + i) * 128:(sl * NTT + i + 1) * 128, :])
            pr.step()
            lnbufs = emit_ln_transpose_tile(pr, xt[:, :], ln_tok,
                                            lambda j, i=i: j * TSL + i * 128,
                                            ps_bf, lnbufs, lnT)
        for o in range(32):
            pr.dma(pr.stage[:, 0:1024], w1c_h[o])
            pr.step()
            pr.on("vector", lambda v: v.tensor_copy(wbf[:, 0:1024], pr.stage[:, 0:1024]))
            pr.step()
            pr.dma(pr.stage[:, 0:1024], w2c_h[o])
            pr.step()
            pr.on("vector", lambda v: v.tensor_copy(wbf[:, 1024:2048], pr.stage[:, 0:1024]))
            pr.step()
            for tc in range(TSL // 512):
                def mm(te, o=o, tc=tc):
                    for c in range(8):
                        te.matmul(ps_a[:], wbf[:, c * 128:(c + 1) * 128],
                                  lnT[:, c * TSL + tc * 512: c * TSL + tc * 512 + 512],
                                  start=(c == 0), stop=(c == 7))
                    for c in range(8):
                        te.matmul(ps_b[:], wbf[:, 1024 + c * 128:1024 + (c + 1) * 128],
                                  lnT[:, c * TSL + tc * 512: c * TSL + tc * 512 + 512],
                                  start=(c == 0), stop=(c == 7))
                pr.on("tensor", mm)
                pr.step()
                pr.on("scalar", lambda sc, o=o: sc.activation(t_sil[:], ps_a[:], AF.Silu,
                                                              scale=s1c[:, o:o + 1]))
                pr.on("vector", lambda v, o=o: v.tensor_scalar(t_gat[:], ps_b[:],
                                                               s2c[:, o:o + 1], None, ALU.mult))
                pr.step()
                pr.on("vector", lambda v, o=o, tc=tc: v.tensor_tensor(
                    hA[:, o * TSL + tc * 512: o * TSL + tc * 512 + 512],
                    t_sil[:], t_gat[:], ALU.mult))
                pr.step()
        for src, dst, wc_h, sc in ((hA, hB, h1c_h, h1sc), (hB, hA, h2c_h, h2sc)):
            for o in range(32):
                for half in range(2):
                    pr.dma(pr.stage[:, :], wc_h[o, :, half * 2048:(half + 1) * 2048])
                    pr.step()
                    pr.on("vector", lambda v, half=half: v.tensor_copy(
                        wbf[:, half * 2048:(half + 1) * 2048], pr.stage[:, :]))
                    pr.step()
                for tc in range(TSL // 512):
                    def mm(te, o=o, tc=tc, src=src):
                        for c in range(32):
                            te.matmul(ps_a[:], wbf[:, c * 128:(c + 1) * 128],
                                      src[:, c * TSL + tc * 512: c * TSL + tc * 512 + 512],
                                      start=(c == 0), stop=(c == 31))
                    pr.on("tensor", mm)
                    pr.step()
                    pr.on("scalar", lambda sc2, o=o, tc=tc, dst=dst, sc=sc: sc2.activation(
                        dst[:, o * TSL + tc * 512: o * TSL + tc * 512 + 512],
                        ps_a[:], AF.Gelu, scale=sc[:, o:o + 1]))
                    pr.step()
        wr3 = hB
        for oh in range(2):
            for i in range(32):
                pr.dma(pr.stage[:, 0:512], w3c_h[oh, i])
                pr.step()
                pr.on("vector", lambda v, i=i: v.tensor_copy(
                    wr3[:, i * 512:(i + 1) * 512], pr.stage[:, 0:512]))
                pr.step()
            for tt in range(NTT):
                def mm(te, oh=oh, tt=tt):
                    for c in range(32):
                        te.matmul(ps_a[:], hA[:, c * TSL + tt * 128: c * TSL + tt * 128 + 128],
                                  wr3[:, c * 512:(c + 1) * 512],
                                  start=(c == 0), stop=(c == 31))
                pr.on("tensor", mm)
                pr.step()
                pr.dma(xt[:, 0:512],
                       x_h[(sl * NTT + tt) * 128:(sl * NTT + tt + 1) * 128,
                           oh * 512:oh * 512 + 512])
                pr.step()

                def ep(v, oh=oh, tt=tt):
                    v.tensor_tensor(t_sil[:], ps_a[:], s3rep[:, oh * 512:oh * 512 + 512],
                                    ALU.mult)
                    v.tensor_tensor(t_gat[:], t_sil[:], xt[:, 0:512], ALU.add)
                pr.on("vector", ep)
                pr.step()
                pr.dma(y_h[(sl * NTT + tt) * 128:(sl * NTT + tt + 1) * 128,
                           oh * 512:oh * 512 + 512], t_gat[:])
                pr.step()
    if standalone:
        return pr.build()
    return


def host_prep_l3f(inp):
    def chunk_lhsT(wT, co):
        I, O = wT.shape
        a = wT.reshape(I // 128, 128, O // co, co)
        return np.ascontiguousarray(a.transpose(2, 1, 0, 3).reshape(O // co, 128,
                                                                    (I // 128) * co))
    w1T = np.ascontiguousarray(inp["L3_ffn_w1_w"].T)
    w2T = np.ascontiguousarray(inp["L3_ffn_w2_w"].T)
    h1T = np.ascontiguousarray(inp["L3_ffn_h1_w"].T)
    h2T = np.ascontiguousarray(inp["L3_ffn_h2_w"].T)
    w3T = np.ascontiguousarray(inp["L3_ffn_w3_w"].T)
    w3c = np.stack([w3T[:, 0:512].reshape(32, 128, 512),
                    w3T[:, 512:1024].reshape(32, 128, 512)])
    cols = lambda s: np.ascontiguousarray(np.asarray(s).reshape(-1, 128).T)
    return {
        "w1c": chunk_lhsT(w1T, 128), "w2c": chunk_lhsT(w2T, 128),
        "h1c": chunk_lhsT(h1T, 128), "h2c": chunk_lhsT(h2T, 128),
        "w3c": np.ascontiguousarray(w3c),
        "s1c": cols(inp["L3_ffn_w1_s"]), "s2c": cols(inp["L3_ffn_w2_s"]),
        "h1sc": cols(inp["L3_ffn_h1_s"]), "h2sc": cols(inp["L3_ffn_h2_s"]),
        "s3row": np.ascontiguousarray(inp["L3_ffn_w3_s"].reshape(1, 1024)),
    }


def emit_top2(pr, logits_ap, w_out_ap, scl, E=8, scale=1.0):
    m1, m2, negm1, d21, eqm, l2, ge2, ex = scl

    def p1(v):
        v.reduce_max(m1[:], logits_ap, axis=AX)
    pr.on("vector", p1)
    pr.step()

    def p2(v):
        v.tensor_scalar(eqm[:], logits_ap, m1[:], None, ALU.is_equal)
        v.tensor_scalar(negm1[:], m1[:], -1.0, None, ALU.mult)
    pr.on("vector", p2)
    pr.step()

    def p3(v):
        v.scalar_tensor_tensor(l2[:], eqm[:], -1e30, logits_ap, ALU.mult, ALU.add)
    pr.on("vector", p3)
    pr.step()
    pr.on("vector", lambda v: v.reduce_max(m2[:], l2[:], axis=AX))
    pr.step()

    def p4(v):
        v.tensor_scalar(ge2[:], logits_ap, m2[:], None, ALU.is_ge)
        v.tensor_tensor(d21[:], m2[:], m1[:], ALU.subtract)
    pr.on("vector", p4)
    pr.step()
    pr.on("scalar", lambda sc: sc.activation(ex[:], logits_ap, AF.Exp, bias=negm1[:]))
    pr.step()
    pr.on("scalar", lambda sc: sc.activation(d21[:], d21[:], AF.Exp))
    pr.step()

    def p5(v):
        v.tensor_scalar(d21[:], d21[:], 1.0, None, ALU.add)
        v.tensor_tensor(ex[:], ex[:], ge2[:], ALU.mult)
    pr.on("vector", p5)
    pr.step()
    pr.on("vector", lambda v: v.reciprocal(m1[:], d21[:]))
    pr.step()

    def p6(v):
        v.tensor_scalar(ex[:], ex[:], m1[:], float(scale), ALU.mult, ALU.mult)
        v.tensor_copy(w_out_ap, ex[:])
    pr.on("vector", p6)
    pr.step()


def mk_top2_scratch(pr, E=8):
    return (pr.sbuf([128, 1], F32), pr.sbuf([128, 1], F32), pr.sbuf([128, 1], F32),
            pr.sbuf([128, 1], F32), pr.sbuf([128, E], F32), pr.sbuf([128, E], F32),
            pr.sbuf([128, E], F32), pr.sbuf([128, E], F32))


def build_l2f(pr=None, px='', x_h=None, y_h=None):
    standalone = pr is None
    if standalone:
        pr = Prog()
    if x_h is None:
        x_h = pr.dram_in(px + "x", [S, D])
    wgT_h = pr.dram_in(px + "wgT", [D, 8])
    bg_h = pr.dram_in(px + "bg_row", [1, 8])
    w1T_h = pr.dram_in(px + "w1T", [8, D, 64])
    w2T_h = pr.dram_in(px + "w2T", [8, 64, D])
    if y_h is None:
        y_h = pr.dram_out(px + "y", [S, D])
    pr.init_consts()

    ld = Loader(pr)
    wgT = ld.load_bf(wgT_h, D, 8)
    w1T = pr.sbuf([128, 8 * 8 * 64], BF16)
    for e in range(8):
        ld.load_bf(w1T_h[e], D, 64, out=w1T, out_off=e * 512, stride=64)
    w2T = pr.sbuf([64, 8 * D], BF16)
    for e in range(8):
        pr.step()
        pr.dma(pr.stage[:64, 0:D], w2T_h[e])
        pr.step()
        pr.on("vector", lambda v, e=e: v.tensor_copy(w2T[:, e * D:(e + 1) * D],
                                                     pr.stage[:64, 0:D]))
    bg_row = ld.load_f32(bg_h, 1, 8)
    bgrep = pr.sbuf([128, 8], F32)
    ones_col = pr.sbuf([1, 128], F32)
    ps_rep = pr.psum([128, 8], F32)
    pr.on("gpsimd", lambda g: g.memset(ones_col[:], 1.0))
    pr.step()
    pr.on("tensor", lambda te: te.matmul(ps_rep[:], ones_col[0:1, :], bg_row[0:1, :],
                                         start=True, stop=True))
    pr.step()
    pr.on("vector", lambda v: v.tensor_copy(bgrep[:], ps_rep[:]))
    pr.step()
    pr.free(ps_rep)

    TSL = 1024
    NSL = S // TSL
    NTT = TSL // 128

    xt = pr.sbuf([128, D], F32)
    ln_tok = pr.sbuf([128, D], BF16)
    lnT = pr.sbuf([128, 8 * TSL], BF16)
    hbuf = pr.sbuf([64, TSL], BF16)
    logit = pr.sbuf([128, NTT * 8], F32)
    wts = pr.sbuf([128, NTT * 8], F32)
    acc = pr.sbuf([128, NTT * D], F32)
    ps_bf = pr.psum_bf_pool(4)
    ps_a = pr.psum([128, 512], F32)
    ps_g = pr.psum([128, 8], F32)
    ps_h = pr.psum([64, 512], F32)
    lnbufs = None
    t2scl = mk_top2_scratch(pr)

    for sl in range(NSL):
        for i in range(NTT):
            pr.dma(xt[:, :], x_h[(sl * NTT + i) * 128:(sl * NTT + i + 1) * 128, :])
            pr.step()
            lnbufs = emit_ln_transpose_tile(pr, xt[:, :], ln_tok,
                                            lambda j, i=i: j * TSL + i * 128,
                                            ps_bf, lnbufs, lnT)
            def mm(te, i=i):
                for c in range(8):
                    te.matmul(ps_g[:], lnT[:, c * TSL + i * 128: c * TSL + i * 128 + 128],
                              wgT[:, c * 8:(c + 1) * 8], start=(c == 0), stop=(c == 7))
            pr.on("tensor", mm)
            pr.step()
            pr.on("vector", lambda v, i=i: v.tensor_tensor(
                logit[:, i * 8:(i + 1) * 8], ps_g[:], bgrep[:], ALU.add))
            pr.step()
            emit_top2(pr, logit[:, i * 8:(i + 1) * 8], wts[:, i * 8:(i + 1) * 8],
                      t2scl, scale=2.0)
        for e in range(8):
            for tc in range(TSL // 512):
                def mm(te, e=e, tc=tc):
                    for c in range(8):
                        te.matmul(ps_h[:, :], w1T[:, e * 512 + c * 64:(e * 512) + (c + 1) * 64],
                                  lnT[:, c * TSL + tc * 512: c * TSL + tc * 512 + 512],
                                  start=(c == 0), stop=(c == 7))
                pr.on("tensor", mm)
                pr.step()
                pr.on("scalar", lambda sc, tc=tc: sc.activation(
                    hbuf[:, tc * 512:(tc + 1) * 512], ps_h[:, :], AF.Gelu))
                pr.step()
            for tt in range(NTT):
                for ch in range(2):
                    def mm2(te, e=e, tt=tt, ch=ch):
                        te.matmul(ps_a[:], hbuf[:, tt * 128:(tt + 1) * 128],
                                  w2T[:, e * D + ch * 512: e * D + ch * 512 + 512],
                                  start=True, stop=True)
                    pr.on("tensor", mm2)
                    pr.step()

                    def ep(v, e=e, tt=tt, ch=ch):
                        dst = acc[:, tt * D + ch * 512: tt * D + ch * 512 + 512]
                        if e == 0:
                            v.tensor_scalar(dst, ps_a[:], wts[:, tt * 8 + e: tt * 8 + e + 1],
                                            None, ALU.mult)
                        else:
                            v.scalar_tensor_tensor(dst, ps_a[:],
                                                   wts[:, tt * 8 + e: tt * 8 + e + 1],
                                                   dst, ALU.mult, ALU.add)
                    pr.on("vector", ep)
                    pr.step()
        for tt in range(NTT):
            pr.dma(xt[:, :], x_h[(sl * NTT + tt) * 128:(sl * NTT + tt + 1) * 128, :])
            pr.step()
            pr.on("vector", lambda v, tt=tt: v.tensor_tensor(
                acc[:, tt * D:(tt + 1) * D], acc[:, tt * D:(tt + 1) * D], xt[:, :], ALU.add))
            pr.step()
            pr.dma(y_h[(sl * NTT + tt) * 128:(sl * NTT + tt + 1) * 128, :],
                   acc[:, tt * D:(tt + 1) * D])
            pr.step()
    if standalone:
        return pr.build()
    return


def host_prep_l2f(inp):
    return {
        "wgT": np.ascontiguousarray(inp["L2_ffn_Wg"].T),
        "bg_row": np.ascontiguousarray(inp["L2_ffn_bg"].reshape(1, 8)),
        "w1T": np.ascontiguousarray(inp["L2_ffn_W1"].transpose(0, 2, 1)),
        "w2T": np.ascontiguousarray(inp["L2_ffn_W2"].transpose(0, 2, 1)),
    }


def build_l2a(pr=None, px='', x_h=None, y_h=None, debug=False):
    standalone = pr is None
    if standalone:
        pr = Prog()
    RMS_EPS = 1.1920929e-07
    if x_h is None:
        x_h = pr.dram_in(px + "x", [S, D])
    wqkvT_h = pr.dram_in(px + "wqkvT", [D, 3 * D])
    bqkv_h = pr.dram_in(px + "bqkv_row", [1, 3 * D])
    gq_h = pr.dram_in(px + "gq_row", [1, D])
    gk_h = pr.dram_in(px + "gk_row", [1, D])
    woutT_h = pr.dram_in(px + "woutT", [D, D])
    bout_h = pr.dram_in(px + "bout_row", [1, D])
    gout_h = pr.dram_in(px + "gout_row", [1, D])
    if y_h is None:
        y_h = pr.dram_out(px + "y", [S, D])
    if debug:
        dbg_qkv = pr.dram_out(px + "dbg_qkv", [128, 3 * D])
        dbg_qf = pr.dram_out(px + "dbg_qf", [128, D], BF16)
        dbg_vk = pr.dram_out(px + "dbg_vk", [65, 1024])
        dbg_attn = pr.dram_out(px + "dbg_attn", [128, D], BF16)
    pr.init_consts()

    ld = Loader(pr)
    wqkvT = ld.load_bf(wqkvT_h, D, 3 * D)
    woutT = ld.load_bf(woutT_h, D, D)

    ones_col = pr.sbuf([1, 128], F32)
    pr.on("gpsimd", lambda g: g.memset(ones_col[:], 1.0))
    pr.step()
    ps_rep = pr.psum([128, 512], F32)

    def make_rep(row_src, n, is_handle=False):
        row = row_src if is_handle else ld.load_f32(row_src, 1, n)
        rep = pr.sbuf([128, n], F32)
        for c0 in range(0, n, 512):
            cw = min(512, n - c0)
            pr.on("tensor", lambda te, c0=c0, cw=cw, row=row: te.matmul(
                ps_rep[:, 0:cw], ones_col[0:1, :], row[0:1, c0:c0 + cw],
                start=True, stop=True))
            pr.step()
            pr.on("vector", lambda v, c0=c0, cw=cw, rep=rep: v.tensor_copy(
                rep[:, c0:c0 + cw], ps_rep[:, 0:cw]))
            pr.step()
        return rep

    bqkv_rep = make_rep(bqkv_h, 3 * D)
    gq_rep = make_rep(gq_h, D)
    gk_rep = make_rep(gk_h, D)
    bout_rep = make_rep(bout_h, D)
    gout_rep = make_rep(gout_h, D)

    jrow = pr.sbuf([1, 32], F32)
    pr.on("gpsimd", lambda g: g.iota(jrow[:], pattern=[[1, 32]], base=0,
                                     channel_multiplier=0,
                                     allow_small_or_imprecise_dtypes=True))
    pr.step()
    pr.on("vector", lambda v: v.tensor_scalar(jrow[:], jrow[:],
                                              -math.log(10000.0) / 32.0, None, ALU.mult))
    pr.step()
    pr.on("scalar", lambda sc: sc.activation(jrow[:], jrow[:], AF.Exp))
    pr.step()
    pr.on("vector", lambda v: v.tensor_scalar(jrow[:], jrow[:], 1.0 / TWO_PI, None, ALU.mult))
    pr.step()
    cj_rep = make_rep(jrow, 32, is_handle=True)
    pcol = pr.sbuf([128, 1], F32)
    pr.on("gpsimd", lambda g: g.iota(pcol[:], pattern=[[0, 1]], base=0,
                                     channel_multiplier=1,
                                     allow_small_or_imprecise_dtypes=True))
    pr.step()

    qfT_d = pr.dram_tmp(px + "qfT_d", [NT, 128, 8 * 128], BF16)
    qfT = pr.sbuf([128, 8 * 128], BF16)
    xt = pr.sbuf([128, D], F32)
    ln_tok = pr.sbuf([128, D], BF16)
    lnT = pr.sbuf([128, 8 * 128], BF16)
    qkv = pr.sbuf([128, 3 * D], F32)
    kf_bf = pr.sbuf([128, D], BF16)
    qf_bf = pr.sbuf([128, D], BF16)
    tcol = pr.sbuf([128, 1], F32)
    ang = pr.sbuf([128, 32], F32)
    sct = pr.sbuf([128, 64], F32)
    screp = pr.sbuf([128, 1024], F32)
    rt1 = pr.sbuf([128, 512], F32)
    rt2 = pr.sbuf([128, 512], F32)
    rt3 = pr.sbuf([128, 512], F32)
    stat = pr.sbuf([128, 4], F32)
    ps_bf = pr.psum_bf_pool(4)
    ps_a = pr.psum([128, 512], F32)
    ps_vk = pr.psum([65, 1024], F32)
    vk_acc = pr.sbuf([65, 1024], F32)
    ps_res = ps_rep[0:65, 0:128]
    ps_trf = ps_rep[:, 128:256]
    vk_sbT = pr.sbuf([128, 8 * 65], BF16)
    res_f = pr.sbuf([65, 128], F32)
    t_at = pr.sbuf([128, 65], F32)
    attn_tok = pr.sbuf([128, D], BF16)
    attnT = pr.sbuf([128, 8 * 128], BF16)
    vpad = pr.sbuf([128, 65], BF16)
    yt = pr.sbuf([128, D], F32)
    lnbufs = None

    pr.on("gpsimd", lambda g: g.memset(vpad[:, 64:65], 1.0))
    pr.step()

    for tt in range(NT):
        pr.dma(xt[:, :], x_h[tt * 128:(tt + 1) * 128, :])
        pr.step()
        lnbufs = emit_ln_transpose_tile(pr, xt[:, :], ln_tok,
                                        lambda j: j * 128, ps_bf, lnbufs, lnT)
        for ch in range(6):
            def mm(te, ch=ch):
                for c in range(8):
                    te.matmul(ps_a[:], lnT[:, c * 128:(c + 1) * 128],
                              wqkvT[:, c * 3072 + ch * 512: c * 3072 + ch * 512 + 512],
                              start=(c == 0), stop=(c == 7))
            pr.on("tensor", mm)
            pr.step()
            pr.on("vector", lambda v, ch=ch: v.tensor_tensor(
                qkv[:, ch * 512:(ch + 1) * 512], ps_a[:],
                bqkv_rep[:, ch * 512:(ch + 1) * 512], ALU.add))
            pr.step()
        for base, si in ((0, 0), (D, 1)):
            for half in range(2):
                pr.on("scalar", lambda sc, base=base, half=half: sc.activation(
                    rt1[:], qkv[:, base + half * 512: base + half * 512 + 512], AF.Square))
                pr.step()
                pr.on("vector", lambda v, half=half: v.reduce_sum(
                    stat[:, 2 + half:3 + half], rt1[:], axis=AX))
                pr.step()
            pr.on("vector", lambda v, si=si: v.tensor_scalar(
                stat[:, si:si + 1], stat[:, 2:3], stat[:, 3:4], None, ALU.add))
            pr.step()
            pr.on("vector", lambda v, si=si: v.tensor_scalar(
                stat[:, si:si + 1], stat[:, si:si + 1], 1.0 / D, RMS_EPS,
                ALU.mult, ALU.add))
            pr.step()
            pr.on("scalar", lambda sc, si=si: sc.activation(
                stat[:, si:si + 1], stat[:, si:si + 1], AF.Sqrt))
            pr.step()

        def recs(v):
            v.reciprocal(stat[:, 2:3], stat[:, 0:1])
            v.reciprocal(stat[:, 3:4], stat[:, 1:2])
        pr.on("vector", recs)
        pr.step()

        def nrm(v):
            v.tensor_scalar(qkv[:, 0:D], qkv[:, 0:D], stat[:, 2:3], None, ALU.mult)
            v.tensor_scalar(qkv[:, D:2 * D], qkv[:, D:2 * D], stat[:, 3:4], None, ALU.mult)
        pr.on("vector", nrm)
        pr.step()

        def gmul(v):
            v.tensor_tensor(qkv[:, 0:D], qkv[:, 0:D], gq_rep[:], ALU.mult)
            v.tensor_tensor(qkv[:, D:2 * D], qkv[:, D:2 * D], gk_rep[:], ALU.mult)
        pr.on("vector", gmul)
        pr.step()

        def angf(v, tt=tt):
            v.tensor_scalar(tcol[:], pcol[:], 1.0, float(128 * tt), ALU.mult, ALU.add)
            v.tensor_scalar(ang[:], cj_rep[:, 0:32], tcol[:, 0:1], None, ALU.mult)
        pr.on("vector", angf)
        pr.step()

        def fracs(v):
            v.tensor_scalar(sct[:, 0:32], ang[:], MAGIC, MAGIC, ALU.add, ALU.subtract)
            v.tensor_tensor(sct[:, 0:32], ang[:], sct[:, 0:32], ALU.subtract)
            v.tensor_scalar(ang[:], ang[:], 0.25, None, ALU.add)
            v.tensor_scalar(sct[:, 32:64], ang[:], MAGIC, MAGIC, ALU.add, ALU.subtract)
            v.tensor_tensor(sct[:, 32:64], ang[:], sct[:, 32:64], ALU.subtract)
        pr.on("vector", fracs)
        pr.step()
        pr.on("scalar", lambda sc: sc.activation(sct[:], sct[:], AF.Sin, scale=TWO_PI))
        pr.step()

        def screp_f(v):
            for h in range(16):
                v.tensor_copy(screp[:, h * 32:(h + 1) * 32], sct[:, 32:64])
                v.tensor_copy(screp[:, 512 + h * 32:512 + (h + 1) * 32], sct[:, 0:32])
        pr.on("vector", screp_f)
        pr.step()

        def d3(t):
            return t.rearrange("p (n o) -> p n o", o=1)
        for base, dst in ((0, qf_bf), (D, kf_bf)):
            ev = qkv[:, base:base + D].rearrange("p (n two) -> p n two", two=2)[:, :, 0:1]
            od = qkv[:, base:base + D].rearrange("p (n two) -> p n two", two=2)[:, :, 1:2]
            dev = dst[:, 0:D].rearrange("p (n two) -> p n two", two=2)[:, :, 0:1]
            dod = dst[:, 0:D].rearrange("p (n two) -> p n two", two=2)[:, :, 1:2]
            cos = screp[:, 0:512]
            sin = screp[:, 512:1024]

            def rope(v, ev=ev, od=od, dev=dev, dod=dod, cos=cos, sin=sin):
                v.tensor_tensor(d3(rt1[:]), ev, d3(cos), ALU.mult)
                v.tensor_tensor(d3(rt2[:]), od, d3(sin), ALU.mult)
                v.tensor_tensor(rt1[:], rt1[:], rt2[:], ALU.subtract)
                v.tensor_tensor(d3(rt2[:]), ev, d3(sin), ALU.mult)
                v.tensor_tensor(d3(rt3[:]), od, d3(cos), ALU.mult)
                v.tensor_tensor(rt2[:], rt2[:], rt3[:], ALU.add)
                v.tensor_scalar(dev, d3(rt1[:]), 0.0, None, ALU.max)
                v.tensor_scalar(dod, d3(rt2[:]), 0.0, None, ALU.max)
            pr.on("vector", rope)
            pr.step()

        if debug and tt == 0:
            pr.dma(dbg_qkv, qkv[:, :])
            pr.dma(dbg_qf, qf_bf[:, :])
            pr.step()
        if tt == 0:
            pr.on("gpsimd", lambda g: g.memset(vk_acc[:, :], 0.0))
            pr.step()
        for h in range(16):
            pr.on("vector", lambda v, h=h: v.tensor_copy(
                vpad[:, 0:64], qkv[:, 2 * D + h * 64: 2 * D + (h + 1) * 64]))
            pr.step()

            def mmvk(te, h=h):
                te.matmul(ps_vk[:, h * 64:(h + 1) * 64], vpad[:, :],
                          kf_bf[:, h * 64:(h + 1) * 64], start=True, stop=True)
            pr.on("tensor", mmvk)
            pr.step()
        pr.on("vector", lambda v: v.tensor_tensor(vk_acc[:, :], vk_acc[:, :],
                                                  ps_vk[:, :], ALU.add))
        pr.step()
        for hp in range(8):
            def trq(te, hp=hp):
                te.transpose(ps_bf[0][0:64, :], qf_bf[:, (2 * hp) * 64:(2 * hp + 1) * 64],
                             pr.ident_bf[:])
                te.transpose(ps_bf[0][64:128, :], qf_bf[:, (2 * hp + 1) * 64:(2 * hp + 2) * 64],
                             pr.ident_bf[:])
            pr.on("tensor", trq)
            pr.step()
            pr.on("vector", lambda v, hp=hp: v.tensor_copy(
                qfT[:, hp * 128:(hp + 1) * 128], ps_bf[0][:]))
            pr.step()
        pr.dma(qfT_d[tt], qfT[:, :])
        pr.step()

    if debug:
        pr.dma(dbg_vk, vk_acc[:, :])
        pr.step()
    vk_sb = pr.sbuf([65, 1024], BF16)
    pr.on("vector", lambda v: v.tensor_copy(vk_sb[:, :], vk_acc[:, :]))
    pr.step()
    for h in range(16):
        def trvk(te, h=h):
            te.transpose(ps_bf[0][(h % 2) * 64:(h % 2) * 64 + 64, 0:65],
                         vk_sb[:, h * 64:(h + 1) * 64], pr.ident_bf[0:65, 0:65])
        pr.on("tensor", trvk)
        pr.step()
        pr.on("vector", lambda v, h=h: v.tensor_copy(
            vk_sbT[(h % 2) * 64:(h % 2) * 64 + 64, (h // 2) * 65:(h // 2) * 65 + 65],
            ps_bf[0][(h % 2) * 64:(h % 2) * 64 + 64, 0:65]))
        pr.step()

    for tt in range(NT):
        pr.dma(qfT[:, :], qfT_d[tt])
        pr.step()
        for h in range(16):
            hp, ho = h // 2, (h % 2) * 64

            def mmres(te, hp=hp, ho=ho, tt=tt):
                te.matmul(ps_res[:, :], vk_sbT[ho:ho + 64, hp * 65:(hp + 1) * 65],
                          qfT[ho:ho + 64, hp * 128:(hp + 1) * 128],
                          start=True, stop=True)
            pr.on("tensor", mmres)
            pr.step()
            pr.on("vector", lambda v: v.tensor_copy(res_f[:, :], ps_res[:, :]))
            pr.step()

            def trres(te):
                te.transpose(ps_trf[:, 0:65], res_f[:, :], pr.ident[0:65, 0:65])
            pr.on("tensor", trres)
            pr.step()

            def nrm(v, h=h):
                v.tensor_copy(t_at[:, :], ps_trf[:, 0:65])
                v.tensor_scalar(t_at[:, 64:65], t_at[:, 64:65], 1e-6, None, ALU.add)
            pr.on("vector", nrm)
            pr.step()
            pr.on("vector", lambda v: v.reciprocal(t_at[:, 64:65], t_at[:, 64:65]))
            pr.step()
            pr.on("vector", lambda v, h=h: v.tensor_scalar(
                attn_tok[:, h * 64:(h + 1) * 64], t_at[:, 0:64], t_at[:, 64:65],
                None, ALU.mult))
            pr.step()
        if debug and tt == 0:
            pr.dma(dbg_attn, attn_tok[:, :])
            pr.step()
        for j0 in (0, 4):
            def tra(te, j0=j0):
                for j in range(4):
                    te.transpose(ps_bf[j][:], attn_tok[:, (j0 + j) * 128:(j0 + j + 1) * 128],
                                 pr.ident_bf[:])
            pr.on("tensor", tra)
            pr.step()

            def eva(v, j0=j0):
                for j in range(4):
                    v.tensor_copy(attnT[:, (j0 + j) * 128:(j0 + j + 1) * 128], ps_bf[j][:])
            pr.on("vector", eva)
            pr.step()
        for ch in range(2):
            def mmo(te, ch=ch, tt=tt):
                for c in range(8):
                    te.matmul(ps_a[:], attnT[:, c * 128:(c + 1) * 128],
                              woutT[:, c * D + ch * 512: c * D + ch * 512 + 512],
                              start=(c == 0), stop=(c == 7))
            pr.on("tensor", mmo)
            pr.step()
            pr.on("vector", lambda v, ch=ch: v.tensor_tensor(
                yt[:, ch * 512:(ch + 1) * 512], ps_a[:],
                bout_rep[:, ch * 512:(ch + 1) * 512], ALU.add))
            pr.step()
        for half in range(2):
            pr.on("scalar", lambda sc, half=half: sc.activation(
                rt1[:], yt[:, half * 512:(half + 1) * 512], AF.Square))
            pr.step()
            pr.on("vector", lambda v, half=half: v.reduce_sum(
                stat[:, 2 + half:3 + half], rt1[:], axis=AX))
            pr.step()

        def fin1(v):
            v.tensor_scalar(stat[:, 0:1], stat[:, 2:3], stat[:, 3:4], None, ALU.add)
        pr.on("vector", fin1)
        pr.step()
        pr.on("vector", lambda v: v.tensor_scalar(stat[:, 0:1], stat[:, 0:1],
                                                  1.0 / D, RMS_EPS, ALU.mult, ALU.add))
        pr.step()
        pr.on("scalar", lambda sc: sc.activation(stat[:, 0:1], stat[:, 0:1], AF.Sqrt))
        pr.step()
        pr.on("vector", lambda v: v.reciprocal(stat[:, 1:2], stat[:, 0:1]))
        pr.step()
        pr.dma(xt[:, :], x_h[tt * 128:(tt + 1) * 128, :])
        pr.step()

        def fin2(v):
            v.tensor_scalar(yt[:, :], yt[:, :], stat[:, 1:2], None, ALU.mult)
            v.tensor_tensor(yt[:, :], yt[:, :], gout_rep[:, :], ALU.mult)
            v.tensor_tensor(yt[:, :], yt[:, :], xt[:, :], ALU.add)
        pr.on("vector", fin2)
        pr.step()
        pr.dma(y_h[tt * 128:(tt + 1) * 128, :], yt[:, :])
        pr.step()
    if standalone:
        return pr.build()
    return


def host_prep_l2a(inp):
    return {
        "wqkvT": np.ascontiguousarray(inp["L2_attn_Wqkv"].T),
        "bqkv_row": np.ascontiguousarray(inp["L2_attn_bqkv"].reshape(1, -1)),
        "gq_row": np.ascontiguousarray(inp["L2_attn_gq"].reshape(1, -1)),
        "gk_row": np.ascontiguousarray(inp["L2_attn_gk"].reshape(1, -1)),
        "woutT": np.ascontiguousarray(inp["L2_attn_Wout"].T),
        "bout_row": np.ascontiguousarray(inp["L2_attn_bout"].reshape(1, -1)),
        "gout_row": np.ascontiguousarray(inp["L2_attn_gout"].reshape(1, -1)),
    }


def build_l0f(pr=None, px='', x_h=None, y_h=None):
    standalone = pr is None
    if standalone:
        pr = Prog()
    if x_h is None:
        x_h = pr.dram_in(px + "x", [S, D])
    gateT_h = pr.dram_in(px + "gateT", [D, 8])
    rmsw_h = pr.dram_in(px + "rmsw_row", [1, D])
    w1c_h = pr.dram_in(px + "w1c", [8, 22, 128, 1024])
    w2c_h = pr.dram_in(px + "w2c", [8, 22, 128, 1024])
    w3t_h = pr.dram_in(px + "w3t", [8, 22, 128, 1024])
    if y_h is None:
        y_h = pr.dram_out(px + "y", [S, D])
    pr.init_consts()

    ld = Loader(pr)
    gateT = ld.load_bf(gateT_h, D, 8)
    ones_col = pr.sbuf([1, 128], F32)
    pr.on("gpsimd", lambda g: g.memset(ones_col[:], 1.0))
    pr.step()
    ps_rep = pr.psum([128, 512], F32)
    rms_row = ld.load_f32(rmsw_h, 1, D)
    rms_rep = pr.sbuf([128, D], F32)
    for c0 in (0, 512):
        pr.on("tensor", lambda te, c0=c0: te.matmul(ps_rep[:], ones_col[0:1, :],
                                                    rms_row[0:1, c0:c0 + 512],
                                                    start=True, stop=True))
        pr.step()
        pr.on("vector", lambda v, c0=c0: v.tensor_copy(rms_rep[:, c0:c0 + 512], ps_rep[:]))
        pr.step()

    TSL = 1024
    NSL = S // TSL
    NTT = TSL // 128

    xt = pr.sbuf([128, D], F32)
    ln_f = pr.sbuf([128, D], F32)
    y_tok = pr.sbuf([128, D], BF16)
    yT = pr.sbuf([128, 8 * TSL], BF16)
    hbuf = pr.sbuf([128, 22 * TSL], BF16)
    w3sb = pr.sbuf([128, 22 * TSL], BF16)
    wbf = pr.sbuf([128, 2048], BF16)
    logit = pr.sbuf([128, NTT * 8], F32)
    wts = pr.sbuf([128, NTT * 8], F32)
    acc = pr.sbuf([128, NTT * D], F32)
    rt = pr.sbuf([128, 512], F32)
    stat2 = pr.sbuf([128, 4], F32)
    ps_bf = pr.psum_bf_pool(4)
    ps_a = pr.psum([128, 512], F32)
    ps_b = pr.psum([128, 512], F32)
    ps_g = pr.psum([128, 8], F32)
    lnbufs = None
    t2scl = mk_top2_scratch(pr)

    def ln_into(tile_idx, dst_f32):
        nonlocal lnbufs
        pr.dma(xt[:, :], x_h[tile_idx * 128:(tile_idx + 1) * 128, :])
        pr.step()
        lnbufs = emit_ln_tile(pr, xt[:, :], dst_f32, bufs=lnbufs)

    for sl in range(NSL):
        for i in range(NTT):
            ln_into(sl * NTT + i, ln_f[:, :])
            for half in range(2):
                pr.on("scalar", lambda sc, half=half: sc.activation(
                    rt[:], ln_f[:, half * 512:(half + 1) * 512], AF.Square))
                pr.step()
                pr.on("vector", lambda v, half=half: v.reduce_sum(
                    stat2[:, half:half + 1], rt[:], axis=AX))
                pr.step()

            def s1(v):
                v.tensor_scalar(stat2[:, 0:1], stat2[:, 0:1], stat2[:, 1:2], None, ALU.add)
            pr.on("vector", s1)
            pr.step()
            pr.on("vector", lambda v: v.tensor_scalar(stat2[:, 0:1], stat2[:, 0:1],
                                                      1.0 / D, 1e-6, ALU.mult, ALU.add))
            pr.step()
            pr.on("scalar", lambda sc: sc.activation(stat2[:, 0:1], stat2[:, 0:1], AF.Sqrt))
            pr.step()
            pr.on("vector", lambda v: v.reciprocal(stat2[:, 1:2], stat2[:, 0:1]))
            pr.step()

            def s2(v):
                v.tensor_scalar(y_tok[:, :], ln_f[:, :], stat2[:, 1:2], None, ALU.mult)
            pr.on("vector", s2)
            pr.step()
            pr.on("vector", lambda v: v.tensor_tensor(y_tok[:, :], y_tok[:, :],
                                                      rms_rep[:, :], ALU.mult))
            pr.step()
            for j0 in (0, 4):
                def tr(te, j0=j0, i=i):
                    for j in range(4):
                        te.transpose(ps_bf[j], y_tok[:, (j0 + j) * 128:(j0 + j + 1) * 128],
                                     pr.ident_bf[:])
                pr.on("tensor", tr)
                pr.step()

                def ev(v, j0=j0, i=i):
                    for j in range(4):
                        v.tensor_copy(yT[:, (j0 + j) * TSL + i * 128:(j0 + j) * TSL + (i + 1) * 128],
                                      ps_bf[j])
                pr.on("vector", ev)
                pr.step()
            def mmg(te, i=i):
                for c in range(8):
                    te.matmul(ps_g[:], yT[:, c * TSL + i * 128: c * TSL + i * 128 + 128],
                              gateT[:, c * 8:(c + 1) * 8], start=(c == 0), stop=(c == 7))
            pr.on("tensor", mmg)
            pr.step()
            pr.on("vector", lambda v, i=i: v.tensor_copy(logit[:, i * 8:(i + 1) * 8], ps_g[:]))
            pr.step()
            emit_top2(pr, logit[:, i * 8:(i + 1) * 8], wts[:, i * 8:(i + 1) * 8], t2scl)

        for e in range(8):
            for i in range(22):
                pr.dma(pr.stage[:, 0:1024], w3t_h[e, i])
                pr.step()
                pr.on("vector", lambda v, i=i: v.tensor_copy(
                    w3sb[:, i * TSL:i * TSL + 1024], pr.stage[:, 0:1024]))
                pr.step()
            for o in range(22):
                pr.dma(pr.stage[:, 0:1024], w1c_h[e, o])
                pr.step()
                pr.on("vector", lambda v: v.tensor_copy(wbf[:, 0:1024], pr.stage[:, 0:1024]))
                pr.step()
                pr.dma(pr.stage[:, 0:1024], w2c_h[e, o])
                pr.step()
                pr.on("vector", lambda v: v.tensor_copy(wbf[:, 1024:2048], pr.stage[:, 0:1024]))
                pr.step()
                for tc in range(TSL // 512):
                    def mm(te, o=o, tc=tc):
                        for c in range(8):
                            te.matmul(ps_a[:], wbf[:, c * 128:(c + 1) * 128],
                                      yT[:, c * TSL + tc * 512: c * TSL + tc * 512 + 512],
                                      start=(c == 0), stop=(c == 7))
                        for c in range(8):
                            te.matmul(ps_b[:], wbf[:, 1024 + c * 128:1024 + (c + 1) * 128],
                                      yT[:, c * TSL + tc * 512: c * TSL + tc * 512 + 512],
                                      start=(c == 0), stop=(c == 7))
                    pr.on("tensor", mm)
                    pr.step()
                    pr.on("scalar", lambda sc: sc.activation(rt[:], ps_a[:], AF.Silu))
                    pr.step()
                    pr.on("vector", lambda v, o=o, tc=tc: v.tensor_tensor(
                        hbuf[:, o * TSL + tc * 512: o * TSL + tc * 512 + 512],
                        rt[:], ps_b[:], ALU.mult))
                    pr.step()
            for tt in range(NTT):
                for ch in range(2):
                    def mm2(te, tt=tt, ch=ch):
                        for i in range(22):
                            te.matmul(ps_a[:], hbuf[:, i * TSL + tt * 128: i * TSL + tt * 128 + 128],
                                      w3sb[:, i * TSL + ch * 512: i * TSL + ch * 512 + 512],
                                      start=(i == 0), stop=(i == 21))
                    pr.on("tensor", mm2)
                    pr.step()

                    def ep(v, e=e, tt=tt, ch=ch):
                        dst = acc[:, tt * D + ch * 512: tt * D + ch * 512 + 512]
                        if e == 0:
                            v.tensor_scalar(dst, ps_a[:], wts[:, tt * 8 + e: tt * 8 + e + 1],
                                            None, ALU.mult)
                        else:
                            v.scalar_tensor_tensor(dst, ps_a[:],
                                                   wts[:, tt * 8 + e: tt * 8 + e + 1],
                                                   dst, ALU.mult, ALU.add)
                    pr.on("vector", ep)
                    pr.step()
        for tt in range(NTT):
            ln_into(sl * NTT + tt, ln_f[:, :])

            def fin(v, tt=tt):
                v.tensor_tensor(acc[:, tt * D:(tt + 1) * D], acc[:, tt * D:(tt + 1) * D],
                                ln_f[:, :], ALU.add)
                v.tensor_tensor(acc[:, tt * D:(tt + 1) * D], acc[:, tt * D:(tt + 1) * D],
                                xt[:, :], ALU.add)
            pr.on("vector", fin)
            pr.step()
            pr.dma(y_h[(sl * NTT + tt) * 128:(sl * NTT + tt + 1) * 128, :],
                   acc[:, tt * D:(tt + 1) * D])
            pr.step()
    if standalone:
        return pr.build()
    return


def host_prep_l0f(inp):
    def chunk_lhsT(wT, co):
        I, O = wT.shape
        a = wT.reshape(I // 128, 128, O // co, co)
        return np.ascontiguousarray(a.transpose(2, 1, 0, 3).reshape(O // co, 128,
                                                                    (I // 128) * co))
    W1, W2, W3 = inp["L0_ffn_W1"], inp["L0_ffn_W2"], inp["L0_ffn_W3"]
    w1c = np.stack([chunk_lhsT(np.ascontiguousarray(W1[e].T), 128) for e in range(8)])
    w2c = np.stack([chunk_lhsT(np.ascontiguousarray(W2[e].T), 128) for e in range(8)])
    w3t = np.stack([np.ascontiguousarray(W3[e].T).reshape(22, 128, 1024) for e in range(8)])
    return {
        "gateT": np.ascontiguousarray(inp["L0_ffn_gate_w"].T),
        "rmsw_row": np.ascontiguousarray(inp["L0_ffn_rms_w"].reshape(1, -1)),
        "w1c": w1c, "w2c": w2c, "w3t": w3t,
    }


def build_l3a(pr=None, px='', x_h=None, y_h=None):
    standalone = pr is None
    if standalone:
        pr = Prog()
    FREQS = [31.0, 37.0, 41.0, 43.0, 47.0]
    if x_h is None:
        x_h = pr.dram_in(px + "x", [S, D])
    wqT_h = pr.dram_in(px + "wqT", [D, D])
    wkT_h = pr.dram_in(px + "wkT", [D, D])
    wvT_h = pr.dram_in(px + "wvT", [D, D])
    woT_h = pr.dram_in(px + "woT", [D, D])
    bq_h = pr.dram_in(px + "bq_row", [1, D])
    bk_h = pr.dram_in(px + "bk_row", [1, D])
    bv_h = pr.dram_in(px + "bv_row", [1, D])
    bo_h = pr.dram_in(px + "bo_row", [1, D])
    alpha_h = pr.dram_in(px + "alpha_row", [1, 80])
    phi_h = pr.dram_in(px + "phi_row", [1, 80])
    temp_h = pr.dram_in(px + "temperature", [1, 1])
    gam_h = pr.dram_in(px + "gamma_row", [1, D])
    bet_h = pr.dram_in(px + "beta_row", [1, D])
    if y_h is None:
        y_h = pr.dram_out(px + "y", [S, D])

    qT_d = pr.dram_tmp(px + "qT_d", [16, 64, S], BF16)
    kT_d = pr.dram_tmp(px + "kT_d", [16, 64, S], BF16)
    v_d = pr.dram_tmp(px + "v_d", [S, D], BF16)
    at_d = pr.dram_tmp(px + "at_d", [S, D], BF16)
    pr.init_consts()

    ld = Loader(pr)
    ones_col = pr.sbuf([1, 128], F32)
    pr.on("gpsimd", lambda g: g.memset(ones_col[:], 1.0))
    pr.step()
    ps_rep = pr.psum([128, 512], F32)

    def make_rep(row_src, n, is_handle=False):
        row = row_src if is_handle else ld.load_f32(row_src, 1, n)
        rep = pr.sbuf([128, n], F32)
        for c0 in range(0, n, 512):
            cw = min(512, n - c0)
            pr.on("tensor", lambda te, c0=c0, cw=cw, row=row: te.matmul(
                ps_rep[:, 0:cw], ones_col[0:1, :], row[0:1, c0:c0 + cw],
                start=True, stop=True))
            pr.step()
            pr.on("vector", lambda v, c0=c0, cw=cw, rep=rep: v.tensor_copy(
                rep[:, c0:c0 + cw], ps_rep[:, 0:cw]))
            pr.step()
        return rep

    bq_rep = make_rep(bq_h, D)
    bk_rep = make_rep(bk_h, D)
    bv_rep = make_rep(bv_h, D)
    bo_rep = make_rep(bo_h, D)
    gam_rep = make_rep(gam_h, D)
    bet_rep = make_rep(bet_h, D)
    al_rep = make_rep(alpha_h, 80)
    ph_rep = make_rep(phi_h, 80)
    pr.on("vector", lambda v: v.tensor_scalar(ph_rep[:], ph_rep[:], 1.0 / TWO_PI,
                                              None, ALU.mult))
    pr.step()
    trow = ld.load_f32(temp_h, 1, 1)
    grow = pr.sbuf([1, 5], F32)
    pr.on("scalar", lambda sc: sc.activation(trow[:], trow[:], AF.Abs))
    pr.step()
    pr.on("vector", lambda v: v.tensor_scalar(trow[:], trow[:], 0.1, None, ALU.add))
    pr.step()
    for fi, f in enumerate(FREQS):
        pr.on("vector", lambda v, fi=fi, f=f: v.tensor_scalar(
            grow[:, fi:fi + 1], trow[:, 0:1], float(16.0 * math.pi * f), None, ALU.mult))
    pr.step()
    pr.on("vector", lambda v: v.reciprocal(grow[:], grow[:]))
    pr.step()
    g_rep = make_rep(grow, 5, is_handle=True)

    xt = pr.sbuf([128, D], F32)
    ln_tok = pr.sbuf([128, D], BF16)
    lnT = pr.sbuf([128, 8 * 128], BF16)
    ps_bf = pr.psum_bf_pool(4)
    ps_a = pr.psum([128, 512], F32)
    prj = pr.sbuf([128, D], BF16)
    lnbufs = None
    if lnbufs is None:
        lnbufs = (pr.sbuf([128, 1], F32), pr.sbuf([128, 1], F32), pr.sbuf([128, 1], F32),
                  pr.sbuf([128, D], F32))
    wqT = ld.load_bf(wqT_h, D, D)
    wkT = ld.load_bf(wkT_h, D, D)
    wvT = ld.load_bf(wvT_h, D, D)

    for tt in range(NT):
        pr.dma(xt[:, :], x_h[tt * 128:(tt + 1) * 128, :])
        pr.step()
        lnbufs = emit_ln_transpose_tile(pr, xt[:, :], ln_tok,
                                        lambda j: j * 128, ps_bf, lnbufs, lnT)
        for wT, brep, dst, trans in ((wqT, bq_rep, qT_d, True), (wkT, bk_rep, kT_d, True),
                                     (wvT, bv_rep, v_d, False)):
            for ch in range(2):
                def mm(te, wT=wT, ch=ch):
                    for c in range(8):
                        te.matmul(ps_a[:], lnT[:, c * 128:(c + 1) * 128],
                                  wT[:, c * D + ch * 512: c * D + ch * 512 + 512],
                                  start=(c == 0), stop=(c == 7))
                pr.on("tensor", mm)
                pr.step()
                pr.on("vector", lambda v, brep=brep, ch=ch: v.tensor_tensor(
                    prj[:, ch * 512:(ch + 1) * 512], ps_a[:],
                    brep[:, ch * 512:(ch + 1) * 512], ALU.add))
                pr.step()
            if not trans:
                pr.dma(v_d[tt * 128:(tt + 1) * 128, :], prj[:, :])
                pr.step()
            else:
                for hp in range(8):
                    def trq(te, hp=hp):
                        te.transpose(ps_bf[0][0:64, :], prj[:, (2 * hp) * 64:(2 * hp + 1) * 64],
                                     pr.ident_bf[:])
                        te.transpose(ps_bf[0][64:128, :],
                                     prj[:, (2 * hp + 1) * 64:(2 * hp + 2) * 64],
                                     pr.ident_bf[:])
                    pr.on("tensor", trq)
                    pr.step()
                    pr.on("vector", lambda v: v.tensor_copy(ln_tok[:, 0:128], ps_bf[0][:]))
                    pr.step()
                    pr.dma(dst[2 * hp, :, tt * 128:(tt + 1) * 128], ln_tok[0:64, 0:128])
                    pr.dma(dst[2 * hp + 1, :, tt * 128:(tt + 1) * 128], ln_tok[64:128, 0:128])
                    pr.step()

    pr.free(wvT)
    pr.free(wkT)
    pr.free(wqT)
    kTh = pr.sbuf([64, S], BF16)
    qTh = pr.sbuf([64, S], BF16)
    vph = pr.sbuf([128, 32 * 65], BF16)
    sig = pr.sbuf([128, 4096], F32)
    rb1 = pr.sbuf([128, 4096], F32)
    rb2 = pr.sbuf([128, 4096], F32)
    accc = pr.sbuf([128, 4096], F32)
    Tw = pr.sbuf([128, 4096], BF16)
    ps_av = pr.psum([65, 512], F32)
    t_at = pr.sbuf([128, 65], F32)
    at_tok = pr.sbuf([128, 64], BF16)
    resn = pr.sbuf([65, 512], F32)
    ps_a4 = pr.psum([128, 2048], F32)

    for h in range(16):
        pr.dma(kTh[:, :], kT_d[h])
        pr.dma(qTh[:, :], qT_d[h])
        pr.step()
        pr.on("gpsimd", lambda g: g.memset(vph[:, :], 0.0))
        pr.step()
        for jt in range(32):
            pr.dma(pr.stage_bf[:, :], v_d[jt * 128:(jt + 1) * 128, :])
            pr.step()

            def cpv(v, jt=jt, h=h):
                v.tensor_copy(vph[:, jt * 65:jt * 65 + 64],
                              pr.stage_bf[:, h * 64:(h + 1) * 64])
                v.memset(vph[:, jt * 65 + 64:jt * 65 + 65], 1.0)
            pr.on("vector", cpv)
            pr.step()
        for qc in range(8):
            for grp in range(4):
                for q4b in range(2):
                    def mm4(te, q4b=q4b, grp=grp, qc=qc):
                        for jj in range(4):
                            jt = grp * 8 + q4b * 4 + jj
                            te.matmul(ps_a4[:, jj * 512:(jj + 1) * 512],
                                      kTh[:, jt * 128:(jt + 1) * 128],
                                      qTh[:, qc * 512:(qc + 1) * 512],
                                      start=True, stop=True)
                    pr.on("tensor", mm4)
                    pr.step()

                    def ev4(v, q4b=q4b):
                        for jj in range(4):
                            j8 = q4b * 4 + jj
                            v.tensor_copy(sig[:, j8 * 512:(j8 + 1) * 512],
                                          ps_a4[:, jj * 512:(jj + 1) * 512])
                    pr.on("vector", ev4)
                    pr.step()
                for fi in range(5):
                    def ewc(v, fi=fi, h=h):
                        if fi == 1:
                            v.tensor_scalar(accc[:], rb1[:],
                                            al_rep[:, h * 5:h * 5 + 1], None, ALU.mult)
                        elif fi > 1:
                            v.scalar_tensor_tensor(accc[:], rb1[:],
                                                   al_rep[:, h * 5 + fi - 1:h * 5 + fi],
                                                   accc[:], ALU.mult, ALU.add)
                        v.tensor_scalar(rb1[:], sig[:], g_rep[:, fi:fi + 1],
                                        ph_rep[:, h * 5 + fi:h * 5 + fi + 1],
                                        ALU.mult, ALU.add)
                        v.tensor_scalar(rb2[:], rb1[:], MAGIC, MAGIC, ALU.add, ALU.subtract)
                        v.tensor_tensor(rb1[:], rb1[:], rb2[:], ALU.subtract)
                    pr.on("vector", ewc)
                    pr.step()
                    pr.on("scalar", lambda sc: sc.activation(rb1[:], rb1[:], AF.Sin,
                                                             scale=TWO_PI))
                    pr.step()

                def ew2l(v, h=h):
                    v.scalar_tensor_tensor(accc[:], rb1[:],
                                           al_rep[:, h * 5 + 4:h * 5 + 5],
                                           accc[:], ALU.mult, ALU.add)
                pr.on("vector", ew2l)
                pr.step()
                pr.on("scalar", lambda sc: sc.activation(Tw[:], accc[:], AF.Sigmoid))
                pr.step()

                def mmav(te, grp=grp):
                    for j8 in range(8):
                        jt = grp * 8 + j8
                        te.matmul(ps_av[:, :], vph[:, jt * 65:(jt + 1) * 65],
                                  Tw[:, j8 * 512:(j8 + 1) * 512],
                                  start=(grp == 0 and j8 == 0),
                                  stop=(grp == 3 and j8 == 7))
                pr.on("tensor", mmav)
                pr.step()
            pr.on("vector", lambda v: v.tensor_copy(resn[:, :], ps_av[:, :]))
            pr.step()
            for q4 in range(4):
                def trat(te, q4=q4):
                    te.transpose(ps_a[:, 0:65], resn[0:65, q4 * 128:(q4 + 1) * 128],
                                 pr.ident[0:65, 0:65])
                pr.on("tensor", trat)
                pr.step()

                def nrm(v):
                    v.tensor_copy(t_at[:, :], ps_a[:, 0:65])
                    v.tensor_scalar(t_at[:, 64:65], t_at[:, 64:65], 1e-10, None, ALU.add)
                pr.on("vector", nrm)
                pr.step()
                pr.on("vector", lambda v: v.reciprocal(t_at[:, 64:65], t_at[:, 64:65]))
                pr.step()
                pr.on("vector", lambda v: v.tensor_scalar(at_tok[:, :], t_at[:, 0:64],
                                                          t_at[:, 64:65], None, ALU.mult))
                pr.step()
                pr.dma(at_d[(qc * 4 + q4) * 128:(qc * 4 + q4 + 1) * 128,
                            h * 64:(h + 1) * 64], at_tok[:, :])
                pr.step()

    woT = ld.load_bf(woT_h, D, D)
    atT = pr.sbuf([128, 8 * 128], BF16)
    yt = pr.sbuf([128, D], F32)
    stat2 = pr.sbuf([128, 4], F32)
    sqb = pr.sbuf([128, D], F32)
    for tt in range(NT):
        pr.dma(ln_tok[:, :], at_d[tt * 128:(tt + 1) * 128, :])
        pr.step()
        for j0 in (0, 4):
            def tra(te, j0=j0):
                for j in range(4):
                    te.transpose(ps_bf[j], ln_tok[:, (j0 + j) * 128:(j0 + j + 1) * 128],
                                 pr.ident_bf[:])
            pr.on("tensor", tra)
            pr.step()

            def eva(v, j0=j0):
                for j in range(4):
                    v.tensor_copy(atT[:, (j0 + j) * 128:(j0 + j + 1) * 128], ps_bf[j])
            pr.on("vector", eva)
            pr.step()
        for ch in range(2):
            def mmo(te, ch=ch):
                for c in range(8):
                    te.matmul(ps_a[:], atT[:, c * 128:(c + 1) * 128],
                              woT[:, c * D + ch * 512: c * D + ch * 512 + 512],
                              start=(c == 0), stop=(c == 7))
            pr.on("tensor", mmo)
            pr.step()
            pr.on("vector", lambda v, ch=ch: v.tensor_tensor(
                yt[:, ch * 512:(ch + 1) * 512], ps_a[:],
                bo_rep[:, ch * 512:(ch + 1) * 512], ALU.add))
            pr.step()
        def lna(v):
            v.reduce_sum(stat2[:, 0:1], yt[:, :], axis=AX)
        pr.on("vector", lna)
        pr.on("scalar", lambda sc: sc.activation(sqb[:], yt[:, :], AF.Square))
        pr.step()

        def lnb(v):
            v.reduce_sum(stat2[:, 1:2], sqb[:], axis=AX)
            v.tensor_scalar(stat2[:, 0:1], stat2[:, 0:1], 1.0 / D, None, ALU.mult)
            v.tensor_scalar(stat2[:, 1:2], stat2[:, 1:2], 1.0 / D, None, ALU.mult)
            v.tensor_tensor(stat2[:, 2:3], stat2[:, 0:1], stat2[:, 0:1], ALU.mult)
            v.tensor_tensor(stat2[:, 1:2], stat2[:, 1:2], stat2[:, 2:3], ALU.subtract)
            v.tensor_scalar(stat2[:, 1:2], stat2[:, 1:2], 1e-5, None, ALU.add)
        pr.on("vector", lnb)
        pr.step()
        pr.on("scalar", lambda sc: sc.activation(stat2[:, 1:2], stat2[:, 1:2], AF.Sqrt))
        pr.step()
        pr.on("vector", lambda v: v.reciprocal(stat2[:, 2:3], stat2[:, 1:2]))
        pr.step()
        pr.dma(xt[:, :], x_h[tt * 128:(tt + 1) * 128, :])
        pr.step()

        def fin(v):
            v.tensor_scalar(yt[:, :], yt[:, :], stat2[:, 0:1], stat2[:, 2:3],
                            ALU.subtract, ALU.mult)
            v.tensor_tensor(yt[:, :], yt[:, :], gam_rep[:, :], ALU.mult)
            v.tensor_tensor(yt[:, :], yt[:, :], bet_rep[:, :], ALU.add)
            v.tensor_tensor(yt[:, :], yt[:, :], xt[:, :], ALU.add)
        pr.on("vector", fin)
        pr.step()
        pr.dma(y_h[tt * 128:(tt + 1) * 128, :], yt[:, :])
        pr.step()
    if standalone:
        return pr.build()
    return


def host_prep_l3a(inp):
    return {
        "wqT": np.ascontiguousarray(inp["L3_attn_Wq"].T),
        "wkT": np.ascontiguousarray(inp["L3_attn_Wk"].T),
        "wvT": np.ascontiguousarray(inp["L3_attn_Wv"].T),
        "woT": np.ascontiguousarray(inp["L3_attn_Wo"].T),
        "bq_row": np.ascontiguousarray(inp["L3_attn_bq"].reshape(1, -1)),
        "bk_row": np.ascontiguousarray(inp["L3_attn_bk"].reshape(1, -1)),
        "bv_row": np.ascontiguousarray(inp["L3_attn_bv"].reshape(1, -1)),
        "bo_row": np.ascontiguousarray(inp["L3_attn_bo"].reshape(1, -1)),
        "alpha_row": np.ascontiguousarray(inp["L3_attn_alpha"].reshape(1, 80)),
        "phi_row": np.ascontiguousarray(inp["L3_attn_phi"].reshape(1, 80)),
        "temperature": np.ascontiguousarray(inp["L3_attn_temperature"].reshape(1, 1)),
        "gamma_row": np.ascontiguousarray(inp["L3_attn_gamma"].reshape(1, -1)),
        "beta_row": np.ascontiguousarray(inp["L3_attn_beta"].reshape(1, -1)),
    }


def _emit_tchain(pr, M_bf, MT_bf, Tt_bf, Ibf, ps_x, ps_y, B, BT, X, sign):
    Xn = Tt_bf
    op0 = ALU.subtract if sign > 0 else ALU.add

    def init(v):
        v.tensor_tensor(Xn[:], Ibf[:], M_bf[:], op0)
        v.tensor_tensor(X[:], Ibf[:], MT_bf[:], op0)
        v.tensor_copy(B[:], M_bf[:])
        v.tensor_copy(BT[:], MT_bf[:])
    pr.on("vector", init)
    pr.step()
    for lvl in range(6):
        def mmsq(te):
            te.matmul(ps_x[:, 0:128], BT[:], B[:], start=True, stop=True)
            te.matmul(ps_x[:, 128:256], B[:], BT[:], start=True, stop=True)
        pr.on("tensor", mmsq)
        pr.step()

        def evsq(v):
            v.tensor_copy(B[:], ps_x[:, 0:128])
            v.tensor_copy(BT[:], ps_x[:, 128:256])
            v.tensor_tensor(B[:], B[:], Ibf[:], ALU.add)
            v.tensor_tensor(BT[:], BT[:], Ibf[:], ALU.add)
        pr.on("vector", evsq)
        pr.step()

        def mmpr(te):
            te.matmul(ps_y[:, 0:128], X[:], B[:], start=True, stop=True)
            te.matmul(ps_y[:, 128:256], B[:], X[:], start=True, stop=True)
        pr.on("tensor", mmpr)
        pr.step()

        def evpr(v, last=(lvl == 5)):
            v.tensor_copy(Xn[:], ps_y[:, 0:128])
            v.tensor_copy(X[:], ps_y[:, 128:256])
            if not last:
                v.tensor_tensor(B[:], B[:], Ibf[:], ALU.subtract)
                v.tensor_tensor(BT[:], BT[:], Ibf[:], ALU.subtract)
        pr.on("vector", evpr)
        pr.step()

    pr.on("vector", lambda v: v.tensor_copy(Tt_bf[:], X[:]))
    pr.step()


L0A_NT = NT
L0A_SKIP = set()


def build_l0a(pr=None, px='', x_h=None, y_h=None):
    standalone = pr is None
    if standalone:
        pr = Prog()
    if x_h is None:
        x_h = pr.dram_in(px + "x", [S, D])
    wqkvT_h = pr.dram_in(px + "wqkvT", [D, 2048])
    convc_h = pr.dram_in(px + "convcols", [128, 64])
    wzT_h = pr.dram_in(px + "wzT", [D, D])
    wbT_h = pr.dram_in(px + "wbT", [D, 16])
    waT_h = pr.dram_in(px + "waT", [D, 16])
    dtb_h = pr.dram_in(px + "dtb_row", [1, 16])
    alog_h = pr.dram_in(px + "alog_row", [1, 16])
    nwt_h = pr.dram_in(px + "nw_tiled_row", [1, D])
    woutT_h = pr.dram_in(px + "woutT", [D, D])
    if y_h is None:
        y_h = pr.dram_out(px + "y", [S, D])
    pr.init_consts()
    onecol = pr.constcol(1.0)
    pr.step()

    ld = Loader(pr)
    ones_col = pr.sbuf([1, 128], F32)
    pr.on("gpsimd", lambda g: g.memset(ones_col[:], 1.0))
    pr.step()
    ps_rep = pr.psum([128, 512], F32)

    def make_rep(row_src, n, is_handle=False, nrep=128):
        row = row_src if is_handle else ld.load_f32(row_src, 1, n)
        rep = pr.sbuf([nrep, n], F32)
        for c0 in range(0, n, 512):
            cw = min(512, n - c0)
            pr.on("tensor", lambda te, c0=c0, cw=cw, row=row: te.matmul(
                ps_rep[0:nrep, 0:cw], ones_col[0:1, 0:nrep], row[0:1, c0:c0 + cw],
                start=True, stop=True))
            pr.step()
            pr.on("vector", lambda v, c0=c0, cw=cw, rep=rep: v.tensor_copy(
                rep[:, c0:c0 + cw], ps_rep[0:nrep, 0:cw]))
            pr.step()
        return rep

    wqkvT = ld.load_bf(wqkvT_h, D, 2048)
    wzT = ld.load_bf(wzT_h, D, D)
    wbaT = ld.load_bf(wbT_h, D, 16)
    waT = ld.load_bf(waT_h, D, 16)
    woutT = ld.load_bf(woutT_h, D, D)
    convc = ld.load_f32(convc_h, 128, 64)
    dtb_rep = make_rep(dtb_h, 16)
    alog_row = ld.load_f32(alog_h, 1, 16)
    pr.on("scalar", lambda sc: sc.activation(alog_row[:], alog_row[:], AF.Exp))
    pr.step()
    pr.on("vector", lambda v: v.tensor_scalar(alog_row[:], alog_row[:], -1.0, None, ALU.mult))
    pr.step()
    nega_rep = make_rep(alog_row, 16, is_handle=True)
    nw_rep = make_rep(nwt_h, D)

    onesmat = pr.sbuf([128, 128], F32)
    pr.on("gpsimd", lambda g: g.memset(onesmat[:], 1.0))
    pr.step()
    trilS = make_tri(pr, 128, strict=True, lower=True)
    triuI8 = make_tri(pr, 128, strict=False, lower=False)
    pr.on("vector", lambda v: v.tensor_scalar(triuI8[:], triuI8[:], 0.125, None, ALU.mult))
    pr.step()

    MIXW = 132
    mixraw = pr.sbuf([128, 16 * MIXW], F32)
    convout = pr.sbuf([128, 16 * 128], BF16)
    xt = pr.sbuf([128, D], F32)
    ln_tok = pr.sbuf([128, D], BF16)
    lnT = pr.sbuf([128, 8 * 128], BF16)
    z_tok = pr.sbuf([128, D], F32)
    beta_tok = pr.sbuf([128, 16], F32)
    g_tok = pr.sbuf([128, 16], F32)
    G_T = pr.sbuf([16, 128], F32)
    G_tok = pr.sbuf([128, 16], F32)
    kts_T = pr.sbuf([16, 128], F32)
    kts_tok = pr.sbuf([128, 16], F32)
    gam_tok = pr.sbuf([128, 16], F32)
    gcrep = pr.sbuf([64, 16], F32)
    rowt = pr.sbuf([1, 16], F32)

    GW = 8
    MW = GW * 128
    KW = GW * 64
    Gcm = pr.sbuf([128, MW], F32)
    Dm_ = pr.sbuf([128, MW], F32)
    Em = pr.sbuf([128, MW], F32)
    E2m = pr.sbuf([128, MW], F32)
    Mbf = pr.sbuf([128, MW], BF16)
    MTbf = pr.sbuf([128, MW], BF16)
    Bs = pr.sbuf([128, MW], BF16)
    BTs = pr.sbuf([128, MW], BF16)
    Xs = pr.sbuf([128, MW], BF16)
    Ttbf = pr.sbuf([128, MW], BF16)
    AqT = pr.sbuf([128, MW], BF16)
    Ibf8 = pr.sbuf([128, MW], BF16)
    trilS8 = pr.sbuf([128, MW], F32)
    triuI8w = pr.sbuf([128, MW], F32)
    Qal = pr.sbuf([64, MW], BF16)
    Kal = pr.sbuf([64, MW], BF16)
    Val = pr.sbuf([64, MW], BF16)
    Ksd = pr.sbuf([128, KW], BF16)
    Vsd = pr.sbuf([128, KW], BF16)
    BV = pr.sbuf([128, KW], BF16)
    BGK = pr.sbuf([128, KW], BF16)
    Ktil = pr.sbuf([128, KW], BF16)
    Pmat = pr.sbuf([128, KW], BF16)
    WT = pr.sbuf([64, MW], BF16)
    Ubf = pr.sbuf([128, KW], BF16)
    Osb = pr.sbuf([128, D], F32)
    Ssb = [pr.sbuf([64, 16 * 64], F32) for _ in range(2)]
    Sbf = pr.sbuf([64, KW], BF16)
    hsc = pr.sbuf([128, D], F32)
    rstd16 = pr.sbuf([128, 16], F32)
    hnorm = pr.sbuf([128, D], BF16)
    hT = pr.sbuf([128, 8 * 128], BF16)
    yt = pr.sbuf([128, D], F32)
    ps_bf = pr.psum_bf_pool(4)
    ps_a = pr.psum([128, 512], F32)
    ps_x = pr.psum([128, 1024], F32)
    ps_y = pr.psum([128, 1024], F32)
    ps_k = pr.psum([128, 128], F32)
    lnbufs = None

    def tile8(dst, src_ap):
        def f(v, dst=dst, src_ap=src_ap):
            for i in range(GW):
                v.tensor_copy(dst[:, i * 128:(i + 1) * 128], src_ap)
        pr.on("vector", f)
        pr.step()
    tile8(Ibf8, pr.ident_bf[:, :])
    tile8(trilS8, trilS[:, :])
    tile8(triuI8w, triuI8[:, :])

    pr.on("gpsimd", lambda g: g.memset(mixraw[:], 0.0))
    pr.step()
    pr.on("gpsimd", lambda g: g.memset(Ssb[0][:], 0.0))
    pr.step()
    if L0A_SKIP:
        for b in (Ksd, Vsd, Qal, Em, E2m, Mbf, MTbf, AqT, Ttbf, Pmat, WT, Ktil, BV, BGK,
                  Bs, BTs, Xs, Gcm, Dm_):
            pr.on("gpsimd", lambda g, b=b: g.memset(b[:], 0.0))
            pr.step()

    for c in range(L0A_NT):
        cur, nxt = Ssb[c % 2], Ssb[(c + 1) % 2]
        pr.dma(xt[:, :], x_h[c * 128:(c + 1) * 128, :])
        pr.step()
        lnbufs = emit_ln_transpose_tile(pr, xt[:, :], ln_tok,
                                        lambda j: j * 128, ps_bf, lnbufs, lnT)
        for fh in range(2):
            def mm(te, fh=fh):
                for ft in range(fh * 8, fh * 8 + 8):
                    for cd in range(8):
                        te.matmul(ps_x[:, (ft % 8) * 128:(ft % 8 + 1) * 128],
                                  wqkvT[:, cd * 2048 + ft * 128: cd * 2048 + (ft + 1) * 128],
                                  lnT[:, cd * 128:(cd + 1) * 128],
                                  start=(cd == 0), stop=(cd == 7))
            pr.on("tensor", mm)
            pr.step()

            def ev(v, fh=fh):
                for ft in range(fh * 8, fh * 8 + 8):
                    v.tensor_copy(mixraw[:, ft * MIXW + 4: ft * MIXW + 132],
                                  ps_x[:, (ft % 8) * 128:(ft % 8 + 1) * 128])
            pr.on("vector", ev)
            pr.step()
        def cv(v):
            for ft in range(16):
                base = ft * MIXW
                v.tensor_scalar(hsc[:, ft * 64:ft * 64 + 128] if False else hsc[:, 0:128],
                                mixraw[:, base + 1: base + 129],
                                convc[:, ft * 4:ft * 4 + 1], None, ALU.mult)
                for j in (1, 2, 3):
                    v.scalar_tensor_tensor(hsc[:, 0:128],
                                           mixraw[:, base + 1 + j: base + 129 + j],
                                           convc[:, ft * 4 + j:ft * 4 + j + 1],
                                           hsc[:, 0:128], ALU.mult, ALU.add)
                v.tensor_copy(yt[:, ft * 64:ft * 64 + 64] if False else yt[:, 0:128], hsc[:, 0:128])
        for ft in range(16):
            def cv1(v, ft=ft):
                base = ft * MIXW
                v.tensor_scalar(hsc[:, 0:128], mixraw[:, base + 1: base + 129],
                                convc[:, ft * 4:ft * 4 + 1], None, ALU.mult)
                for j in (1, 2, 3):
                    v.scalar_tensor_tensor(hsc[:, 0:128],
                                           mixraw[:, base + 1 + j: base + 129 + j],
                                           convc[:, ft * 4 + j:ft * 4 + j + 1],
                                           hsc[:, 0:128], ALU.mult, ALU.add)
            pr.on("vector", cv1)
            pr.step()
            pr.on("scalar", lambda sc, ft=ft: sc.activation(
                convout[:, ft * 128:(ft + 1) * 128], hsc[:, 0:128], AF.Silu))
            pr.step()

        def halo(v):
            for ft in range(16):
                base = ft * MIXW
                v.tensor_copy(mixraw[:, base:base + 4], mixraw[:, base + 128:base + 132])
        pr.on("vector", halo)
        pr.step()
        for ch in range(2):
            def mmz(te, ch=ch):
                for cd in range(8):
                    te.matmul(ps_a[:], lnT[:, cd * 128:(cd + 1) * 128],
                              wzT[:, cd * D + ch * 512: cd * D + ch * 512 + 512],
                              start=(cd == 0), stop=(cd == 7))
            pr.on("tensor", mmz)
            pr.step()
            pr.on("vector", lambda v, ch=ch: v.tensor_copy(
                z_tok[:, ch * 512:(ch + 1) * 512], ps_a[:]))
            pr.step()

        def mmba(te):
            for cd in range(8):
                te.matmul(ps_x[:, 0:16], lnT[:, cd * 128:(cd + 1) * 128],
                          wbaT[:, cd * 16:(cd + 1) * 16], start=(cd == 0), stop=(cd == 7))
            for cd in range(8):
                te.matmul(ps_x[:, 16:32], lnT[:, cd * 128:(cd + 1) * 128],
                          waT[:, cd * 16:(cd + 1) * 16], start=(cd == 0), stop=(cd == 7))
        pr.on("tensor", mmba)
        pr.step()
        pr.on("scalar", lambda sc: sc.activation(beta_tok[:], ps_x[:, 0:16], AF.Sigmoid))
        pr.step()
        pr.on("vector", lambda v: v.tensor_tensor(g_tok[:], ps_x[:, 16:32], dtb_rep[:, 0:16],
                                                  ALU.add))
        pr.step()
        pr.on("scalar", lambda sc: sc.activation(hsc[:, 0:16], g_tok[:], AF.Abs))
        pr.step()
        pr.on("scalar", lambda sc: sc.activation(hsc[:, 0:16], hsc[:, 0:16], AF.Exp,
                                                 scale=-1.0))
        pr.step()
        pr.on("scalar", lambda sc: sc.activation(hsc[:, 0:16], hsc[:, 0:16], AF.Ln,
                                                 bias=onecol[:, 0:1]))
        pr.step()

        def spl(v):
            v.tensor_scalar(g_tok[:], g_tok[:], 0.0, None, ALU.max)
            v.tensor_tensor(g_tok[:], g_tok[:], hsc[:, 0:16], ALU.add)
            v.tensor_tensor(g_tok[:], g_tok[:], nega_rep[:, 0:16], ALU.mult)
        pr.on("vector", spl)
        pr.step()
        if "gpre" in L0A_SKIP:
            continue
        pr.on("tensor", lambda te: te.transpose(ps_k[0:16, :], g_tok[:, :], pr.ident[:]))
        pr.step()
        pr.on("vector", lambda v: v.tensor_copy(G_T[:, :], ps_k[0:16, :]))
        pr.step()
        pr.on("vector", lambda v: v.tensor_tensor_scan(G_T[:, :], G_T[:, :], G_T[:, :],
                                                       0.0, ALU.add, ALU.bypass))
        pr.step()
        pr.on("vector", lambda v: v.tensor_scalar(kts_T[:, :], G_T[:, :],
                                                  G_T[:, 127:128], None, ALU.subtract))
        pr.step()
        pr.on("scalar", lambda sc: sc.activation(kts_T[:, :], kts_T[:, :], AF.Exp,
                                                 scale=-1.0))
        pr.step()

        def trg2(te):
            te.transpose(ps_x[:, 0:16], G_T[:, :], pr.ident[0:16, 0:16])
            te.transpose(ps_x[:, 16:32], kts_T[:, :], pr.ident[0:16, 0:16])
            te.transpose(ps_x[0:1, 32:48], G_T[:, 127:128], pr.ident[0:16, 0:16])
        pr.on("tensor", trg2)
        pr.step()

        def evg(v):
            v.tensor_copy(G_tok[:, :], ps_x[:, 0:16])
            v.tensor_copy(kts_tok[:, :], ps_x[:, 16:32])
            v.tensor_copy(rowt[:, :], ps_x[0:1, 32:48])
        pr.on("vector", evg)
        pr.step()
        pr.on("scalar", lambda sc: sc.activation(gam_tok[:, :], G_tok[:, :], AF.Exp))
        pr.step()
        pr.on("scalar", lambda sc: sc.activation(rowt[:, :], rowt[:, :], AF.Exp))
        pr.step()
        pr.on("tensor", lambda te: te.matmul(ps_x[0:64, 0:16], ones_col[0:1, 0:64],
                                             rowt[0:1, :], start=True, stop=True))
        pr.step()
        pr.on("vector", lambda v: v.tensor_copy(gcrep[:, :], ps_x[0:64, 0:16]))
        pr.step()

        for g8 in (() if "heads" in L0A_SKIP else range(2)):
            H0 = g8 * 8

            if "trkv" not in L0A_SKIP:
                def qcp(v, H0=H0):
                    for i in range(GW):
                        hh = H0 + i
                        kh = hh // 2
                        qft, qrow = kh // 2, (kh % 2) * 64
                        kft, krow = 4 + kh // 2, (kh % 2) * 64
                        vft, vrow = 8 + hh // 2, (hh % 2) * 64
                        v.tensor_copy(Qal[:, i * 128:(i + 1) * 128],
                                      convout[qrow:qrow + 64, qft * 128:(qft + 1) * 128])
                        v.tensor_copy(Kal[:, i * 128:(i + 1) * 128],
                                      convout[krow:krow + 64, kft * 128:(kft + 1) * 128])
                        v.tensor_copy(Val[:, i * 128:(i + 1) * 128],
                                      convout[vrow:vrow + 64, vft * 128:(vft + 1) * 128])
                pr.on("vector", qcp)
                pr.step()
            for rnd in (() if "trkv" in L0A_SKIP else range(4)):
                def trk(te, rnd=rnd):
                    src_b = Kal if rnd < 2 else Val
                    for i in range(4):
                        hi = (rnd % 2) * 4 + i
                        te.transpose(ps_bf[i][:, 0:64],
                                     src_b[:, hi * 128:(hi + 1) * 128],
                                     pr.ident_bf[0:64, 0:64])
                pr.on("tensor", trk)
                pr.step()

                def evk(v, rnd=rnd):
                    dst = Ksd if rnd < 2 else Vsd
                    for i in range(4):
                        hi = (rnd % 2) * 4 + i
                        v.tensor_copy(dst[:, hi * 64:(hi + 1) * 64], ps_bf[i][:, 0:64])
                pr.on("vector", evk)
                pr.step()

            if "gcmd" not in L0A_SKIP:
                def gcmf(v, H0=H0):
                    for i in range(GW):
                        v.tensor_scalar(Gcm[:, i * 128:(i + 1) * 128], onesmat[:, :],
                                        G_tok[:, H0 + i:H0 + i + 1], None, ALU.mult)
                pr.on("vector", gcmf)
                pr.step()

                def gtr(te):
                    for i in range(GW):
                        te.transpose(ps_x[:, i * 128:(i + 1) * 128],
                                     Gcm[:, i * 128:(i + 1) * 128], pr.ident[:])
                pr.on("tensor", gtr)
                pr.step()
                pr.on("vector", lambda v: v.tensor_tensor(Dm_[:, :], Gcm[:, :], ps_x[:, :],
                                                          ALU.subtract))
                pr.step()

                def emin(v):
                    v.tensor_scalar(Em[:, :], Dm_[:, :], 0.0, None, ALU.min)
                    v.tensor_scalar(E2m[:, :], Dm_[:, :], -1.0, 0.0, ALU.mult, ALU.min)
                pr.on("vector", emin)
                pr.step()
                pr.on("scalar", lambda sc: sc.activation(Em[:, :], Em[:, :], AF.Exp))
                pr.step()
                pr.on("scalar", lambda sc: sc.activation(E2m[:, :], E2m[:, :], AF.Exp))
                pr.step()

            if "mkk" not in L0A_SKIP:
                def mkk(te, H0=H0):
                    for i in range(GW):
                        Kfm = Kal[:, i * 128:(i + 1) * 128]
                        Qfm = Qal[:, i * 128:(i + 1) * 128]
                        te.matmul(ps_x[:, i * 128:(i + 1) * 128], Kfm, Kfm,
                                  start=True, stop=True)
                        te.matmul(ps_y[:, i * 128:(i + 1) * 128], Kfm, Qfm,
                                  start=True, stop=True)
                pr.on("tensor", mkk)
                pr.step()

                def mkm(v, H0=H0):
                    v.tensor_tensor(Dm_[:, :], ps_x[:, :], Em[:, :], ALU.mult)
                    for i in range(GW):
                        v.tensor_scalar(Dm_[:, i * 128:(i + 1) * 128],
                                        Dm_[:, i * 128:(i + 1) * 128],
                                        beta_tok[:, H0 + i:H0 + i + 1], None, ALU.mult)
                    v.tensor_tensor(Mbf[:, :], Dm_[:, :], trilS8[:, :], ALU.mult)
                    v.tensor_tensor(Gcm[:, :], ps_y[:, :], E2m[:, :], ALU.mult)
                    v.tensor_tensor(AqT[:, :], Gcm[:, :], triuI8w[:, :], ALU.mult)
                pr.on("vector", mkm)
                pr.step()

            for rnd in (() if "mt" in L0A_SKIP else range(2)):
                def mtr(te, rnd=rnd):
                    for i in range(4):
                        te.transpose(ps_bf[i], Mbf[:, (rnd * 4 + i) * 128:(rnd * 4 + i + 1) * 128],
                                     pr.ident_bf[:])
                pr.on("tensor", mtr)
                pr.step()

                def mte(v, rnd=rnd):
                    for i in range(4):
                        v.tensor_copy(MTbf[:, (rnd * 4 + i) * 128:(rnd * 4 + i + 1) * 128],
                                      ps_bf[i])
                pr.on("vector", mte)
                pr.step()

            def tini(v):
                v.tensor_tensor(Ttbf[:, :], Ibf8[:, :], Mbf[:, :], ALU.subtract)
                v.tensor_tensor(Xs[:, :], Ibf8[:, :], MTbf[:, :], ALU.subtract)
                v.tensor_copy(Bs[:, :], Mbf[:, :])
                v.tensor_copy(BTs[:, :], MTbf[:, :])
            pr.on("vector", tini)
            pr.step()
            for lvl in (() if "tchain" in L0A_SKIP else range(6)):
                def msq(te):
                    for i in range(GW):
                        sl = slice(i * 128, (i + 1) * 128)
                        te.matmul(ps_x[:, sl], BTs[:, sl], Bs[:, sl], start=True, stop=True)
                        te.matmul(ps_y[:, sl], Bs[:, sl], BTs[:, sl], start=True, stop=True)
                pr.on("tensor", msq)
                pr.step()

                def esq(v):
                    v.tensor_copy(Bs[:, :], ps_x[:, :])
                    v.tensor_copy(BTs[:, :], ps_y[:, :])
                    v.tensor_tensor(Bs[:, :], Bs[:, :], Ibf8[:, :], ALU.add)
                    v.tensor_tensor(BTs[:, :], BTs[:, :], Ibf8[:, :], ALU.add)
                pr.on("vector", esq)
                pr.step()

                def mpr(te):
                    for i in range(GW):
                        sl = slice(i * 128, (i + 1) * 128)
                        te.matmul(ps_x[:, sl], Xs[:, sl], Bs[:, sl], start=True, stop=True)
                        te.matmul(ps_y[:, sl], Bs[:, sl], Xs[:, sl], start=True, stop=True)
                pr.on("tensor", mpr)
                pr.step()

                def epr(v, last=(lvl == 5)):
                    v.tensor_copy(Ttbf[:, :], ps_x[:, :])
                    v.tensor_copy(Xs[:, :], ps_y[:, :])
                    if not last:
                        v.tensor_tensor(Bs[:, :], Bs[:, :], Ibf8[:, :], ALU.subtract)
                        v.tensor_tensor(BTs[:, :], BTs[:, :], Ibf8[:, :], ALU.subtract)
                pr.on("vector", epr)
                pr.step()
            pr.on("vector", lambda v: v.tensor_copy(Ttbf[:, :], Xs[:, :]))
            pr.step()

            def bvf(v, H0=H0):
                for i in range(GW):
                    h = H0 + i
                    v.tensor_scalar(BV[:, i * 64:(i + 1) * 64], Vsd[:, i * 64:(i + 1) * 64],
                                    beta_tok[:, h:h + 1], None, ALU.mult)
                    v.tensor_scalar(Gcm[:, 0:1], beta_tok[:, h:h + 1],
                                    gam_tok[:, h:h + 1], None, ALU.mult)
                    v.tensor_scalar(BGK[:, i * 64:(i + 1) * 64], Ksd[:, i * 64:(i + 1) * 64],
                                    Gcm[:, 0:1], None, ALU.mult)
                    v.tensor_scalar(Ktil[:, i * 64:(i + 1) * 64], Ksd[:, i * 64:(i + 1) * 64],
                                    kts_tok[:, h:h + 1], None, ALU.mult)
            pr.on("vector", bvf)
            pr.step()

            def mpw(te):
                for i in range(GW):
                    te.matmul(ps_x[:, i * 64:(i + 1) * 64],
                              Ttbf[:, i * 128:(i + 1) * 128], BV[:, i * 64:(i + 1) * 64],
                              start=True, stop=True)
                    te.matmul(ps_y[0:64, i * 128:(i + 1) * 128],
                              BGK[:, i * 64:(i + 1) * 64], Ttbf[:, i * 128:(i + 1) * 128],
                              start=True, stop=True)
            pr.on("tensor", mpw)
            pr.step()

            def evpw(v):
                v.tensor_copy(Pmat[:, 0:KW], ps_x[:, 0:KW])
                v.tensor_copy(WT[:, :], ps_y[0:64, :])
            pr.on("vector", evpw)
            pr.step()

            if "seq" in L0A_SKIP:
                pr.on("gpsimd", lambda g: g.memset(Osb[:, :], 0.25))
                pr.step()
                continue
            pr.on("vector", lambda v, H0=H0, cur=cur: v.tensor_copy(
                Sbf[:, :], cur[:, H0 * 64:(H0 + GW) * 64]))
            pr.step()

            def mseq(te):
                for i in range(GW):
                    te.matmul(ps_x[:, i * 64:(i + 1) * 64], WT[:, i * 128:(i + 1) * 128],
                              Sbf[:, i * 64:(i + 1) * 64], start=True, stop=True)
                    te.matmul(ps_y[:, i * 64:(i + 1) * 64], Qal[:, i * 128:(i + 1) * 128],
                              Sbf[:, i * 64:(i + 1) * 64], start=True, stop=True)
            pr.on("tensor", mseq)
            pr.step()
            pr.on("vector", lambda v: v.tensor_tensor(Ubf[:, :], Pmat[:, :], ps_x[:, 0:KW],
                                                      ALU.subtract))
            pr.step()

            def mou(te):
                for i in range(GW):
                    te.matmul(ps_x[:, i * 64:(i + 1) * 64], AqT[:, i * 128:(i + 1) * 128],
                              Ubf[:, i * 64:(i + 1) * 64], start=True, stop=True)
                    te.matmul(ps_y[0:64, 512 + i * 64:512 + (i + 1) * 64],
                              Ktil[:, i * 64:(i + 1) * 64], Ubf[:, i * 64:(i + 1) * 64],
                              start=True, stop=True)
            pr.on("tensor", mou)
            pr.step()

            def evo(v, H0=H0, cur=cur, nxt=nxt):
                for i in range(GW):
                    h = H0 + i
                    v.tensor_scalar(Osb[:, h * 64:(h + 1) * 64], ps_y[:, i * 64:(i + 1) * 64],
                                    gam_tok[:, h:h + 1], 0.125, ALU.mult, ALU.mult)
                v.tensor_tensor(Osb[:, H0 * 64:(H0 + GW) * 64],
                                Osb[:, H0 * 64:(H0 + GW) * 64], ps_x[:, 0:KW], ALU.add)
                for i in range(GW):
                    h = H0 + i
                    v.scalar_tensor_tensor(nxt[:, h * 64:(h + 1) * 64],
                                           cur[:, h * 64:(h + 1) * 64],
                                           gcrep[:, h:h + 1],
                                           ps_y[0:64, 512 + i * 64:512 + (i + 1) * 64],
                                           ALU.mult, ALU.add)
            pr.on("vector", evo)
            pr.step()
        if "heads" in L0A_SKIP:
            pr.on("gpsimd", lambda g: g.memset(Osb[:, :], 0.5))
            pr.step()
        pr.on("scalar", lambda sc: sc.activation(hsc[:, :], z_tok[:, :], AF.Silu))
        pr.step()
        pr.on("vector", lambda v: v.tensor_tensor(hsc[:, :], hsc[:, :], Osb[:, :], ALU.mult))
        pr.step()
        pr.on("scalar", lambda sc: sc.activation(yt[:, :], hsc[:, :], AF.Square))
        pr.step()
        pr.on("vector", lambda v: v.reduce_sum(
            rstd16[:, :].rearrange("p (a o) -> p a o", o=1),
            yt[:, :].rearrange("p (a n) -> p a n", a=16), axis=AX))
        pr.step()
        pr.on("vector", lambda v: v.tensor_scalar(rstd16[:, :], rstd16[:, :], 1.0 / 64,
                                                  1e-6, ALU.mult, ALU.add))
        pr.step()
        pr.on("scalar", lambda sc: sc.activation(rstd16[:, :], rstd16[:, :], AF.Sqrt))
        pr.step()
        pr.on("vector", lambda v: v.reciprocal(rstd16[:, :], rstd16[:, :]))
        pr.step()

        def hnf(v):
            for h in range(16):
                v.tensor_scalar(hnorm[:, h * 64:(h + 1) * 64], hsc[:, h * 64:(h + 1) * 64],
                                rstd16[:, h:h + 1], None, ALU.mult)
        pr.on("vector", hnf)
        pr.step()
        pr.on("vector", lambda v: v.tensor_tensor(hnorm[:, :], hnorm[:, :], nw_rep[:, :],
                                                  ALU.mult))
        pr.step()
        for j0 in (0, 4):
            def trh(te, j0=j0):
                for j in range(4):
                    te.transpose(ps_bf[j], hnorm[:, (j0 + j) * 128:(j0 + j + 1) * 128],
                                 pr.ident_bf[:])
            pr.on("tensor", trh)
            pr.step()

            def evh(v, j0=j0):
                for j in range(4):
                    v.tensor_copy(hT[:, (j0 + j) * 128:(j0 + j + 1) * 128], ps_bf[j])
            pr.on("vector", evh)
            pr.step()
        for ch in range(2):
            def mmo(te, ch=ch):
                for cd in range(8):
                    te.matmul(ps_a[:], hT[:, cd * 128:(cd + 1) * 128],
                              woutT[:, cd * D + ch * 512: cd * D + ch * 512 + 512],
                              start=(cd == 0), stop=(cd == 7))
            pr.on("tensor", mmo)
            pr.step()
            pr.on("vector", lambda v, ch=ch: v.tensor_tensor(
                yt[:, ch * 512:(ch + 1) * 512], ps_a[:], xt[:, ch * 512:(ch + 1) * 512],
                ALU.add))
            pr.step()
        pr.dma(y_h[c * 128:(c + 1) * 128, :], yt[:, :])
        pr.step()
    if standalone:
        return pr.build()
    return


def host_prep_l0a(inp):
    cw = inp["L0_attn_conv_w"][:, 0, :]
    convcols = np.ascontiguousarray(cw.reshape(16, 128, 4).transpose(1, 0, 2).reshape(128, 64))
    return {
        "wqkvT": np.ascontiguousarray(inp["L0_attn_W_qkv"].T),
        "convcols": convcols,
        "wzT": np.ascontiguousarray(inp["L0_attn_W_z"].T),
        "wbT": np.ascontiguousarray(inp["L0_attn_W_b"].T),
        "waT": np.ascontiguousarray(inp["L0_attn_W_a"].T),
        "dtb_row": np.ascontiguousarray(inp["L0_attn_dt_bias"].reshape(1, 16)),
        "alog_row": np.ascontiguousarray(inp["L0_attn_A_log"].reshape(1, 16)),
        "nw_tiled_row": np.ascontiguousarray(np.tile(inp["L0_attn_norm_weight"], 16).reshape(1, D)),
        "woutT": np.ascontiguousarray(inp["L0_attn_W_out"].T),
    }


def build_l1a(pr=None, px='', x_h=None, y_h=None):
    standalone = pr is None
    if standalone:
        pr = Prog()
    if x_h is None:
        x_h = pr.dram_in(px + "x", [S, D])
    wq_h = pr.dram_in(px + "wq", [D, D])
    wk_h = pr.dram_in(px + "wk", [D, D])
    wv_h = pr.dram_in(px + "wv", [D, D])
    wgam_h = pr.dram_in(px + "wgam", [D, 16])
    wf1_h = pr.dram_in(px + "wf1", [D, 64])
    wf2_h = pr.dram_in(px + "wf2", [64, D])
    wog1_h = pr.dram_in(px + "wog1", [D, 64])
    wog2_h = pr.dram_in(px + "wog2", [64, D])
    nw_h = pr.dram_in(px + "nw_row", [1, D])
    wo_h = pr.dram_in(px + "wo", [D, D])
    if y_h is None:
        y_h = pr.dram_out(px + "y", [S, D])
    pr.init_consts()
    onecol = pr.constcol(1.0)
    pr.step()

    ld = Loader(pr)
    ones_col = pr.sbuf([1, 128], F32)
    pr.on("gpsimd", lambda g: g.memset(ones_col[:], 1.0))
    pr.step()
    ps_rep = pr.psum([128, 512], F32)

    def make_rep(row_src, n):
        row = ld.load_f32(row_src, 1, n)
        rep = pr.sbuf([128, n], F32)
        for c0 in range(0, n, 512):
            cw = min(512, n - c0)
            pr.on("tensor", lambda te, c0=c0, cw=cw, row=row: te.matmul(
                ps_rep[:, 0:cw], ones_col[0:1, :], row[0:1, c0:c0 + cw],
                start=True, stop=True))
            pr.step()
            pr.on("vector", lambda v, c0=c0, cw=cw, rep=rep: v.tensor_copy(
                rep[:, c0:c0 + cw], ps_rep[:, 0:cw]))
            pr.step()
        return rep

    wq = ld.load_bf(wq_h, D, D)
    wk = ld.load_bf(wk_h, D, D)
    wv = ld.load_bf(wv_h, D, D)
    wgam = ld.load_bf(wgam_h, D, 16)
    wf1 = ld.load_bf(wf1_h, D, 64)
    wog1 = ld.load_bf(wog1_h, D, 64)
    wo = ld.load_bf(wo_h, D, D)
    wf2 = pr.sbuf([64, D], BF16)
    wog2 = pr.sbuf([64, D], BF16)
    for dst, hh in ((wf2, wf2_h), (wog2, wog2_h)):
        pr.step()
        pr.dma(pr.stage[0:64, 0:D], hh)
        pr.step()
        pr.on("vector", lambda v, dst=dst: v.tensor_copy(dst[:, :], pr.stage[0:64, 0:D]))
        pr.step()
    nw_rep = make_rep(nw_h, D)

    trilS64 = make_tri(pr, 64, strict=True, lower=True)
    triuI64 = make_tri(pr, 64, strict=False, lower=False)
    triuS64 = make_tri(pr, 64, strict=True, lower=False)

    xt = pr.sbuf([128, D], F32)
    ln_tok = pr.sbuf([128, D], BF16)
    lnT = pr.sbuf([128, 8 * 128], BF16)
    q_tok = pr.sbuf([128, D], F32)
    kn_tok = pr.sbuf([128, D], F32)
    v_tok = pr.sbuf([128, D], BF16)
    a_tok = pr.sbuf([128, D], F32)
    gam16 = pr.sbuf([128, 16], F32)
    f1fm = pr.sbuf([64, 128], BF16)
    Gfm = pr.sbuf([128, 8 * 128], F32)
    gkfm = pr.sbuf([128, 8 * 128], F32)
    eGtot = pr.sbuf([128, 8], F32)
    eGcols = pr.sbuf([64, 16], F32)
    Fq = pr.sbuf([128, D], F32)
    Fk = pr.sbuf([128, D], F32)
    Fc = pr.sbuf([128, D], F32)
    Fs = pr.sbuf([128, D], F32)
    FA = pr.sbuf([128, D], F32)
    qt_p = pr.sbuf([128, 512], BF16)
    kt_p = pr.sbuf([128, 512], BF16)
    kc_p = pr.sbuf([128, 512], BF16)
    Qt_p = pr.sbuf([128, 512], BF16)
    At_p = pr.sbuf([128, 512], BF16)
    at_p = pr.sbuf([128, 512], BF16)
    Kb_p = pr.sbuf([128, 512], BF16)
    GW = 8
    MW = GW * 128
    KW = GW * 64
    qtF = pr.sbuf([64, MW], BF16)
    ktF = pr.sbuf([64, MW], BF16)
    kcF = pr.sbuf([64, MW], BF16)
    QtF = pr.sbuf([64, MW], BF16)
    AtF = pr.sbuf([64, MW], BF16)
    atF = pr.sbuf([64, MW], BF16)
    Nbf = pr.sbuf([128, MW], BF16)
    NTbf = pr.sbuf([128, MW], BF16)
    AqT = pr.sbuf([128, MW], BF16)
    Bs = pr.sbuf([128, MW], BF16)
    BTs = pr.sbuf([128, MW], BF16)
    Xs = pr.sbuf([128, MW], BF16)
    Ttbf = pr.sbuf([128, MW], BF16)
    Ibf8 = pr.sbuf([128, MW], BF16)
    rhsb = pr.sbuf([128, KW], BF16)
    Ubf = pr.sbuf([128, KW], BF16)
    Osb = pr.sbuf([128, D], F32)
    Ssb = [pr.sbuf([64, 16 * 64], F32) for _ in range(2)]
    Sbf = pr.sbuf([64, KW], BF16)
    hsc = pr.sbuf([128, D], F32)
    hnorm = pr.sbuf([128, D], BF16)
    hT = pr.sbuf([128, 8 * 128], BF16)
    yt = pr.sbuf([128, D], F32)
    st4 = pr.sbuf([128, 16], F32)
    ps_bf = pr.psum_bf_pool(4)
    ps_a = pr.psum([128, 512], F32)
    ps_x = pr.psum([128, 1024], F32)
    ps_y = pr.psum([128, 1024], F32)
    lnbufs = None

    def tile8(dst, src_ap):
        def f(v, dst=dst, src_ap=src_ap):
            for i in range(GW):
                v.tensor_copy(dst[:, i * 128:(i + 1) * 128], src_ap)
        pr.on("vector", f)
        pr.step()
    tile8(Ibf8, pr.ident_bf[:, :])

    pr.on("gpsimd", lambda g: g.memset(Ssb[0][:], 0.0))
    pr.step()
    pr.on("gpsimd", lambda g: g.memset(Nbf[:], 0.0))
    pr.step()
    pr.on("gpsimd", lambda g: g.memset(NTbf[:], 0.0))
    pr.step()
    pr.on("gpsimd", lambda g: g.memset(AqT[:], 0.0))
    pr.step()

    for c in range(NT):
        cur, nxt = Ssb[c % 2], Ssb[(c + 1) % 2]
        pr.dma(xt[:, :], x_h[c * 128:(c + 1) * 128, :])
        pr.step()
        lnbufs = emit_ln_transpose_tile(pr, xt[:, :], ln_tok,
                                        lambda j: j * 128, ps_bf, lnbufs, lnT)
        for (wmat, n_out, tag) in ((wq, D, "q"), (wk, D, "k"), (wv, D, "v")):
            for ch in range(n_out // 512):
                def mmp(te, wmat=wmat, ch=ch):
                    for cd in range(8):
                        te.matmul(ps_a[:], lnT[:, cd * 128:(cd + 1) * 128],
                                  wmat[:, cd * D + ch * 512: cd * D + ch * 512 + 512],
                                  start=(cd == 0), stop=(cd == 7))
                pr.on("tensor", mmp)
                pr.step()
                if tag == "q":
                    pr.on("scalar", lambda sc, ch=ch: sc.activation(
                        q_tok[:, ch * 512:(ch + 1) * 512], ps_a[:], AF.Silu))
                elif tag == "k":
                    pr.on("scalar", lambda sc, ch=ch: sc.activation(
                        kn_tok[:, ch * 512:(ch + 1) * 512], ps_a[:], AF.Silu))
                else:
                    pr.on("vector", lambda v, ch=ch: v.tensor_copy(
                        v_tok[:, ch * 512:(ch + 1) * 512], ps_a[:]))
                pr.step()
        pr.on("scalar", lambda sc: sc.activation(hsc[:, :], kn_tok[:, :], AF.Square))
        pr.step()
        pr.on("vector", lambda v: v.reduce_sum(
            st4[:, :].rearrange("p (a o) -> p a o", o=1),
            hsc[:, :].rearrange("p (a n) -> p a n", a=16), axis=AX))
        pr.step()
        pr.on("scalar", lambda sc: sc.activation(st4[:, :], st4[:, :], AF.Sqrt))
        pr.step()
        pr.on("vector", lambda v: v.tensor_scalar(st4[:, :], st4[:, :], 1e-12, None,
                                                  ALU.max))
        pr.step()
        pr.on("vector", lambda v: v.reciprocal(st4[:, :], st4[:, :]))
        pr.step()

        def knn(v):
            for h in range(16):
                v.tensor_scalar(kn_tok[:, h * 64:(h + 1) * 64], kn_tok[:, h * 64:(h + 1) * 64],
                                st4[:, h:h + 1], None, ALU.mult)
        pr.on("vector", knn)
        pr.step()
        def mmg(te):
            for cd in range(8):
                te.matmul(ps_x[:, 0:16], lnT[:, cd * 128:(cd + 1) * 128],
                          wgam[:, cd * 16:(cd + 1) * 16], start=(cd == 0), stop=(cd == 7))
            for cd in range(8):
                te.matmul(ps_x[:, 16:80], lnT[:, cd * 128:(cd + 1) * 128],
                          wf1[:, cd * 64:(cd + 1) * 64], start=(cd == 0), stop=(cd == 7))
            for cd in range(8):
                te.matmul(ps_x[:, 80:144], lnT[:, cd * 128:(cd + 1) * 128],
                          wog1[:, cd * 64:(cd + 1) * 64], start=(cd == 0), stop=(cd == 7))
        pr.on("tensor", mmg)
        pr.step()
        pr.on("scalar", lambda sc: sc.activation(gam16[:, :], ps_x[:, 0:16], AF.Sigmoid))
        pr.step()
        pr.on("vector", lambda v: v.tensor_scalar(gam16[:, :], gam16[:, :], -1.0, None,
                                                  ALU.mult))
        pr.step()
        def evf(v):
            v.tensor_copy(hsc[:, 0:64], ps_x[:, 16:80])
            v.tensor_copy(hsc[:, 64:128], ps_x[:, 80:144])
        pr.on("vector", evf)
        pr.step()
        pr.on("vector", lambda v: v.tensor_copy(ln_tok[:, 0:128], hsc[:, 0:128]))
        pr.step()

        def trf(te):
            te.transpose(ps_bf[0], ln_tok[:, 0:128], pr.ident_bf[:])
        pr.on("tensor", trf)
        pr.step()
        pr.on("vector", lambda v: v.tensor_copy(f1fm[:, :], ps_bf[0][0:64, :]))
        pr.step()
        ogfm = ps_bf[0][64:128, :]
        og1fm = pr.stage_bf

        def cpog(v):
            v.tensor_copy(og1fm[0:64, 0:128], ps_bf[0][64:128, :])
        pr.on("vector", cpog)
        pr.step()
        for i in range(8):
            def mmgk(te, i=i):
                te.matmul(ps_x[:, i * 128:(i + 1) * 128], wf2[:, i * 128:(i + 1) * 128],
                          f1fm[:, :], start=True, stop=True)
            pr.on("tensor", mmgk)
            pr.step()
        def lsig1(v):
            v.tensor_copy(gkfm[:, :], ps_x[:, :])
        pr.on("vector", lsig1)
        pr.step()
        pr.on("scalar", lambda sc: sc.activation(Gfm[:, :], gkfm[:, :], AF.Abs))
        pr.step()
        pr.on("scalar", lambda sc: sc.activation(Gfm[:, :], Gfm[:, :], AF.Exp, scale=-1.0))
        pr.step()
        pr.on("scalar", lambda sc: sc.activation(Gfm[:, :], Gfm[:, :], AF.Ln,
                                                 bias=onecol[:, 0:1]))
        pr.step()

        def lsig2(v):
            v.tensor_scalar(gkfm[:, :], gkfm[:, :], 0.0, None, ALU.min)
            v.tensor_tensor(gkfm[:, :], gkfm[:, :], Gfm[:, :], ALU.subtract)
        pr.on("vector", lsig2)
        pr.step()
        def trfield(dstT, negate):
            pr.on("scalar", lambda sc: sc.activation(hsc[:, :], gkfm[:, :], AF.Exp,
                                                     scale=(-1.0 if negate else 1.0)))
            pr.step()

            def trx(te):
                for j in range(8):
                    te.transpose(ps_x[:, j * 128:(j + 1) * 128],
                                 hsc[:, j * 128:(j + 1) * 128], pr.ident[:])
            pr.on("tensor", trx)
            pr.step()
            pr.on("vector", lambda v, dstT=dstT: v.tensor_copy(dstT[:, :], ps_x[:, :]))
            pr.step()
        trfield(FA, False)

        def mka(v):
            for h in range(16):
                v.tensor_scalar(a_tok[:, h * 64:(h + 1) * 64],
                                kn_tok[:, h * 64:(h + 1) * 64],
                                gam16[:, h:h + 1], None, ALU.mult)
            v.tensor_tensor(a_tok[:, :], a_tok[:, :], FA[:, :], ALU.mult)
        pr.on("vector", mka)
        pr.step()
        def scn(v):
            for i in range(8):
                v.tensor_tensor_scan(Gfm[:, i * 128:(i + 1) * 128],
                                     gkfm[:, i * 128:(i + 1) * 128],
                                     gkfm[:, i * 128:(i + 1) * 128], 0.0,
                                     ALU.add, ALU.bypass)
        pr.on("vector", scn)
        pr.step()
        def ffq(v):
            for i in range(8):
                sl = slice(i * 128, i * 128 + 64)
                sr = slice(i * 128 + 64, (i + 1) * 128)
                v.tensor_copy(gkfm[:, sl], Gfm[:, sl])
                v.tensor_scalar(gkfm[:, sr], Gfm[:, sr], Gfm[:, i * 128 + 63:i * 128 + 64],
                                None, ALU.subtract)
        pr.on("vector", ffq)
        pr.step()

        trfield(Fq, False)
        trfield(Fk, True)
        def ffa(v):
            v.tensor_copy(gkfm[:, :], Gfm[:, :])
        pr.on("vector", ffa)
        pr.step()
        trfield(FA, False)
        def ffc(v):
            for i in range(8):
                v.tensor_scalar(gkfm[:, i * 128:(i + 1) * 128],
                                Gfm[:, i * 128:(i + 1) * 128],
                                Gfm[:, i * 128 + 63:i * 128 + 64], None, ALU.subtract)
        pr.on("vector", ffc)
        pr.step()
        trfield(Fc, True)
        def ffs(v):
            for i in range(8):
                v.tensor_scalar(gkfm[:, i * 128:(i + 1) * 128],
                                Gfm[:, i * 128:(i + 1) * 128],
                                Gfm[:, i * 128 + 127:i * 128 + 128], None, ALU.subtract)
        pr.on("vector", ffs)
        pr.step()
        trfield(Fs, True)
        def egt(v):
            for i in range(8):
                v.tensor_copy(eGtot[:, i:i + 1], Gfm[:, i * 128 + 127:i * 128 + 128])
        pr.on("vector", egt)
        pr.step()
        pr.on("scalar", lambda sc: sc.activation(eGtot[:, :], eGtot[:, :], AF.Exp))
        pr.step()

        def egc(v):
            for h in range(16):
                ti, trow = h // 2, (h % 2) * 64
                v.tensor_copy(eGcols[:, h:h + 1], eGtot[trow:trow + 64, ti:ti + 1])
        pr.on("vector", egc)
        pr.step()
        for g8 in range(2):
            H0 = g8 * 8
            gsl = slice(H0 * 64, (H0 + 8) * 64)

            def prod(v, gsl=gsl):
                v.tensor_tensor(qt_p[:, :], q_tok[:, gsl], Fq[:, gsl], ALU.mult)
                v.tensor_tensor(kt_p[:, :], kn_tok[:, gsl], Fk[:, gsl], ALU.mult)
                v.tensor_tensor(kc_p[:, :], kn_tok[:, gsl], Fc[:, gsl], ALU.mult)
                v.tensor_tensor(Qt_p[:, :], q_tok[:, gsl], FA[:, gsl], ALU.mult)
                v.tensor_tensor(At_p[:, :], a_tok[:, gsl], FA[:, gsl], ALU.mult)
                v.tensor_tensor(at_p[:, :], a_tok[:, gsl], Fq[:, gsl], ALU.mult)
                v.tensor_tensor(Kb_p[:, :], kn_tok[:, gsl], Fs[:, gsl], ALU.mult)
            pr.on("vector", prod)
            pr.step()
            for fld_src, fld_dst in ((qt_p, qtF), (kt_p, ktF), (kc_p, kcF),
                                     (Qt_p, QtF), (At_p, AtF), (at_p, atF)):
                for rnd in range(2):
                    def trp(te, fld_src=fld_src, rnd=rnd):
                        for j in range(4):
                            hi = rnd * 4 + j
                            te.transpose(ps_bf[j][0:64, 0:128],
                                         fld_src[:, hi * 64:(hi + 1) * 64],
                                         pr.ident_bf[:])
                    pr.on("tensor", trp)
                    pr.step()

                    def evp(v, fld_dst=fld_dst, rnd=rnd):
                        for j in range(4):
                            i = rnd * 4 + j
                            v.tensor_copy(fld_dst[:, i * 128:(i + 1) * 128],
                                          ps_bf[j][0:64, :])
                    pr.on("vector", evp)
                    pr.step()
            def mmn(te):
                for i in range(GW):
                    b = i * 128
                    a1 = atF[:, b:b + 64]
                    a2 = atF[:, b + 64:b + 128]
                    k1 = ktF[:, b:b + 64]
                    k2 = ktF[:, b + 64:b + 128]
                    kc1 = kcF[:, b:b + 64]
                    q1 = qtF[:, b:b + 64]
                    q2 = qtF[:, b + 64:b + 128]
                    te.matmul(ps_x[0:64, b:b + 64], a1, k1, start=True, stop=True)
                    te.matmul(ps_x[64:128, b + 64:b + 128], a2, k2, start=True, stop=True)
                    te.matmul(ps_x[64:128, b:b + 64], a2, kc1, start=True, stop=True)
                    te.matmul(ps_y[0:64, b:b + 64], k1, q1, start=True, stop=True)
                    te.matmul(ps_y[64:128, b + 64:b + 128], k2, q2, start=True, stop=True)
                    te.matmul(ps_y[0:64, b + 64:b + 128], kc1, q2, start=True, stop=True)
            pr.on("tensor", mmn)
            pr.step()

            def evn(v):
                for i in range(GW):
                    b = i * 128
                    v.tensor_tensor(Nbf[0:64, b:b + 64], ps_x[0:64, b:b + 64],
                                    trilS64[:, :], ALU.mult)
                    v.tensor_tensor(Nbf[64:128, b + 64:b + 128],
                                    ps_x[64:128, b + 64:b + 128], trilS64[:, :], ALU.mult)
                    v.tensor_copy(Nbf[64:128, b:b + 64], ps_x[64:128, b:b + 64])
                    v.tensor_tensor(AqT[0:64, b:b + 64], ps_y[0:64, b:b + 64],
                                    triuI64[:, :], ALU.mult)
                    v.tensor_tensor(AqT[64:128, b + 64:b + 128],
                                    ps_y[64:128, b + 64:b + 128], triuI64[:, :], ALU.mult)
                    v.tensor_copy(AqT[0:64, b + 64:b + 128], ps_y[0:64, b + 64:b + 128])
            pr.on("vector", evn)
            pr.step()
            def mmnt(te):
                for i in range(GW):
                    b = i * 128
                    a1 = atF[:, b:b + 64]
                    a2 = atF[:, b + 64:b + 128]
                    k1 = ktF[:, b:b + 64]
                    k2 = ktF[:, b + 64:b + 128]
                    kc1 = kcF[:, b:b + 64]
                    te.matmul(ps_x[0:64, b:b + 64], k1, a1, start=True, stop=True)
                    te.matmul(ps_x[64:128, b + 64:b + 128], k2, a2, start=True, stop=True)
                    te.matmul(ps_x[0:64, b + 64:b + 128], kc1, a2, start=True, stop=True)
            pr.on("tensor", mmnt)
            pr.step()

            def evnt(v):
                for i in range(GW):
                    b = i * 128
                    v.tensor_tensor(NTbf[0:64, b:b + 64], ps_x[0:64, b:b + 64],
                                    triuS64[:, :], ALU.mult)
                    v.tensor_tensor(NTbf[64:128, b + 64:b + 128],
                                    ps_x[64:128, b + 64:b + 128], triuS64[:, :], ALU.mult)
                    v.tensor_copy(NTbf[0:64, b + 64:b + 128], ps_x[0:64, b + 64:b + 128])
            pr.on("vector", evnt)
            pr.step()
            def tini(v):
                v.tensor_tensor(Ttbf[:, :], Ibf8[:, :], Nbf[:, :], ALU.add)
                v.tensor_tensor(Xs[:, :], Ibf8[:, :], NTbf[:, :], ALU.add)
                v.tensor_copy(Bs[:, :], Nbf[:, :])
                v.tensor_copy(BTs[:, :], NTbf[:, :])
            pr.on("vector", tini)
            pr.step()
            for lvl in range(6):
                def msq(te):
                    for i in range(GW):
                        sl = slice(i * 128, (i + 1) * 128)
                        te.matmul(ps_x[:, sl], BTs[:, sl], Bs[:, sl], start=True, stop=True)
                        te.matmul(ps_y[:, sl], Bs[:, sl], BTs[:, sl], start=True, stop=True)
                pr.on("tensor", msq)
                pr.step()

                def esq(v):
                    v.tensor_copy(Bs[:, :], ps_x[:, :])
                    v.tensor_copy(BTs[:, :], ps_y[:, :])
                    v.tensor_tensor(Bs[:, :], Bs[:, :], Ibf8[:, :], ALU.add)
                    v.tensor_tensor(BTs[:, :], BTs[:, :], Ibf8[:, :], ALU.add)
                pr.on("vector", esq)
                pr.step()

                def mpr(te):
                    for i in range(GW):
                        sl = slice(i * 128, (i + 1) * 128)
                        te.matmul(ps_x[:, sl], Xs[:, sl], Bs[:, sl], start=True, stop=True)
                        te.matmul(ps_y[:, sl], Bs[:, sl], Xs[:, sl], start=True, stop=True)
                pr.on("tensor", mpr)
                pr.step()

                def epr(v, last=(lvl == 5)):
                    v.tensor_copy(Ttbf[:, :], ps_x[:, :])
                    v.tensor_copy(Xs[:, :], ps_y[:, :])
                    if not last:
                        v.tensor_tensor(Bs[:, :], Bs[:, :], Ibf8[:, :], ALU.subtract)
                        v.tensor_tensor(BTs[:, :], BTs[:, :], Ibf8[:, :], ALU.subtract)
                pr.on("vector", epr)
                pr.step()
            pr.on("vector", lambda v: v.tensor_copy(Ttbf[:, :], Xs[:, :]))
            pr.step()

            pr.on("vector", lambda v, H0=H0, cur=cur: v.tensor_copy(
                Sbf[:, :], cur[:, H0 * 64:(H0 + GW) * 64]))
            pr.step()

            def mmr(te):
                for i in range(GW):
                    te.matmul(ps_x[:, i * 64:(i + 1) * 64], AtF[:, i * 128:(i + 1) * 128],
                              Sbf[:, i * 64:(i + 1) * 64], start=True, stop=True)
            pr.on("tensor", mmr)
            pr.step()

            def evr(v, H0=H0):
                v.tensor_tensor(rhsb[:, :], v_tok[:, H0 * 64:(H0 + GW) * 64],
                                ps_x[:, 0:KW], ALU.add)
            pr.on("vector", evr)
            pr.step()

            def mmu(te):
                for i in range(GW):
                    te.matmul(ps_x[:, i * 64:(i + 1) * 64], Ttbf[:, i * 128:(i + 1) * 128],
                              rhsb[:, i * 64:(i + 1) * 64], start=True, stop=True)
            pr.on("tensor", mmu)
            pr.step()
            pr.on("vector", lambda v: v.tensor_copy(Ubf[:, :], ps_x[:, 0:KW]))
            pr.step()
            def mmo(te):
                for i in range(GW):
                    te.matmul(ps_y[:, i * 64:(i + 1) * 64], QtF[:, i * 128:(i + 1) * 128],
                              Sbf[:, i * 64:(i + 1) * 64], start=True, stop=False)
                    te.matmul(ps_y[:, i * 64:(i + 1) * 64], AqT[:, i * 128:(i + 1) * 128],
                              Ubf[:, i * 64:(i + 1) * 64], start=False, stop=True)
            pr.on("tensor", mmo)
            pr.step()

            def mmd(te):
                for i in range(GW):
                    te.matmul(ps_x[0:64, 512 + i * 64:512 + (i + 1) * 64],
                              Kb_p[:, i * 64:(i + 1) * 64],
                              Ubf[:, i * 64:(i + 1) * 64], start=True, stop=True)
            pr.on("tensor", mmd)
            pr.step()

            def evo(v, H0=H0, cur=cur, nxt=nxt):
                v.tensor_copy(Osb[:, H0 * 64:(H0 + GW) * 64], ps_y[:, 0:KW])
                for i in range(GW):
                    h = H0 + i
                    v.scalar_tensor_tensor(nxt[:, h * 64:(h + 1) * 64],
                                           cur[:, h * 64:(h + 1) * 64],
                                           eGcols[:, h:h + 1],
                                           ps_x[0:64, 512 + i * 64:512 + (i + 1) * 64],
                                           ALU.mult, ALU.add)
            pr.on("vector", evo)
            pr.step()

        for ch in range(2):
            def mmog(te, ch=ch):
                te.matmul(ps_a[:], og1fm[0:64, 0:128],
                          wog2[:, ch * 512:(ch + 1) * 512], start=True, stop=True)
            pr.on("tensor", mmog)
            pr.step()
            pr.on("scalar", lambda sc, ch=ch: sc.activation(
                hsc[:, ch * 512:(ch + 1) * 512], ps_a[:], AF.Sigmoid))
            pr.step()
        pr.on("vector", lambda v: v.tensor_tensor(hsc[:, :], Osb[:, :], hsc[:, :],
                                                  ALU.mult))
        pr.step()
        def lna(v):
            v.reduce_sum(st4[:, 0:1], hsc[:, :], axis=AX)
        pr.on("vector", lna)
        pr.on("scalar", lambda sc: sc.activation(yt[:, :], hsc[:, :], AF.Square))
        pr.step()

        def lnb(v):
            v.reduce_sum(st4[:, 1:2], yt[:, :], axis=AX)
            v.tensor_scalar(st4[:, 0:1], st4[:, 0:1], 1.0 / D, None, ALU.mult)
            v.tensor_scalar(st4[:, 1:2], st4[:, 1:2], 1.0 / D, None, ALU.mult)
            v.tensor_tensor(st4[:, 2:3], st4[:, 0:1], st4[:, 0:1], ALU.mult)
            v.tensor_tensor(st4[:, 1:2], st4[:, 1:2], st4[:, 2:3], ALU.subtract)
            v.tensor_scalar(st4[:, 1:2], st4[:, 1:2], 1e-5, None, ALU.add)
        pr.on("vector", lnb)
        pr.step()
        pr.on("scalar", lambda sc: sc.activation(st4[:, 1:2], st4[:, 1:2], AF.Sqrt))
        pr.step()
        pr.on("vector", lambda v: v.reciprocal(st4[:, 2:3], st4[:, 1:2]))
        pr.step()

        def nrm2(v):
            v.tensor_scalar(hsc[:, :], hsc[:, :], st4[:, 0:1], st4[:, 2:3],
                            ALU.subtract, ALU.mult)
            v.tensor_tensor(hnorm[:, :], hsc[:, :], nw_rep[:, :], ALU.mult)
        pr.on("vector", nrm2)
        pr.step()
        for j0 in (0, 4):
            def trh(te, j0=j0):
                for j in range(4):
                    te.transpose(ps_bf[j], hnorm[:, (j0 + j) * 128:(j0 + j + 1) * 128],
                                 pr.ident_bf[:])
            pr.on("tensor", trh)
            pr.step()

            def evh(v, j0=j0):
                for j in range(4):
                    v.tensor_copy(hT[:, (j0 + j) * 128:(j0 + j + 1) * 128], ps_bf[j])
            pr.on("vector", evh)
            pr.step()
        for ch in range(2):
            def mmw(te, ch=ch):
                for cd in range(8):
                    te.matmul(ps_a[:], hT[:, cd * 128:(cd + 1) * 128],
                              wo[:, cd * D + ch * 512: cd * D + ch * 512 + 512],
                              start=(cd == 0), stop=(cd == 7))
            pr.on("tensor", mmw)
            pr.step()
            pr.on("vector", lambda v, ch=ch: v.tensor_tensor(
                yt[:, ch * 512:(ch + 1) * 512], ps_a[:], xt[:, ch * 512:(ch + 1) * 512],
                ALU.add))
            pr.step()
        pr.dma(y_h[c * 128:(c + 1) * 128, :], yt[:, :])
        pr.step()
    if standalone:
        return pr.build()
    return


def host_prep_l1a(inp):
    return {
        "wq": inp["L1_attn_Wq"], "wk": inp["L1_attn_Wk"], "wv": inp["L1_attn_Wv"],
        "wgam": inp["L1_attn_Wgamma"], "wf1": inp["L1_attn_Wf1"], "wf2": inp["L1_attn_Wf2"],
        "wog1": inp["L1_attn_Wog1"], "wog2": inp["L1_attn_Wog2"],
        "nw_row": np.ascontiguousarray(inp["L1_attn_norm_w"].reshape(1, -1)),
        "wo": inp["L1_attn_Wo"],
    }


_PROGS = {}

_BLOCKS = [
    ("l0a", build_l0a, host_prep_l0a),
    ("l0f", build_l0f, host_prep_l0f),
    ("l1a", build_l1a, host_prep_l1a),
    ("l1f", build_l1f, lambda inp: {
        "W_rT": np.ascontiguousarray(inp["L1_ffn_W_r"].T),
        "W_kT": np.ascontiguousarray(inp["L1_ffn_W_k"].T),
        "W_vT": np.ascontiguousarray(inp["L1_ffn_W_v"].T),
        "mu_r_cols": np.ascontiguousarray(inp["L1_ffn_mu_r"].reshape(8, 128).T),
        "mu_k_cols": np.ascontiguousarray(inp["L1_ffn_mu_k"].reshape(8, 128).T),
    }),
    ("l2a", build_l2a, host_prep_l2a),
    ("l2f", build_l2f, host_prep_l2f),
    ("l3a", build_l3a, host_prep_l3a),
    ("l3f", build_l3f, host_prep_l3f),
]


def build_fused():
    pr = Prog()
    pr.init_consts()
    x0 = pr.dram_in("x", [S, D])
    xs = [x0] + [pr.dram_tmp("xt%d" % i, [S, D]) for i in range(len(_BLOCKS) - 1)]
    y = pr.dram_out("y", [S, D])
    xs.append(y)
    for bi, (name, build_fn, _) in enumerate(_BLOCKS):
        m = pr.mark()
        build_fn(pr=pr, px=name + "_", x_h=xs[bi], y_h=xs[bi + 1])
        pr.free_to(m)
    return pr.build()


def _weight_map(inputs):
    wmap = {}
    for name, _, prep_fn in _BLOCKS:
        for k, v in prep_fn(inputs).items():
            wmap[name + "_" + k] = v
    return wmap


def kernel(**inputs):
    x = np.ascontiguousarray(np.asarray(inputs["L0_attn_hidden_states"], np.float32))
    B = x.shape[0]
    assert B == NCORES
    if "fused" not in _PROGS:
        _PROGS["fused"] = build_fused()
    nc = _PROGS["fused"]
    wmap = _weight_map(inputs)
    in_maps = [{"x": np.ascontiguousarray(x[b]), **wmap} for b in range(B)]
    res = run_bass_kernel_spmd(nc, in_maps, core_ids=list(range(B)))
    return np.stack([np.asarray(res.results[b]["y"], np.float32) for b in range(B)])
